# version 01
# baseline (speedup 1.0000x reference)
import numpy as np
import ml_dtypes
from contextlib import ExitStack
import concourse.bass as bass
import concourse.mybir as mybir
from concourse.bass_utils import run_bass_kernel_spmd

F32 = mybir.dt.float32
BF16 = mybir.dt.bfloat16
AF = mybir.ActivationFunctionType
ALU = mybir.AluOpType
AX = mybir.AxisListType

D = 2048
T = 2048
NDC = 16
NTC = 16
NW = 4
WIN = 512
NH = 16
DH = 128
DF = 8192
NFC = 64
KSEL = 1024
NE = 8
FMOE = 1408
NJC = 11
GLA_C = 32
INV_SQRT_DH = float(1.0 / np.sqrt(128.0))
EXP_BIAS = -8.0

ENGS = ("sync", "scalar", "vector", "tensor", "gpsimd")



def pack_inputs(i):
    f = np.ascontiguousarray
    p = {}
    p["wq0"] = f(i["L0_attn_W_Q"].reshape(16, 128, 16, 128).transpose(0, 3, 2, 1))
    p["wk0"] = f(i["L0_attn_W_K"].reshape(16, 128, 16, 128).transpose(0, 3, 2, 1))
    p["wv0"] = f(i["L0_attn_W_V"].reshape(16, 128, 16, 128).transpose(0, 3, 2, 1))
    p["wo0"] = f(i["L0_attn_W_O"].reshape(16, 16, 128, 128).transpose(0, 3, 1, 2))
    p["w1p"] = f(i["L0_ffn_w1"].reshape(16, 128, 64, 128).transpose(2, 1, 0, 3))
    p["w2p"] = f(i["L0_ffn_w2"].reshape(64, 128, 2048))
    p["b1p"] = f(i["L0_ffn_b1"].reshape(64, 128))
    p["b2p"] = f(i["L0_ffn_b2"].reshape(16, 128))
    p["wrp"] = f(i["L0_ffn_w_r"].reshape(16, 128))
    for nm, k in (("wq1", "L1_attn_Wq"), ("wk1", "L1_attn_Wk"),
                  ("wv1", "L1_attn_Wv"), ("wg1", "L1_attn_Wg")):
        p[nm] = f(i[k].reshape(16, 128, 16, 128).transpose(0, 3, 2, 1))
    p["wo1"] = f(np.ascontiguousarray(i["L1_attn_Wo"].T).reshape(16, 128, 2048))
    p["gmp"] = f(i["L1_attn_gamma"].reshape(16, 128))
    p["btp"] = f(i["L1_attn_beta"].reshape(16, 128))
    p["gwp"] = f(np.ascontiguousarray(i["L1_ffn_gate_weight"].T)
                 .reshape(16, 128, 8).transpose(1, 0, 2))
    gu = i["L1_ffn_gate_up_weights"]
    p["gup"] = f(gu.reshape(8, 22, 128, 16, 128).transpose(0, 1, 4, 3, 2))
    sgu = i["L1_ffn_shared_gate_up"]
    sa = np.concatenate([sgu[0:1408], sgu[2816:4224]], axis=0)
    sb_ = np.concatenate([sgu[1408:2816], sgu[4224:5632]], axis=0)
    p["gsa"] = f(sa.reshape(22, 128, 16, 128).transpose(0, 3, 2, 1))
    p["gsb"] = f(sb_.reshape(22, 128, 16, 128).transpose(0, 3, 2, 1))
    dn = i["L1_ffn_down_weights"]
    p["dnp"] = f(dn.transpose(0, 2, 1).reshape(8, 11, 128, 2048).transpose(0, 2, 1, 3))
    sd = np.ascontiguousarray(i["L1_ffn_shared_down"].T)
    p["dsa"] = f(sd[0:1408].reshape(11, 128, 2048).transpose(1, 0, 2))
    p["dsb"] = f(sd[1408:2816].reshape(11, 128, 2048).transpose(1, 0, 2))
    return p



class SemC:
    def __init__(self, h):
        self.h = h
        self.v = 0

    def inc(self, k=1):
        self.v += k
        return self.v


class Ph:

    def __init__(self, bld):
        self.bld = bld
        self.nc = bld.nc
        self.q = {n: [] for n in ENGS}

    def add(self, en, fn):
        self.q[en].append(fn)

    def op(self, en, fn, waits=(), inc=None):
        tgt = inc[0].inc(inc[1]) if inc is not None else None

        def g(eng, fn=fn, waits=tuple(waits), inc=inc):
            for s, v in waits:
                if v > 0:
                    eng.wait_ge(s.h, v)
            ins = fn(eng)
            if inc is not None:
                ins.then_inc(inc[0].h, inc[1])
        self.add(en, g)
        return tgt

    def dma(self, en, out, in_, waits=(), sem=None, **kw):
        if isinstance(sem, str):
            sem = self.bld.dsem(sem)
        if sem is None:
            sem = {"sync": self.bld.qs, "scalar": self.bld.qc, "gpsimd": self.bld.qg}[en]
        tgt = self.op(en, lambda e: e.dma_start(out=out, in_=in_, **kw),
                      waits=waits, inc=(sem, 16))
        return (sem, tgt)

    def wdma(self, *a, **kw):
        return self.dma(*a, **kw)

    def fence(self):
        b = self.bld
        qtgts = [(b.qs, b.qs.v), (b.qc, b.qc.v), (b.qg, b.qg.v)]
        qtgts += [(s, s.v) for s in b.dsems.values()]
        tgt = b.bar.v + 5
        b.bar.v = tgt
        for en in ENGS:
            def g(eng, tgt=tgt, qtgts=tuple(qtgts)):
                eng.sem_inc(b.bar.h, 1)
                eng.wait_ge(b.bar.h, tgt)
                for s, v in qtgts:
                    if v > 0:
                        eng.wait_ge(s.h, v)
            self.add(en, g)

    def run(self):
        with self.nc.Block() as blk:
            for en in ENGS:
                fns = self.q[en]
                if fns:
                    getattr(blk, en)(lambda eng, fns=fns: [fn(eng) for fn in fns])


def rr(ap, spec, **kw):
    return ap.rearrange(spec, **kw)


class Build:
    def __init__(self):
        self.nc = bass.Bass("TRN2", target_bir_lowering=False, debug=False)
        self.inp = {}
        self.estack = ExitStack()
        nc = self.nc
        s = lambda n: SemC(self.estack.enter_context(nc.semaphore(n)))
        self.qs, self.qc, self.qg = s("qs"), s("qc"), s("qg")
        self.sv, self.ss, self.st, self.sg = s("sv"), s("ss"), s("st"), s("sg")
        self.bar = s("bar")
        self._mksem = s
        self.dsems = {}

    def dsem(self, name):
        if name not in self.dsems:
            self.dsems[name] = self._mksem("d_" + name)
        return self.dsems[name]

    def din(self, name, shape, dtype=F32):
        ap = self.nc.dram_tensor(name, list(shape), dtype, kind="ExternalInput").ap()
        self.inp[name] = ap
        return ap

    def dout(self, name, shape, dtype=F32):
        return self.nc.dram_tensor(name, list(shape), dtype, kind="ExternalOutput").ap()

    def dtmp(self, name, shape, dtype=F32):
        return self.nc.dram_tensor(name, list(shape), dtype, kind="Internal").ap()

    def sb(self, es, name, shape, dtype):
        return es.enter_context(self.nc.sbuf_tensor(name, list(shape), dtype))

    def ps(self, es, name, shape=(128, 512), dtype=F32):
        return es.enter_context(self.nc.psum_tensor(name, list(shape), dtype))

    def emit_consts(self):
        es = self.estack
        self.ident_f = self.sb(es, "ident_f", [128, 128], F32)
        self.ident_b = self.sb(es, "ident_b", [128, 128], BF16)
        self.ones_col_f = self.sb(es, "ones_col_f", [128, 1], F32)
        self.ones_col_b = self.sb(es, "ones_col_b", [128, 1], BF16)
        self.ones_row_f = self.sb(es, "ones_row_f", [1, 128], F32)
        self.bdtril = self.sb(es, "bdtril", [128, 128], BF16)
        self.scanmask = self.sb(es, "scanmask", [128, WIN], BF16)
        self.fc = self.sb(es, "fconsts", [128, 8], F32)
        ph = Ph(self)

        def mk(_):
            P = self._pool_ops = []
            for t, fill in ((self.ident_f, 1.0), (self.ident_b, 1.0)):
                P.append(lambda e, t=t: e.memset(t[:], 0.0))
                P.append(lambda e, t=t, fill=fill: e.affine_select(
                    out=t[:], in_=t[:], compare_op=ALU.not_equal, fill=fill,
                    base=0, pattern=[[-1, 128]], channel_multiplier=1))
            P.append(lambda e: e.memset(self.ones_col_f[:], 1.0))
            P.append(lambda e: e.memset(self.ones_col_b[:], 1.0))
            P.append(lambda e: e.memset(self.ones_row_f[:], 1.0))
            P.append(lambda e: e.memset(self.fc[:, 0:1], 1e-5))
            P.append(lambda e: e.memset(self.fc[:, 1:2], EXP_BIAS))
            P.append(lambda e: e.memset(self.fc[:, 2:3], 1.0))
            P.append(lambda e: e.memset(self.bdtril[:], 0.0))
            for b32 in range(4):
                sub = self.bdtril[:, b32 * 32:(b32 + 1) * 32]
                C = b32 * 32
                P.append(lambda e, sub=sub, C=C: e.affine_select(
                    out=sub, in_=sub, compare_op=ALU.is_ge, fill=1.0,
                    base=C - 1, pattern=[[0, 32]], channel_multiplier=-1))
                P.append(lambda e, sub=sub, C=C: e.affine_select(
                    out=sub, in_=sub, compare_op=ALU.is_ge, fill=0.0,
                    base=C, pattern=[[1, 32]], channel_multiplier=-1))
            P.append(lambda e: e.memset(self.scanmask[:], 1.0))
            P.append(lambda e: e.memset(self.scanmask[:, 0:WIN:GLA_C], 0.0))
        def mk_chained(e):
            self._pool_ops = []
            mk(None)
            for f in self._pool_ops:
                tgt = self.sg.inc(1)
                f(e).then_inc(self.sg.h, 1)
                e.wait_ge(self.sg.h, tgt)
        ph.op("gpsimd", mk_chained)
        ph.fence()
        ph.run()

    def emit_transpose_in(self, tag, x_tok, xT_dram):
        es = ExitStack()
        ph = Ph(self)
        st, sv, qs = self.st, self.sv, self.qs
        xin = [self.sb(es, f"ti_x{i}_{tag}", [128, D], F32) for i in range(2)]
        stg = [self.sb(es, f"ti_s{i}_{tag}", [128, D], F32) for i in range(2)]
        pt = [self.ps(es, f"ti_p{i}_{tag}", (128, 1024), F32) for i in range(2)]
        evac_done = [0, 0]
        store_done = [(qs, 0), (qs, 0)]
        xin_war = [0, 0]
        for tc in range(NTC):
            r = tc % 2
            wld = ph.dma("sync", xin[r][:], x_tok[tc * 128:(tc + 1) * 128, :],
                         waits=[(st, xin_war[r])], sem=f"ti{r}_{tag}")
            for half in range(2):
                for dc8 in range(8):
                    dc = half * 8 + dc8
                    w0 = []
                    if dc == 0:
                        w0.append(wld)
                    if dc8 == 0:
                        w0.append((sv, evac_done[r]))
                    ph.op("tensor", lambda e, r=r, dc=dc, dc8=dc8: e.transpose(
                        pt[r][:, dc8 * 128:(dc8 + 1) * 128],
                        xin[r][:, dc * 128:(dc + 1) * 128], self.ident_f[:]),
                        waits=w0,
                        inc=(st, 1) if dc8 == 7 else None)
                tmm = st.v
                evac_done[r] = ph.op("vector", lambda e, r=r, half=half: e.tensor_copy(
                    stg[r][:, half * 1024:(half + 1) * 1024], pt[r][:]),
                    waits=[(st, tmm), store_done[r]], inc=(sv, 1))
            xin_war[r] = st.v
            store_done[r] = ph.dma(
                "sync",
                rr(xT_dram, "(dc p) t -> p dc t", p=128)[:, :, tc * 128:(tc + 1) * 128],
                rr(stg[r][:], "p (dc t) -> p dc t", dc=16),
                waits=[(sv, evac_done[r])], sem=f"tis{r}_{tag}")
        ph.fence()
        ph.run()
        es.close()

    def emit_ln(self, tag, xT_dram, biga, router=False, gate=False, src_bf16=None,
                gb=False, out_dram=None):
        sv, ss, st, qs = self.sv, self.ss, self.st, self.qs
        if router:
            self.maskrow = self.sb(getattr(self, "bstack", None) or self.estack,
                                   "maskrow", [1, T], F32)
            self.rowtmp = self.dtmp("rowtmp", [16, 128], F32)
        if gate:
            self.gl_dram = self.dtmp("gl_dram", [8, T], F32)
        es = ExitStack()
        ph = Ph(self)
        rows = self.sb(es, f"ln_rows_{tag}", [1, 4 * T], F32)
        r_sumx = rows[0:1, 0:T]
        r_sumsq = rows[0:1, T:2 * T]
        r_mu = rows[0:1, 2 * T:3 * T]
        r_rs = rows[0:1, 3 * T:4 * T]
        xt = ([self.sb(es, f"ln_x{i}_{tag}", [128, T], F32) for i in range(2)]
              if src_bf16 is None else None)
        sq = self.sb(es, f"ln_sq_{tag}", [128, T], F32)
        mu_bc = self.sb(es, f"ln_mu_{tag}", [128, T], F32)
        rs_bc = self.sb(es, f"ln_rs_{tag}", [128, T], F32)
        t1 = self.sb(es, f"ln_t1_{tag}", [128, T], F32)
        t2 = sq
        if out_dram is not None:
            ostg = [self.sb(es, f"ln_os0_{tag}", [128, T], BF16)]
        if gb:
            gm_sb = self.sb(es, f"ln_gm_{tag}", [128, 16], F32)
            bt_sb = self.sb(es, f"ln_bt_{tag}", [128, 16], F32)
            g16 = self.sb(es, f"ln_g16_{tag}", [16, 128], F32)
        ps4 = self.ps(es, f"ln_ps_{tag}", (128, 2048), F32)

        xt_war = [[], []]
        gb_done = 0
        if gb:
            for nm, dst in (("gmp", gm_sb), ("btp", bt_sb)):
                wld = ph.dma("sync", g16[:], self.inp[nm], sem=f"lng16_{tag}",
                             waits=[(st, self.st.v)])
                tmm = ph.op("tensor", lambda e, dst=dst: e.transpose(
                    ps4[0:128, 0:16], g16[:], self.ident_f[0:16, 0:16]),
                    waits=[wld, (sv, gb_done)], inc=(st, 1))
                gb_done = ph.op("vector", lambda e, dst=dst: e.tensor_copy(
                    dst[:], ps4[0:128, 0:16]), waits=[(st, tmm)], inc=(sv, 1))

        def src_tile(dc):
            if src_bf16 is not None:
                return src_bf16[:, dc * T:(dc + 1) * T], []
            b = xt[dc % 2]
            wld = ph.dma("sync", b[:], xT_dram[dc * 128:(dc + 1) * 128, :],
                         waits=list(xt_war[dc % 2]), sem=f"lnx{dc % 2}_{tag}")
            return b[:], [wld]

        ones = self.ones_col_b if src_bf16 is not None else self.ones_col_f

        for dc in range(NDC):
            bap, w_ld = src_tile(dc)
            for w in range(NW):
                ph.op("tensor", lambda e, bap=bap, w=w, dc=dc: e.matmul(
                    ps4[0:1, w * WIN:(w + 1) * WIN], ones[:],
                    bap[:, w * WIN:(w + 1) * WIN], start=(dc == 0), stop=(dc == NDC - 1)),
                    waits=(w_ld + ([(sv, gb_done)] if dc == 0 else [])) if w == 0 else (),
                    inc=(st, 1))
            if src_bf16 is None:
                xt_war[dc % 2] = [(st, st.v)]
        v_sumx = ph.op("vector", lambda e: e.tensor_copy(r_sumx, ps4[0:1, 0:T]),
                       waits=[(st, st.v)], inc=(sv, 1))
        sq_war = 0
        for dc in range(NDC):
            bap, w_ld = src_tile(dc)
            sqd = ph.op("scalar", lambda e, bap=bap: e.activation(
                out=sq[:], in_=bap, func=AF.Square),
                waits=w_ld + [(st, sq_war)], inc=(ss, 1))
            for w in range(NW):
                ph.op("tensor", lambda e, w=w, dc=dc: e.matmul(
                    ps4[0:1, w * WIN:(w + 1) * WIN], self.ones_col_f[:],
                    sq[:, w * WIN:(w + 1) * WIN], start=(dc == 0), stop=(dc == NDC - 1)),
                    waits=[(ss, sqd), (sv, v_sumx)] if w == 0 else (),
                    inc=(st, 1))
            sq_war = st.v
            if src_bf16 is None:
                xt_war[dc % 2] = [(ss, sqd)]
        v_sums = ph.op("vector", lambda e: e.tensor_copy(r_sumsq, ps4[0:1, 0:T]),
                       waits=[(st, st.v)], inc=(sv, 1))
        ph.op("scalar", lambda e: e.activation(out=r_mu, in_=r_sumx, func=AF.Copy,
                                               scale=1.0 / D), waits=[(sv, v_sums)])
        ph.op("scalar", lambda e: e.activation(out=r_rs, in_=r_sumsq, func=AF.Copy,
                                               scale=1.0 / D))
        ph.op("scalar", lambda e: e.activation(out=r_sumx, in_=r_mu, func=AF.Square),
              inc=(ss, 1))
        s_mu2 = ss.v
        v_var = ph.op("vector", lambda e: e.tensor_sub(r_sumsq, r_rs, r_sumx),
                      waits=[(ss, s_mu2)], inc=(sv, 1))
        ph.op("scalar", lambda e: e.activation(out=r_sumx, in_=r_sumsq, func=AF.Ln,
                                               bias=self.fc[0:1, 0:1]),
              waits=[(sv, v_var)])
        s_rows = ph.op("scalar", lambda e: e.activation(out=r_rs, in_=r_sumx, func=AF.Exp,
                                                        scale=-0.5), inc=(ss, 1))
        v_bc = v_sums
        for (row, dstb) in ((r_mu, mu_bc), (r_rs, rs_bc)):
            for w in range(NW):
                tmm = ph.op("tensor", lambda e, row=row, w=w: e.matmul(
                    ps4[:, 0:WIN], self.ones_row_f[:], row[0:1, w * WIN:(w + 1) * WIN],
                    start=True, stop=True),
                    waits=[(ss, s_rows), (sv, v_bc)], inc=(st, 1))
                v_bc = ph.op("vector", lambda e, dstb=dstb, w=w: e.tensor_copy(
                    dstb[:, w * WIN:(w + 1) * WIN], ps4[:, 0:WIN]),
                    waits=[(st, tmm)], inc=(sv, 1))

        if router:
            wr_sb = self.sb(es, "wr_sb", [128, 16], F32)
            m16 = self.sb(es, "m16", [16, 128], F32)
            wld = ph.dma("sync", m16[:], self.inp["wrp"], sem=f"lnr1_{tag}")
            tmm = ph.op("tensor", lambda e: e.transpose(
                ps4[0:128, 0:16], m16[:], self.ident_f[0:16, 0:16]),
                waits=[wld, (sv, v_bc)], inc=(st, 1))
            v_wr = ph.op("vector", lambda e: e.tensor_copy(wr_sb[:], ps4[0:128, 0:16]),
                         waits=[(st, tmm)], inc=(sv, 1))
        if gate:
            glrow_sb = self.sb(es, "glrow_sb", [8, T], F32)
            gw_sb = self.sb(es, "gw_sb", [128, 16 * 8], F32)
            gwld = ph.dma("sync", gw_sb[:], rr(self.inp["gwp"], "p dc e -> p (dc e)"),
                          sem=f"lnr2_{tag}")

        self._ln_ost_war = [[], []]
        rtr_war = self.st.v if (router or gate) else 0
        for dc in range(NDC):
            bap, w_ld = src_tile(dc)
            ph.op("vector", lambda e, bap=bap: e.tensor_sub(t1[:], bap, mu_bc[:]),
                  waits=w_ld + [(sv, v_bc), (st, rtr_war)], inc=(sv, 1))
            if src_bf16 is None:
                xt_war[dc % 2] = [(sv, sv.v)]
            nrm = ph.op("vector", lambda e: e.tensor_mul(t2[:], t1[:], rs_bc[:]),
                        inc=(sv, 1))
            if out_dram is not None:
                ro = 0
                vo = ph.op("vector", lambda e, dc=dc, ro=ro: e.tensor_scalar(
                    out=ostg[ro][:], in0=t2[:],
                    scalar1=gm_sb[:, dc:dc + 1], scalar2=bt_sb[:, dc:dc + 1],
                    op0=ALU.mult, op1=ALU.add) if gb else e.tensor_copy(ostg[ro][:], t2[:]),
                    waits=self._ln_ost_war[ro], inc=(sv, 1))
                osw = ph.dma("sync", out_dram[dc * 128:(dc + 1) * 128, :], ostg[ro][:],
                             waits=[(sv, vo)], sem=f"lnos_{tag}")
                self._ln_ost_war[ro] = [osw]
            elif gb:
                ph.op("vector", lambda e, dc=dc: e.tensor_scalar(
                    out=biga[:, dc * T:(dc + 1) * T], in0=t2[:],
                    scalar1=gm_sb[:, dc:dc + 1], scalar2=bt_sb[:, dc:dc + 1],
                    op0=ALU.mult, op1=ALU.add), inc=(sv, 1))
            else:
                ph.op("vector", lambda e, dc=dc: e.tensor_copy(
                    biga[:, dc * T:(dc + 1) * T], t2[:]), inc=(sv, 1))
            if router:
                for w in range(NW):
                    ph.op("tensor", lambda e, w=w, dc=dc: e.matmul(
                        ps4[0:1, w * WIN:(w + 1) * WIN], wr_sb[:, dc:dc + 1],
                        t2[:, w * WIN:(w + 1) * WIN],
                        start=(dc == 0), stop=(dc == NDC - 1)),
                        waits=[(sv, nrm), (sv, v_wr)] if w == 0 else (),
                        inc=(st, 1) if w == NW - 1 else None)
                rtr_war = st.v
            if gate:
                for w in range(NW):
                    ph.op("tensor", lambda e, w=w, dc=dc: e.matmul(
                        ps4[0:8, w * WIN:(w + 1) * WIN], gw_sb[:, dc * 8:(dc + 1) * 8],
                        t2[:, w * WIN:(w + 1) * WIN],
                        start=(dc == 0), stop=(dc == NDC - 1)),
                        waits=([(sv, nrm)] + ([gwld] if dc == 0 else [])) if w == 0 else (),
                        inc=(st, 1) if w == NW - 1 else None)
                rtr_war = st.v

        if gate:
            vgl = ph.op("vector", lambda e: e.tensor_copy(glrow_sb[:], ps4[0:8, 0:T]),
                        waits=[(st, st.v)], inc=(sv, 1))
            ph.dma("sync", self.gl_dram, glrow_sb[:], waits=[(sv, vgl)])

        if getattr(self, "dbg_ln", None):
            d_rows = self.dbg_ln["rows"]
            d_mu = self.dbg_ln["mu"]
            d_rs = self.dbg_ln["rs"]
            ph.dma("sync", d_rows, rows[:], waits=[(self.ss, self.ss.v), (self.sv, self.sv.v)])
            ph.dma("sync", d_mu, mu_bc[:], waits=[(self.sv, v_bc)])
            ph.dma("sync", d_rs, rs_bc[:], waits=[(self.sv, v_bc)])

        if router:
            sc_bc = mu_bc
            sc_col = self.sb(es, "sc_col", [128, 16], F32)
            rank_col = self.sb(es, "rank_col", [128, 16], F32)
            mask_col = self.sb(es, "mask_col", [128, 16], F32)
            scr_row = r_sumx
            v_scrow = 0
            for w in range(NW):
                v_scrow = ph.op("vector", lambda e, w=w: e.tensor_copy(
                    scr_row[0:1, w * WIN:(w + 1) * WIN], ps4[0:1, w * WIN:(w + 1) * WIN]),
                    waits=[(st, st.v)] if w == 0 else (), inc=(sv, 1))
            v_bc2 = 0
            for w in range(NW):
                tmm = ph.op("tensor", lambda e, w=w: e.matmul(
                    ps4[:, 0:WIN], self.ones_row_f[:], scr_row[0:1, w * WIN:(w + 1) * WIN],
                    start=True, stop=True), waits=[(sv, v_scrow), (sv, v_bc2)],
                    inc=(st, 1))
                v_bc2 = ph.op("vector", lambda e, w=w: e.tensor_copy(
                    sc_bc[:, w * WIN:(w + 1) * WIN], ps4[:, 0:WIN]),
                    waits=[(st, tmm)], inc=(sv, 1))
            wst = ph.dma("sync", rr(self.rowtmp, "a t -> (a t)").rearrange("(a t) -> a t", a=1),
                         scr_row, waits=[(sv, v_scrow)], sem=f"lnr3_{tag}")
            wld = ph.dma("sync", m16[:], self.rowtmp, waits=[wst], sem=f"lnr4_{tag}")
            tmm = ph.op("tensor", lambda e: e.transpose(
                ps4[0:128, 0:16], m16[:], self.ident_f[0:16, 0:16]),
                waits=[wld, (sv, v_bc2)], inc=(st, 1))
            v_col = ph.op("vector", lambda e: e.tensor_copy(sc_col[:], ps4[0:128, 0:16]),
                          waits=[(st, tmm)], inc=(sv, 1))
            for tc in range(NTC):
                ph.op("vector", lambda e, tc=tc: e.tensor_scalar(
                    out=t1[:], in0=sc_bc[:], scalar1=sc_col[:, tc:tc + 1], scalar2=None,
                    op0=ALU.is_gt, op1=ALU.add, accum_out=rank_col[:, tc:tc + 1]),
                    waits=[(sv, v_col)] if tc == 0 else ())
            ph.op("vector", lambda e: e.tensor_copy(t1[:, 0:128], sc_bc[:, 0:128]))
            v_rank = ph.op("vector", lambda e: e.tensor_scalar(
                out=mask_col[:], in0=rank_col[:], scalar1=float(KSEL), scalar2=None,
                op0=ALU.is_lt), inc=(sv, 1))
            tmm = ph.op("tensor", lambda e: e.transpose(
                ps4[0:16, 0:128], mask_col[:], self.ident_f[:]),
                waits=[(sv, v_rank)], inc=(st, 1))
            v_m16 = ph.op("vector", lambda e: e.tensor_copy(m16[:], ps4[0:16, 0:128]),
                          waits=[(st, tmm)], inc=(sv, 1))
            wst = ph.dma("sync", self.rowtmp, m16[:], waits=[(sv, v_m16)], sem=f"lnr5_{tag}")
            ph.dma("sync", self.maskrow[:],
                   rr(self.rowtmp, "a t -> (a t)").rearrange("(a t) -> a t", a=1),
                   waits=[wst], sem=f"lnr6_{tag}")
            if getattr(self, "dbg_ln", None):
                ph.dma("sync", self.dbg_ln["sc_col"], sc_col[:], waits=[(sv, v_col)])
                ph.dma("sync", self.dbg_ln["rank"], rank_col[:], waits=[(sv, v_rank)])
                ph.dma("sync", self.dbg_ln["sc_bc"], sc_bc[:], waits=[(sv, v_bc2)])
                ph.dma("sync", self.dbg_ln["mask_col"], mask_col[:], waits=[(sv, v_rank)])
                ph.dma("sync", self.dbg_ln["m16"], m16[:], waits=[(sv, v_m16)])
        ph.fence()
        ph.run()
        es.close()

    def emit_attn0(self, biga, z_sb):
        sv, ss, st, qg = self.sv, self.ss, self.st, self.qg
        es = ExitStack()
        ph = Ph(self)
        q_sb = self.sb(es, "a0_q", [128, T], BF16)
        self.cmask = self.sb(es, "a0_cmask", [128, 4 * WIN], BF16)
        def _mkcmask(e):
            tgt = self.sg.inc(1)
            e.memset(self.cmask[:], 0.0).then_inc(self.sg.h, 1)
            e.wait_ge(self.sg.h, tgt)
            for dlt in range(4):
                sub = self.cmask[:, dlt * WIN:(dlt + 1) * WIN]
                tgt = self.sg.inc(1)
                e.affine_select(out=sub, in_=sub, compare_op=ALU.is_ge, fill=-1.0e5,
                                base=-128 * dlt, pattern=[[1, WIN]],
                                channel_multiplier=-1).then_inc(self.sg.h, 1)
                e.wait_ge(self.sg.h, tgt)
        ph.op("gpsimd", _mkcmask)
        cmask_rdy = self.sg.v
        k_sb = self.sb(es, "a0_k", [128, T], BF16)
        v_sb = self.sb(es, "a0_v", [128, T], BF16)
        pt_sb = self.sb(es, "a0_pt", [128, 16 * WIN], BF16)
        dtmp = self.sb(es, "a0_dt", [128, WIN], F32)
        linv_bc = self.sb(es, "a0_li", [128, WIN], F32)
        lrow = self.sb(es, "a0_lr", [1, WIN], F32)
        wring = [[self.sb(es, f"a0_w{j}{r}", [128, 2048], BF16) for j in range(3)]
                 for r in range(2)]
        ps_a = self.ps(es, "a0_pa")
        ps_b = self.ps(es, "a0_pb")
        ps_z = self.ps(es, "a0_pz")
        ps_l = self.ps(es, "a0_pl")
        ps_c = self.ps(es, "a0_pc")
        ps_v = self.ps(es, "a0_pv")
        exp_done = [0, 0]
        l_war = 0
        z_war = 0
        dtmp_war = 0

        wnames = ("wq0", "wk0", "wv0")
        wld = [(qg, 0), (qg, 0)]
        wfree = [[], []]
        for j, nm in enumerate(wnames):
            wld[0] = ph.dma("gpsimd", wring[0][j][:],
                            rr(self.inp[nm], "h p dc e -> h p (dc e)")[0], sem="a0w0")
        evac_pp = [0, 0]
        pp_war = [[], []]
        for h in range(NH):
            r = h % 2
            wq, wk, wv = wring[r]
            if h + 1 < NH:
                for j, nm in enumerate(wnames):
                    wld[1 - r] = ph.dma("gpsimd", wring[1 - r][j][:],
                                        rr(self.inp[nm], "h p dc e -> h p (dc e)")[h + 1],
                                        waits=wfree[1 - r] if j == 0 else (),
                                        sem=f"a0w{1 - r}")
            pp = (ps_a, ps_b)
            grp = 0
            proj_done_q = proj_done_k = proj_done_v = None
            for (wt, dst, eng) in ((wq, q_sb, "scalar"), (wk, k_sb, "vector")):
                for w in range(NW):
                    bank = pp[grp % 2]
                    for dc in range(NDC):
                        ph.op("tensor", lambda e, wt=wt, dc=dc, w=w, bank=bank: e.matmul(
                            bank[:, 0:WIN], wt[:, dc * 128:(dc + 1) * 128],
                            biga[:, dc * T + w * WIN: dc * T + (w + 1) * WIN],
                            start=(dc == 0), stop=(dc == NDC - 1)),
                            waits=([wld[r]] + pp_war[grp % 2]) if dc == 0 else (),
                            inc=(st, 1) if dc == NDC - 1 else None)
                    mmv = st.v
                    if eng == "scalar":
                        tgt = ph.op("scalar", lambda e, dst=dst, w=w, bank=bank: e.activation(
                            out=dst[:, w * WIN:(w + 1) * WIN], in_=bank[:, 0:WIN],
                            func=AF.Copy), waits=[(ss, 0), (st, mmv)], inc=(ss, 1))
                        pp_war[grp % 2] = [(ss, tgt)]
                    else:
                        tgt = ph.op("vector", lambda e, dst=dst, w=w, bank=bank: e.tensor_copy(
                            dst[:, w * WIN:(w + 1) * WIN], bank[:, 0:WIN]),
                            waits=[(st, mmv)], inc=(sv, 1))
                        pp_war[grp % 2] = [(sv, tgt)]
                    grp += 1
                if eng == "scalar":
                    proj_done_q = tgt
                else:
                    proj_done_k = tgt
            v_war = []
            for tc in range(NTC):
                for dc in range(NDC):
                    ph.op("tensor", lambda e, tc=tc, dc=dc, wv=wv: e.matmul(
                        ps_v[:, 0:128],
                        biga[:, dc * T + tc * 128: dc * T + (tc + 1) * 128],
                        wv[:, dc * 128:(dc + 1) * 128],
                        start=(dc == 0), stop=(dc == NDC - 1)),
                        waits=v_war if dc == 0 else (),
                        inc=(st, 1) if dc == NDC - 1 else None)
                mmv = st.v
                vv = ph.op("vector", lambda e, tc=tc: e.tensor_copy(
                    v_sb[:, tc * 128:(tc + 1) * 128], ps_v[:, 0:128]),
                    waits=[(st, mmv)], inc=(sv, 1))
                v_war = [(sv, vv)]
            proj_done_v = vv
            wfree[r] = []
            pvmm = 0
            for w in range(NW):
                nkc = 4 * w + 4
                for kc in range(nkc):
                    bank = (ps_a, ps_b)[kc % 2]
                    wts = [(ss, proj_done_q), (sv, proj_done_k), (ss, exp_done[kc % 2])]
                    ph.op("tensor", lambda e, kc=kc, w=w, bank=bank: e.matmul(
                        bank[:, 0:WIN], k_sb[:, kc * 128:(kc + 1) * 128],
                        q_sb[:, w * WIN:(w + 1) * WIN], start=True, stop=True),
                        waits=wts, inc=(st, 1))
                    smm = st.v
                    diag = kc >= 4 * w
                    if diag:
                        dlt = kc - 4 * w
                        vv = ph.op("vector", lambda e, bank=bank, dlt=dlt: e.scalar_tensor_tensor(
                            out=dtmp[:], in0=bank[:, 0:WIN], scalar=1.0,
                            in1=self.cmask[:, dlt * WIN:(dlt + 1) * WIN],
                            op0=ALU.mult, op1=ALU.add),
                            waits=[(st, smm), (ss, dtmp_war), (self.sg, cmask_rdy)],
                            inc=(sv, 1))
                        src = dtmp
                        wts2 = [(sv, vv)]
                    else:
                        src = bank
                        wts2 = [(st, smm)]
                    ed = ph.op("scalar", lambda e, src=src, kc=kc: e.activation(
                        out=pt_sb[:, kc * WIN:(kc + 1) * WIN],
                        in_=(src[:, 0:WIN]), func=AF.Exp,
                        bias=self.fc[:, 1:2], scale=INV_SQRT_DH),
                        waits=wts2, inc=(ss, 1))
                    exp_done[kc % 2] = ed
                    if diag:
                        dtmp_war = ed
                    ph.op("tensor", lambda e, kc=kc: e.matmul(
                        ps_l[0:1, 0:WIN], self.ones_col_b[:],
                        pt_sb[:, kc * WIN:(kc + 1) * WIN],
                        start=(kc == 0), stop=(kc == nkc - 1)),
                        waits=[(ss, ed)] + ([(sv, l_war)] if kc == 0 else []))
                    ph.op("tensor", lambda e, kc=kc: e.matmul(
                        ps_z[:, 0:WIN], v_sb[:, kc * 128:(kc + 1) * 128],
                        pt_sb[:, kc * WIN:(kc + 1) * WIN],
                        start=(kc == 0), stop=(kc == nkc - 1)),
                        waits=[(sv, proj_done_v), (sv, z_war)] if kc == 0 else (),
                        inc=(st, 1) if kc == nkc - 1 else None)
                pvmm = st.v
                vr = ph.op("vector", lambda e: e.reciprocal(lrow[:], ps_l[0:1, 0:WIN]),
                           waits=[(st, pvmm)], inc=(sv, 1))
                l_war = vr
                bmm = ph.op("tensor", lambda e: e.matmul(
                    ps_c[:, 0:WIN], self.ones_row_f[:], lrow[:], start=True, stop=True),
                    waits=[(sv, vr)], inc=(st, 1))
                bev = ph.op("vector", lambda e: e.tensor_copy(linv_bc[:], ps_c[:, 0:WIN]),
                            waits=[(st, bmm)], inc=(sv, 1))
                z_war = ph.op("vector", lambda e, h=h, w=w: e.tensor_tensor(
                    out=z_sb[:, h * T + w * WIN: h * T + (w + 1) * WIN],
                    in0=ps_z[:, 0:WIN], in1=linv_bc[:], op=ALU.mult),
                    waits=[(sv, bev)], inc=(sv, 1))
            wfree[r] = [(st, pvmm)]
            pp_war[0] = pp_war[1] = [(st, pvmm)]
            if h == 0 and getattr(self, "dbg_attn", None):
                d = self.dbg_attn
                ph.dma("sync", d["q"], q_sb[:], waits=[(ss, proj_done_q)])
                ph.dma("sync", d["k"], k_sb[:], waits=[(sv, proj_done_k)])
                ph.dma("sync", d["v"], v_sb[:], waits=[(sv, proj_done_v)])
                ph.dma("sync", d["pt"], pt_sb[:], waits=[(st, pvmm)])
        ph.fence()
        ph.run()
        es.close()

    def emit_oproj0(self, z_sb, wo_area, x0T_dram, x1T_dram, accum):
        sv, ss, st, qs, qg = self.sv, self.ss, self.st, self.qs, self.qg
        es = ExitStack()
        ph = Ph(self)
        x0s = [self.sb(es, f"o0_x{i}", [128, T], F32) for i in range(2)]
        stage = [self.sb(es, f"o0_s{i}", [128, WIN], F32) for i in range(2)]
        ps_o = [self.ps(es, f"o0_p{i}") for i in range(2)]
        for h in range(NH):
            wo_ld = ph.dma("gpsimd", wo_area[:, h * 2048:(h + 1) * 2048],
                           rr(self.inp["wo0"], "h p dc e -> h p (dc e)")[h], sem="o0wo")
        x0_war = [[], []]
        stage_war = [[], []]
        evac = [0, 0]
        for dc in range(NDC):
            r = dc % 2
            xw = ph.dma("sync", x0s[r][:], x0T_dram[dc * 128:(dc + 1) * 128, :],
                        waits=x0_war[r], sem=f"o0x{r}")
            for w in range(NW):
                g = (dc * NW + w) % 2
                for h in range(NH):
                    ph.op("tensor", lambda e, h=h, dc=dc, w=w, g=g: e.matmul(
                        ps_o[g][:, 0:WIN],
                        wo_area[:, h * 2048 + dc * 128: h * 2048 + (dc + 1) * 128],
                        z_sb[:, h * T + w * WIN: h * T + (w + 1) * WIN],
                        start=(h == 0), stop=(h == NH - 1)),
                        waits=[wo_ld, (sv, evac[g])] if h == 0 else (),
                        inc=(st, 1) if h == NH - 1 else None)
                mmv = st.v
                ev = ph.op("vector", lambda e, r=r, w=w, g=g: e.tensor_tensor(
                    out=stage[g][:], in0=ps_o[g][:, 0:WIN],
                    in1=x0s[r][:, w * WIN:(w + 1) * WIN], op=ALU.add),
                    waits=[(st, mmv), xw] + stage_war[g],
                    inc=(sv, 1))
                evac[g] = ev
                dst = x1T_dram[dc * 128:(dc + 1) * 128, w * WIN:(w + 1) * WIN]
                sw = ph.dma("gpsimd" if accum else "sync", dst, stage[g][:],
                            waits=[(sv, ev)], sem=f"o0s{g}",
                            **({"accum_op": ALU.add} if accum else {}))
                stage_war[g] = [sw]
            x0_war[r] = [(sv, evac[0]), (sv, evac[1])]
        ph.fence()
        ph.run()
        es.close()

    def emit_ffn0(self, biga, h_dram, x1T_dram):
        sv, ss, st, qs, qg = self.sv, self.ss, self.st, self.qs, self.qg
        es = ExitStack()
        ph = Ph(self)
        w1t = [self.sb(es, f"f0_w1{i}", [128, 2048], BF16) for i in range(2)]
        hstg = [self.sb(es, f"f0_h{i}", [128, 2048], BF16) for i in range(2)]
        b1c = self.sb(es, "f0_b1c", [128, 64], F32)
        b2c = self.sb(es, "f0_b2c", [128, 16], F32)
        m16 = self.sb(es, "f0_m16", [64, 128], F32)
        mask_bc = self.sb(es, "f0_mbc", [128, T], F32)
        es_a = ExitStack()
        ps_h = self.ps(es_a, "f0_ph", (128, 1024), F32)
        ps_c = self.ps(es_a, "f0_pc")

        wld = ph.dma("sync", m16[:], self.inp["b1p"], sem="f0b")
        tmm = ph.op("tensor", lambda e: e.transpose(
            ps_c[0:128, 0:64], m16[:], self.ident_f[0:64, 0:64]),
            waits=[wld], inc=(st, 1))
        v1 = ph.op("vector", lambda e: e.tensor_copy(b1c[:], ps_c[0:128, 0:64]),
                   waits=[(st, tmm)], inc=(sv, 1))
        wld = ph.dma("sync", m16[0:16, :], self.inp["b2p"], sem="f0b",
                     waits=[(st, st.v)])
        tmm = ph.op("tensor", lambda e: e.transpose(
            ps_c[0:128, 0:16], m16[0:16, :], self.ident_f[0:16, 0:16]),
            waits=[wld, (sv, v1)], inc=(st, 1))
        v2 = ph.op("vector", lambda e: e.tensor_copy(b2c[:], ps_c[0:128, 0:16]),
                   waits=[(st, tmm)], inc=(sv, 1))
        v_mb = v2
        for w in range(NW):
            tmm = ph.op("tensor", lambda e, w=w: e.matmul(
                ps_c[:, 0:WIN], self.ones_row_f[:],
                self.maskrow[0:1, w * WIN:(w + 1) * WIN], start=True, stop=True),
                waits=[(sv, v_mb)], inc=(st, 1))
            v_mb = ph.op("vector", lambda e, w=w: e.tensor_copy(
                mask_bc[:, w * WIN:(w + 1) * WIN], ps_c[:, 0:WIN]),
                waits=[(st, tmm)], inc=(sv, 1))

        if getattr(self, "dbg_ffn", None):
            ph.dma("sync", self.dbg_ffn["mask_bc"], mask_bc[:], waits=[(sv, v_mb)])
            ph.dma("sync", self.dbg_ffn["maskrow"], self.maskrow[:])
        h_st = [(qs, 0)] * NFC
        w1ld = [(qg, 0), (qg, 0)]
        w1war = [[], []]
        w1ld[0] = ph.dma("gpsimd", w1t[0][:],
                         rr(self.inp["w1p"], "f p dc e -> f p (dc e)")[0], sem="f0w10")
        hst_war = [[], []]
        gel = [0, 0]
        for fc in range(NFC):
            r = fc % 2
            if fc + 1 < NFC:
                w1ld[1 - r] = ph.dma("gpsimd", w1t[1 - r][:],
                                     rr(self.inp["w1p"], "f p dc e -> f p (dc e)")[fc + 1],
                                     waits=w1war[1 - r], sem=f"f0w1{1 - r}")
            for w in range(NW):
                g = w % 2
                for dc in range(NDC):
                    ph.op("tensor", lambda e, fc=fc, w=w, dc=dc, r=r, g=g: e.matmul(
                        ps_h[:, g * WIN:(g + 1) * WIN],
                        w1t[r][:, dc * 128:(dc + 1) * 128],
                        biga[:, dc * T + w * WIN: dc * T + (w + 1) * WIN],
                        start=(dc == 0), stop=(dc == NDC - 1)),
                        waits=[w1ld[r], (ss, gel[g])] if dc == 0 else (),
                        inc=(st, 1) if dc == NDC - 1 else None)
                mmv = st.v
                gel[g] = ph.op("scalar", lambda e, fc=fc, w=w, r=r, g=g: e.activation(
                    out=hstg[r][:, w * WIN:(w + 1) * WIN], in_=ps_h[:, g * WIN:(g + 1) * WIN],
                    func=AF.Gelu_apprx_tanh, bias=b1c[:, fc:fc + 1]),
                    waits=[(st, mmv), (sv, v2)] + (hst_war[r] if w == 0 else []),
                    inc=(ss, 1))
            w1war[r] = [(st, st.v)]
            hw = ph.dma("sync", h_dram[fc], hstg[r][:], waits=[(ss, gel[0]), (ss, gel[1])],
                        sem=f"f0h{r}")
            h_st[fc] = hw
            hst_war[r] = [hw]

        ph.fence()
        ph.run()
        es_a.close()
        ph = Ph(self)
        h_all = [(self.dsem("f0h0"), self.dsem("f0h0").v),
                 (self.dsem("f0h1"), self.dsem("f0h1").v)]
        w2t = [self.sb(es, f"f0_w2{i}", [128, WIN], BF16) for i in range(3)]
        ht = [self.sb(es, f"f0_ht{i}", [128, 1024], BF16) for i in range(3)]
        stg = [self.sb(es, f"f0_s{i}", [128, WIN], F32) for i in range(2)]
        es_b = ExitStack()
        ps8 = self.ps(es_b, "f0_p8", (128, 4096), F32)
        self._f0_ev = 0
        w2war = [[] for _ in range(3)]
        htwar = [[] for _ in range(3)]
        stg_war = [[], []]
        for dq in range(4):
            for wp in range(2):
                evacs = st.v
                for fc in range(NFC):
                    r3 = fc % 3
                    w2ld = ph.dma("gpsimd", w2t[r3][:],
                                  self.inp["w2p"][fc][:, dq * WIN:(dq + 1) * WIN],
                                  waits=w2war[r3], sem=f"f0w2{r3}")
                    htld = ph.dma("sync", ht[r3][:],
                                  h_dram[fc][:, wp * 1024:(wp + 1) * 1024],
                                  waits=htwar[r3] + (h_all if fc < 3 else []),
                                  sem=f"f0ht{r3}")
                    for dcl in range(4):
                        for wl in range(2):
                            ph.op("tensor", lambda e, fc=fc, dcl=dcl, wl=wl, r3=r3: e.matmul(
                                ps8[:, (dcl * 2 + wl) * WIN:(dcl * 2 + wl + 1) * WIN],
                                w2t[r3][:, dcl * 128:(dcl + 1) * 128],
                                ht[r3][:, wl * WIN:(wl + 1) * WIN],
                                start=(fc == 0), stop=(fc == NFC - 1)),
                                waits=([w2ld, htld] +
                                       ([(sv, self._f0_ev)] if fc == 0 else []))
                                      if (dcl == 0 and wl == 0) else (),
                                inc=(st, 1) if (dcl == 3 and wl == 1) else None)
                    w2war[r3] = [(st, st.v)]
                    htwar[r3] = [(st, st.v)]
                mmv = st.v
                for dcl in range(4):
                    for wl in range(2):
                        dc = dq * 4 + dcl
                        w = wp * 2 + wl
                        g = (dcl * 2 + wl) % 2
                        ev = ph.op("vector", lambda e, dcl=dcl, wl=wl, dc=dc, w=w, g=g:
                                   e.scalar_tensor_tensor(
                                       out=stg[g][:],
                                       in0=ps8[:, (dcl * 2 + wl) * WIN:(dcl * 2 + wl + 1) * WIN],
                                       scalar=b2c[:, dc:dc + 1],
                                       in1=mask_bc[:, w * WIN:(w + 1) * WIN],
                                       op0=ALU.add, op1=ALU.mult),
                                   waits=[(st, mmv), (sv, v_mb)] + stg_war[g],
                                   inc=(sv, 1))
                        sw = ph.dma("sync",
                                    x1T_dram[dc * 128:(dc + 1) * 128, w * WIN:(w + 1) * WIN],
                                    stg[g][:], waits=[(sv, ev)], sem=f"f0s{g}")
                        stg_war[g] = [sw]
                self._f0_ev = sv.v
        ph.fence()
        ph.run()
        es_b.close()
        es.close()

    def emit_gla(self, biga, ybuf):
        sv, ss, st, qs, qg = self.sv, self.ss, self.st, self.qs, self.qg
        es = ExitStack()
        ph = Ph(self)
        wring = [[self.sb(es, f"g_w{j}{r}", [128, 2048], BF16) for j in range(4)]
                 for r in range(2)]
        g_sb = self.sb(es, "g_g", [128, T], F32)
        L_sb = self.sb(es, "g_L", [128, T], F32)
        sA = self.sb(es, "g_sA", [128, WIN], F32)
        sB = self.sb(es, "g_sB", [128, WIN], F32)
        sC = self.sb(es, "g_sC", [128, WIN], F32)
        ktf = self.sb(es, "g_ktf", [128, WIN], F32)
        eend = self.sb(es, "g_eend", [128, 64], F32)
        kpw = self.sb(es, "g_kpw", [128, WIN], BF16)
        qt = self.sb(es, "g_qt", [128, T], BF16)
        ktb = self.sb(es, "g_ktb", [128, T], BF16)
        kptok = self.sb(es, "g_kpt", [128, T], BF16)
        v_sb = self.sb(es, "g_v", [128, T], BF16)
        S_f = [self.sb(es, f"g_S{i}", [128, 128], F32) for i in range(2)]
        S_b = self.sb(es, "g_Sb", [128, 128], BF16)
        Mm = self.sb(es, "g_Mm", [128, 128], BF16)
        ps_a = self.ps(es, "g_pa")
        ps_b = self.ps(es, "g_pb")
        ps_v = self.ps(es, "g_pv")
        ps_t = self.ps(es, "g_pt", (128, 2048), BF16)
        ps_y = self.ps(es, "g_py")
        ps_s = self.ps(es, "g_ps")
        ps_o = self.ps(es, "g_po")

        self._gla_mm_war = 0
        self._gla_S_war = 0
        self._gla_sb_done = 0
        self._gla_qw_war = 0
        self._gla_em_war = 0
        self._gla_tp2 = 0
        wnames = ("wg1", "wq1", "wk1", "wv1")
        wld = [(qg, 0), (qg, 0)]
        wfree = [[], []]
        for j, nm in enumerate(wnames):
            wld[0] = ph.dma("gpsimd", wring[0][j][:],
                            rr(self.inp[nm], "h p dc e -> h p (dc e)")[0], sem="gw0")

        def eend_ap(w):
            hdl = eend[:].tensor
            return bass.AP(hdl, w * 16, [[64, 128], [1, 16], [0, 32]])

        pp_war = [[], []]
        for h in range(NH):
            r = h % 2
            wg, wq, wk, wv = wring[r]
            if h + 1 < NH:
                for j, nm in enumerate(wnames):
                    wld[1 - r] = ph.dma("gpsimd", wring[1 - r][j][:],
                                        rr(self.inp[nm], "h p dc e -> h p (dc e)")[h + 1],
                                        waits=wfree[1 - r] if j == 0 else (),
                                        sem=f"gw{1 - r}")
            scr_war = 0
            for w in range(NW):
                bank = (ps_a, ps_b)[w % 2]
                for dc in range(NDC):
                    ph.op("tensor", lambda e, dc=dc, w=w, wg=wg, bank=bank: e.matmul(
                        bank[:, 0:WIN], wg[:, dc * 128:(dc + 1) * 128],
                        biga[:, dc * T + w * WIN: dc * T + (w + 1) * WIN],
                        start=(dc == 0), stop=(dc == NDC - 1)),
                        waits=([wld[r]] + pp_war[w % 2]) if dc == 0 else (),
                        inc=(st, 1) if dc == NDC - 1 else None)
                mmv = st.v
                vzm = ph.op("vector", lambda e, bank=bank: e.tensor_scalar_min(
                    sA[:], bank[:, 0:WIN], 0.0),
                    waits=[(st, mmv), (ss, scr_war)], inc=(sv, 1))
                saz = ph.op("scalar", lambda e, bank=bank: e.activation(
                    out=sB[:], in_=bank[:, 0:WIN], func=AF.Abs),
                    waits=[(st, mmv), (sv, vzm)], inc=(ss, 1))
                pp_war[w % 2] = [(ss, saz)]
                ph.op("scalar", lambda e: e.activation(out=sC[:], in_=sB[:], func=AF.Exp,
                                                       scale=-1.0))
                slb = ph.op("scalar", lambda e: e.activation(
                    out=sB[:], in_=sC[:], func=AF.Ln, bias=self.fc[:, 2:3]), inc=(ss, 1))
                scr_war = slb
                ph.op("vector", lambda e, w=w: e.tensor_sub(
                    g_sb[:, w * WIN:(w + 1) * WIN], sA[:], sB[:]),
                    waits=[(ss, slb)], inc=(sv, 1))
            g_done = sv.v
            vL = 0
            for w in range(NW):
                vL = ph.op("vector", lambda e, w=w: e.tensor_tensor_scan(
                    out=L_sb[:, w * WIN:(w + 1) * WIN], data0=self.scanmask[:],
                    data1=g_sb[:, w * WIN:(w + 1) * WIN], initial=0.0,
                    op0=ALU.mult, op1=ALU.add),
                    waits=[(sv, g_done)] if w == 0 else (), inc=(sv, 1))
            sEe = ph.op("scalar", lambda e: e.activation(
                out=eend[:], in_=L_sb[:, 31:T:GLA_C], func=AF.Exp),
                waits=[(sv, vL)], inc=(ss, 1))
            kpt_done = 0
            tp_war = 0
            for w in range(NW):
                bank = (ps_a, ps_b)[w % 2]
                bank2 = (ps_a, ps_b)[(w + 1) % 2]
                for dc in range(NDC):
                    ph.op("tensor", lambda e, dc=dc, w=w, wq=wq, bank=bank: e.matmul(
                        bank[:, 0:WIN], wq[:, dc * 128:(dc + 1) * 128],
                        biga[:, dc * T + w * WIN: dc * T + (w + 1) * WIN],
                        start=(dc == 0), stop=(dc == NDC - 1)),
                        waits=pp_war[w % 2] if dc == 0 else (),
                        inc=(st, 1) if dc == NDC - 1 else None)
                mmq = st.v
                sEw = ph.op("scalar", lambda e, w=w: e.activation(
                    out=sA[:], in_=L_sb[:, w * WIN:(w + 1) * WIN], func=AF.Exp),
                    waits=[(ss, sEe), (sv, self._gla_qw_war)], inc=(ss, 1))
                vq = ph.op("vector", lambda e, w=w, bank=bank: e.scalar_tensor_tensor(
                    out=qt[:, w * WIN:(w + 1) * WIN], in0=bank[:, 0:WIN],
                    scalar=INV_SQRT_DH, in1=sA[:], op0=ALU.mult, op1=ALU.mult),
                    waits=[(st, mmq), (ss, sEw)], inc=(sv, 1))
                self._gla_qw_war = vq
                pp_war[w % 2] = [(sv, vq)]
                for dc in range(NDC):
                    ph.op("tensor", lambda e, dc=dc, w=w, wk=wk, bank2=bank2: e.matmul(
                        bank2[:, 0:WIN], wk[:, dc * 128:(dc + 1) * 128],
                        biga[:, dc * T + w * WIN: dc * T + (w + 1) * WIN],
                        start=(dc == 0), stop=(dc == NDC - 1)),
                        waits=pp_war[(w + 1) % 2] if dc == 0 else (),
                        inc=(st, 1) if dc == NDC - 1 else None)
                mmk = st.v
                sEm = ph.op("scalar", lambda e, w=w: e.activation(
                    out=sC[:], in_=L_sb[:, w * WIN:(w + 1) * WIN], func=AF.Exp,
                    scale=-1.0), waits=[(sv, self._gla_em_war)], inc=(ss, 1))
                vktf = ph.op("vector", lambda e, bank2=bank2: e.tensor_mul(
                    ktf[:], bank2[:, 0:WIN], sC[:]),
                    waits=[(st, mmk), (ss, sEm)], inc=(sv, 1))
                self._gla_em_war = vktf
                pp_war[(w + 1) % 2] = [(sv, vktf)]
                ph.op("vector", lambda e, w=w: e.tensor_copy(
                    ktb[:, w * WIN:(w + 1) * WIN], ktf[:]))
                vkp = ph.op("vector", lambda e, w=w: e.tensor_tensor(
                    out=kpw[:], in0=ktf[:], in1=eend_ap(w), op=ALU.mult),
                    waits=[(ss, sEe), (sv, tp_war)], inc=(sv, 1))
                for j4 in range(4):
                    tc = w * 4 + j4
                    g2 = j4 % 2
                    tmm = ph.op("tensor", lambda e, j4=j4, g2=g2: e.transpose(
                        ps_t[:, g2 * 1024:g2 * 1024 + 128], kpw[:, j4 * 128:(j4 + 1) * 128],
                        self.ident_b[:]),
                        waits=[(sv, vkp)] if j4 == 0 else [(sv, self._gla_tp2)],
                        inc=(st, 1))
                    kpt_done = ph.op("vector", lambda e, tc=tc, g2=g2: e.tensor_copy(
                        kptok[:, tc * 128:(tc + 1) * 128],
                        ps_t[:, g2 * 1024:g2 * 1024 + 128]),
                        waits=[(st, tmm)], inc=(sv, 1))
                    if j4 >= 1:
                        self._gla_tp2 = kpt_done
                tp_war = kpt_done
            v_war = []
            for tc in range(NTC):
                for dc in range(NDC):
                    ph.op("tensor", lambda e, tc=tc, dc=dc, wv=wv: e.matmul(
                        ps_v[:, 0:128],
                        biga[:, dc * T + tc * 128: dc * T + (tc + 1) * 128],
                        wv[:, dc * 128:(dc + 1) * 128],
                        start=(dc == 0), stop=(dc == NDC - 1)),
                        waits=v_war if dc == 0 else (),
                        inc=(st, 1) if dc == NDC - 1 else None)
                mmv = st.v
                vv = ph.op("vector", lambda e, tc=tc: e.tensor_copy(
                    v_sb[:, tc * 128:(tc + 1) * 128], ps_v[:, 0:128]),
                    waits=[(st, mmv)], inc=(sv, 1))
                v_war = [(sv, vv)]
            v_done = vv
            vS = ph.op("vector", lambda e: e.memset(S_f[1][:], 0.0), inc=(sv, 1))
            sSb = ph.op("scalar", lambda e: e.activation(out=S_b[:], in_=S_f[1][:],
                                                         func=AF.Copy),
                        waits=[(sv, vS)], inc=(ss, 1))
            self._gla_sb_done = sSb
            qk_ready = kpt_done
            y_evac = 0
            for tb in range(NTC):
                smm = ph.op("tensor", lambda e, tb=tb: e.matmul(
                    ps_s[:, 0:128], ktb[:, tb * 128:(tb + 1) * 128],
                    qt[:, tb * 128:(tb + 1) * 128], start=True, stop=True),
                    waits=[(sv, self._gla_mm_war), (sv, qk_ready)],
                    inc=(st, 1))
                vMm = ph.op("vector", lambda e: e.scalar_tensor_tensor(
                    out=Mm[:], in0=ps_s[:, 0:128], scalar=1.0, in1=self.bdtril[:],
                    op0=ALU.mult, op1=ALU.mult),
                    waits=[(st, smm)], inc=(sv, 1))
                self._gla_mm_war = vMm
                ph.op("tensor", lambda e, tb=tb: e.matmul(
                    ps_y[:, 0:128], v_sb[:, tb * 128:(tb + 1) * 128], Mm[:],
                    start=True, stop=False),
                    waits=[(sv, vMm), (sv, v_done), (sv, y_evac)])
                for c4 in range(4):
                    chunk = tb * 4 + c4
                    base = c4 * 32
                    tp = (96, 0) if base == 96 else None
                    ph.op("tensor", lambda e, tb=tb, c4=c4: e.matmul(
                        ps_y[:, c4 * 32:(c4 + 1) * 32], S_b[:],
                        qt[:, (tb * 128 + c4 * 32):(tb * 128 + c4 * 32) + 32],
                        start=False, stop=(c4 == 3)),
                        waits=[(ss, self._gla_sb_done)])
                    omm = ph.op("tensor", lambda e, tb=tb, c4=c4, base=base, tp=tp: e.matmul(
                        ps_o[:, 0:128],
                        kptok[base:base + 32, tb * 128:(tb + 1) * 128],
                        v_sb[base:base + 32, tb * 128:(tb + 1) * 128],
                        start=True, stop=True, tile_position=tp),
                        waits=[(sv, kpt_done if tb == 0 else 0), (sv, self._gla_S_war)],
                        inc=(st, 1))
                    par = chunk % 2
                    vSn = ph.op("vector", lambda e, chunk=chunk, par=par: e.scalar_tensor_tensor(
                        out=S_f[par][:], in0=S_f[1 - par][:],
                        scalar=eend[:, chunk:chunk + 1],
                        in1=ps_o[:, 0:128], op0=ALU.mult, op1=ALU.add),
                        waits=[(st, omm)], inc=(sv, 1))
                    self._gla_S_war = vSn
                    self._gla_sb_done = ph.op("scalar", lambda e, par=par: e.activation(
                        out=S_b[:], in_=S_f[par][:], func=AF.Copy),
                        waits=[(sv, vSn)], inc=(ss, 1))
                y_evac = ph.op("vector", lambda e, h=h, tb=tb: e.tensor_copy(
                    ybuf[:, h * T + tb * 128: h * T + (tb + 1) * 128], ps_y[:, 0:128]),
                    waits=[(st, st.v)], inc=(sv, 1))
            wfree[r] = [(st, st.v)]
            head_done = y_evac
        ph.fence()
        ph.run()
        es.close()

    def emit_moe(self, biga, act_sb, x2T_dram):
        sv, ss, st, qs, qg = self.sv, self.ss, self.st, self.qs, self.qg
        es = ExitStack()
        ph = Ph(self)
        ps8 = self.ps(es, "m_p8", (128, 4096), F32)
        glr = [self.sb(es, f"m_glr{i}", [8, 128], F32) for i in range(2)]
        self._m_s0war = 0
        self._m_s8war = 0
        compr = self.sb(es, "m_cpr", [8, T], F32)
        crow = self.sb(es, "m_crw", [1, T], F32)
        dn_sb = self.sb(es, "m_dn", [128, NJC * 2048], BF16)
        gur = [[self.sb(es, f"m_gu{i}{r}", [128, 2048], BF16) for i in range(2)]
               for r in range(2)]
        c_bc = self.sb(es, "m_cbc", [128, T], F32)
        sA = self.sb(es, "m_sA", [128, WIN], F32)
        stage = [self.sb(es, f"m_st{i}", [128, WIN], F32) for i in range(2)]

        glt_all = self.sb(es, "m_glta", [128, 128], F32)
        prb_all = self.sb(es, "m_prba", [128, 128], F32)
        prb2_all = self.sb(es, "m_prb2a", [128, 128], F32)
        m8_all = self.sb(es, "m_m8a", [128, 128], F32)
        msk_all = self.sb(es, "m_mska", [128, 128], F32)
        cmb_all = self.sb(es, "m_cmba", [128, 128], F32)
        rca = self.sb(es, "m_rca", [128, 64], F32)

        s0 = []
        glwar = [0, 0]
        for tc in range(NTC):
            rg = tc % 2
            glld = ph.dma("sync", glr[rg][:], self.gl_dram[0:8, tc * 128:(tc + 1) * 128],
                          waits=[(self.st, glwar[rg])], sem=f"mgl{rg}")
            tmm = ph.op("tensor", lambda e, rg=rg, tc=tc: e.transpose(
                ps8[0:128, (tc % 4) * 512:(tc % 4) * 512 + 8], glr[rg][:],
                self.ident_f[0:8, 0:8]),
                waits=[glld, (sv, self._m_s0war)], inc=(st, 1))
            glwar[rg] = tmm
            s0.append(ph.op("vector", lambda e, tc=tc: e.tensor_copy(
                glt_all[:, tc * 8:(tc + 1) * 8],
                ps8[0:128, (tc % 4) * 512:(tc % 4) * 512 + 8]),
                waits=[(st, tmm)], inc=(sv, 1)))
            if tc % 4 == 3:
                self._m_s0war = s0[-1]
        s1 = []
        for tc in range(NTC):
            s1.append(ph.op("vector", lambda e, tc=tc: e.reduce_max(
                rca[:, tc * 4:tc * 4 + 1], glt_all[:, tc * 8:(tc + 1) * 8],
                axis=AX.X, negate=True), inc=(sv, 1)))
        s2 = []
        for tc in range(NTC):
            s2.append(ph.op("scalar", lambda e, tc=tc: e.activation(
                out=prb_all[:, tc * 8:(tc + 1) * 8], in_=glt_all[:, tc * 8:(tc + 1) * 8],
                func=AF.Exp, bias=rca[:, tc * 4:tc * 4 + 1],
                accum_out=rca[:, tc * 4 + 1:tc * 4 + 2]),
                waits=[(sv, s1[tc])] if tc == 0 else [(sv, s1[tc])], inc=(ss, 1)))
        ph.op("scalar", lambda e: e.activation(out=rca[:, 3:4], in_=rca[:, 0:1],
                                               func=AF.Copy), inc=(ss, 1))
        s3 = []
        for tc in range(NTC):
            s3.append(ph.op("vector", lambda e, tc=tc: e.reciprocal(
                rca[:, tc * 4 + 2:tc * 4 + 3], rca[:, tc * 4 + 1:tc * 4 + 2]),
                waits=[(ss, s2[tc] + 1)], inc=(sv, 1)))
        s4 = []
        for tc in range(NTC):
            s4.append(ph.op("vector", lambda e, tc=tc: e.tensor_scalar_mul(
                prb2_all[:, tc * 8:(tc + 1) * 8], prb_all[:, tc * 8:(tc + 1) * 8],
                rca[:, tc * 4 + 2:tc * 4 + 3]), inc=(sv, 1)))
        s5 = []
        for tc in range(NTC):
            s5.append(ph.op("vector", lambda e, tc=tc: e.max(
                out=m8_all[:, tc * 8:(tc + 1) * 8], in_=prb2_all[:, tc * 8:(tc + 1) * 8]),
                inc=(sv, 1)))
        for tc in range(NTC):
            ph.op("vector", lambda e, tc=tc: e.tensor_scalar(
                out=msk_all[:, tc * 8:(tc + 1) * 8], in0=prb2_all[:, tc * 8:(tc + 1) * 8],
                scalar1=m8_all[:, tc * 8 + 1:tc * 8 + 2], scalar2=None, op0=ALU.is_ge))
        s7 = []
        for tc in range(NTC):
            s7.append(ph.op("vector", lambda e, tc=tc: e.tensor_mul(
                cmb_all[:, tc * 8:(tc + 1) * 8], prb2_all[:, tc * 8:(tc + 1) * 8],
                msk_all[:, tc * 8:(tc + 1) * 8]), inc=(sv, 1)))
        v_cmb = 0
        for tc in range(NTC):
            tmm = ph.op("tensor", lambda e, tc=tc: e.transpose(
                ps8[0:8, (4 + tc % 4) * 512:(4 + tc % 4) * 512 + 128],
                cmb_all[:, tc * 8:(tc + 1) * 8], self.ident_f[:]),
                waits=[(sv, s7[min(tc + 2, NTC - 1)]), (sv, self._m_s8war)], inc=(st, 1))
            v_cmb = ph.op("vector", lambda e, tc=tc: e.tensor_copy(
                compr[0:8, tc * 128:(tc + 1) * 128],
                ps8[0:8, (4 + tc % 4) * 512:(4 + tc % 4) * 512 + 128]),
                waits=[(st, tmm)], inc=(sv, 1))
            if tc % 4 == 3:
                self._m_s8war = v_cmb
        if getattr(self, "dbg_moe", None):
            ph.dma("sync", self.dbg_moe["compr"], compr[:], waits=[(sv, v_cmb)])

        guw = {"g": ("gup", None), "a": ("gsa", "dsa"), "b": ("gsb", "dsb")}
        experts = [("g", e) for e in range(NE)] + [("a", 0), ("b", 0)]
        dn_ld = 0
        dn_war = []
        gu_ld = [[0, 0], [0, 0]]
        gu_war = [[], []]
        cbc_war = 0
        act_war = [0, 0]
        self._m_ev = [[], []]
        x2_base = [[0] * NW for _ in range(NDC)]

        def gu_ap(kind, e, j):
            if kind == "g":
                return rr(self.inp["gup"], "e f p dc q -> e f p (dc q)")[e][j]
            return rr(self.inp["gsa" if kind == "a" else "gsb"],
                      "f p dc q -> f p (dc q)")[j]

        def dn_ap(kind, e):
            if kind == "g":
                return rr(self.inp["dnp"], "e p f d -> e p (f d)")[e]
            return rr(self.inp["dsa" if kind == "a" else "dsb"], "p f d -> p (f d)")

        for ei, (kind, e) in enumerate(experts):
            routed = kind == "g"
            prev_ps_users = self._m_ev[0] + self._m_ev[1] + [(sv, v_cmb)]
            dn_ld = ph.dma("gpsimd", dn_sb[:], dn_ap(kind, e), waits=dn_war, sem="mdn")
            if routed:
                v_cb = 0
                cld = ph.dma("sync", crow[:], compr[e:e + 1, :],
                             waits=[(sv, v_cmb), (st, cbc_war)], sem="mcrw")
                for w in range(NW):
                    tmm = ph.op("tensor", lambda e_, w=w: e_.matmul(
                        ps8[:, 3 * 512:4 * 512], self.ones_row_f[:],
                        crow[0:1, w * WIN:(w + 1) * WIN], start=True, stop=True),
                        waits=[cld, (st, cbc_war), (sv, v_cb)] +
                              self._m_ev[0] + self._m_ev[1], inc=(st, 1))
                    v_cb = ph.op("vector", lambda e_, w=w: e_.tensor_copy(
                        c_bc[:, w * WIN:(w + 1) * WIN], ps8[:, 3 * 512:4 * 512]),
                        waits=[(st, tmm)] + self._m_ev[0] + self._m_ev[1], inc=(sv, 1))
                cbc_done = v_cb
            for j in range(NJC):
                r = j % 2
                gu_ld[r][0] = ph.dma("gpsimd", gur[r][0][:], gu_ap(kind, e, j),
                                     waits=gu_war[r], sem=f"mgu{r}")
                gu_ld[r][1] = ph.dma("gpsimd", gur[r][1][:], gu_ap(kind, e, j + NJC),
                                     waits=(), sem=f"mgu{r}")
                for w in range(NW):
                    for half, buf in enumerate(gur[r]):
                        for dc in range(NDC):
                            ph.op("tensor", lambda e_, dc=dc, w=w, buf=buf, half=half, r=r: e_.matmul(
                                ps8[:, half * WIN:(half + 1) * WIN],
                                buf[:, dc * 128:(dc + 1) * 128],
                                biga[:, dc * T + w * WIN: dc * T + (w + 1) * WIN],
                                start=(dc == 0), stop=(dc == NDC - 1)),
                                waits=([gu_ld[r][half]] +
                                       ([(ss, act_war[0]), (sv, act_war[1])] if half == 0 else []) +
                                       (prev_ps_users if (j == 0 and w == 0) else []))
                                      if dc == 0 else (),
                                inc=(st, 1) if dc == NDC - 1 else None)
                    mmv = st.v
                    s_si = ph.op("scalar", lambda e_: e_.activation(
                        out=sA[:], in_=ps8[:, 0:WIN], func=AF.Silu),
                        waits=[(st, mmv)], inc=(ss, 1))
                    v_ac = ph.op("vector", lambda e_, j=j, w=w: e_.tensor_tensor(
                        out=act_sb[:, j * T + w * WIN: j * T + (w + 1) * WIN],
                        in0=sA[:], in1=ps8[:, WIN:2 * WIN], op=ALU.mult),
                        waits=[(ss, s_si), (st, mmv)], inc=(sv, 1))
                    act_war = [s_si, v_ac]
                gu_war[r] = [(st, st.v)]
            act_done = sv.v
            for dh in range(2):
                for w in range(NW):
                    for j in range(NJC):
                        for dcl in range(8):
                            dc = dh * 8 + dcl
                            ph.op("tensor", lambda e_, j=j, dcl=dcl, dc=dc, w=w: e_.matmul(
                                ps8[:, dcl * WIN:(dcl + 1) * WIN],
                                dn_sb[:, j * 2048 + dc * 128: j * 2048 + (dc + 1) * 128],
                                act_sb[:, j * T + w * WIN: j * T + (w + 1) * WIN],
                                start=(j == 0), stop=(j == NJC - 1)),
                                waits=([(sv, act_done), dn_ld] +
                                       (self._m_ev[0] + self._m_ev[1] if j == 0 else []))
                                      if (j == 0 and dcl == 0) else (),
                                inc=(st, 1) if (j == NJC - 1 and dcl == 7) else None)
                    mmv = st.v
                    for dcl in range(8):
                        dc = dh * 8 + dcl
                        gsl = dcl % 2
                        if routed:
                            ev = ph.op("vector", lambda e_, dcl=dcl, w=w, gsl=gsl: e_.tensor_tensor(
                                out=stage[gsl][:], in0=ps8[:, dcl * WIN:(dcl + 1) * WIN],
                                in1=c_bc[:, w * WIN:(w + 1) * WIN], op=ALU.mult),
                                waits=[(st, mmv), (sv, cbc_done)] + self._m_ev[gsl],
                                inc=(sv, 1))
                        else:
                            ev = ph.op("vector", lambda e_, dcl=dcl, gsl=gsl: e_.tensor_copy(
                                stage[gsl][:], ps8[:, dcl * WIN:(dcl + 1) * WIN]),
                                waits=[(st, mmv)] + self._m_ev[gsl], inc=(sv, 1))
                        dst = x2T_dram[dc * 128:(dc + 1) * 128, w * WIN:(w + 1) * WIN]
                        if ei == 0:
                            sw = ph.dma("gpsimd", dst, stage[gsl][:], waits=[(sv, ev)],
                                        sem=f"mst{gsl}")
                            x2_base[dc][w] = sw
                        else:
                            sw = ph.dma("gpsimd", dst, stage[gsl][:],
                                        waits=[(sv, ev), x2_base[dc][w]],
                                        accum_op=ALU.add, sem=f"mst{gsl}")
                        self._m_ev[gsl] = [sw]
            dn_war = [(st, st.v)]
            cbc_war = st.v
        ph.fence()
        ph.run()
        es.close()

    def emit_oproj1(self, xlnT_dram, wo_area, x1T_dram, x2T_dram):
        sv, ss, st, qs, qg = self.sv, self.ss, self.st, self.qs, self.qg
        es = ExitStack()
        ph = Ph(self)
        xl = [self.sb(es, f"o1_x{i}", [128, 16 * WIN], BF16) for i in range(2)]
        x1s = [self.sb(es, f"o1_r{i}", [128, WIN], F32) for i in range(2)]
        stage = [self.sb(es, f"o1_s{i}", [128, WIN], F32) for i in range(2)]
        ps_o = [self.ps(es, f"o1_p{i}") for i in range(2)]
        for dic in range(NDC):
            wlds = ph.dma("gpsimd", wo_area[:, dic * 2048:(dic + 1) * 2048],
                          self.inp["wo1"][dic], sem="o1wo")
        xl_war = [[], []]
        x1_war = [[], []]
        stage_war = [[], []]
        evac = [0, 0]
        for w in range(NW):
            rw = w % 2
            xld = ph.dma("sync", rr(xl[rw][:], "p (dic t) -> p dic t", dic=16),
                         rr(xlnT_dram, "(dic p) t -> p dic t", p=128)[:, :, w * WIN:(w + 1) * WIN],
                         waits=xl_war[rw], sem=f"o1x{rw}")
            for dcp in range(NDC):
                g = (w * NDC + dcp) % 2
                rx = dcp % 2
                x1d = ph.dma("sync", x1s[rx][:],
                             x1T_dram[dcp * 128:(dcp + 1) * 128, w * WIN:(w + 1) * WIN],
                             waits=x1_war[rx], sem=f"o1r{rx}")
                for dic in range(NDC):
                    ph.op("tensor", lambda e, dic=dic, dcp=dcp, w=w, g=g, rw=rw: e.matmul(
                        ps_o[g][:, 0:WIN],
                        wo_area[:, dic * 2048 + dcp * 128: dic * 2048 + (dcp + 1) * 128],
                        xl[rw][:, dic * WIN:(dic + 1) * WIN],
                        start=(dic == 0), stop=(dic == NDC - 1)),
                        waits=[wlds, xld, (sv, evac[g])] if dic == 0 else (),
                        inc=(st, 1) if dic == NDC - 1 else None)
                mmv = st.v
                ev = ph.op("vector", lambda e, g=g, rx=rx: e.tensor_tensor(
                    out=stage[g][:], in0=ps_o[g][:, 0:WIN], in1=x1s[rx][:], op=ALU.add),
                    waits=[(st, mmv), x1d] + stage_war[g], inc=(sv, 1))
                evac[g] = ev
                x1_war[rx] = [(sv, ev)]
                sw = ph.dma("gpsimd",
                            x2T_dram[dcp * 128:(dcp + 1) * 128, w * WIN:(w + 1) * WIN],
                            stage[g][:], waits=[(sv, ev)], accum_op=ALU.add,
                            sem=f"o1s{g}")
                stage_war[g] = [sw]
            xl_war[rw] = [(st, st.v)]
        ph.fence()
        ph.run()
        es.close()

    def emit_out(self, x2T_dram, out_tok):
        sv, st, qs = self.sv, self.st, self.qs
        es = ExitStack()
        ph = Ph(self)
        xin = [self.sb(es, f"fo_x{i}", [128, T], F32) for i in range(2)]
        stg = [self.sb(es, f"fo_s{i}", [128, 2048], F32) for i in range(2)]
        pt = [self.ps(es, f"fo_p{i}", (128, 1024), F32) for i in range(2)]
        evac_done = [0, 0]
        store_done = [(qs, 0), (qs, 0)]
        xin_war = [0, 0]
        for dc in range(NDC):
            r = dc % 2
            wld = ph.dma("sync", xin[r][:], x2T_dram[dc * 128:(dc + 1) * 128, :],
                         waits=[(st, xin_war[r])], sem=f"fo{r}")
            for half in range(2):
                for tc8 in range(8):
                    tc = half * 8 + tc8
                    w0 = []
                    if tc == 0:
                        w0.append(wld)
                    if tc8 == 0:
                        w0.append((sv, evac_done[r]))
                    ph.op("tensor", lambda e, r=r, tc=tc, tc8=tc8: e.transpose(
                        pt[r][:, tc8 * 128:(tc8 + 1) * 128],
                        xin[r][:, tc * 128:(tc + 1) * 128], self.ident_f[:]),
                        waits=w0, inc=(st, 1) if tc8 == 7 else None)
                tmm = st.v
                evac_done[r] = ph.op("vector", lambda e, r=r, half=half: e.tensor_copy(
                    stg[r][:, half * 1024:(half + 1) * 1024], pt[r][:]),
                    waits=[(st, tmm), store_done[r]], inc=(sv, 1))
            xin_war[r] = st.v
            store_done[r] = ph.dma(
                "sync",
                rr(out_tok, "(tc p) d -> p tc d", p=128)[:, :, dc * 128:(dc + 1) * 128],
                rr(stg[r][:], "p (tc t) -> p tc t", tc=16),
                waits=[(sv, evac_done[r])], sem=f"fos{r}")
        ph.fence()
        ph.run()
        es.close()



def build_full():
    b = Build()
    x_in = b.din("x", [T, D], F32)
    for nm, shp in (("wq0", [16, 128, 16, 128]), ("wk0", [16, 128, 16, 128]),
                    ("wv0", [16, 128, 16, 128]), ("wo0", [16, 128, 16, 128]),
                    ("w1p", [64, 128, 16, 128]), ("w2p", [64, 128, 2048]),
                    ("b1p", [64, 128]), ("b2p", [16, 128]), ("wrp", [16, 128]),
                    ("wq1", [16, 128, 16, 128]), ("wk1", [16, 128, 16, 128]),
                    ("wv1", [16, 128, 16, 128]), ("wg1", [16, 128, 16, 128]),
                    ("wo1", [16, 128, 2048]), ("gmp", [16, 128]), ("btp", [16, 128]),
                    ("gwp", [128, 16, 8]), ("gup", [8, 22, 128, 16, 128]),
                    ("gsa", [22, 128, 16, 128]), ("gsb", [22, 128, 16, 128]),
                    ("dnp", [8, 128, 11, 2048]), ("dsa", [128, 11, 2048]),
                    ("dsb", [128, 11, 2048])):
        b.din(nm, shp, F32)
    x0T = b.dtmp("x0T", [D, T], F32)
    x1T = b.dtmp("x1T", [D, T], F32)
    x2T = b.dtmp("x2T", [D, T], F32)
    h_d = b.dtmp("h_d", [64, 128, 2048], BF16)
    xlnT = b.dtmp("xlnT", [D, T], BF16)
    out = b.dout("out", [T, D], F32)
    biga = b.sb(b.estack, "BIGA", [128, NDC * T], BF16)

    b.emit_consts()
    b.bstack = ExitStack()
    b.emit_transpose_in("b0", x_in, x0T)
    b.emit_ln("b0", x0T, biga, router=True)
    zstack = ExitStack()
    z_sb = b.sb(zstack, "ZBUF", [128, NH * T], BF16)
    b.emit_attn0(biga, z_sb)
    b.emit_ffn0(biga, h_d, x1T)
    b.emit_oproj0(z_sb, biga, x0T, x1T, accum=True)
    zstack.close()
    b.bstack.close()
    b.bstack = ExitStack()
    b.emit_ln("b1", x1T, biga, gate=True)
    ystack = ExitStack()
    ybuf = b.sb(ystack, "YBUF", [128, NH * T], BF16)
    b.emit_gla(biga, ybuf)
    b.emit_ln("yg", None, None, src_bf16=ybuf, gb=True, out_dram=xlnT)
    ystack.close()
    astack = ExitStack()
    act_sb = b.sb(astack, "ACT", [128, NJC * T], BF16)
    b.emit_moe(biga, act_sb, x2T)
    astack.close()
    b.emit_oproj1(xlnT, biga, x1T, x2T)
    b.bstack.close()
    b.emit_out(x2T, out)
    b.estack.close()
    return b


def kernel(**inputs):
    pk = pack_inputs(inputs)
    x = np.ascontiguousarray(inputs["L0_attn_x"].astype(np.float32))
    b = build_full()
    in_maps = []
    for c in range(8):
        m = dict(pk)
        m["x"] = np.ascontiguousarray(x[c])
        in_maps.append(m)
    res = run_bass_kernel_spmd(b.nc, in_maps, core_ids=list(range(8)))
    return np.stack([np.asarray(res.results[c]["out"]) for c in range(8)], axis=0)


# version 02
# speedup vs baseline: 1.2518x; 1.2518x over previous
import numpy as np
import ml_dtypes
from contextlib import ExitStack
import concourse.bass as bass
import concourse.mybir as mybir
from concourse.bass_utils import run_bass_kernel_spmd

F32 = mybir.dt.float32
BF16 = mybir.dt.bfloat16
AF = mybir.ActivationFunctionType
ALU = mybir.AluOpType
AX = mybir.AxisListType

D = 2048
T = 2048
NDC = 16
NTC = 16
NW = 4
WIN = 512
NH = 16
DH = 128
DF = 8192
NFC = 64
KSEL = 1024
NE = 8
FMOE = 1408
NJC = 11
GLA_C = 32
INV_SQRT_DH = float(1.0 / np.sqrt(128.0))
EXP_BIAS = -8.0

ENGS = ("sync", "scalar", "vector", "tensor", "gpsimd")



def pack_inputs(i):
    f = np.ascontiguousarray
    p = {}
    p["wq0"] = f(i["L0_attn_W_Q"].reshape(16, 128, 16, 128).transpose(0, 3, 2, 1))
    p["wk0"] = f(i["L0_attn_W_K"].reshape(16, 128, 16, 128).transpose(0, 3, 2, 1))
    p["wv0"] = f(i["L0_attn_W_V"].reshape(16, 128, 16, 128).transpose(0, 3, 2, 1))
    p["wo0"] = f(i["L0_attn_W_O"].reshape(16, 16, 128, 128).transpose(0, 3, 1, 2))
    p["w1p"] = f(i["L0_ffn_w1"].reshape(16, 128, 64, 128).transpose(2, 1, 0, 3))
    p["w2p"] = f(i["L0_ffn_w2"].reshape(64, 128, 2048))
    p["b1p"] = f(i["L0_ffn_b1"].reshape(64, 128))
    p["b2p"] = f(i["L0_ffn_b2"].reshape(16, 128))
    p["wrp"] = f(i["L0_ffn_w_r"].reshape(16, 128))
    for nm, k in (("wq1", "L1_attn_Wq"), ("wk1", "L1_attn_Wk"),
                  ("wv1", "L1_attn_Wv"), ("wg1", "L1_attn_Wg")):
        p[nm] = f(i[k].reshape(16, 128, 16, 128).transpose(0, 3, 2, 1))
    p["wo1"] = f(np.ascontiguousarray(i["L1_attn_Wo"].T).reshape(16, 128, 2048))
    p["gmp"] = f(i["L1_attn_gamma"].reshape(16, 128))
    p["btp"] = f(i["L1_attn_beta"].reshape(16, 128))
    p["gwp"] = f(np.ascontiguousarray(i["L1_ffn_gate_weight"].T)
                 .reshape(16, 128, 8).transpose(1, 0, 2))
    gu = i["L1_ffn_gate_up_weights"]
    p["gup"] = f(gu.reshape(8, 22, 128, 16, 128).transpose(0, 1, 4, 3, 2))
    sgu = i["L1_ffn_shared_gate_up"]
    sa = np.concatenate([sgu[0:1408], sgu[2816:4224]], axis=0)
    sb_ = np.concatenate([sgu[1408:2816], sgu[4224:5632]], axis=0)
    p["gsa"] = f(sa.reshape(22, 128, 16, 128).transpose(0, 3, 2, 1))
    p["gsb"] = f(sb_.reshape(22, 128, 16, 128).transpose(0, 3, 2, 1))
    dn = i["L1_ffn_down_weights"]
    p["dnp"] = f(dn.transpose(0, 2, 1).reshape(8, 11, 128, 2048).transpose(0, 2, 1, 3))
    sd = np.ascontiguousarray(i["L1_ffn_shared_down"].T)
    p["dsa"] = f(sd[0:1408].reshape(11, 128, 2048).transpose(1, 0, 2))
    p["dsb"] = f(sd[1408:2816].reshape(11, 128, 2048).transpose(1, 0, 2))
    return p



class SemC:
    def __init__(self, h):
        self.h = h
        self.v = 0

    def inc(self, k=1):
        self.v += k
        return self.v


class Ph:

    def __init__(self, bld):
        self.bld = bld
        self.nc = bld.nc
        self.q = {n: [] for n in ENGS}

    def add(self, en, fn):
        self.q[en].append(fn)

    def op(self, en, fn, waits=(), inc=None):
        tgt = inc[0].inc(inc[1]) if inc is not None else None

        def g(eng, fn=fn, waits=tuple(waits), inc=inc):
            for s, v in waits:
                if v > 0:
                    eng.wait_ge(s.h, v)
            ins = fn(eng)
            if inc is not None:
                ins.then_inc(inc[0].h, inc[1])
        self.add(en, g)
        return tgt

    def dma(self, en, out, in_, waits=(), sem=None, **kw):
        if isinstance(sem, str):
            sem = self.bld.dsem(sem)
        if sem is None:
            sem = {"sync": self.bld.qs, "scalar": self.bld.qc, "gpsimd": self.bld.qg}[en]
        tgt = self.op(en, lambda e: e.dma_start(out=out, in_=in_, **kw),
                      waits=waits, inc=(sem, 16))
        return (sem, tgt)

    def wdma(self, *a, **kw):
        return self.dma(*a, **kw)

    def fence(self):
        b = self.bld
        qtgts = [(b.qs, b.qs.v), (b.qc, b.qc.v), (b.qg, b.qg.v)]
        qtgts += [(s, s.v) for s in b.dsems.values()]
        tgt = b.bar.v + 5
        b.bar.v = tgt
        for en in ENGS:
            def g(eng, tgt=tgt, qtgts=tuple(qtgts)):
                eng.sem_inc(b.bar.h, 1)
                eng.wait_ge(b.bar.h, tgt)
                for s, v in qtgts:
                    if v > 0:
                        eng.wait_ge(s.h, v)
            self.add(en, g)

    def run(self):
        with self.nc.Block() as blk:
            for en in ENGS:
                fns = self.q[en]
                if fns:
                    getattr(blk, en)(lambda eng, fns=fns: [fn(eng) for fn in fns])


def rr(ap, spec, **kw):
    return ap.rearrange(spec, **kw)


class Build:
    def __init__(self):
        self.nc = bass.Bass("TRN2", target_bir_lowering=False, debug=False)
        self.inp = {}
        self.estack = ExitStack()
        nc = self.nc
        s = lambda n: SemC(self.estack.enter_context(nc.semaphore(n)))
        self.qs, self.qc, self.qg = s("qs"), s("qc"), s("qg")
        self.sv, self.ss, self.st, self.sg = s("sv"), s("ss"), s("st"), s("sg")
        self.bar = s("bar")
        self._mksem = s
        self.dsems = {}

    def dsem(self, name):
        if name not in self.dsems:
            self.dsems[name] = self._mksem("d_" + name)
        return self.dsems[name]

    def din(self, name, shape, dtype=F32):
        ap = self.nc.dram_tensor(name, list(shape), dtype, kind="ExternalInput").ap()
        self.inp[name] = ap
        return ap

    def dout(self, name, shape, dtype=F32):
        return self.nc.dram_tensor(name, list(shape), dtype, kind="ExternalOutput").ap()

    def dtmp(self, name, shape, dtype=F32):
        return self.nc.dram_tensor(name, list(shape), dtype, kind="Internal").ap()

    def sb(self, es, name, shape, dtype):
        return es.enter_context(self.nc.sbuf_tensor(name, list(shape), dtype))

    def ps(self, es, name, shape=(128, 512), dtype=F32):
        return es.enter_context(self.nc.psum_tensor(name, list(shape), dtype))

    def emit_consts(self):
        es = self.estack
        self.ident_f = self.sb(es, "ident_f", [128, 128], F32)
        self.ident_b = self.sb(es, "ident_b", [128, 128], BF16)
        self.ones_col_f = self.sb(es, "ones_col_f", [128, 1], F32)
        self.ones_col_b = self.sb(es, "ones_col_b", [128, 1], BF16)
        self.ones_row_f = self.sb(es, "ones_row_f", [1, 128], F32)
        self.bdtril = self.sb(es, "bdtril", [128, 128], BF16)
        self.scanmask = self.sb(es, "scanmask", [128, WIN], BF16)
        self.fc = self.sb(es, "fconsts", [128, 8], F32)
        ph = Ph(self)

        def mk(_):
            P = self._pool_ops = []
            for t, fill in ((self.ident_f, 1.0), (self.ident_b, 1.0)):
                P.append(lambda e, t=t: e.memset(t[:], 0.0))
                P.append(lambda e, t=t, fill=fill: e.affine_select(
                    out=t[:], in_=t[:], compare_op=ALU.not_equal, fill=fill,
                    base=0, pattern=[[-1, 128]], channel_multiplier=1))
            P.append(lambda e: e.memset(self.ones_col_f[:], 1.0))
            P.append(lambda e: e.memset(self.ones_col_b[:], 1.0))
            P.append(lambda e: e.memset(self.ones_row_f[:], 1.0))
            P.append(lambda e: e.memset(self.fc[:, 0:1], 1e-5))
            P.append(lambda e: e.memset(self.fc[:, 1:2], EXP_BIAS))
            P.append(lambda e: e.memset(self.fc[:, 2:3], 1.0))
            P.append(lambda e: e.memset(self.bdtril[:], 0.0))
            for b32 in range(4):
                sub = self.bdtril[:, b32 * 32:(b32 + 1) * 32]
                C = b32 * 32
                P.append(lambda e, sub=sub, C=C: e.affine_select(
                    out=sub, in_=sub, compare_op=ALU.is_ge, fill=1.0,
                    base=C - 1, pattern=[[0, 32]], channel_multiplier=-1))
                P.append(lambda e, sub=sub, C=C: e.affine_select(
                    out=sub, in_=sub, compare_op=ALU.is_ge, fill=0.0,
                    base=C, pattern=[[1, 32]], channel_multiplier=-1))
            P.append(lambda e: e.memset(self.scanmask[:], 1.0))
            P.append(lambda e: e.memset(self.scanmask[:, 0:WIN:GLA_C], 0.0))
        def mk_chained(e):
            self._pool_ops = []
            mk(None)
            for f in self._pool_ops:
                tgt = self.sg.inc(1)
                f(e).then_inc(self.sg.h, 1)
                e.wait_ge(self.sg.h, tgt)
        ph.op("gpsimd", mk_chained)
        ph.fence()
        ph.run()

    def emit_transpose_in(self, tag, x_tok, xT_dram):
        es = ExitStack()
        ph = Ph(self)
        st, sv, qs = self.st, self.sv, self.qs
        xin = [self.sb(es, f"ti_x{i}_{tag}", [128, D], F32) for i in range(2)]
        stg = [self.sb(es, f"ti_s{i}_{tag}", [128, D], F32) for i in range(2)]
        pt = [self.ps(es, f"ti_p{i}_{tag}", (128, 1024), F32) for i in range(2)]
        evac_done = [0, 0]
        store_done = [(qs, 0), (qs, 0)]
        xin_war = [0, 0]
        for tc in range(NTC):
            r = tc % 2
            wld = ph.dma("sync", xin[r][:], x_tok[tc * 128:(tc + 1) * 128, :],
                         waits=[(st, xin_war[r])], sem=f"ti{r}_{tag}")
            for half in range(2):
                for dc8 in range(8):
                    dc = half * 8 + dc8
                    w0 = []
                    if dc == 0:
                        w0.append(wld)
                    if dc8 == 0:
                        w0.append((sv, evac_done[r]))
                    ph.op("tensor", lambda e, r=r, dc=dc, dc8=dc8: e.transpose(
                        pt[r][:, dc8 * 128:(dc8 + 1) * 128],
                        xin[r][:, dc * 128:(dc + 1) * 128], self.ident_f[:]),
                        waits=w0,
                        inc=(st, 1) if dc8 == 7 else None)
                tmm = st.v
                evac_done[r] = ph.op("vector", lambda e, r=r, half=half: e.tensor_copy(
                    stg[r][:, half * 1024:(half + 1) * 1024], pt[r][:]),
                    waits=[(st, tmm), store_done[r]], inc=(sv, 1))
            xin_war[r] = st.v
            store_done[r] = ph.dma(
                "sync",
                rr(xT_dram, "(dc p) t -> p dc t", p=128)[:, :, tc * 128:(tc + 1) * 128],
                rr(stg[r][:], "p (dc t) -> p dc t", dc=16),
                waits=[(sv, evac_done[r])], sem=f"tis{r}_{tag}")
        ph.fence()
        ph.run()
        es.close()

    def emit_ln(self, tag, xT_dram, biga, router=False, gate=False, src_bf16=None,
                gb=False, out_dram=None):
        sv, ss, st, qs = self.sv, self.ss, self.st, self.qs
        if router:
            self.maskrow = self.sb(getattr(self, "bstack", None) or self.estack,
                                   "maskrow", [1, T], F32)
            self.rowtmp = self.dtmp("rowtmp", [16, 128], F32)
        if gate:
            self.gl_dram = self.dtmp("gl_dram", [8, T], F32)
        es = ExitStack()
        ph = Ph(self)
        rows = self.sb(es, f"ln_rows_{tag}", [1, 4 * T], F32)
        r_sumx = rows[0:1, 0:T]
        r_sumsq = rows[0:1, T:2 * T]
        r_mu = rows[0:1, 2 * T:3 * T]
        r_rs = rows[0:1, 3 * T:4 * T]
        xt = ([self.sb(es, f"ln_x{i}_{tag}", [128, T], F32) for i in range(2)]
              if src_bf16 is None else None)
        sq = self.sb(es, f"ln_sq_{tag}", [128, T], F32)
        mu_bc = self.sb(es, f"ln_mu_{tag}", [128, T], F32)
        rs_bc = self.sb(es, f"ln_rs_{tag}", [128, T], F32)
        t1 = self.sb(es, f"ln_t1_{tag}", [128, T], F32)
        t2 = sq
        if out_dram is not None:
            ostg = [self.sb(es, f"ln_os0_{tag}", [128, T], BF16)]
        if gb:
            gm_sb = self.sb(es, f"ln_gm_{tag}", [128, 16], F32)
            bt_sb = self.sb(es, f"ln_bt_{tag}", [128, 16], F32)
            g16 = self.sb(es, f"ln_g16_{tag}", [16, 128], F32)
        ps4 = self.ps(es, f"ln_ps_{tag}", (128, 2048), F32)

        xt_war = [[], []]
        gb_done = 0
        if gb:
            for nm, dst in (("gmp", gm_sb), ("btp", bt_sb)):
                wld = ph.dma("sync", g16[:], self.inp[nm], sem=f"lng16_{tag}",
                             waits=[(st, self.st.v)])
                tmm = ph.op("tensor", lambda e, dst=dst: e.transpose(
                    ps4[0:128, 0:16], g16[:], self.ident_f[0:16, 0:16]),
                    waits=[wld, (sv, gb_done)], inc=(st, 1))
                gb_done = ph.op("vector", lambda e, dst=dst: e.tensor_copy(
                    dst[:], ps4[0:128, 0:16]), waits=[(st, tmm)], inc=(sv, 1))

        def src_tile(dc):
            if src_bf16 is not None:
                return src_bf16[:, dc * T:(dc + 1) * T], []
            b = xt[dc % 2]
            wld = ph.dma("sync", b[:], xT_dram[dc * 128:(dc + 1) * 128, :],
                         waits=list(xt_war[dc % 2]), sem=f"lnx{dc % 2}_{tag}")
            return b[:], [wld]

        ones = self.ones_col_b if src_bf16 is not None else self.ones_col_f

        for dc in range(NDC):
            bap, w_ld = src_tile(dc)
            for w in range(NW):
                ph.op("tensor", lambda e, bap=bap, w=w, dc=dc: e.matmul(
                    ps4[0:1, w * WIN:(w + 1) * WIN], ones[:],
                    bap[:, w * WIN:(w + 1) * WIN], start=(dc == 0), stop=(dc == NDC - 1)),
                    waits=(w_ld + ([(sv, gb_done)] if dc == 0 else [])) if w == 0 else (),
                    inc=(st, 1))
            if src_bf16 is None:
                xt_war[dc % 2] = [(st, st.v)]
        v_sumx = ph.op("vector", lambda e: e.tensor_copy(r_sumx, ps4[0:1, 0:T]),
                       waits=[(st, st.v)], inc=(sv, 1))
        sq_war = 0
        for dc in range(NDC):
            bap, w_ld = src_tile(dc)
            sqd = ph.op("scalar", lambda e, bap=bap: e.activation(
                out=sq[:], in_=bap, func=AF.Square),
                waits=w_ld + [(st, sq_war)], inc=(ss, 1))
            for w in range(NW):
                ph.op("tensor", lambda e, w=w, dc=dc: e.matmul(
                    ps4[0:1, w * WIN:(w + 1) * WIN], self.ones_col_f[:],
                    sq[:, w * WIN:(w + 1) * WIN], start=(dc == 0), stop=(dc == NDC - 1)),
                    waits=[(ss, sqd), (sv, v_sumx)] if w == 0 else (),
                    inc=(st, 1))
            sq_war = st.v
            if src_bf16 is None:
                xt_war[dc % 2] = [(ss, sqd)]
        v_sums = ph.op("vector", lambda e: e.tensor_copy(r_sumsq, ps4[0:1, 0:T]),
                       waits=[(st, st.v)], inc=(sv, 1))
        ph.op("scalar", lambda e: e.activation(out=r_mu, in_=r_sumx, func=AF.Copy,
                                               scale=1.0 / D), waits=[(sv, v_sums)])
        ph.op("scalar", lambda e: e.activation(out=r_rs, in_=r_sumsq, func=AF.Copy,
                                               scale=1.0 / D))
        ph.op("scalar", lambda e: e.activation(out=r_sumx, in_=r_mu, func=AF.Square),
              inc=(ss, 1))
        s_mu2 = ss.v
        v_var = ph.op("vector", lambda e: e.tensor_sub(r_sumsq, r_rs, r_sumx),
                      waits=[(ss, s_mu2)], inc=(sv, 1))
        ph.op("scalar", lambda e: e.activation(out=r_sumx, in_=r_sumsq, func=AF.Ln,
                                               bias=self.fc[0:1, 0:1]),
              waits=[(sv, v_var)])
        s_rows = ph.op("scalar", lambda e: e.activation(out=r_rs, in_=r_sumx, func=AF.Exp,
                                                        scale=-0.5), inc=(ss, 1))
        v_bc = v_sums
        for (row, dstb) in ((r_mu, mu_bc), (r_rs, rs_bc)):
            for w in range(NW):
                tmm = ph.op("tensor", lambda e, row=row, w=w: e.matmul(
                    ps4[:, 0:WIN], self.ones_row_f[:], row[0:1, w * WIN:(w + 1) * WIN],
                    start=True, stop=True),
                    waits=[(ss, s_rows), (sv, v_bc)], inc=(st, 1))
                v_bc = ph.op("vector", lambda e, dstb=dstb, w=w: e.tensor_copy(
                    dstb[:, w * WIN:(w + 1) * WIN], ps4[:, 0:WIN]),
                    waits=[(st, tmm)], inc=(sv, 1))

        if router:
            wr_sb = self.sb(es, "wr_sb", [128, 16], F32)
            m16 = self.sb(es, "m16", [16, 128], F32)
            wld = ph.dma("sync", m16[:], self.inp["wrp"], sem=f"lnr1_{tag}")
            tmm = ph.op("tensor", lambda e: e.transpose(
                ps4[0:128, 0:16], m16[:], self.ident_f[0:16, 0:16]),
                waits=[wld, (sv, v_bc)], inc=(st, 1))
            v_wr = ph.op("vector", lambda e: e.tensor_copy(wr_sb[:], ps4[0:128, 0:16]),
                         waits=[(st, tmm)], inc=(sv, 1))
        if gate:
            glrow_sb = self.sb(es, "glrow_sb", [8, T], F32)
            gw_sb = self.sb(es, "gw_sb", [128, 16 * 8], F32)
            gwld = ph.dma("sync", gw_sb[:], rr(self.inp["gwp"], "p dc e -> p (dc e)"),
                          sem=f"lnr2_{tag}")

        self._ln_ost_war = [[], []]
        rtr_war = self.st.v if (router or gate) else 0
        for dc in range(NDC):
            bap, w_ld = src_tile(dc)
            ph.op("vector", lambda e, bap=bap: e.tensor_sub(t1[:], bap, mu_bc[:]),
                  waits=w_ld + [(sv, v_bc), (st, rtr_war)], inc=(sv, 1))
            if src_bf16 is None:
                xt_war[dc % 2] = [(sv, sv.v)]
            nrm = ph.op("vector", lambda e: e.tensor_mul(t2[:], t1[:], rs_bc[:]),
                        inc=(sv, 1))
            if out_dram is not None:
                ro = 0
                vo = ph.op("vector", lambda e, dc=dc, ro=ro: e.tensor_scalar(
                    out=ostg[ro][:], in0=t2[:],
                    scalar1=gm_sb[:, dc:dc + 1], scalar2=bt_sb[:, dc:dc + 1],
                    op0=ALU.mult, op1=ALU.add) if gb else e.tensor_copy(ostg[ro][:], t2[:]),
                    waits=self._ln_ost_war[ro], inc=(sv, 1))
                osw = ph.dma("sync", out_dram[dc * 128:(dc + 1) * 128, :], ostg[ro][:],
                             waits=[(sv, vo)], sem=f"lnos_{tag}")
                self._ln_ost_war[ro] = [osw]
            elif gb:
                ph.op("vector", lambda e, dc=dc: e.tensor_scalar(
                    out=biga[:, dc * T:(dc + 1) * T], in0=t2[:],
                    scalar1=gm_sb[:, dc:dc + 1], scalar2=bt_sb[:, dc:dc + 1],
                    op0=ALU.mult, op1=ALU.add), inc=(sv, 1))
            else:
                ph.op("vector", lambda e, dc=dc: e.tensor_copy(
                    biga[:, dc * T:(dc + 1) * T], t2[:]), inc=(sv, 1))
            if router:
                for w in range(NW):
                    ph.op("tensor", lambda e, w=w, dc=dc: e.matmul(
                        ps4[0:1, w * WIN:(w + 1) * WIN], wr_sb[:, dc:dc + 1],
                        t2[:, w * WIN:(w + 1) * WIN],
                        start=(dc == 0), stop=(dc == NDC - 1)),
                        waits=[(sv, nrm), (sv, v_wr)] if w == 0 else (),
                        inc=(st, 1) if w == NW - 1 else None)
                rtr_war = st.v
            if gate:
                for w in range(NW):
                    ph.op("tensor", lambda e, w=w, dc=dc: e.matmul(
                        ps4[0:8, w * WIN:(w + 1) * WIN], gw_sb[:, dc * 8:(dc + 1) * 8],
                        t2[:, w * WIN:(w + 1) * WIN],
                        start=(dc == 0), stop=(dc == NDC - 1)),
                        waits=([(sv, nrm)] + ([gwld] if dc == 0 else [])) if w == 0 else (),
                        inc=(st, 1) if w == NW - 1 else None)
                rtr_war = st.v

        if gate:
            vgl = ph.op("vector", lambda e: e.tensor_copy(glrow_sb[:], ps4[0:8, 0:T]),
                        waits=[(st, st.v)], inc=(sv, 1))
            ph.dma("sync", self.gl_dram, glrow_sb[:], waits=[(sv, vgl)])

        if getattr(self, "dbg_ln", None):
            d_rows = self.dbg_ln["rows"]
            d_mu = self.dbg_ln["mu"]
            d_rs = self.dbg_ln["rs"]
            ph.dma("sync", d_rows, rows[:], waits=[(self.ss, self.ss.v), (self.sv, self.sv.v)])
            ph.dma("sync", d_mu, mu_bc[:], waits=[(self.sv, v_bc)])
            ph.dma("sync", d_rs, rs_bc[:], waits=[(self.sv, v_bc)])

        if router:
            sc_bc = mu_bc
            sc_col = self.sb(es, "sc_col", [128, 16], F32)
            rank_col = self.sb(es, "rank_col", [128, 16], F32)
            mask_col = self.sb(es, "mask_col", [128, 16], F32)
            scr_row = r_sumx
            v_scrow = 0
            for w in range(NW):
                v_scrow = ph.op("vector", lambda e, w=w: e.tensor_copy(
                    scr_row[0:1, w * WIN:(w + 1) * WIN], ps4[0:1, w * WIN:(w + 1) * WIN]),
                    waits=[(st, st.v)] if w == 0 else (), inc=(sv, 1))
            v_bc2 = 0
            for w in range(NW):
                tmm = ph.op("tensor", lambda e, w=w: e.matmul(
                    ps4[:, 0:WIN], self.ones_row_f[:], scr_row[0:1, w * WIN:(w + 1) * WIN],
                    start=True, stop=True), waits=[(sv, v_scrow), (sv, v_bc2)],
                    inc=(st, 1))
                v_bc2 = ph.op("vector", lambda e, w=w: e.tensor_copy(
                    sc_bc[:, w * WIN:(w + 1) * WIN], ps4[:, 0:WIN]),
                    waits=[(st, tmm)], inc=(sv, 1))
            wst = ph.dma("sync", rr(self.rowtmp, "a t -> (a t)").rearrange("(a t) -> a t", a=1),
                         scr_row, waits=[(sv, v_scrow)], sem=f"lnr3_{tag}")
            wld = ph.dma("sync", m16[:], self.rowtmp, waits=[wst], sem=f"lnr4_{tag}")
            tmm = ph.op("tensor", lambda e: e.transpose(
                ps4[0:128, 0:16], m16[:], self.ident_f[0:16, 0:16]),
                waits=[wld, (sv, v_bc2)], inc=(st, 1))
            v_col = ph.op("vector", lambda e: e.tensor_copy(sc_col[:], ps4[0:128, 0:16]),
                          waits=[(st, tmm)], inc=(sv, 1))
            for tc in range(NTC):
                ph.op("vector", lambda e, tc=tc: e.tensor_scalar(
                    out=t1[:], in0=sc_bc[:], scalar1=sc_col[:, tc:tc + 1], scalar2=None,
                    op0=ALU.is_gt, op1=ALU.add, accum_out=rank_col[:, tc:tc + 1]),
                    waits=[(sv, v_col)] if tc == 0 else ())
            ph.op("vector", lambda e: e.tensor_copy(t1[:, 0:128], sc_bc[:, 0:128]))
            v_rank = ph.op("vector", lambda e: e.tensor_scalar(
                out=mask_col[:], in0=rank_col[:], scalar1=float(KSEL), scalar2=None,
                op0=ALU.is_lt), inc=(sv, 1))
            tmm = ph.op("tensor", lambda e: e.transpose(
                ps4[0:16, 0:128], mask_col[:], self.ident_f[:]),
                waits=[(sv, v_rank)], inc=(st, 1))
            v_m16 = ph.op("vector", lambda e: e.tensor_copy(m16[:], ps4[0:16, 0:128]),
                          waits=[(st, tmm)], inc=(sv, 1))
            wst = ph.dma("sync", self.rowtmp, m16[:], waits=[(sv, v_m16)], sem=f"lnr5_{tag}")
            ph.dma("sync", self.maskrow[:],
                   rr(self.rowtmp, "a t -> (a t)").rearrange("(a t) -> a t", a=1),
                   waits=[wst], sem=f"lnr6_{tag}")
            if getattr(self, "dbg_ln", None):
                ph.dma("sync", self.dbg_ln["sc_col"], sc_col[:], waits=[(sv, v_col)])
                ph.dma("sync", self.dbg_ln["rank"], rank_col[:], waits=[(sv, v_rank)])
                ph.dma("sync", self.dbg_ln["sc_bc"], sc_bc[:], waits=[(sv, v_bc2)])
                ph.dma("sync", self.dbg_ln["mask_col"], mask_col[:], waits=[(sv, v_rank)])
                ph.dma("sync", self.dbg_ln["m16"], m16[:], waits=[(sv, v_m16)])
        ph.fence()
        ph.run()
        es.close()

    def emit_attn0(self, biga, z_sb):
        sv, ss, st, qg = self.sv, self.ss, self.st, self.qg
        es = ExitStack()
        ph = Ph(self)
        q_sb = self.sb(es, "a0_q", [128, T], BF16)
        self.cmask = self.sb(es, "a0_cmask", [128, 4 * WIN], BF16)
        def _mkcmask(e):
            tgt = self.sg.inc(1)
            e.memset(self.cmask[:], 0.0).then_inc(self.sg.h, 1)
            e.wait_ge(self.sg.h, tgt)
            for dlt in range(4):
                sub = self.cmask[:, dlt * WIN:(dlt + 1) * WIN]
                tgt = self.sg.inc(1)
                e.affine_select(out=sub, in_=sub, compare_op=ALU.is_ge, fill=-1.0e5,
                                base=-128 * dlt, pattern=[[1, WIN]],
                                channel_multiplier=-1).then_inc(self.sg.h, 1)
                e.wait_ge(self.sg.h, tgt)
        ph.op("gpsimd", _mkcmask)
        cmask_rdy = self.sg.v
        k_sb = self.sb(es, "a0_k", [128, T], BF16)
        v_sb = self.sb(es, "a0_v", [128, T], BF16)
        pt_sb = self.sb(es, "a0_pt", [128, 16 * WIN], BF16)
        dtmp = self.sb(es, "a0_dt", [128, WIN], F32)
        linv_bc = self.sb(es, "a0_li", [128, WIN], F32)
        lrow = self.sb(es, "a0_lr", [1, WIN], F32)
        wring = [[self.sb(es, f"a0_w{j}{r}", [128, 2048], BF16) for j in range(3)]
                 for r in range(2)]
        ps_a = self.ps(es, "a0_pa")
        ps_b = self.ps(es, "a0_pb")
        ps_z = self.ps(es, "a0_pz")
        ps_l = self.ps(es, "a0_pl")
        ps_c = self.ps(es, "a0_pc")
        ps_v = self.ps(es, "a0_pv")
        exp_done = [0, 0]
        l_war = 0
        z_war = 0
        dtmp_war = 0

        wnames = ("wq0", "wk0", "wv0")
        wld = [(qg, 0), (qg, 0)]
        wfree = [[], []]
        for j, nm in enumerate(wnames):
            wld[0] = ph.dma("gpsimd", wring[0][j][:],
                            rr(self.inp[nm], "h p dc e -> h p (dc e)")[0], sem="a0w0")
        evac_pp = [0, 0]
        pp_war = [[], []]
        for h in range(NH):
            r = h % 2
            wq, wk, wv = wring[r]
            if h + 1 < NH:
                for j, nm in enumerate(wnames):
                    wld[1 - r] = ph.dma("gpsimd", wring[1 - r][j][:],
                                        rr(self.inp[nm], "h p dc e -> h p (dc e)")[h + 1],
                                        waits=wfree[1 - r] if j == 0 else (),
                                        sem=f"a0w{1 - r}")
            pp = (ps_a, ps_b)
            grp = 0
            proj_done_q = proj_done_k = proj_done_v = None
            for (wt, dst, eng) in ((wq, q_sb, "scalar"), (wk, k_sb, "vector")):
                for w in range(NW):
                    bank = pp[grp % 2]
                    for dc in range(NDC):
                        ph.op("tensor", lambda e, wt=wt, dc=dc, w=w, bank=bank: e.matmul(
                            bank[:, 0:WIN], wt[:, dc * 128:(dc + 1) * 128],
                            biga[:, dc * T + w * WIN: dc * T + (w + 1) * WIN],
                            start=(dc == 0), stop=(dc == NDC - 1)),
                            waits=([wld[r]] + pp_war[grp % 2]) if dc == 0 else (),
                            inc=(st, 1) if dc == NDC - 1 else None)
                    mmv = st.v
                    if eng == "scalar":
                        tgt = ph.op("scalar", lambda e, dst=dst, w=w, bank=bank: e.activation(
                            out=dst[:, w * WIN:(w + 1) * WIN], in_=bank[:, 0:WIN],
                            func=AF.Copy), waits=[(ss, 0), (st, mmv)], inc=(ss, 1))
                        pp_war[grp % 2] = [(ss, tgt)]
                    else:
                        tgt = ph.op("vector", lambda e, dst=dst, w=w, bank=bank: e.tensor_copy(
                            dst[:, w * WIN:(w + 1) * WIN], bank[:, 0:WIN]),
                            waits=[(st, mmv)], inc=(sv, 1))
                        pp_war[grp % 2] = [(sv, tgt)]
                    grp += 1
                if eng == "scalar":
                    proj_done_q = tgt
                else:
                    proj_done_k = tgt
            v_war = []
            for tc in range(NTC):
                for dc in range(NDC):
                    ph.op("tensor", lambda e, tc=tc, dc=dc, wv=wv: e.matmul(
                        ps_v[:, 0:128],
                        biga[:, dc * T + tc * 128: dc * T + (tc + 1) * 128],
                        wv[:, dc * 128:(dc + 1) * 128],
                        start=(dc == 0), stop=(dc == NDC - 1)),
                        waits=v_war if dc == 0 else (),
                        inc=(st, 1) if dc == NDC - 1 else None)
                mmv = st.v
                vv = ph.op("vector", lambda e, tc=tc: e.tensor_copy(
                    v_sb[:, tc * 128:(tc + 1) * 128], ps_v[:, 0:128]),
                    waits=[(st, mmv)], inc=(sv, 1))
                v_war = [(sv, vv)]
            proj_done_v = vv
            wfree[r] = []
            pvmm = 0
            for w in range(NW):
                nkc = 4 * w + 4
                for kc in range(nkc):
                    bank = (ps_a, ps_b)[kc % 2]
                    wts = [(ss, proj_done_q), (sv, proj_done_k), (ss, exp_done[kc % 2])]
                    ph.op("tensor", lambda e, kc=kc, w=w, bank=bank: e.matmul(
                        bank[:, 0:WIN], k_sb[:, kc * 128:(kc + 1) * 128],
                        q_sb[:, w * WIN:(w + 1) * WIN], start=True, stop=True),
                        waits=wts, inc=(st, 1))
                    smm = st.v
                    diag = kc >= 4 * w
                    if diag:
                        dlt = kc - 4 * w
                        vv = ph.op("vector", lambda e, bank=bank, dlt=dlt: e.scalar_tensor_tensor(
                            out=dtmp[:], in0=bank[:, 0:WIN], scalar=1.0,
                            in1=self.cmask[:, dlt * WIN:(dlt + 1) * WIN],
                            op0=ALU.mult, op1=ALU.add),
                            waits=[(st, smm), (ss, dtmp_war), (self.sg, cmask_rdy)],
                            inc=(sv, 1))
                        src = dtmp
                        wts2 = [(sv, vv)]
                    else:
                        src = bank
                        wts2 = [(st, smm)]
                    ed = ph.op("scalar", lambda e, src=src, kc=kc: e.activation(
                        out=pt_sb[:, kc * WIN:(kc + 1) * WIN],
                        in_=(src[:, 0:WIN]), func=AF.Exp,
                        bias=self.fc[:, 1:2], scale=INV_SQRT_DH),
                        waits=wts2, inc=(ss, 1))
                    exp_done[kc % 2] = ed
                    if diag:
                        dtmp_war = ed
                    ph.op("tensor", lambda e, kc=kc: e.matmul(
                        ps_l[0:1, 0:WIN], self.ones_col_b[:],
                        pt_sb[:, kc * WIN:(kc + 1) * WIN],
                        start=(kc == 0), stop=(kc == nkc - 1)),
                        waits=[(ss, ed)] + ([(sv, l_war)] if kc == 0 else []))
                    ph.op("tensor", lambda e, kc=kc: e.matmul(
                        ps_z[:, 0:WIN], v_sb[:, kc * 128:(kc + 1) * 128],
                        pt_sb[:, kc * WIN:(kc + 1) * WIN],
                        start=(kc == 0), stop=(kc == nkc - 1)),
                        waits=[(sv, proj_done_v), (sv, z_war)] if kc == 0 else (),
                        inc=(st, 1) if kc == nkc - 1 else None)
                pvmm = st.v
                vr = ph.op("vector", lambda e: e.reciprocal(lrow[:], ps_l[0:1, 0:WIN]),
                           waits=[(st, pvmm)], inc=(sv, 1))
                l_war = vr
                bmm = ph.op("tensor", lambda e: e.matmul(
                    ps_c[:, 0:WIN], self.ones_row_f[:], lrow[:], start=True, stop=True),
                    waits=[(sv, vr)], inc=(st, 1))
                bev = ph.op("vector", lambda e: e.tensor_copy(linv_bc[:], ps_c[:, 0:WIN]),
                            waits=[(st, bmm)], inc=(sv, 1))
                z_war = ph.op("vector", lambda e, h=h, w=w: e.tensor_tensor(
                    out=z_sb[:, h * T + w * WIN: h * T + (w + 1) * WIN],
                    in0=ps_z[:, 0:WIN], in1=linv_bc[:], op=ALU.mult),
                    waits=[(sv, bev)], inc=(sv, 1))
            wfree[r] = [(st, pvmm)]
            pp_war[0] = pp_war[1] = [(st, pvmm)]
            if h == 0 and getattr(self, "dbg_attn", None):
                d = self.dbg_attn
                ph.dma("sync", d["q"], q_sb[:], waits=[(ss, proj_done_q)])
                ph.dma("sync", d["k"], k_sb[:], waits=[(sv, proj_done_k)])
                ph.dma("sync", d["v"], v_sb[:], waits=[(sv, proj_done_v)])
                ph.dma("sync", d["pt"], pt_sb[:], waits=[(st, pvmm)])
        ph.fence()
        ph.run()
        es.close()

    def emit_oproj0(self, z_sb, wo_area, x0T_dram, x1T_dram, accum):
        sv, ss, st, qs, qg = self.sv, self.ss, self.st, self.qs, self.qg
        es = ExitStack()
        ph = Ph(self)
        x0s = [self.sb(es, f"o0_x{i}", [128, T], F32) for i in range(2)]
        stage = [self.sb(es, f"o0_s{i}", [128, WIN], F32) for i in range(2)]
        ps_o = [self.ps(es, f"o0_p{i}") for i in range(2)]
        for h in range(NH):
            wo_ld = ph.dma("gpsimd", wo_area[:, h * 2048:(h + 1) * 2048],
                           rr(self.inp["wo0"], "h p dc e -> h p (dc e)")[h], sem="o0wo")
        x0_war = [[], []]
        stage_war = [[], []]
        evac = [0, 0]
        for dc in range(NDC):
            r = dc % 2
            xw = ph.dma("sync", x0s[r][:], x0T_dram[dc * 128:(dc + 1) * 128, :],
                        waits=x0_war[r], sem=f"o0x{r}")
            for w in range(NW):
                g = (dc * NW + w) % 2
                for h in range(NH):
                    ph.op("tensor", lambda e, h=h, dc=dc, w=w, g=g: e.matmul(
                        ps_o[g][:, 0:WIN],
                        wo_area[:, h * 2048 + dc * 128: h * 2048 + (dc + 1) * 128],
                        z_sb[:, h * T + w * WIN: h * T + (w + 1) * WIN],
                        start=(h == 0), stop=(h == NH - 1)),
                        waits=[wo_ld, (sv, evac[g])] if h == 0 else (),
                        inc=(st, 1) if h == NH - 1 else None)
                mmv = st.v
                ev = ph.op("vector", lambda e, r=r, w=w, g=g: e.tensor_tensor(
                    out=stage[g][:], in0=ps_o[g][:, 0:WIN],
                    in1=x0s[r][:, w * WIN:(w + 1) * WIN], op=ALU.add),
                    waits=[(st, mmv), xw] + stage_war[g],
                    inc=(sv, 1))
                evac[g] = ev
                dst = x1T_dram[dc * 128:(dc + 1) * 128, w * WIN:(w + 1) * WIN]
                sw = ph.dma("gpsimd" if accum else "sync", dst, stage[g][:],
                            waits=[(sv, ev)], sem=f"o0s{g}",
                            **({"accum_op": ALU.add} if accum else {}))
                stage_war[g] = [sw]
            x0_war[r] = [(sv, evac[0]), (sv, evac[1])]
        ph.fence()
        ph.run()
        es.close()

    def emit_ffn0(self, biga, h_dram, x1T_dram):
        sv, ss, st, qs, qg = self.sv, self.ss, self.st, self.qs, self.qg
        es = ExitStack()
        ph = Ph(self)
        w1t = [self.sb(es, f"f0_w1{i}", [128, 2048], BF16) for i in range(2)]
        hstg = [self.sb(es, f"f0_h{i}", [128, 2048], BF16) for i in range(2)]
        b1c = self.sb(es, "f0_b1c", [128, 64], F32)
        b2c = self.sb(es, "f0_b2c", [128, 16], F32)
        m16 = self.sb(es, "f0_m16", [64, 128], F32)
        mask_bc = self.sb(es, "f0_mbc", [128, T], F32)
        es_a = ExitStack()
        ps_h = self.ps(es_a, "f0_ph", (128, 1024), F32)
        ps_c = self.ps(es_a, "f0_pc")

        wld = ph.dma("sync", m16[:], self.inp["b1p"], sem="f0b")
        tmm = ph.op("tensor", lambda e: e.transpose(
            ps_c[0:128, 0:64], m16[:], self.ident_f[0:64, 0:64]),
            waits=[wld], inc=(st, 1))
        v1 = ph.op("vector", lambda e: e.tensor_copy(b1c[:], ps_c[0:128, 0:64]),
                   waits=[(st, tmm)], inc=(sv, 1))
        wld = ph.dma("sync", m16[0:16, :], self.inp["b2p"], sem="f0b",
                     waits=[(st, st.v)])
        tmm = ph.op("tensor", lambda e: e.transpose(
            ps_c[0:128, 0:16], m16[0:16, :], self.ident_f[0:16, 0:16]),
            waits=[wld, (sv, v1)], inc=(st, 1))
        v2 = ph.op("vector", lambda e: e.tensor_copy(b2c[:], ps_c[0:128, 0:16]),
                   waits=[(st, tmm)], inc=(sv, 1))
        v_mb = v2
        for w in range(NW):
            tmm = ph.op("tensor", lambda e, w=w: e.matmul(
                ps_c[:, 0:WIN], self.ones_row_f[:],
                self.maskrow[0:1, w * WIN:(w + 1) * WIN], start=True, stop=True),
                waits=[(sv, v_mb)], inc=(st, 1))
            v_mb = ph.op("vector", lambda e, w=w: e.tensor_copy(
                mask_bc[:, w * WIN:(w + 1) * WIN], ps_c[:, 0:WIN]),
                waits=[(st, tmm)], inc=(sv, 1))

        if getattr(self, "dbg_ffn", None):
            ph.dma("sync", self.dbg_ffn["mask_bc"], mask_bc[:], waits=[(sv, v_mb)])
            ph.dma("sync", self.dbg_ffn["maskrow"], self.maskrow[:])
        h_st = [(qs, 0)] * NFC
        w1ld = [(qg, 0), (qg, 0)]
        w1war = [[], []]
        w1ld[0] = ph.dma("gpsimd", w1t[0][:],
                         rr(self.inp["w1p"], "f p dc e -> f p (dc e)")[0], sem="f0w10")
        hst_war = [[], []]
        gel = [0, 0]
        for fc in range(NFC):
            r = fc % 2
            if fc + 1 < NFC:
                w1ld[1 - r] = ph.dma("gpsimd", w1t[1 - r][:],
                                     rr(self.inp["w1p"], "f p dc e -> f p (dc e)")[fc + 1],
                                     waits=w1war[1 - r], sem=f"f0w1{1 - r}")
            for w in range(NW):
                g = w % 2
                for dc in range(NDC):
                    ph.op("tensor", lambda e, fc=fc, w=w, dc=dc, r=r, g=g: e.matmul(
                        ps_h[:, g * WIN:(g + 1) * WIN],
                        w1t[r][:, dc * 128:(dc + 1) * 128],
                        biga[:, dc * T + w * WIN: dc * T + (w + 1) * WIN],
                        start=(dc == 0), stop=(dc == NDC - 1)),
                        waits=[w1ld[r], (ss, gel[g])] if dc == 0 else (),
                        inc=(st, 1) if dc == NDC - 1 else None)
                mmv = st.v
                gel[g] = ph.op("scalar", lambda e, fc=fc, w=w, r=r, g=g: e.activation(
                    out=hstg[r][:, w * WIN:(w + 1) * WIN], in_=ps_h[:, g * WIN:(g + 1) * WIN],
                    func=AF.Gelu_apprx_tanh, bias=b1c[:, fc:fc + 1]),
                    waits=[(st, mmv), (sv, v2)] + (hst_war[r] if w == 0 else []),
                    inc=(ss, 1))
            w1war[r] = [(st, st.v)]
            hw = ph.dma("sync", h_dram[fc], hstg[r][:], waits=[(ss, gel[0]), (ss, gel[1])],
                        sem=f"f0h{r}")
            h_st[fc] = hw
            hst_war[r] = [hw]

        ph.fence()
        ph.run()
        es_a.close()
        ph = Ph(self)
        h_all = [(self.dsem("f0h0"), self.dsem("f0h0").v),
                 (self.dsem("f0h1"), self.dsem("f0h1").v)]
        w2t = [self.sb(es, f"f0_w2{i}", [128, WIN], BF16) for i in range(3)]
        ht = [self.sb(es, f"f0_ht{i}", [128, 1024], BF16) for i in range(3)]
        stg = [self.sb(es, f"f0_s{i}", [128, WIN], F32) for i in range(2)]
        es_b = ExitStack()
        ps8 = self.ps(es_b, "f0_p8", (128, 4096), F32)
        self._f0_ev = 0
        w2war = [[] for _ in range(3)]
        htwar = [[] for _ in range(3)]
        stg_war = [[], []]
        for dq in range(4):
            for wp in range(2):
                evacs = st.v
                for fc in range(NFC):
                    r3 = fc % 3
                    w2ld = ph.dma("gpsimd", w2t[r3][:],
                                  self.inp["w2p"][fc][:, dq * WIN:(dq + 1) * WIN],
                                  waits=w2war[r3], sem=f"f0w2{r3}")
                    htld = ph.dma("sync", ht[r3][:],
                                  h_dram[fc][:, wp * 1024:(wp + 1) * 1024],
                                  waits=htwar[r3] + (h_all if fc < 3 else []),
                                  sem=f"f0ht{r3}")
                    for dcl in range(4):
                        for wl in range(2):
                            ph.op("tensor", lambda e, fc=fc, dcl=dcl, wl=wl, r3=r3: e.matmul(
                                ps8[:, (dcl * 2 + wl) * WIN:(dcl * 2 + wl + 1) * WIN],
                                w2t[r3][:, dcl * 128:(dcl + 1) * 128],
                                ht[r3][:, wl * WIN:(wl + 1) * WIN],
                                start=(fc == 0), stop=(fc == NFC - 1)),
                                waits=([w2ld, htld] +
                                       ([(sv, self._f0_ev)] if fc == 0 else []))
                                      if (dcl == 0 and wl == 0) else (),
                                inc=(st, 1) if (dcl == 3 and wl == 1) else None)
                    w2war[r3] = [(st, st.v)]
                    htwar[r3] = [(st, st.v)]
                mmv = st.v
                for dcl in range(4):
                    for wl in range(2):
                        dc = dq * 4 + dcl
                        w = wp * 2 + wl
                        g = (dcl * 2 + wl) % 2
                        ev = ph.op("vector", lambda e, dcl=dcl, wl=wl, dc=dc, w=w, g=g:
                                   e.scalar_tensor_tensor(
                                       out=stg[g][:],
                                       in0=ps8[:, (dcl * 2 + wl) * WIN:(dcl * 2 + wl + 1) * WIN],
                                       scalar=b2c[:, dc:dc + 1],
                                       in1=mask_bc[:, w * WIN:(w + 1) * WIN],
                                       op0=ALU.add, op1=ALU.mult),
                                   waits=[(st, mmv), (sv, v_mb)] + stg_war[g],
                                   inc=(sv, 1))
                        sw = ph.dma("sync",
                                    x1T_dram[dc * 128:(dc + 1) * 128, w * WIN:(w + 1) * WIN],
                                    stg[g][:], waits=[(sv, ev)], sem=f"f0s{g}")
                        stg_war[g] = [sw]
                self._f0_ev = sv.v
        ph.fence()
        ph.run()
        es_b.close()
        es.close()

    def emit_gla(self, biga, ybuf):
        sv, ss, st, qs, qg = self.sv, self.ss, self.st, self.qs, self.qg
        es = ExitStack()
        ph = Ph(self)
        wring = [[self.sb(es, f"g_w{j}{r}", [128, 2048], BF16) for j in range(4)]
                 for r in range(2)]
        g_sb = self.sb(es, "g_g", [128, T], F32)
        L_sb = self.sb(es, "g_L", [128, T], F32)
        sA = self.sb(es, "g_sA", [128, WIN], F32)
        sB = self.sb(es, "g_sB", [128, WIN], F32)
        sC = self.sb(es, "g_sC", [128, WIN], F32)
        ktf = self.sb(es, "g_ktf", [128, WIN], F32)
        eend = self.sb(es, "g_eend", [128, 64], F32)
        kpw = self.sb(es, "g_kpw", [128, WIN], BF16)
        qt = self.sb(es, "g_qt", [128, T], BF16)
        ktb = self.sb(es, "g_ktb", [128, T], BF16)
        kptok = self.sb(es, "g_kpt", [128, T], BF16)
        v_sb = self.sb(es, "g_v", [128, T], BF16)
        S_f = [self.sb(es, f"g_S{i}", [128, 128], F32) for i in range(2)]
        S_b = self.sb(es, "g_Sb", [128, 128], BF16)
        Mm = self.sb(es, "g_Mm", [128, 128], BF16)
        ps_a = self.ps(es, "g_pa")
        ps_b = self.ps(es, "g_pb")
        ps_v = self.ps(es, "g_pv")
        ps_t = self.ps(es, "g_pt", (128, 2048), BF16)
        ps_y = self.ps(es, "g_py")
        ps_s = self.ps(es, "g_ps")
        ps_o = self.ps(es, "g_po")

        self._gla_mm_war = 0
        self._gla_S_war = 0
        self._gla_sb_done = 0
        self._gla_qw_war = 0
        self._gla_em_war = 0
        self._gla_tp2 = 0
        wnames = ("wg1", "wq1", "wk1", "wv1")
        wld = [(qg, 0), (qg, 0)]
        wfree = [[], []]
        for j, nm in enumerate(wnames):
            wld[0] = ph.dma("gpsimd", wring[0][j][:],
                            rr(self.inp[nm], "h p dc e -> h p (dc e)")[0], sem="gw0")

        def eend_ap(w):
            hdl = eend[:].tensor
            return bass.AP(hdl, w * 16, [[64, 128], [1, 16], [0, 32]])

        pp_war = [[], []]
        for h in range(NH):
            r = h % 2
            wg, wq, wk, wv = wring[r]
            if h + 1 < NH:
                for j, nm in enumerate(wnames):
                    wld[1 - r] = ph.dma("gpsimd", wring[1 - r][j][:],
                                        rr(self.inp[nm], "h p dc e -> h p (dc e)")[h + 1],
                                        waits=wfree[1 - r] if j == 0 else (),
                                        sem=f"gw{1 - r}")
            scr_war = 0
            for w in range(NW):
                bank = (ps_a, ps_b)[w % 2]
                for dc in range(NDC):
                    ph.op("tensor", lambda e, dc=dc, w=w, wg=wg, bank=bank: e.matmul(
                        bank[:, 0:WIN], wg[:, dc * 128:(dc + 1) * 128],
                        biga[:, dc * T + w * WIN: dc * T + (w + 1) * WIN],
                        start=(dc == 0), stop=(dc == NDC - 1)),
                        waits=([wld[r]] + pp_war[w % 2]) if dc == 0 else (),
                        inc=(st, 1) if dc == NDC - 1 else None)
                mmv = st.v
                vzm = ph.op("vector", lambda e, bank=bank: e.tensor_scalar_min(
                    sA[:], bank[:, 0:WIN], 0.0),
                    waits=[(st, mmv), (ss, scr_war)], inc=(sv, 1))
                saz = ph.op("scalar", lambda e, bank=bank: e.activation(
                    out=sB[:], in_=bank[:, 0:WIN], func=AF.Abs),
                    waits=[(st, mmv), (sv, vzm)], inc=(ss, 1))
                pp_war[w % 2] = [(ss, saz)]
                ph.op("scalar", lambda e: e.activation(out=sC[:], in_=sB[:], func=AF.Exp,
                                                       scale=-1.0))
                slb = ph.op("scalar", lambda e: e.activation(
                    out=sB[:], in_=sC[:], func=AF.Ln, bias=self.fc[:, 2:3]), inc=(ss, 1))
                scr_war = slb
                ph.op("vector", lambda e, w=w: e.tensor_sub(
                    g_sb[:, w * WIN:(w + 1) * WIN], sA[:], sB[:]),
                    waits=[(ss, slb)], inc=(sv, 1))
            g_done = sv.v
            vL = 0
            for w in range(NW):
                vL = ph.op("vector", lambda e, w=w: e.tensor_tensor_scan(
                    out=L_sb[:, w * WIN:(w + 1) * WIN], data0=self.scanmask[:],
                    data1=g_sb[:, w * WIN:(w + 1) * WIN], initial=0.0,
                    op0=ALU.mult, op1=ALU.add),
                    waits=[(sv, g_done)] if w == 0 else (), inc=(sv, 1))
            sEe = ph.op("scalar", lambda e: e.activation(
                out=eend[:], in_=L_sb[:, 31:T:GLA_C], func=AF.Exp),
                waits=[(sv, vL)], inc=(ss, 1))
            kpt_done = 0
            tp_war = 0
            for w in range(NW):
                bank = (ps_a, ps_b)[w % 2]
                bank2 = (ps_a, ps_b)[(w + 1) % 2]
                for dc in range(NDC):
                    ph.op("tensor", lambda e, dc=dc, w=w, wq=wq, bank=bank: e.matmul(
                        bank[:, 0:WIN], wq[:, dc * 128:(dc + 1) * 128],
                        biga[:, dc * T + w * WIN: dc * T + (w + 1) * WIN],
                        start=(dc == 0), stop=(dc == NDC - 1)),
                        waits=pp_war[w % 2] if dc == 0 else (),
                        inc=(st, 1) if dc == NDC - 1 else None)
                mmq = st.v
                sEw = ph.op("scalar", lambda e, w=w: e.activation(
                    out=sA[:], in_=L_sb[:, w * WIN:(w + 1) * WIN], func=AF.Exp),
                    waits=[(ss, sEe), (sv, self._gla_qw_war)], inc=(ss, 1))
                vq = ph.op("vector", lambda e, w=w, bank=bank: e.scalar_tensor_tensor(
                    out=qt[:, w * WIN:(w + 1) * WIN], in0=bank[:, 0:WIN],
                    scalar=INV_SQRT_DH, in1=sA[:], op0=ALU.mult, op1=ALU.mult),
                    waits=[(st, mmq), (ss, sEw)], inc=(sv, 1))
                self._gla_qw_war = vq
                pp_war[w % 2] = [(sv, vq)]
                for dc in range(NDC):
                    ph.op("tensor", lambda e, dc=dc, w=w, wk=wk, bank2=bank2: e.matmul(
                        bank2[:, 0:WIN], wk[:, dc * 128:(dc + 1) * 128],
                        biga[:, dc * T + w * WIN: dc * T + (w + 1) * WIN],
                        start=(dc == 0), stop=(dc == NDC - 1)),
                        waits=pp_war[(w + 1) % 2] if dc == 0 else (),
                        inc=(st, 1) if dc == NDC - 1 else None)
                mmk = st.v
                sEm = ph.op("scalar", lambda e, w=w: e.activation(
                    out=sC[:], in_=L_sb[:, w * WIN:(w + 1) * WIN], func=AF.Exp,
                    scale=-1.0), waits=[(sv, self._gla_em_war)], inc=(ss, 1))
                vktf = ph.op("vector", lambda e, bank2=bank2: e.tensor_mul(
                    ktf[:], bank2[:, 0:WIN], sC[:]),
                    waits=[(st, mmk), (ss, sEm)], inc=(sv, 1))
                self._gla_em_war = vktf
                pp_war[(w + 1) % 2] = [(sv, vktf)]
                ph.op("vector", lambda e, w=w: e.tensor_copy(
                    ktb[:, w * WIN:(w + 1) * WIN], ktf[:]))
                vkp = ph.op("vector", lambda e, w=w: e.tensor_tensor(
                    out=kpw[:], in0=ktf[:], in1=eend_ap(w), op=ALU.mult),
                    waits=[(ss, sEe), (sv, tp_war)], inc=(sv, 1))
                for j4 in range(4):
                    tc = w * 4 + j4
                    g2 = j4 % 2
                    tmm = ph.op("tensor", lambda e, j4=j4, g2=g2: e.transpose(
                        ps_t[:, g2 * 1024:g2 * 1024 + 128], kpw[:, j4 * 128:(j4 + 1) * 128],
                        self.ident_b[:]),
                        waits=[(sv, vkp)] if j4 == 0 else [(sv, self._gla_tp2)],
                        inc=(st, 1))
                    kpt_done = ph.op("vector", lambda e, tc=tc, g2=g2: e.tensor_copy(
                        kptok[:, tc * 128:(tc + 1) * 128],
                        ps_t[:, g2 * 1024:g2 * 1024 + 128]),
                        waits=[(st, tmm)], inc=(sv, 1))
                    if j4 >= 1:
                        self._gla_tp2 = kpt_done
                tp_war = kpt_done
            v_war = []
            for tc in range(NTC):
                for dc in range(NDC):
                    ph.op("tensor", lambda e, tc=tc, dc=dc, wv=wv: e.matmul(
                        ps_v[:, 0:128],
                        biga[:, dc * T + tc * 128: dc * T + (tc + 1) * 128],
                        wv[:, dc * 128:(dc + 1) * 128],
                        start=(dc == 0), stop=(dc == NDC - 1)),
                        waits=v_war if dc == 0 else (),
                        inc=(st, 1) if dc == NDC - 1 else None)
                mmv = st.v
                vv = ph.op("vector", lambda e, tc=tc: e.tensor_copy(
                    v_sb[:, tc * 128:(tc + 1) * 128], ps_v[:, 0:128]),
                    waits=[(st, mmv)], inc=(sv, 1))
                v_war = [(sv, vv)]
            v_done = vv
            vS = ph.op("vector", lambda e: e.memset(S_f[1][:], 0.0), inc=(sv, 1))
            sSb = ph.op("scalar", lambda e: e.activation(out=S_b[:], in_=S_f[1][:],
                                                         func=AF.Copy),
                        waits=[(sv, vS)], inc=(ss, 1))
            self._gla_sb_done = sSb
            qk_ready = kpt_done
            y_evac = 0
            for tb in range(NTC):
                smm = ph.op("tensor", lambda e, tb=tb: e.matmul(
                    ps_s[:, 0:128], ktb[:, tb * 128:(tb + 1) * 128],
                    qt[:, tb * 128:(tb + 1) * 128], start=True, stop=True),
                    waits=[(sv, self._gla_mm_war), (sv, qk_ready)],
                    inc=(st, 1))
                vMm = ph.op("vector", lambda e: e.scalar_tensor_tensor(
                    out=Mm[:], in0=ps_s[:, 0:128], scalar=1.0, in1=self.bdtril[:],
                    op0=ALU.mult, op1=ALU.mult),
                    waits=[(st, smm)], inc=(sv, 1))
                self._gla_mm_war = vMm
                ph.op("tensor", lambda e, tb=tb: e.matmul(
                    ps_y[:, 0:128], v_sb[:, tb * 128:(tb + 1) * 128], Mm[:],
                    start=True, stop=False),
                    waits=[(sv, vMm), (sv, v_done), (sv, y_evac)])
                for c4 in range(4):
                    chunk = tb * 4 + c4
                    base = c4 * 32
                    tp = (96, 0) if base == 96 else None
                    ph.op("tensor", lambda e, tb=tb, c4=c4: e.matmul(
                        ps_y[:, c4 * 32:(c4 + 1) * 32], S_b[:],
                        qt[:, (tb * 128 + c4 * 32):(tb * 128 + c4 * 32) + 32],
                        start=False, stop=(c4 == 3)),
                        waits=[(ss, self._gla_sb_done)])
                    omm = ph.op("tensor", lambda e, tb=tb, c4=c4, base=base, tp=tp: e.matmul(
                        ps_o[:, 0:128],
                        kptok[base:base + 32, tb * 128:(tb + 1) * 128],
                        v_sb[base:base + 32, tb * 128:(tb + 1) * 128],
                        start=True, stop=True, tile_position=tp),
                        waits=[(sv, kpt_done if tb == 0 else 0), (sv, self._gla_S_war)],
                        inc=(st, 1))
                    par = chunk % 2
                    vSn = ph.op("vector", lambda e, chunk=chunk, par=par: e.scalar_tensor_tensor(
                        out=S_f[par][:], in0=S_f[1 - par][:],
                        scalar=eend[:, chunk:chunk + 1],
                        in1=ps_o[:, 0:128], op0=ALU.mult, op1=ALU.add),
                        waits=[(st, omm)], inc=(sv, 1))
                    self._gla_S_war = vSn
                    self._gla_sb_done = ph.op("scalar", lambda e, par=par: e.activation(
                        out=S_b[:], in_=S_f[par][:], func=AF.Copy),
                        waits=[(sv, vSn)], inc=(ss, 1))
                y_evac = ph.op("vector", lambda e, h=h, tb=tb: e.tensor_copy(
                    ybuf[:, h * T + tb * 128: h * T + (tb + 1) * 128], ps_y[:, 0:128]),
                    waits=[(st, st.v)], inc=(sv, 1))
            wfree[r] = [(st, st.v)]
            head_done = y_evac
        ph.fence()
        ph.run()
        es.close()

    def emit_moe(self, biga, act_sb, x2T_dram):
        sv, ss, st, qs, qg = self.sv, self.ss, self.st, self.qs, self.qg
        es = ExitStack()
        ph = Ph(self)
        ps8 = self.ps(es, "m_p8", (128, 4096), F32)
        glr = [self.sb(es, f"m_glr{i}", [8, 128], F32) for i in range(2)]
        self._m_s0war = 0
        self._m_s8war = 0
        compr = self.sb(es, "m_cpr", [8, T], F32)
        crow = self.sb(es, "m_crw", [1, T], F32)
        dn_sb = self.sb(es, "m_dn", [128, NJC * 2048], BF16)
        gur = [[self.sb(es, f"m_gu{i}{r}", [128, 2048], BF16) for i in range(2)]
               for r in range(2)]
        c_bc = self.sb(es, "m_cbc", [128, T], F32)

        stage = [self.sb(es, f"m_st{i}", [128, WIN], F32) for i in range(4)]
        sA2 = stage[0:2]

        glt_all = self.sb(es, "m_glta", [128, 128], F32)
        prb_all = self.sb(es, "m_prba", [128, 128], F32)
        prb2_all = self.sb(es, "m_prb2a", [128, 128], F32)
        m8_all = self.sb(es, "m_m8a", [128, 128], F32)
        msk_all = glt_all
        cmb_all = self.sb(es, "m_cmba", [128, 128], F32)
        rca = self.sb(es, "m_rca", [128, 64], F32)

        s0 = []
        glwar = [0, 0]
        for tc in range(NTC):
            rg = tc % 2
            glld = ph.dma("sync", glr[rg][:], self.gl_dram[0:8, tc * 128:(tc + 1) * 128],
                          waits=[(self.st, glwar[rg])], sem=f"mgl{rg}")
            tmm = ph.op("tensor", lambda e, rg=rg, tc=tc: e.transpose(
                ps8[0:128, (tc % 4) * 512:(tc % 4) * 512 + 8], glr[rg][:],
                self.ident_f[0:8, 0:8]),
                waits=[glld, (sv, self._m_s0war)], inc=(st, 1))
            glwar[rg] = tmm
            s0.append(ph.op("vector", lambda e, tc=tc: e.tensor_copy(
                glt_all[:, tc * 8:(tc + 1) * 8],
                ps8[0:128, (tc % 4) * 512:(tc % 4) * 512 + 8]),
                waits=[(st, tmm)], inc=(sv, 1)))
            if tc % 4 == 3:
                self._m_s0war = s0[-1]
        s1 = []
        for tc in range(NTC):
            s1.append(ph.op("vector", lambda e, tc=tc: e.reduce_max(
                rca[:, tc * 4:tc * 4 + 1], glt_all[:, tc * 8:(tc + 1) * 8],
                axis=AX.X, negate=True), inc=(sv, 1)))
        s2 = []
        for tc in range(NTC):
            s2.append(ph.op("scalar", lambda e, tc=tc: e.activation(
                out=prb_all[:, tc * 8:(tc + 1) * 8], in_=glt_all[:, tc * 8:(tc + 1) * 8],
                func=AF.Exp, bias=rca[:, tc * 4:tc * 4 + 1],
                accum_out=rca[:, tc * 4 + 1:tc * 4 + 2]),
                waits=[(sv, s1[tc])] if tc == 0 else [(sv, s1[tc])], inc=(ss, 1)))
        ph.op("scalar", lambda e: e.activation(out=rca[:, 3:4], in_=rca[:, 0:1],
                                               func=AF.Copy), inc=(ss, 1))
        s3 = []
        for tc in range(NTC):
            s3.append(ph.op("vector", lambda e, tc=tc: e.reciprocal(
                rca[:, tc * 4 + 2:tc * 4 + 3], rca[:, tc * 4 + 1:tc * 4 + 2]),
                waits=[(ss, s2[tc] + 1)], inc=(sv, 1)))
        s4 = []
        for tc in range(NTC):
            s4.append(ph.op("vector", lambda e, tc=tc: e.tensor_scalar_mul(
                prb2_all[:, tc * 8:(tc + 1) * 8], prb_all[:, tc * 8:(tc + 1) * 8],
                rca[:, tc * 4 + 2:tc * 4 + 3]), inc=(sv, 1)))
        s5 = []
        for tc in range(NTC):
            s5.append(ph.op("vector", lambda e, tc=tc: e.max(
                out=m8_all[:, tc * 8:(tc + 1) * 8], in_=prb2_all[:, tc * 8:(tc + 1) * 8]),
                inc=(sv, 1)))
        for tc in range(NTC):
            ph.op("vector", lambda e, tc=tc: e.tensor_scalar(
                out=msk_all[:, tc * 8:(tc + 1) * 8], in0=prb2_all[:, tc * 8:(tc + 1) * 8],
                scalar1=m8_all[:, tc * 8 + 1:tc * 8 + 2], scalar2=None, op0=ALU.is_ge))
        s7 = []
        for tc in range(NTC):
            s7.append(ph.op("vector", lambda e, tc=tc: e.tensor_mul(
                cmb_all[:, tc * 8:(tc + 1) * 8], prb2_all[:, tc * 8:(tc + 1) * 8],
                msk_all[:, tc * 8:(tc + 1) * 8]), inc=(sv, 1)))
        v_cmb = 0
        for tc in range(NTC):
            tmm = ph.op("tensor", lambda e, tc=tc: e.transpose(
                ps8[0:8, (4 + tc % 4) * 512:(4 + tc % 4) * 512 + 128],
                cmb_all[:, tc * 8:(tc + 1) * 8], self.ident_f[:]),
                waits=[(sv, s7[min(tc + 2, NTC - 1)]), (sv, self._m_s8war)], inc=(st, 1))
            v_cmb = ph.op("vector", lambda e, tc=tc: e.tensor_copy(
                compr[0:8, tc * 128:(tc + 1) * 128],
                ps8[0:8, (4 + tc % 4) * 512:(4 + tc % 4) * 512 + 128]),
                waits=[(st, tmm)], inc=(sv, 1))
            if tc % 4 == 3:
                self._m_s8war = v_cmb
        if getattr(self, "dbg_moe", None):
            ph.dma("sync", self.dbg_moe["compr"], compr[:], waits=[(sv, v_cmb)])

        guw = {"g": ("gup", None), "a": ("gsa", "dsa"), "b": ("gsb", "dsb")}
        experts = [("g", e) for e in range(NE)] + [("a", 0), ("b", 0)]
        dn_ld = 0
        dn_war = []
        gu_ld = [[0, 0], [0, 0]]
        gu_war = [[], []]
        cbc_war = 0
        act_war = [0, 0]
        self._m_ev = [[], [], [], []]
        self._m_bank_war = [[], []]
        act_war2 = [[0, 0], [0, 0]]
        x2_base = [[0] * NW for _ in range(NDC)]

        def gu_ap(kind, e, j):
            if kind == "g":
                return rr(self.inp["gup"], "e f p dc q -> e f p (dc q)")[e][j]
            return rr(self.inp["gsa" if kind == "a" else "gsb"],
                      "f p dc q -> f p (dc q)")[j]

        def dn_ap(kind, e):
            if kind == "g":
                return rr(self.inp["dnp"], "e p f d -> e p (f d)")[e]
            return rr(self.inp["dsa" if kind == "a" else "dsb"], "p f d -> p (f d)")

        for ei, (kind, e) in enumerate(experts):
            routed = kind == "g"
            prev_ps_users = (self._m_ev[0] + self._m_ev[1] + self._m_ev[2] +
                             self._m_ev[3] + [(sv, v_cmb)])
            dn_ld = ph.dma("gpsimd", dn_sb[:], dn_ap(kind, e), waits=dn_war, sem="mdn")
            if routed:
                v_cb = 0
                cld = ph.dma("sync", crow[:], compr[e:e + 1, :],
                             waits=[(sv, v_cmb), (st, cbc_war)], sem="mcrw")
                for w in range(NW):
                    tmm = ph.op("tensor", lambda e_, w=w: e_.matmul(
                        ps8[:, 3 * 512:4 * 512], self.ones_row_f[:],
                        crow[0:1, w * WIN:(w + 1) * WIN], start=True, stop=True),
                        waits=[cld, (st, cbc_war), (sv, v_cb)] +
                              self._m_ev[0] + self._m_ev[1] + self._m_ev[2] +
                              self._m_ev[3], inc=(st, 1))
                    v_cb = ph.op("vector", lambda e_, w=w: e_.tensor_copy(
                        c_bc[:, w * WIN:(w + 1) * WIN], ps8[:, 3 * 512:4 * 512]),
                        waits=[(st, tmm)] + self._m_ev[0] + self._m_ev[1] +
                              self._m_ev[2] + self._m_ev[3], inc=(sv, 1))
                cbc_done = v_cb
            for j in range(NJC):
                r = j % 2
                gu_ld[r][0] = ph.dma("gpsimd", gur[r][0][:], gu_ap(kind, e, j),
                                     waits=gu_war[r], sem=f"mgu{r}")
                gu_ld[r][1] = ph.dma("gpsimd", gur[r][1][:], gu_ap(kind, e, j + NJC),
                                     waits=(), sem=f"mgu{r}")
                for w in range(NW):
                    gb = (w % 2) * 4
                    for half, buf in enumerate(gur[r]):
                        for dc in range(NDC):
                            ph.op("tensor", lambda e_, dc=dc, w=w, buf=buf, half=half, gb=gb: e_.matmul(
                                ps8[:, (gb + half) * WIN:(gb + half + 1) * WIN],
                                buf[:, dc * 128:(dc + 1) * 128],
                                biga[:, dc * T + w * WIN: dc * T + (w + 1) * WIN],
                                start=(dc == 0), stop=(dc == NDC - 1)),
                                waits=([gu_ld[r][half]] +
                                       ([(ss, act_war2[w % 2][0]), (sv, act_war2[w % 2][1])]
                                        if half == 0 else []) +
                                       (prev_ps_users if (j == 0 and w == 0) else []))
                                      if dc == 0 else (),
                                inc=(st, 1) if dc == NDC - 1 else None)
                    mmv = st.v
                    s_si = ph.op("scalar", lambda e_, gb=gb, sAb=(w % 2): e_.activation(
                        out=sA2[sAb][:], in_=ps8[:, gb * WIN:(gb + 1) * WIN], func=AF.Silu),
                        waits=[(st, mmv)] + self._m_ev[w % 2], inc=(ss, 1))
                    v_ac = ph.op("vector", lambda e_, j=j, w=w, gb=gb, sAb=(w % 2): e_.tensor_tensor(
                        out=act_sb[:, j * T + w * WIN: j * T + (w + 1) * WIN],
                        in0=sA2[sAb][:], in1=ps8[:, (gb + 1) * WIN:(gb + 2) * WIN],
                        op=ALU.mult),
                        waits=[(ss, s_si), (st, mmv)], inc=(sv, 1))
                    act_war2[w % 2] = [s_si, v_ac]
                gu_war[r] = [(st, st.v)]
            act_done = sv.v
            for gi, (dq, w) in enumerate([(dq, w) for dq in range(4) for w in range(NW)]):
                bs = (gi % 2) * 4
                for j in range(NJC):
                    for dcl in range(4):
                        dc = dq * 4 + dcl
                        ph.op("tensor", lambda e_, j=j, dcl=dcl, dc=dc, w=w, bs=bs: e_.matmul(
                            ps8[:, (bs + dcl) * WIN:(bs + dcl + 1) * WIN],
                            dn_sb[:, j * 2048 + dc * 128: j * 2048 + (dc + 1) * 128],
                            act_sb[:, j * T + w * WIN: j * T + (w + 1) * WIN],
                            start=(j == 0), stop=(j == NJC - 1)),
                            waits=([(sv, act_done), dn_ld] + self._m_bank_war[gi % 2] +
                                   (self._m_ev[0] + self._m_ev[1] + self._m_ev[2] +
                                    self._m_ev[3] if ei == 0 and gi == 0 else []))
                                  if (j == 0 and dcl == 0) else (),
                            inc=(st, 1) if (j == NJC - 1 and dcl == 3) else None)
                mmv = st.v
                evs = []
                for dcl in range(4):
                    dc = dq * 4 + dcl
                    gsl = dcl
                    if routed:
                        ev = ph.op("vector", lambda e_, dcl=dcl, w=w, gsl=gsl, bs=bs: e_.tensor_tensor(
                            out=stage[gsl][:], in0=ps8[:, (bs + dcl) * WIN:(bs + dcl + 1) * WIN],
                            in1=c_bc[:, w * WIN:(w + 1) * WIN], op=ALU.mult),
                            waits=[(st, mmv), (sv, cbc_done)] + self._m_ev[gsl],
                            inc=(sv, 1))
                    else:
                        ev = ph.op("vector", lambda e_, dcl=dcl, gsl=gsl, bs=bs: e_.tensor_copy(
                            stage[gsl][:], ps8[:, (bs + dcl) * WIN:(bs + dcl + 1) * WIN]),
                            waits=[(st, mmv)] + self._m_ev[gsl], inc=(sv, 1))
                    evs.append(ev)
                    dst = x2T_dram[dc * 128:(dc + 1) * 128, w * WIN:(w + 1) * WIN]
                    if ei == 0:
                        sw = ph.dma("gpsimd", dst, stage[gsl][:], waits=[(sv, ev)],
                                    sem=f"mst{gsl}")
                        x2_base[dc][w] = sw
                    else:
                        sw = ph.dma("gpsimd", dst, stage[gsl][:],
                                    waits=[(sv, ev), x2_base[dc][w]],
                                    accum_op=ALU.add, sem=f"mst{gsl}")
                    self._m_ev[gsl] = [sw]
                self._m_bank_war[gi % 2] = [(sv, evs[-1])]
            dn_war = [(st, st.v)]
            cbc_war = st.v
        ph.fence()
        ph.run()
        es.close()

    def emit_oproj1(self, xlnT_dram, wo_area, x1T_dram, x2T_dram):
        sv, ss, st, qs, qg = self.sv, self.ss, self.st, self.qs, self.qg
        es = ExitStack()
        ph = Ph(self)
        xl = [self.sb(es, f"o1_x{i}", [128, 16 * WIN], BF16) for i in range(2)]
        x1s = [self.sb(es, f"o1_r{i}", [128, WIN], F32) for i in range(2)]
        stage = [self.sb(es, f"o1_s{i}", [128, WIN], F32) for i in range(2)]
        ps_o = [self.ps(es, f"o1_p{i}") for i in range(2)]
        for dic in range(NDC):
            wlds = ph.dma("gpsimd", wo_area[:, dic * 2048:(dic + 1) * 2048],
                          self.inp["wo1"][dic], sem="o1wo")
        xl_war = [[], []]
        x1_war = [[], []]
        stage_war = [[], []]
        evac = [0, 0]
        for w in range(NW):
            rw = w % 2
            xld = ph.dma("sync", rr(xl[rw][:], "p (dic t) -> p dic t", dic=16),
                         rr(xlnT_dram, "(dic p) t -> p dic t", p=128)[:, :, w * WIN:(w + 1) * WIN],
                         waits=xl_war[rw], sem=f"o1x{rw}")
            for dcp in range(NDC):
                g = (w * NDC + dcp) % 2
                rx = dcp % 2
                x1d = ph.dma("sync", x1s[rx][:],
                             x1T_dram[dcp * 128:(dcp + 1) * 128, w * WIN:(w + 1) * WIN],
                             waits=x1_war[rx], sem=f"o1r{rx}")
                for dic in range(NDC):
                    ph.op("tensor", lambda e, dic=dic, dcp=dcp, w=w, g=g, rw=rw: e.matmul(
                        ps_o[g][:, 0:WIN],
                        wo_area[:, dic * 2048 + dcp * 128: dic * 2048 + (dcp + 1) * 128],
                        xl[rw][:, dic * WIN:(dic + 1) * WIN],
                        start=(dic == 0), stop=(dic == NDC - 1)),
                        waits=[wlds, xld, (sv, evac[g])] if dic == 0 else (),
                        inc=(st, 1) if dic == NDC - 1 else None)
                mmv = st.v
                ev = ph.op("vector", lambda e, g=g, rx=rx: e.tensor_tensor(
                    out=stage[g][:], in0=ps_o[g][:, 0:WIN], in1=x1s[rx][:], op=ALU.add),
                    waits=[(st, mmv), x1d] + stage_war[g], inc=(sv, 1))
                evac[g] = ev
                x1_war[rx] = [(sv, ev)]
                sw = ph.dma("gpsimd",
                            x2T_dram[dcp * 128:(dcp + 1) * 128, w * WIN:(w + 1) * WIN],
                            stage[g][:], waits=[(sv, ev)], accum_op=ALU.add,
                            sem=f"o1s{g}")
                stage_war[g] = [sw]
            xl_war[rw] = [(st, st.v)]
        ph.fence()
        ph.run()
        es.close()

    def emit_out(self, x2T_dram, out_tok):
        sv, st, qs = self.sv, self.st, self.qs
        es = ExitStack()
        ph = Ph(self)
        xin = [self.sb(es, f"fo_x{i}", [128, T], F32) for i in range(2)]
        stg = [self.sb(es, f"fo_s{i}", [128, 2048], F32) for i in range(2)]
        pt = [self.ps(es, f"fo_p{i}", (128, 1024), F32) for i in range(2)]
        evac_done = [0, 0]
        store_done = [(qs, 0), (qs, 0)]
        xin_war = [0, 0]
        for dc in range(NDC):
            r = dc % 2
            wld = ph.dma("sync", xin[r][:], x2T_dram[dc * 128:(dc + 1) * 128, :],
                         waits=[(st, xin_war[r])], sem=f"fo{r}")
            for half in range(2):
                for tc8 in range(8):
                    tc = half * 8 + tc8
                    w0 = []
                    if tc == 0:
                        w0.append(wld)
                    if tc8 == 0:
                        w0.append((sv, evac_done[r]))
                    ph.op("tensor", lambda e, r=r, tc=tc, tc8=tc8: e.transpose(
                        pt[r][:, tc8 * 128:(tc8 + 1) * 128],
                        xin[r][:, tc * 128:(tc + 1) * 128], self.ident_f[:]),
                        waits=w0, inc=(st, 1) if tc8 == 7 else None)
                tmm = st.v
                evac_done[r] = ph.op("vector", lambda e, r=r, half=half: e.tensor_copy(
                    stg[r][:, half * 1024:(half + 1) * 1024], pt[r][:]),
                    waits=[(st, tmm), store_done[r]], inc=(sv, 1))
            xin_war[r] = st.v
            store_done[r] = ph.dma(
                "sync",
                rr(out_tok, "(tc p) d -> p tc d", p=128)[:, :, dc * 128:(dc + 1) * 128],
                rr(stg[r][:], "p (tc t) -> p tc t", tc=16),
                waits=[(sv, evac_done[r])], sem=f"fos{r}")
        ph.fence()
        ph.run()
        es.close()



def build_full():
    b = Build()
    x_in = b.din("x", [T, D], F32)
    for nm, shp in (("wq0", [16, 128, 16, 128]), ("wk0", [16, 128, 16, 128]),
                    ("wv0", [16, 128, 16, 128]), ("wo0", [16, 128, 16, 128]),
                    ("w1p", [64, 128, 16, 128]), ("w2p", [64, 128, 2048]),
                    ("b1p", [64, 128]), ("b2p", [16, 128]), ("wrp", [16, 128]),
                    ("wq1", [16, 128, 16, 128]), ("wk1", [16, 128, 16, 128]),
                    ("wv1", [16, 128, 16, 128]), ("wg1", [16, 128, 16, 128]),
                    ("wo1", [16, 128, 2048]), ("gmp", [16, 128]), ("btp", [16, 128]),
                    ("gwp", [128, 16, 8]), ("gup", [8, 22, 128, 16, 128]),
                    ("gsa", [22, 128, 16, 128]), ("gsb", [22, 128, 16, 128]),
                    ("dnp", [8, 128, 11, 2048]), ("dsa", [128, 11, 2048]),
                    ("dsb", [128, 11, 2048])):
        b.din(nm, shp, F32)
    x0T = b.dtmp("x0T", [D, T], F32)
    x1T = b.dtmp("x1T", [D, T], F32)
    x2T = b.dtmp("x2T", [D, T], F32)
    h_d = b.dtmp("h_d", [64, 128, 2048], BF16)
    xlnT = b.dtmp("xlnT", [D, T], BF16)
    out = b.dout("out", [T, D], F32)
    biga = b.sb(b.estack, "BIGA", [128, NDC * T], BF16)

    b.emit_consts()
    b.bstack = ExitStack()
    b.emit_transpose_in("b0", x_in, x0T)
    b.emit_ln("b0", x0T, biga, router=True)
    zstack = ExitStack()
    z_sb = b.sb(zstack, "ZBUF", [128, NH * T], BF16)
    b.emit_attn0(biga, z_sb)
    b.emit_ffn0(biga, h_d, x1T)
    b.emit_oproj0(z_sb, biga, x0T, x1T, accum=True)
    zstack.close()
    b.bstack.close()
    b.bstack = ExitStack()
    b.emit_ln("b1", x1T, biga, gate=True)
    ystack = ExitStack()
    ybuf = b.sb(ystack, "YBUF", [128, NH * T], BF16)
    b.emit_gla(biga, ybuf)
    b.emit_ln("yg", None, None, src_bf16=ybuf, gb=True, out_dram=xlnT)
    ystack.close()
    astack = ExitStack()
    act_sb = b.sb(astack, "ACT", [128, NJC * T], BF16)
    b.emit_moe(biga, act_sb, x2T)
    astack.close()
    b.emit_oproj1(xlnT, biga, x1T, x2T)
    b.bstack.close()
    b.emit_out(x2T, out)
    b.estack.close()
    return b


def kernel(**inputs):
    pk = pack_inputs(inputs)
    x = np.ascontiguousarray(inputs["L0_attn_x"].astype(np.float32))
    b = build_full()
    in_maps = []
    for c in range(8):
        m = dict(pk)
        m["x"] = np.ascontiguousarray(x[c])
        in_maps.append(m)
    res = run_bass_kernel_spmd(b.nc, in_maps, core_ids=list(range(8)))
    return np.stack([np.asarray(res.results[c]["out"]) for c in range(8)], axis=0)


# version 04
# speedup vs baseline: 1.3390x; 1.0697x over previous
import numpy as np
import ml_dtypes
from contextlib import ExitStack
import concourse.bass as bass
import concourse.mybir as mybir
from concourse.bass_utils import run_bass_kernel_spmd

F32 = mybir.dt.float32
BF16 = mybir.dt.bfloat16
AF = mybir.ActivationFunctionType
ALU = mybir.AluOpType
AX = mybir.AxisListType

D = 2048
T = 2048
NDC = 16
NTC = 16
NW = 4
WIN = 512
NH = 16
DH = 128
DF = 8192
NFC = 64
KSEL = 1024
NE = 8
FMOE = 1408
NJC = 11
GLA_C = 32
INV_SQRT_DH = float(1.0 / np.sqrt(128.0))
EXP_BIAS = -8.0

ENGS = ("sync", "scalar", "vector", "tensor", "gpsimd")



def pack_inputs(i):
    f = np.ascontiguousarray
    p = {}
    p["wq0"] = f(i["L0_attn_W_Q"].reshape(16, 128, 16, 128).transpose(0, 3, 2, 1))
    p["wk0"] = f(i["L0_attn_W_K"].reshape(16, 128, 16, 128).transpose(0, 3, 2, 1))
    p["wv0"] = f(i["L0_attn_W_V"].reshape(16, 128, 16, 128).transpose(0, 3, 2, 1))
    p["wo0"] = f(i["L0_attn_W_O"].reshape(16, 16, 128, 128).transpose(0, 3, 1, 2))
    p["w1p"] = f(i["L0_ffn_w1"].reshape(16, 128, 64, 128).transpose(2, 1, 0, 3))
    p["w2p"] = f(i["L0_ffn_w2"].reshape(64, 128, 2048))
    p["b1p"] = f(i["L0_ffn_b1"].reshape(64, 128))
    p["b2p"] = f(i["L0_ffn_b2"].reshape(16, 128))
    p["wrp"] = f(i["L0_ffn_w_r"].reshape(16, 128))
    for nm, k in (("wq1", "L1_attn_Wq"), ("wk1", "L1_attn_Wk"),
                  ("wv1", "L1_attn_Wv"), ("wg1", "L1_attn_Wg")):
        p[nm] = f(i[k].reshape(16, 128, 16, 128).transpose(0, 3, 2, 1))
    p["wo1"] = f(np.ascontiguousarray(i["L1_attn_Wo"].T).reshape(16, 128, 2048))
    p["gmp"] = f(i["L1_attn_gamma"].reshape(16, 128))
    p["btp"] = f(i["L1_attn_beta"].reshape(16, 128))
    p["gwp"] = f(np.ascontiguousarray(i["L1_ffn_gate_weight"].T)
                 .reshape(16, 128, 8).transpose(1, 0, 2))
    gu = i["L1_ffn_gate_up_weights"]
    p["gup"] = f(gu.reshape(8, 22, 128, 16, 128).transpose(0, 1, 4, 3, 2))
    sgu = i["L1_ffn_shared_gate_up"]
    sa = np.concatenate([sgu[0:1408], sgu[2816:4224]], axis=0)
    sb_ = np.concatenate([sgu[1408:2816], sgu[4224:5632]], axis=0)
    p["gsa"] = f(sa.reshape(22, 128, 16, 128).transpose(0, 3, 2, 1))
    p["gsb"] = f(sb_.reshape(22, 128, 16, 128).transpose(0, 3, 2, 1))
    dn = i["L1_ffn_down_weights"]
    p["dnp"] = f(dn.transpose(0, 2, 1).reshape(8, 11, 128, 2048).transpose(0, 2, 1, 3))
    sd = np.ascontiguousarray(i["L1_ffn_shared_down"].T)
    p["dsa"] = f(sd[0:1408].reshape(11, 128, 2048).transpose(1, 0, 2))
    p["dsb"] = f(sd[1408:2816].reshape(11, 128, 2048).transpose(1, 0, 2))
    return p



class SemC:
    def __init__(self, h):
        self.h = h
        self.v = 0

    def inc(self, k=1):
        self.v += k
        return self.v


class Ph:

    def __init__(self, bld):
        self.bld = bld
        self.nc = bld.nc
        self.q = {n: [] for n in ENGS}

    def add(self, en, fn):
        self.q[en].append(fn)

    def op(self, en, fn, waits=(), inc=None):
        tgt = inc[0].inc(inc[1]) if inc is not None else None

        def g(eng, fn=fn, waits=tuple(waits), inc=inc):
            for s, v in waits:
                if v > 0:
                    eng.wait_ge(s.h, v)
            ins = fn(eng)
            if inc is not None:
                ins.then_inc(inc[0].h, inc[1])
        self.add(en, g)
        return tgt

    def dma(self, en, out, in_, waits=(), sem=None, **kw):
        if isinstance(sem, str):
            sem = self.bld.dsem(sem)
        if sem is None:
            sem = {"sync": self.bld.qs, "scalar": self.bld.qc, "gpsimd": self.bld.qg}[en]
        tgt = self.op(en, lambda e: e.dma_start(out=out, in_=in_, **kw),
                      waits=waits, inc=(sem, 16))
        return (sem, tgt)

    def wdma(self, *a, **kw):
        return self.dma(*a, **kw)

    def fence(self):
        b = self.bld
        qtgts = [(b.qs, b.qs.v), (b.qc, b.qc.v), (b.qg, b.qg.v)]
        qtgts += [(s, s.v) for s in b.dsems.values()]
        tgt = b.bar.v + 5
        b.bar.v = tgt
        for en in ENGS:
            def g(eng, tgt=tgt, qtgts=tuple(qtgts)):
                eng.sem_inc(b.bar.h, 1)
                eng.wait_ge(b.bar.h, tgt)
                for s, v in qtgts:
                    if v > 0:
                        eng.wait_ge(s.h, v)
            self.add(en, g)

    def run(self):
        with self.nc.Block() as blk:
            for en in ENGS:
                fns = self.q[en]
                if fns:
                    getattr(blk, en)(lambda eng, fns=fns: [fn(eng) for fn in fns])


def rr(ap, spec, **kw):
    return ap.rearrange(spec, **kw)


class Build:
    def __init__(self):
        self.nc = bass.Bass("TRN2", target_bir_lowering=False, debug=False)
        self.inp = {}
        self.estack = ExitStack()
        nc = self.nc
        s = lambda n: SemC(self.estack.enter_context(nc.semaphore(n)))
        self.qs, self.qc, self.qg = s("qs"), s("qc"), s("qg")
        self.sv, self.ss, self.st, self.sg = s("sv"), s("ss"), s("st"), s("sg")
        self.bar = s("bar")
        self._mksem = s
        self.dsems = {}

    def dsem(self, name):
        if name not in self.dsems:
            self.dsems[name] = self._mksem("d_" + name)
        return self.dsems[name]

    def din(self, name, shape, dtype=F32):
        ap = self.nc.dram_tensor(name, list(shape), dtype, kind="ExternalInput").ap()
        self.inp[name] = ap
        return ap

    def dout(self, name, shape, dtype=F32):
        return self.nc.dram_tensor(name, list(shape), dtype, kind="ExternalOutput").ap()

    def dtmp(self, name, shape, dtype=F32):
        return self.nc.dram_tensor(name, list(shape), dtype, kind="Internal").ap()

    def sb(self, es, name, shape, dtype):
        return es.enter_context(self.nc.sbuf_tensor(name, list(shape), dtype))

    def ps(self, es, name, shape=(128, 512), dtype=F32):
        return es.enter_context(self.nc.psum_tensor(name, list(shape), dtype))

    def emit_consts(self):
        es = self.estack
        self.ident_f = self.sb(es, "ident_f", [128, 128], F32)
        self.ident_b = self.sb(es, "ident_b", [128, 128], BF16)
        self.ones_col_f = self.sb(es, "ones_col_f", [128, 1], F32)
        self.ones_col_b = self.sb(es, "ones_col_b", [128, 1], BF16)
        self.ones_row_f = self.sb(es, "ones_row_f", [1, 128], F32)
        self.bdtril = self.sb(es, "bdtril", [128, 128], BF16)
        self.scanmask = self.sb(es, "scanmask", [128, WIN], BF16)
        self.fc = self.sb(es, "fconsts", [128, 8], F32)
        ph = Ph(self)

        def mk(_):
            P = self._pool_ops = []
            for t, fill in ((self.ident_f, 1.0), (self.ident_b, 1.0)):
                P.append(lambda e, t=t: e.memset(t[:], 0.0))
                P.append(lambda e, t=t, fill=fill: e.affine_select(
                    out=t[:], in_=t[:], compare_op=ALU.not_equal, fill=fill,
                    base=0, pattern=[[-1, 128]], channel_multiplier=1))
            P.append(lambda e: e.memset(self.ones_col_f[:], 1.0))
            P.append(lambda e: e.memset(self.ones_col_b[:], 1.0))
            P.append(lambda e: e.memset(self.ones_row_f[:], 1.0))
            P.append(lambda e: e.memset(self.fc[:, 0:1], 1e-5))
            P.append(lambda e: e.memset(self.fc[:, 1:2], EXP_BIAS))
            P.append(lambda e: e.memset(self.fc[:, 2:3], 1.0))
            P.append(lambda e: e.memset(self.bdtril[:], 0.0))
            for b32 in range(4):
                sub = self.bdtril[:, b32 * 32:(b32 + 1) * 32]
                C = b32 * 32
                P.append(lambda e, sub=sub, C=C: e.affine_select(
                    out=sub, in_=sub, compare_op=ALU.is_ge, fill=1.0,
                    base=C - 1, pattern=[[0, 32]], channel_multiplier=-1))
                P.append(lambda e, sub=sub, C=C: e.affine_select(
                    out=sub, in_=sub, compare_op=ALU.is_ge, fill=0.0,
                    base=C, pattern=[[1, 32]], channel_multiplier=-1))
            P.append(lambda e: e.memset(self.scanmask[:], 1.0))
            P.append(lambda e: e.memset(self.scanmask[:, 0:WIN:GLA_C], 0.0))
        def mk_chained(e):
            self._pool_ops = []
            mk(None)
            for f in self._pool_ops:
                tgt = self.sg.inc(1)
                f(e).then_inc(self.sg.h, 1)
                e.wait_ge(self.sg.h, tgt)
        ph.op("gpsimd", mk_chained)
        ph.fence()
        ph.run()

    def emit_transpose_in(self, tag, x_tok, xT_dram):
        es = ExitStack()
        ph = Ph(self)
        st, sv, qs = self.st, self.sv, self.qs
        xin = [self.sb(es, f"ti_x{i}_{tag}", [128, D], F32) for i in range(2)]
        stg = [self.sb(es, f"ti_s{i}_{tag}", [128, D], F32) for i in range(2)]
        pt = [self.ps(es, f"ti_p{i}_{tag}", (128, 1024), F32) for i in range(2)]
        evac_done = [0, 0]
        store_done = [(qs, 0), (qs, 0)]
        xin_war = [0, 0]
        for tc in range(NTC):
            r = tc % 2
            wld = ph.dma("sync", xin[r][:], x_tok[tc * 128:(tc + 1) * 128, :],
                         waits=[(st, xin_war[r])], sem=f"ti{r}_{tag}")
            for half in range(2):
                for dc8 in range(8):
                    dc = half * 8 + dc8
                    w0 = []
                    if dc == 0:
                        w0.append(wld)
                    if dc8 == 0:
                        w0.append((sv, evac_done[r]))
                    ph.op("tensor", lambda e, r=r, dc=dc, dc8=dc8: e.transpose(
                        pt[r][:, dc8 * 128:(dc8 + 1) * 128],
                        xin[r][:, dc * 128:(dc + 1) * 128], self.ident_f[:]),
                        waits=w0,
                        inc=(st, 1) if dc8 == 7 else None)
                tmm = st.v
                evac_done[r] = ph.op("vector", lambda e, r=r, half=half: e.tensor_copy(
                    stg[r][:, half * 1024:(half + 1) * 1024], pt[r][:]),
                    waits=[(st, tmm), store_done[r]], inc=(sv, 1))
            xin_war[r] = st.v
            store_done[r] = ph.dma(
                "sync",
                rr(xT_dram, "(dc p) t -> p dc t", p=128)[:, :, tc * 128:(tc + 1) * 128],
                rr(stg[r][:], "p (dc t) -> p dc t", dc=16),
                waits=[(sv, evac_done[r])], sem=f"tis{r}_{tag}")
        ph.fence()
        ph.run()
        es.close()

    def emit_ln(self, tag, xT_dram, biga, router=False, gate=False, src_bf16=None,
                gb=False, out_dram=None):
        sv, ss, st, qs = self.sv, self.ss, self.st, self.qs
        if router:
            self.maskrow = self.sb(getattr(self, "bstack", None) or self.estack,
                                   "maskrow", [1, T], F32)
            self.rowtmp = self.dtmp("rowtmp", [16, 128], F32)
        if gate:
            self.gl_dram = self.dtmp("gl_dram", [8, T], F32)
        es = ExitStack()
        ph = Ph(self)
        rows = self.sb(es, f"ln_rows_{tag}", [1, 4 * T], F32)
        r_sumx = rows[0:1, 0:T]
        r_sumsq = rows[0:1, T:2 * T]
        r_mu = rows[0:1, 2 * T:3 * T]
        r_rs = rows[0:1, 3 * T:4 * T]
        xt = ([self.sb(es, f"ln_x{i}_{tag}", [128, T], F32) for i in range(2)]
              if src_bf16 is None else None)
        sq = self.sb(es, f"ln_sq_{tag}", [128, T], F32)
        mu_bc = self.sb(es, f"ln_mu_{tag}", [128, T], F32)
        rs_bc = self.sb(es, f"ln_rs_{tag}", [128, T], F32)
        t1 = self.sb(es, f"ln_t1_{tag}", [128, T], F32)
        t2 = sq
        if out_dram is not None:
            ostg = [self.sb(es, f"ln_os0_{tag}", [128, T], BF16)]
        if gb:
            gm_sb = self.sb(es, f"ln_gm_{tag}", [128, 16], F32)
            bt_sb = self.sb(es, f"ln_bt_{tag}", [128, 16], F32)
            g16 = self.sb(es, f"ln_g16_{tag}", [16, 128], F32)
        ps4 = self.ps(es, f"ln_ps_{tag}", (128, 2048), F32)

        xt_war = [[], []]
        gb_done = 0
        if gb:
            for nm, dst in (("gmp", gm_sb), ("btp", bt_sb)):
                wld = ph.dma("sync", g16[:], self.inp[nm], sem=f"lng16_{tag}",
                             waits=[(st, self.st.v)])
                tmm = ph.op("tensor", lambda e, dst=dst: e.transpose(
                    ps4[0:128, 0:16], g16[:], self.ident_f[0:16, 0:16]),
                    waits=[wld, (sv, gb_done)], inc=(st, 1))
                gb_done = ph.op("vector", lambda e, dst=dst: e.tensor_copy(
                    dst[:], ps4[0:128, 0:16]), waits=[(st, tmm)], inc=(sv, 1))

        def src_tile(dc):
            if src_bf16 is not None:
                return src_bf16[:, dc * T:(dc + 1) * T], []
            b = xt[dc % 2]
            wld = ph.dma("sync", b[:], xT_dram[dc * 128:(dc + 1) * 128, :],
                         waits=list(xt_war[dc % 2]), sem=f"lnx{dc % 2}_{tag}")
            return b[:], [wld]

        ones = self.ones_col_b if src_bf16 is not None else self.ones_col_f

        for dc in range(NDC):
            bap, w_ld = src_tile(dc)
            for w in range(NW):
                ph.op("tensor", lambda e, bap=bap, w=w, dc=dc: e.matmul(
                    ps4[0:1, w * WIN:(w + 1) * WIN], ones[:],
                    bap[:, w * WIN:(w + 1) * WIN], start=(dc == 0), stop=(dc == NDC - 1)),
                    waits=(w_ld + ([(sv, gb_done)] if dc == 0 else [])) if w == 0 else (),
                    inc=(st, 1))
            if src_bf16 is None:
                xt_war[dc % 2] = [(st, st.v)]
        v_sumx = ph.op("vector", lambda e: e.tensor_copy(r_sumx, ps4[0:1, 0:T]),
                       waits=[(st, st.v)], inc=(sv, 1))
        sq_war = 0
        for dc in range(NDC):
            bap, w_ld = src_tile(dc)
            sqd = ph.op("scalar", lambda e, bap=bap: e.activation(
                out=sq[:], in_=bap, func=AF.Square),
                waits=w_ld + [(st, sq_war)], inc=(ss, 1))
            for w in range(NW):
                ph.op("tensor", lambda e, w=w, dc=dc: e.matmul(
                    ps4[0:1, w * WIN:(w + 1) * WIN], self.ones_col_f[:],
                    sq[:, w * WIN:(w + 1) * WIN], start=(dc == 0), stop=(dc == NDC - 1)),
                    waits=[(ss, sqd), (sv, v_sumx)] if w == 0 else (),
                    inc=(st, 1))
            sq_war = st.v
            if src_bf16 is None:
                xt_war[dc % 2] = [(ss, sqd)]
        v_sums = ph.op("vector", lambda e: e.tensor_copy(r_sumsq, ps4[0:1, 0:T]),
                       waits=[(st, st.v)], inc=(sv, 1))
        ph.op("scalar", lambda e: e.activation(out=r_mu, in_=r_sumx, func=AF.Copy,
                                               scale=1.0 / D), waits=[(sv, v_sums)])
        ph.op("scalar", lambda e: e.activation(out=r_rs, in_=r_sumsq, func=AF.Copy,
                                               scale=1.0 / D))
        ph.op("scalar", lambda e: e.activation(out=r_sumx, in_=r_mu, func=AF.Square),
              inc=(ss, 1))
        s_mu2 = ss.v
        v_var = ph.op("vector", lambda e: e.tensor_sub(r_sumsq, r_rs, r_sumx),
                      waits=[(ss, s_mu2)], inc=(sv, 1))
        ph.op("scalar", lambda e: e.activation(out=r_sumx, in_=r_sumsq, func=AF.Ln,
                                               bias=self.fc[0:1, 0:1]),
              waits=[(sv, v_var)])
        s_rows = ph.op("scalar", lambda e: e.activation(out=r_rs, in_=r_sumx, func=AF.Exp,
                                                        scale=-0.5), inc=(ss, 1))
        v_bc = v_sums
        for (row, dstb) in ((r_mu, mu_bc), (r_rs, rs_bc)):
            for w in range(NW):
                tmm = ph.op("tensor", lambda e, row=row, w=w: e.matmul(
                    ps4[:, 0:WIN], self.ones_row_f[:], row[0:1, w * WIN:(w + 1) * WIN],
                    start=True, stop=True),
                    waits=[(ss, s_rows), (sv, v_bc)], inc=(st, 1))
                v_bc = ph.op("vector", lambda e, dstb=dstb, w=w: e.tensor_copy(
                    dstb[:, w * WIN:(w + 1) * WIN], ps4[:, 0:WIN]),
                    waits=[(st, tmm)], inc=(sv, 1))

        if router:
            wr_sb = self.sb(es, "wr_sb", [128, 16], F32)
            m16 = self.sb(es, "m16", [16, 128], F32)
            wld = ph.dma("sync", m16[:], self.inp["wrp"], sem=f"lnr1_{tag}")
            tmm = ph.op("tensor", lambda e: e.transpose(
                ps4[0:128, 0:16], m16[:], self.ident_f[0:16, 0:16]),
                waits=[wld, (sv, v_bc)], inc=(st, 1))
            v_wr = ph.op("vector", lambda e: e.tensor_copy(wr_sb[:], ps4[0:128, 0:16]),
                         waits=[(st, tmm)], inc=(sv, 1))
        if gate:
            glrow_sb = self.sb(es, "glrow_sb", [8, T], F32)
            gw_sb = self.sb(es, "gw_sb", [128, 16 * 8], F32)
            gwld = ph.dma("sync", gw_sb[:], rr(self.inp["gwp"], "p dc e -> p (dc e)"),
                          sem=f"lnr2_{tag}")

        self._ln_ost_war = [[], []]
        rtr_war = self.st.v if (router or gate) else 0
        for dc in range(NDC):
            bap, w_ld = src_tile(dc)
            ph.op("vector", lambda e, bap=bap: e.tensor_sub(t1[:], bap, mu_bc[:]),
                  waits=w_ld + [(sv, v_bc), (st, rtr_war)], inc=(sv, 1))
            if src_bf16 is None:
                xt_war[dc % 2] = [(sv, sv.v)]
            nrm = ph.op("vector", lambda e: e.tensor_mul(t2[:], t1[:], rs_bc[:]),
                        inc=(sv, 1))
            if out_dram is not None:
                ro = 0
                vo = ph.op("vector", lambda e, dc=dc, ro=ro: e.tensor_scalar(
                    out=ostg[ro][:], in0=t2[:],
                    scalar1=gm_sb[:, dc:dc + 1], scalar2=bt_sb[:, dc:dc + 1],
                    op0=ALU.mult, op1=ALU.add) if gb else e.tensor_copy(ostg[ro][:], t2[:]),
                    waits=self._ln_ost_war[ro], inc=(sv, 1))
                osw = ph.dma("sync", out_dram[dc * 128:(dc + 1) * 128, :], ostg[ro][:],
                             waits=[(sv, vo)], sem=f"lnos_{tag}")
                self._ln_ost_war[ro] = [osw]
            elif gb:
                ph.op("vector", lambda e, dc=dc: e.tensor_scalar(
                    out=biga[:, dc * T:(dc + 1) * T], in0=t2[:],
                    scalar1=gm_sb[:, dc:dc + 1], scalar2=bt_sb[:, dc:dc + 1],
                    op0=ALU.mult, op1=ALU.add), inc=(sv, 1))
            else:
                ph.op("vector", lambda e, dc=dc: e.tensor_copy(
                    biga[:, dc * T:(dc + 1) * T], t2[:]), inc=(sv, 1))
            if router:
                for w in range(NW):
                    ph.op("tensor", lambda e, w=w, dc=dc: e.matmul(
                        ps4[0:1, w * WIN:(w + 1) * WIN], wr_sb[:, dc:dc + 1],
                        t2[:, w * WIN:(w + 1) * WIN],
                        start=(dc == 0), stop=(dc == NDC - 1)),
                        waits=[(sv, nrm), (sv, v_wr)] if w == 0 else (),
                        inc=(st, 1) if w == NW - 1 else None)
                rtr_war = st.v
            if gate:
                for w in range(NW):
                    ph.op("tensor", lambda e, w=w, dc=dc: e.matmul(
                        ps4[0:8, w * WIN:(w + 1) * WIN], gw_sb[:, dc * 8:(dc + 1) * 8],
                        t2[:, w * WIN:(w + 1) * WIN],
                        start=(dc == 0), stop=(dc == NDC - 1)),
                        waits=([(sv, nrm)] + ([gwld] if dc == 0 else [])) if w == 0 else (),
                        inc=(st, 1) if w == NW - 1 else None)
                rtr_war = st.v

        if gate:
            vgl = ph.op("vector", lambda e: e.tensor_copy(glrow_sb[:], ps4[0:8, 0:T]),
                        waits=[(st, st.v)], inc=(sv, 1))
            ph.dma("sync", self.gl_dram, glrow_sb[:], waits=[(sv, vgl)])

        if getattr(self, "dbg_ln", None):
            d_rows = self.dbg_ln["rows"]
            d_mu = self.dbg_ln["mu"]
            d_rs = self.dbg_ln["rs"]
            ph.dma("sync", d_rows, rows[:], waits=[(self.ss, self.ss.v), (self.sv, self.sv.v)])
            ph.dma("sync", d_mu, mu_bc[:], waits=[(self.sv, v_bc)])
            ph.dma("sync", d_rs, rs_bc[:], waits=[(self.sv, v_bc)])

        if router:
            sc_bc = mu_bc
            sc_col = self.sb(es, "sc_col", [128, 16], F32)
            rank_col = self.sb(es, "rank_col", [128, 16], F32)
            mask_col = self.sb(es, "mask_col", [128, 16], F32)
            scr_row = r_sumx
            v_scrow = 0
            for w in range(NW):
                v_scrow = ph.op("vector", lambda e, w=w: e.tensor_copy(
                    scr_row[0:1, w * WIN:(w + 1) * WIN], ps4[0:1, w * WIN:(w + 1) * WIN]),
                    waits=[(st, st.v)] if w == 0 else (), inc=(sv, 1))
            v_bc2 = 0
            for w in range(NW):
                tmm = ph.op("tensor", lambda e, w=w: e.matmul(
                    ps4[:, 0:WIN], self.ones_row_f[:], scr_row[0:1, w * WIN:(w + 1) * WIN],
                    start=True, stop=True), waits=[(sv, v_scrow), (sv, v_bc2)],
                    inc=(st, 1))
                v_bc2 = ph.op("vector", lambda e, w=w: e.tensor_copy(
                    sc_bc[:, w * WIN:(w + 1) * WIN], ps4[:, 0:WIN]),
                    waits=[(st, tmm)], inc=(sv, 1))
            wst = ph.dma("sync", rr(self.rowtmp, "a t -> (a t)").rearrange("(a t) -> a t", a=1),
                         scr_row, waits=[(sv, v_scrow)], sem=f"lnr3_{tag}")
            wld = ph.dma("sync", m16[:], self.rowtmp, waits=[wst], sem=f"lnr4_{tag}")
            tmm = ph.op("tensor", lambda e: e.transpose(
                ps4[0:128, 0:16], m16[:], self.ident_f[0:16, 0:16]),
                waits=[wld, (sv, v_bc2)], inc=(st, 1))
            v_col = ph.op("vector", lambda e: e.tensor_copy(sc_col[:], ps4[0:128, 0:16]),
                          waits=[(st, tmm)], inc=(sv, 1))
            for tc in range(NTC):
                ph.op("vector", lambda e, tc=tc: e.tensor_scalar(
                    out=t1[:], in0=sc_bc[:], scalar1=sc_col[:, tc:tc + 1], scalar2=None,
                    op0=ALU.is_gt, op1=ALU.add, accum_out=rank_col[:, tc:tc + 1]),
                    waits=[(sv, v_col)] if tc == 0 else ())
            ph.op("vector", lambda e: e.tensor_copy(t1[:, 0:128], sc_bc[:, 0:128]))
            v_rank = ph.op("vector", lambda e: e.tensor_scalar(
                out=mask_col[:], in0=rank_col[:], scalar1=float(KSEL), scalar2=None,
                op0=ALU.is_lt), inc=(sv, 1))
            tmm = ph.op("tensor", lambda e: e.transpose(
                ps4[0:16, 0:128], mask_col[:], self.ident_f[:]),
                waits=[(sv, v_rank)], inc=(st, 1))
            v_m16 = ph.op("vector", lambda e: e.tensor_copy(m16[:], ps4[0:16, 0:128]),
                          waits=[(st, tmm)], inc=(sv, 1))
            wst = ph.dma("sync", self.rowtmp, m16[:], waits=[(sv, v_m16)], sem=f"lnr5_{tag}")
            ph.dma("sync", self.maskrow[:],
                   rr(self.rowtmp, "a t -> (a t)").rearrange("(a t) -> a t", a=1),
                   waits=[wst], sem=f"lnr6_{tag}")
            if getattr(self, "dbg_ln", None):
                ph.dma("sync", self.dbg_ln["sc_col"], sc_col[:], waits=[(sv, v_col)])
                ph.dma("sync", self.dbg_ln["rank"], rank_col[:], waits=[(sv, v_rank)])
                ph.dma("sync", self.dbg_ln["sc_bc"], sc_bc[:], waits=[(sv, v_bc2)])
                ph.dma("sync", self.dbg_ln["mask_col"], mask_col[:], waits=[(sv, v_rank)])
                ph.dma("sync", self.dbg_ln["m16"], m16[:], waits=[(sv, v_m16)])
        ph.fence()
        ph.run()
        es.close()

    def emit_attn0(self, biga, z_sb):
        sv, ss, st, qg = self.sv, self.ss, self.st, self.qg
        es = ExitStack()
        ph = Ph(self)
        q_sb = [self.sb(es, f"a0_q{r}", [128, T], BF16) for r in range(2)]
        k_sb = [self.sb(es, f"a0_k{r}", [128, T], BF16) for r in range(2)]
        v_sb = [self.sb(es, f"a0_v{r}", [128, T], BF16) for r in range(2)]
        self.cmask = self.sb(es, "a0_cmask", [128, 4 * WIN], BF16)
        pt_sb = self.sb(es, "a0_pt", [128, 8 * WIN], BF16)
        linv_bc = self.sb(es, "a0_li", [128, WIN], F32)
        lrow = self.sb(es, "a0_lr", [1, WIN], F32)
        wring = [[self.sb(es, f"a0_w{j}{r}", [128, 2048], BF16) for j in range(3)]
                 for r in range(2)]
        ps_qk = self.ps(es, "a0_pq", (128, 1024), F32)
        ps_s = self.ps(es, "a0_ps", (128, 2048), F32)
        ps_z = self.ps(es, "a0_pz")
        ps_lc = self.ps(es, "a0_pl")

        def _mkcmask(e):
            tgt = self.sg.inc(1)
            e.memset(self.cmask[:], 0.0).then_inc(self.sg.h, 1)
            e.wait_ge(self.sg.h, tgt)
            for dlt in range(4):
                sub = self.cmask[:, dlt * WIN:(dlt + 1) * WIN]
                tgt = self.sg.inc(1)
                e.affine_select(out=sub, in_=sub, compare_op=ALU.is_ge, fill=-1.0e5,
                                base=-128 * dlt, pattern=[[1, WIN]],
                                channel_multiplier=-1).then_inc(self.sg.h, 1)
                e.wait_ge(self.sg.h, tgt)
        ph.op("gpsimd", _mkcmask)
        cmask_rdy = self.sg.v

        wnames = ("wq0", "wk0", "wv0")
        wld = [(qg, 0), (qg, 0)]
        wfree = [[], []]
        for j, nm in enumerate(wnames):
            wld[0] = ph.dma("gpsimd", wring[0][j][:],
                            rr(self.inp[nm], "h p dc e -> h p (dc e)")[0], sem="a0w0")

        self._a0_qk_war = [[], []]

        def prep_gen(h):
            r = h % 2
            wq, wk, wv = wring[r]
            g = 0
            for (wt, dst, eng) in ((wq, q_sb[r], "scalar"), (wk, k_sb[r], "vector")):
                for w in range(NW):
                    bank = g % 2
                    for dc in range(NDC):
                        ph.op("tensor", lambda e, wt=wt, dc=dc, w=w, bank=bank: e.matmul(
                            ps_qk[:, bank * WIN:(bank + 1) * WIN],
                            wt[:, dc * 128:(dc + 1) * 128],
                            biga[:, dc * T + w * WIN: dc * T + (w + 1) * WIN],
                            start=(dc == 0), stop=(dc == NDC - 1)),
                            waits=([wld[r]] + self._a0_qk_war[bank]) if dc == 0 else (),
                            inc=(st, 1) if dc == NDC - 1 else None)
                    mmv = st.v
                    if eng == "scalar":
                        tgt = ph.op("scalar", lambda e, dst=dst, w=w, bank=bank: e.activation(
                            out=dst[:, w * WIN:(w + 1) * WIN],
                            in_=ps_qk[:, bank * WIN:(bank + 1) * WIN],
                            func=AF.Copy), waits=[(st, mmv)], inc=(ss, 1))
                        self._a0_qk_war[bank] = [(ss, tgt)]
                    else:
                        tgt = ph.op("vector", lambda e, dst=dst, w=w, bank=bank: e.tensor_copy(
                            dst[:, w * WIN:(w + 1) * WIN],
                            ps_qk[:, bank * WIN:(bank + 1) * WIN]),
                            waits=[(st, mmv)], inc=(sv, 1))
                        self._a0_qk_war[bank] = [(sv, tgt)]
                    g += 1
                    yield ("qk", tgt)
            for tcg in range(4):
                bank = g % 2
                for tc4 in range(4):
                    tc = tcg * 4 + tc4
                    for dc in range(NDC):
                        ph.op("tensor", lambda e, tc=tc, tc4=tc4, dc=dc, wv=wv, bank=bank: e.matmul(
                            ps_qk[:, bank * WIN + tc4 * 128: bank * WIN + (tc4 + 1) * 128],
                            biga[:, dc * T + tc * 128: dc * T + (tc + 1) * 128],
                            wv[:, dc * 128:(dc + 1) * 128],
                            start=(dc == 0), stop=(dc == NDC - 1)),
                            waits=self._a0_qk_war[bank] if (dc == 0 and tc4 == 0) else (),
                            inc=(st, 1) if (dc == NDC - 1 and tc4 == 3) else None)
                mmv = st.v
                tgt = ph.op("vector", lambda e, tcg=tcg, r=r, bank=bank: e.tensor_copy(
                    v_sb[r][:, tcg * 512:(tcg + 1) * 512],
                    ps_qk[:, bank * WIN:(bank + 1) * WIN]),
                    waits=[(st, mmv)], inc=(sv, 1))
                self._a0_qk_war[bank] = [(sv, tgt)]
                g += 1
                yield ("v", tgt)

        gen = prep_gen(0)
        last = None
        for last in gen:
            pass
        prep_done = {0: last[1]}

        z_war = 0
        l_war = 0
        exp_done = 0
        ptwar = [0, 0]
        for h in range(NH):
            r = h % 2
            if h + 1 < NH:
                for j, nm in enumerate(wnames):
                    wld[1 - r] = ph.dma("gpsimd", wring[1 - r][j][:],
                                        rr(self.inp[nm], "h p dc e -> h p (dc e)")[h + 1],
                                        waits=wfree[1 - r] if j == 0 else (),
                                        sem=f"a0w{1 - r}")
                nxt = prep_gen(h + 1)
            else:
                nxt = None
            qv = prep_done[h]
            for w in range(NW):
                ngrp = w + 1
                for grp in range(ngrp):
                    diag = grp == ngrp - 1
                    n4 = 4
                    for i4 in range(n4):
                        kc = grp * 4 + i4
                        ph.op("tensor", lambda e, kc=kc, w=w, i4=i4, r=r, diag=diag: e.matmul(
                            ps_s[:, i4 * WIN:(i4 + 1) * WIN],
                            k_sb[r][:, kc * 128:(kc + 1) * 128],
                            q_sb[r][:, w * WIN:(w + 1) * WIN], start=True, stop=not diag),
                            waits=([(ss, exp_done), (sv, qv),
                                    (self.sg, cmask_rdy)] if i4 == 0 else ()),
                            inc=(st, 1) if (i4 == n4 - 1 and not diag) else None)
                        if diag:
                            ph.op("tensor", lambda e, i4=i4: e.matmul(
                                ps_s[:, i4 * WIN:(i4 + 1) * WIN], self.ident_b[:],
                                self.cmask[:, i4 * WIN:(i4 + 1) * WIN],
                                start=False, stop=True),
                                inc=(st, 1) if i4 == n4 - 1 else None)
                    smm = st.v
                    gslot = grp % 2
                    exp_done = ph.op("scalar", lambda e, gslot=gslot: e.activation(
                        out=pt_sb[:, gslot * 4 * WIN:(gslot + 1) * 4 * WIN],
                        in_=ps_s[:, 0:4 * WIN], func=AF.Exp,
                        bias=self.fc[:, 1:2], scale=INV_SQRT_DH),
                        waits=[(st, smm), (st, ptwar[gslot])], inc=(ss, 1))
                    if nxt is not None:
                        n = next(nxt, None)
                        if n is None:
                            nxt = None
                        else:
                            prep_done[h + 1] = n[1]
                    nkc = 4 * w + 4
                    for i4 in range(n4):
                        kc = grp * 4 + i4
                        pcol = (gslot * 4 + i4) * WIN
                        ph.op("tensor", lambda e, kc=kc, pcol=pcol: e.matmul(
                            ps_lc[0:1, 0:WIN], self.ones_col_b[:],
                            pt_sb[:, pcol:pcol + WIN],
                            start=(kc == 0), stop=(kc == nkc - 1)),
                            waits=([(ss, exp_done)] + ([(sv, l_war)] if kc == 0 else []))
                                  if i4 == 0 else ())
                        ph.op("tensor", lambda e, kc=kc, r=r, pcol=pcol: e.matmul(
                            ps_z[:, 0:WIN], v_sb[r][:, kc * 128:(kc + 1) * 128],
                            pt_sb[:, pcol:pcol + WIN],
                            start=(kc == 0), stop=(kc == nkc - 1)),
                            waits=[(sv, z_war)] if kc == 0 else (),
                            inc=(st, 1) if kc == nkc - 1 else None)
                    ptwar[gslot] = st.v
                pvmm = st.v
                vr = ph.op("vector", lambda e: e.reciprocal(lrow[:], ps_lc[0:1, 0:WIN]),
                           waits=[(st, pvmm)], inc=(sv, 1))
                l_war = vr
                bmm = ph.op("tensor", lambda e: e.matmul(
                    ps_lc[:, 0:WIN], self.ones_row_f[:], lrow[:], start=True, stop=True),
                    waits=[(sv, vr)], inc=(st, 1))
                bev = ph.op("vector", lambda e: e.tensor_copy(linv_bc[:], ps_lc[:, 0:WIN]),
                            waits=[(st, bmm)], inc=(sv, 1))
                z_war = ph.op("vector", lambda e, h=h, w=w: e.tensor_tensor(
                    out=z_sb[:, h * T + w * WIN: h * T + (w + 1) * WIN],
                    in0=ps_z[:, 0:WIN], in1=linv_bc[:], op=ALU.mult),
                    waits=[(sv, bev)], inc=(sv, 1))
            if nxt is not None:
                for n in nxt:
                    prep_done[h + 1] = n[1]
                nxt = None
            wfree[r] = [(st, st.v)]
        ph.fence()
        ph.run()
        es.close()

    def emit_oproj0(self, z_sb, wo_area, x0T_dram, x1T_dram, accum):
        sv, ss, st, qs, qg = self.sv, self.ss, self.st, self.qs, self.qg
        es = ExitStack()
        ph = Ph(self)
        x0s = [self.sb(es, f"o0_x{i}", [128, T], F32) for i in range(2)]
        stage = [self.sb(es, f"o0_s{i}", [128, WIN], F32) for i in range(2)]
        ps_o = [self.ps(es, f"o0_p{i}") for i in range(2)]
        for h in range(NH):
            wo_ld = ph.dma("gpsimd", wo_area[:, h * 2048:(h + 1) * 2048],
                           rr(self.inp["wo0"], "h p dc e -> h p (dc e)")[h], sem="o0wo")
        x0_war = [[], []]
        stage_war = [[], []]
        evac = [0, 0]
        for dc in range(NDC):
            r = dc % 2
            xw = ph.dma("sync", x0s[r][:], x0T_dram[dc * 128:(dc + 1) * 128, :],
                        waits=x0_war[r], sem=f"o0x{r}")
            for w in range(NW):
                g = (dc * NW + w) % 2
                for h in range(NH):
                    ph.op("tensor", lambda e, h=h, dc=dc, w=w, g=g: e.matmul(
                        ps_o[g][:, 0:WIN],
                        wo_area[:, h * 2048 + dc * 128: h * 2048 + (dc + 1) * 128],
                        z_sb[:, h * T + w * WIN: h * T + (w + 1) * WIN],
                        start=(h == 0), stop=(h == NH - 1)),
                        waits=[wo_ld, (sv, evac[g])] if h == 0 else (),
                        inc=(st, 1) if h == NH - 1 else None)
                mmv = st.v
                ev = ph.op("vector", lambda e, r=r, w=w, g=g: e.tensor_tensor(
                    out=stage[g][:], in0=ps_o[g][:, 0:WIN],
                    in1=x0s[r][:, w * WIN:(w + 1) * WIN], op=ALU.add),
                    waits=[(st, mmv), xw] + stage_war[g],
                    inc=(sv, 1))
                evac[g] = ev
                dst = x1T_dram[dc * 128:(dc + 1) * 128, w * WIN:(w + 1) * WIN]
                sw = ph.dma("gpsimd" if accum else "sync", dst, stage[g][:],
                            waits=[(sv, ev)], sem=f"o0s{g}",
                            **({"accum_op": ALU.add} if accum else {}))
                stage_war[g] = [sw]
            x0_war[r] = [(sv, evac[0]), (sv, evac[1])]
        ph.fence()
        ph.run()
        es.close()

    def emit_ffn0(self, biga, h_dram, x1T_dram):
        sv, ss, st, qs, qg = self.sv, self.ss, self.st, self.qs, self.qg
        es = ExitStack()
        ph = Ph(self)
        w1t = [self.sb(es, f"f0_w1{i}", [128, 2048], BF16) for i in range(2)]
        hstg = [self.sb(es, f"f0_h{i}", [128, 2048], BF16) for i in range(2)]
        b1c = self.sb(es, "f0_b1c", [128, 64], F32)
        b2c = self.sb(es, "f0_b2c", [128, 16], F32)
        m16 = self.sb(es, "f0_m16", [64, 128], F32)
        mask_bc = self.sb(es, "f0_mbc", [128, T], F32)
        es_a = ExitStack()
        ps_h = self.ps(es_a, "f0_ph", (128, 1024), F32)
        ps_c = self.ps(es_a, "f0_pc")

        wld = ph.dma("sync", m16[:], self.inp["b1p"], sem="f0b")
        tmm = ph.op("tensor", lambda e: e.transpose(
            ps_c[0:128, 0:64], m16[:], self.ident_f[0:64, 0:64]),
            waits=[wld], inc=(st, 1))
        v1 = ph.op("vector", lambda e: e.tensor_copy(b1c[:], ps_c[0:128, 0:64]),
                   waits=[(st, tmm)], inc=(sv, 1))
        wld = ph.dma("sync", m16[0:16, :], self.inp["b2p"], sem="f0b",
                     waits=[(st, st.v)])
        tmm = ph.op("tensor", lambda e: e.transpose(
            ps_c[0:128, 0:16], m16[0:16, :], self.ident_f[0:16, 0:16]),
            waits=[wld, (sv, v1)], inc=(st, 1))
        v2 = ph.op("vector", lambda e: e.tensor_copy(b2c[:], ps_c[0:128, 0:16]),
                   waits=[(st, tmm)], inc=(sv, 1))
        v_mb = v2
        for w in range(NW):
            tmm = ph.op("tensor", lambda e, w=w: e.matmul(
                ps_c[:, 0:WIN], self.ones_row_f[:],
                self.maskrow[0:1, w * WIN:(w + 1) * WIN], start=True, stop=True),
                waits=[(sv, v_mb)], inc=(st, 1))
            v_mb = ph.op("vector", lambda e, w=w: e.tensor_copy(
                mask_bc[:, w * WIN:(w + 1) * WIN], ps_c[:, 0:WIN]),
                waits=[(st, tmm)], inc=(sv, 1))

        if getattr(self, "dbg_ffn", None):
            ph.dma("sync", self.dbg_ffn["mask_bc"], mask_bc[:], waits=[(sv, v_mb)])
            ph.dma("sync", self.dbg_ffn["maskrow"], self.maskrow[:])
        h_st = [(qs, 0)] * NFC
        w1ld = [(qg, 0), (qg, 0)]
        w1war = [[], []]
        w1ld[0] = ph.dma("gpsimd", w1t[0][:],
                         rr(self.inp["w1p"], "f p dc e -> f p (dc e)")[0], sem="f0w10")
        hst_war = [[], []]
        gel = [0, 0]
        for fc in range(NFC):
            r = fc % 2
            if fc + 1 < NFC:
                w1ld[1 - r] = ph.dma("gpsimd", w1t[1 - r][:],
                                     rr(self.inp["w1p"], "f p dc e -> f p (dc e)")[fc + 1],
                                     waits=w1war[1 - r], sem=f"f0w1{1 - r}")
            for w in range(NW):
                g = w % 2
                for dc in range(NDC):
                    ph.op("tensor", lambda e, fc=fc, w=w, dc=dc, r=r, g=g: e.matmul(
                        ps_h[:, g * WIN:(g + 1) * WIN],
                        w1t[r][:, dc * 128:(dc + 1) * 128],
                        biga[:, dc * T + w * WIN: dc * T + (w + 1) * WIN],
                        start=(dc == 0), stop=(dc == NDC - 1)),
                        waits=[w1ld[r], (ss, gel[g])] if dc == 0 else (),
                        inc=(st, 1) if dc == NDC - 1 else None)
                mmv = st.v
                gel[g] = ph.op("scalar", lambda e, fc=fc, w=w, r=r, g=g: e.activation(
                    out=hstg[r][:, w * WIN:(w + 1) * WIN], in_=ps_h[:, g * WIN:(g + 1) * WIN],
                    func=AF.Gelu_apprx_tanh, bias=b1c[:, fc:fc + 1]),
                    waits=[(st, mmv), (sv, v2)] + (hst_war[r] if w == 0 else []),
                    inc=(ss, 1))
            w1war[r] = [(st, st.v)]
            hw = ph.dma("sync", h_dram[fc], hstg[r][:], waits=[(ss, gel[0]), (ss, gel[1])],
                        sem=f"f0h{r}")
            h_st[fc] = hw
            hst_war[r] = [hw]

        ph.fence()
        ph.run()
        es_a.close()
        ph = Ph(self)
        h_all = [(self.dsem("f0h0"), self.dsem("f0h0").v),
                 (self.dsem("f0h1"), self.dsem("f0h1").v)]
        w2t = [self.sb(es, f"f0_w2{i}", [128, WIN], BF16) for i in range(3)]
        ht = [self.sb(es, f"f0_ht{i}", [128, 1024], BF16) for i in range(3)]
        stg = [self.sb(es, f"f0_s{i}", [128, WIN], F32) for i in range(2)]
        es_b = ExitStack()
        ps8 = self.ps(es_b, "f0_p8", (128, 4096), F32)
        self._f0_ev = 0
        w2war = [[] for _ in range(3)]
        htwar = [[] for _ in range(3)]
        stg_war = [[], []]
        for dq in range(4):
            for wp in range(2):
                evacs = st.v
                for fc in range(NFC):
                    r3 = fc % 3
                    w2ld = ph.dma("gpsimd", w2t[r3][:],
                                  self.inp["w2p"][fc][:, dq * WIN:(dq + 1) * WIN],
                                  waits=w2war[r3], sem=f"f0w2{r3}")
                    htld = ph.dma("sync", ht[r3][:],
                                  h_dram[fc][:, wp * 1024:(wp + 1) * 1024],
                                  waits=htwar[r3] + (h_all if fc < 3 else []),
                                  sem=f"f0ht{r3}")
                    for dcl in range(4):
                        for wl in range(2):
                            ph.op("tensor", lambda e, fc=fc, dcl=dcl, wl=wl, r3=r3: e.matmul(
                                ps8[:, (dcl * 2 + wl) * WIN:(dcl * 2 + wl + 1) * WIN],
                                w2t[r3][:, dcl * 128:(dcl + 1) * 128],
                                ht[r3][:, wl * WIN:(wl + 1) * WIN],
                                start=(fc == 0), stop=(fc == NFC - 1)),
                                waits=([w2ld, htld] +
                                       ([(sv, self._f0_ev)] if fc == 0 else []))
                                      if (dcl == 0 and wl == 0) else (),
                                inc=(st, 1) if (dcl == 3 and wl == 1) else None)
                    w2war[r3] = [(st, st.v)]
                    htwar[r3] = [(st, st.v)]
                mmv = st.v
                for dcl in range(4):
                    for wl in range(2):
                        dc = dq * 4 + dcl
                        w = wp * 2 + wl
                        g = (dcl * 2 + wl) % 2
                        ev = ph.op("vector", lambda e, dcl=dcl, wl=wl, dc=dc, w=w, g=g:
                                   e.scalar_tensor_tensor(
                                       out=stg[g][:],
                                       in0=ps8[:, (dcl * 2 + wl) * WIN:(dcl * 2 + wl + 1) * WIN],
                                       scalar=b2c[:, dc:dc + 1],
                                       in1=mask_bc[:, w * WIN:(w + 1) * WIN],
                                       op0=ALU.add, op1=ALU.mult),
                                   waits=[(st, mmv), (sv, v_mb)] + stg_war[g],
                                   inc=(sv, 1))
                        sw = ph.dma("sync",
                                    x1T_dram[dc * 128:(dc + 1) * 128, w * WIN:(w + 1) * WIN],
                                    stg[g][:], waits=[(sv, ev)], sem=f"f0s{g}")
                        stg_war[g] = [sw]
                self._f0_ev = sv.v
        ph.fence()
        ph.run()
        es_b.close()
        es.close()

    def emit_gla(self, biga, ybuf):
        sv, ss, st, qs, qg = self.sv, self.ss, self.st, self.qs, self.qg
        es = ExitStack()
        ph = Ph(self)
        wring = [[self.sb(es, f"g_w{j}{r}", [128, 2048], BF16) for j in range(4)]
                 for r in range(2)]
        g_sb = self.sb(es, "g_g", [128, T], F32)
        L_sb = self.sb(es, "g_L", [128, T], F32)
        sA = self.sb(es, "g_sA", [128, WIN], F32)
        sB = self.sb(es, "g_sB", [128, WIN], F32)
        sC = self.sb(es, "g_sC", [128, WIN], F32)
        ktf = self.sb(es, "g_ktf", [128, WIN], F32)
        eend = self.sb(es, "g_eend", [128, 64], F32)
        kpw = self.sb(es, "g_kpw", [128, WIN], BF16)
        qt = self.sb(es, "g_qt", [128, T], BF16)
        ktb = self.sb(es, "g_ktb", [128, T], BF16)
        kptok = self.sb(es, "g_kpt", [128, T], BF16)
        v_sb = self.sb(es, "g_v", [128, T], BF16)
        S_f = [self.sb(es, f"g_S{i}", [128, 128], F32) for i in range(2)]
        S_b = self.sb(es, "g_Sb", [128, 128], BF16)
        Mm = self.sb(es, "g_Mm", [128, 128], BF16)
        ps_a = self.ps(es, "g_pa")
        ps_b = self.ps(es, "g_pb")
        ps_v = self.ps(es, "g_pv")
        ps_t = self.ps(es, "g_pt", (128, 2048), BF16)
        ps_y = self.ps(es, "g_py")
        ps_s = self.ps(es, "g_ps")
        ps_o = self.ps(es, "g_po")

        self._gla_mm_war = 0
        self._gla_S_war = 0
        self._gla_sb_done = 0
        self._gla_qw_war = 0
        self._gla_em_war = 0
        self._gla_tp2 = 0
        wnames = ("wg1", "wq1", "wk1", "wv1")
        wld = [(qg, 0), (qg, 0)]
        wfree = [[], []]
        for j, nm in enumerate(wnames):
            wld[0] = ph.dma("gpsimd", wring[0][j][:],
                            rr(self.inp[nm], "h p dc e -> h p (dc e)")[0], sem="gw0")

        def eend_ap(w):
            hdl = eend[:].tensor
            return bass.AP(hdl, w * 16, [[64, 128], [1, 16], [0, 32]])

        pp_war = [[], []]
        for h in range(NH):
            r = h % 2
            wg, wq, wk, wv = wring[r]
            if h + 1 < NH:
                for j, nm in enumerate(wnames):
                    wld[1 - r] = ph.dma("gpsimd", wring[1 - r][j][:],
                                        rr(self.inp[nm], "h p dc e -> h p (dc e)")[h + 1],
                                        waits=wfree[1 - r] if j == 0 else (),
                                        sem=f"gw{1 - r}")
            scr_war = 0
            for w in range(NW):
                bank = (ps_a, ps_b)[w % 2]
                for dc in range(NDC):
                    ph.op("tensor", lambda e, dc=dc, w=w, wg=wg, bank=bank: e.matmul(
                        bank[:, 0:WIN], wg[:, dc * 128:(dc + 1) * 128],
                        biga[:, dc * T + w * WIN: dc * T + (w + 1) * WIN],
                        start=(dc == 0), stop=(dc == NDC - 1)),
                        waits=([wld[r]] + pp_war[w % 2]) if dc == 0 else (),
                        inc=(st, 1) if dc == NDC - 1 else None)
                mmv = st.v
                vzm = ph.op("vector", lambda e, bank=bank: e.tensor_scalar_min(
                    sA[:], bank[:, 0:WIN], 0.0),
                    waits=[(st, mmv), (ss, scr_war)], inc=(sv, 1))
                saz = ph.op("scalar", lambda e, bank=bank: e.activation(
                    out=sB[:], in_=bank[:, 0:WIN], func=AF.Abs),
                    waits=[(st, mmv), (sv, vzm)], inc=(ss, 1))
                pp_war[w % 2] = [(ss, saz)]
                ph.op("scalar", lambda e: e.activation(out=sC[:], in_=sB[:], func=AF.Exp,
                                                       scale=-1.0))
                slb = ph.op("scalar", lambda e: e.activation(
                    out=sB[:], in_=sC[:], func=AF.Ln, bias=self.fc[:, 2:3]), inc=(ss, 1))
                scr_war = slb
                ph.op("vector", lambda e, w=w: e.tensor_sub(
                    g_sb[:, w * WIN:(w + 1) * WIN], sA[:], sB[:]),
                    waits=[(ss, slb)], inc=(sv, 1))
            g_done = sv.v
            vL = 0
            for w in range(NW):
                vL = ph.op("vector", lambda e, w=w: e.tensor_tensor_scan(
                    out=L_sb[:, w * WIN:(w + 1) * WIN], data0=self.scanmask[:],
                    data1=g_sb[:, w * WIN:(w + 1) * WIN], initial=0.0,
                    op0=ALU.mult, op1=ALU.add),
                    waits=[(sv, g_done)] if w == 0 else (), inc=(sv, 1))
            sEe = ph.op("scalar", lambda e: e.activation(
                out=eend[:], in_=L_sb[:, 31:T:GLA_C], func=AF.Exp),
                waits=[(sv, vL)], inc=(ss, 1))
            kpt_done = 0
            tp_war = 0
            for w in range(NW):
                bank = (ps_a, ps_b)[w % 2]
                bank2 = (ps_a, ps_b)[(w + 1) % 2]
                for dc in range(NDC):
                    ph.op("tensor", lambda e, dc=dc, w=w, wq=wq, bank=bank: e.matmul(
                        bank[:, 0:WIN], wq[:, dc * 128:(dc + 1) * 128],
                        biga[:, dc * T + w * WIN: dc * T + (w + 1) * WIN],
                        start=(dc == 0), stop=(dc == NDC - 1)),
                        waits=pp_war[w % 2] if dc == 0 else (),
                        inc=(st, 1) if dc == NDC - 1 else None)
                mmq = st.v
                sEw = ph.op("scalar", lambda e, w=w: e.activation(
                    out=sA[:], in_=L_sb[:, w * WIN:(w + 1) * WIN], func=AF.Exp),
                    waits=[(ss, sEe), (sv, self._gla_qw_war)], inc=(ss, 1))
                vq = ph.op("vector", lambda e, w=w, bank=bank: e.scalar_tensor_tensor(
                    out=qt[:, w * WIN:(w + 1) * WIN], in0=bank[:, 0:WIN],
                    scalar=INV_SQRT_DH, in1=sA[:], op0=ALU.mult, op1=ALU.mult),
                    waits=[(st, mmq), (ss, sEw)], inc=(sv, 1))
                self._gla_qw_war = vq
                pp_war[w % 2] = [(sv, vq)]
                for dc in range(NDC):
                    ph.op("tensor", lambda e, dc=dc, w=w, wk=wk, bank2=bank2: e.matmul(
                        bank2[:, 0:WIN], wk[:, dc * 128:(dc + 1) * 128],
                        biga[:, dc * T + w * WIN: dc * T + (w + 1) * WIN],
                        start=(dc == 0), stop=(dc == NDC - 1)),
                        waits=pp_war[(w + 1) % 2] if dc == 0 else (),
                        inc=(st, 1) if dc == NDC - 1 else None)
                mmk = st.v
                sEm = ph.op("scalar", lambda e, w=w: e.activation(
                    out=sC[:], in_=L_sb[:, w * WIN:(w + 1) * WIN], func=AF.Exp,
                    scale=-1.0), waits=[(sv, self._gla_em_war)], inc=(ss, 1))
                vktf = ph.op("vector", lambda e, bank2=bank2: e.tensor_mul(
                    ktf[:], bank2[:, 0:WIN], sC[:]),
                    waits=[(st, mmk), (ss, sEm)], inc=(sv, 1))
                self._gla_em_war = vktf
                pp_war[(w + 1) % 2] = [(sv, vktf)]
                ph.op("vector", lambda e, w=w: e.tensor_copy(
                    ktb[:, w * WIN:(w + 1) * WIN], ktf[:]))
                vkp = ph.op("vector", lambda e, w=w: e.tensor_tensor(
                    out=kpw[:], in0=ktf[:], in1=eend_ap(w), op=ALU.mult),
                    waits=[(ss, sEe), (sv, tp_war)], inc=(sv, 1))
                for j4 in range(4):
                    tc = w * 4 + j4
                    g2 = j4 % 2
                    tmm = ph.op("tensor", lambda e, j4=j4, g2=g2: e.transpose(
                        ps_t[:, g2 * 1024:g2 * 1024 + 128], kpw[:, j4 * 128:(j4 + 1) * 128],
                        self.ident_b[:]),
                        waits=[(sv, vkp)] if j4 == 0 else [(sv, self._gla_tp2)],
                        inc=(st, 1))
                    kpt_done = ph.op("vector", lambda e, tc=tc, g2=g2: e.tensor_copy(
                        kptok[:, tc * 128:(tc + 1) * 128],
                        ps_t[:, g2 * 1024:g2 * 1024 + 128]),
                        waits=[(st, tmm)], inc=(sv, 1))
                    if j4 >= 1:
                        self._gla_tp2 = kpt_done
                tp_war = kpt_done
            v_war = []
            for tc in range(NTC):
                for dc in range(NDC):
                    ph.op("tensor", lambda e, tc=tc, dc=dc, wv=wv: e.matmul(
                        ps_v[:, 0:128],
                        biga[:, dc * T + tc * 128: dc * T + (tc + 1) * 128],
                        wv[:, dc * 128:(dc + 1) * 128],
                        start=(dc == 0), stop=(dc == NDC - 1)),
                        waits=v_war if dc == 0 else (),
                        inc=(st, 1) if dc == NDC - 1 else None)
                mmv = st.v
                vv = ph.op("vector", lambda e, tc=tc: e.tensor_copy(
                    v_sb[:, tc * 128:(tc + 1) * 128], ps_v[:, 0:128]),
                    waits=[(st, mmv)], inc=(sv, 1))
                v_war = [(sv, vv)]
            v_done = vv
            vS = ph.op("vector", lambda e: e.memset(S_f[1][:], 0.0), inc=(sv, 1))
            sSb = ph.op("scalar", lambda e: e.activation(out=S_b[:], in_=S_f[1][:],
                                                         func=AF.Copy),
                        waits=[(sv, vS)], inc=(ss, 1))
            self._gla_sb_done = sSb
            qk_ready = kpt_done
            y_evac = 0
            for tb in range(NTC):
                smm = ph.op("tensor", lambda e, tb=tb: e.matmul(
                    ps_s[:, 0:128], ktb[:, tb * 128:(tb + 1) * 128],
                    qt[:, tb * 128:(tb + 1) * 128], start=True, stop=True),
                    waits=[(sv, self._gla_mm_war), (sv, qk_ready)],
                    inc=(st, 1))
                vMm = ph.op("vector", lambda e: e.scalar_tensor_tensor(
                    out=Mm[:], in0=ps_s[:, 0:128], scalar=1.0, in1=self.bdtril[:],
                    op0=ALU.mult, op1=ALU.mult),
                    waits=[(st, smm)], inc=(sv, 1))
                self._gla_mm_war = vMm
                ph.op("tensor", lambda e, tb=tb: e.matmul(
                    ps_y[:, 0:128], v_sb[:, tb * 128:(tb + 1) * 128], Mm[:],
                    start=True, stop=False),
                    waits=[(sv, vMm), (sv, v_done), (sv, y_evac)])
                for c4 in range(4):
                    chunk = tb * 4 + c4
                    base = c4 * 32
                    tp = (96, 0) if base == 96 else None
                    ph.op("tensor", lambda e, tb=tb, c4=c4: e.matmul(
                        ps_y[:, c4 * 32:(c4 + 1) * 32], S_b[:],
                        qt[:, (tb * 128 + c4 * 32):(tb * 128 + c4 * 32) + 32],
                        start=False, stop=(c4 == 3)),
                        waits=[(ss, self._gla_sb_done)])
                    omm = ph.op("tensor", lambda e, tb=tb, c4=c4, base=base, tp=tp: e.matmul(
                        ps_o[:, 0:128],
                        kptok[base:base + 32, tb * 128:(tb + 1) * 128],
                        v_sb[base:base + 32, tb * 128:(tb + 1) * 128],
                        start=True, stop=True, tile_position=tp),
                        waits=[(sv, kpt_done if tb == 0 else 0), (sv, self._gla_S_war)],
                        inc=(st, 1))
                    par = chunk % 2
                    vSn = ph.op("vector", lambda e, chunk=chunk, par=par: e.scalar_tensor_tensor(
                        out=S_f[par][:], in0=S_f[1 - par][:],
                        scalar=eend[:, chunk:chunk + 1],
                        in1=ps_o[:, 0:128], op0=ALU.mult, op1=ALU.add),
                        waits=[(st, omm)], inc=(sv, 1))
                    self._gla_S_war = vSn
                    self._gla_sb_done = ph.op("scalar", lambda e, par=par: e.activation(
                        out=S_b[:], in_=S_f[par][:], func=AF.Copy),
                        waits=[(sv, vSn)], inc=(ss, 1))
                y_evac = ph.op("vector", lambda e, h=h, tb=tb: e.tensor_copy(
                    ybuf[:, h * T + tb * 128: h * T + (tb + 1) * 128], ps_y[:, 0:128]),
                    waits=[(st, st.v)], inc=(sv, 1))
            wfree[r] = [(st, st.v)]
            head_done = y_evac
        ph.fence()
        ph.run()
        es.close()

    def emit_moe(self, biga, act_sb, x2T_dram):
        sv, ss, st, qs, qg = self.sv, self.ss, self.st, self.qs, self.qg
        es = ExitStack()
        ph = Ph(self)
        ps8 = self.ps(es, "m_p8", (128, 4096), F32)
        glr = [self.sb(es, f"m_glr{i}", [8, 128], F32) for i in range(2)]
        self._m_s0war = 0
        self._m_s8war = 0
        compr = self.sb(es, "m_cpr", [8, T], F32)
        crow = self.sb(es, "m_crw", [1, T], F32)
        dn_sb = self.sb(es, "m_dn", [128, NJC * 2048], BF16)
        gur = [[self.sb(es, f"m_gu{i}{r}", [128, 2048], BF16) for i in range(2)]
               for r in range(2)]
        c_bc = self.sb(es, "m_cbc", [128, T], F32)

        stage = [self.sb(es, f"m_st{i}", [128, WIN], F32) for i in range(4)]
        sA2 = stage[0:2]

        glt_all = self.sb(es, "m_glta", [128, 128], F32)
        prb_all = self.sb(es, "m_prba", [128, 128], F32)
        prb2_all = self.sb(es, "m_prb2a", [128, 128], F32)
        m8_all = self.sb(es, "m_m8a", [128, 128], F32)
        msk_all = glt_all
        cmb_all = self.sb(es, "m_cmba", [128, 128], F32)
        rca = self.sb(es, "m_rca", [128, 64], F32)

        s0 = []
        glwar = [0, 0]
        for tc in range(NTC):
            rg = tc % 2
            glld = ph.dma("sync", glr[rg][:], self.gl_dram[0:8, tc * 128:(tc + 1) * 128],
                          waits=[(self.st, glwar[rg])], sem=f"mgl{rg}")
            tmm = ph.op("tensor", lambda e, rg=rg, tc=tc: e.transpose(
                ps8[0:128, (tc % 4) * 512:(tc % 4) * 512 + 8], glr[rg][:],
                self.ident_f[0:8, 0:8]),
                waits=[glld, (sv, self._m_s0war)], inc=(st, 1))
            glwar[rg] = tmm
            s0.append(ph.op("vector", lambda e, tc=tc: e.tensor_copy(
                glt_all[:, tc * 8:(tc + 1) * 8],
                ps8[0:128, (tc % 4) * 512:(tc % 4) * 512 + 8]),
                waits=[(st, tmm)], inc=(sv, 1)))
            if tc % 4 == 3:
                self._m_s0war = s0[-1]
        s1 = []
        for tc in range(NTC):
            s1.append(ph.op("vector", lambda e, tc=tc: e.reduce_max(
                rca[:, tc * 4:tc * 4 + 1], glt_all[:, tc * 8:(tc + 1) * 8],
                axis=AX.X, negate=True), inc=(sv, 1)))
        s2 = []
        for tc in range(NTC):
            s2.append(ph.op("scalar", lambda e, tc=tc: e.activation(
                out=prb_all[:, tc * 8:(tc + 1) * 8], in_=glt_all[:, tc * 8:(tc + 1) * 8],
                func=AF.Exp, bias=rca[:, tc * 4:tc * 4 + 1],
                accum_out=rca[:, tc * 4 + 1:tc * 4 + 2]),
                waits=[(sv, s1[tc])] if tc == 0 else [(sv, s1[tc])], inc=(ss, 1)))
        ph.op("scalar", lambda e: e.activation(out=rca[:, 3:4], in_=rca[:, 0:1],
                                               func=AF.Copy), inc=(ss, 1))
        s3 = []
        for tc in range(NTC):
            s3.append(ph.op("vector", lambda e, tc=tc: e.reciprocal(
                rca[:, tc * 4 + 2:tc * 4 + 3], rca[:, tc * 4 + 1:tc * 4 + 2]),
                waits=[(ss, s2[tc] + 1)], inc=(sv, 1)))
        s4 = []
        for tc in range(NTC):
            s4.append(ph.op("vector", lambda e, tc=tc: e.tensor_scalar_mul(
                prb2_all[:, tc * 8:(tc + 1) * 8], prb_all[:, tc * 8:(tc + 1) * 8],
                rca[:, tc * 4 + 2:tc * 4 + 3]), inc=(sv, 1)))
        s5 = []
        for tc in range(NTC):
            s5.append(ph.op("vector", lambda e, tc=tc: e.max(
                out=m8_all[:, tc * 8:(tc + 1) * 8], in_=prb2_all[:, tc * 8:(tc + 1) * 8]),
                inc=(sv, 1)))
        for tc in range(NTC):
            ph.op("vector", lambda e, tc=tc: e.tensor_scalar(
                out=msk_all[:, tc * 8:(tc + 1) * 8], in0=prb2_all[:, tc * 8:(tc + 1) * 8],
                scalar1=m8_all[:, tc * 8 + 1:tc * 8 + 2], scalar2=None, op0=ALU.is_ge))
        s7 = []
        for tc in range(NTC):
            s7.append(ph.op("vector", lambda e, tc=tc: e.tensor_mul(
                cmb_all[:, tc * 8:(tc + 1) * 8], prb2_all[:, tc * 8:(tc + 1) * 8],
                msk_all[:, tc * 8:(tc + 1) * 8]), inc=(sv, 1)))
        v_cmb = 0
        for tc in range(NTC):
            tmm = ph.op("tensor", lambda e, tc=tc: e.transpose(
                ps8[0:8, (4 + tc % 4) * 512:(4 + tc % 4) * 512 + 128],
                cmb_all[:, tc * 8:(tc + 1) * 8], self.ident_f[:]),
                waits=[(sv, s7[min(tc + 2, NTC - 1)]), (sv, self._m_s8war)], inc=(st, 1))
            v_cmb = ph.op("vector", lambda e, tc=tc: e.tensor_copy(
                compr[0:8, tc * 128:(tc + 1) * 128],
                ps8[0:8, (4 + tc % 4) * 512:(4 + tc % 4) * 512 + 128]),
                waits=[(st, tmm)], inc=(sv, 1))
            if tc % 4 == 3:
                self._m_s8war = v_cmb
        if getattr(self, "dbg_moe", None):
            ph.dma("sync", self.dbg_moe["compr"], compr[:], waits=[(sv, v_cmb)])

        guw = {"g": ("gup", None), "a": ("gsa", "dsa"), "b": ("gsb", "dsb")}
        experts = [("g", e) for e in range(NE)] + [("a", 0), ("b", 0)]
        dn_ld = 0
        dn_war = []
        gu_ld = [[0, 0], [0, 0]]
        gu_war = [[], []]
        cbc_war = 0
        act_war = [0, 0]
        self._m_ev = [[], [], [], []]
        self._m_bank_war = [[], []]
        act_war2 = [[0, 0], [0, 0]]
        x2_base = [[0] * NW for _ in range(NDC)]

        def gu_ap(kind, e, j):
            if kind == "g":
                return rr(self.inp["gup"], "e f p dc q -> e f p (dc q)")[e][j]
            return rr(self.inp["gsa" if kind == "a" else "gsb"],
                      "f p dc q -> f p (dc q)")[j]

        def dn_ap(kind, e):
            if kind == "g":
                return rr(self.inp["dnp"], "e p f d -> e p (f d)")[e]
            return rr(self.inp["dsa" if kind == "a" else "dsb"], "p f d -> p (f d)")

        for ei, (kind, e) in enumerate(experts):
            routed = kind == "g"
            prev_ps_users = (self._m_ev[0] + self._m_ev[1] + self._m_ev[2] +
                             self._m_ev[3] + [(sv, v_cmb)])
            dn_ld = ph.dma("gpsimd", dn_sb[:], dn_ap(kind, e), waits=dn_war, sem="mdn")
            if routed:
                v_cb = 0
                cld = ph.dma("sync", crow[:], compr[e:e + 1, :],
                             waits=[(sv, v_cmb), (st, cbc_war)], sem="mcrw")
                for w in range(NW):
                    tmm = ph.op("tensor", lambda e_, w=w: e_.matmul(
                        ps8[:, 3 * 512:4 * 512], self.ones_row_f[:],
                        crow[0:1, w * WIN:(w + 1) * WIN], start=True, stop=True),
                        waits=[cld, (st, cbc_war), (sv, v_cb)] +
                              self._m_ev[0] + self._m_ev[1] + self._m_ev[2] +
                              self._m_ev[3], inc=(st, 1))
                    v_cb = ph.op("vector", lambda e_, w=w: e_.tensor_copy(
                        c_bc[:, w * WIN:(w + 1) * WIN], ps8[:, 3 * 512:4 * 512]),
                        waits=[(st, tmm)] + self._m_ev[0] + self._m_ev[1] +
                              self._m_ev[2] + self._m_ev[3], inc=(sv, 1))
                cbc_done = v_cb
            for j in range(NJC):
                r = j % 2
                gu_ld[r][0] = ph.dma("gpsimd", gur[r][0][:], gu_ap(kind, e, j),
                                     waits=gu_war[r], sem=f"mgu{r}")
                gu_ld[r][1] = ph.dma("gpsimd", gur[r][1][:], gu_ap(kind, e, j + NJC),
                                     waits=(), sem=f"mgu{r}")
                for w in range(NW):
                    gb = (w % 2) * 4
                    for half, buf in enumerate(gur[r]):
                        for dc in range(NDC):
                            ph.op("tensor", lambda e_, dc=dc, w=w, buf=buf, half=half, gb=gb: e_.matmul(
                                ps8[:, (gb + half) * WIN:(gb + half + 1) * WIN],
                                buf[:, dc * 128:(dc + 1) * 128],
                                biga[:, dc * T + w * WIN: dc * T + (w + 1) * WIN],
                                start=(dc == 0), stop=(dc == NDC - 1)),
                                waits=([gu_ld[r][half]] +
                                       ([(ss, act_war2[w % 2][0]), (sv, act_war2[w % 2][1])]
                                        if half == 0 else []) +
                                       (prev_ps_users if (j == 0 and w == 0) else []))
                                      if dc == 0 else (),
                                inc=(st, 1) if dc == NDC - 1 else None)
                    mmv = st.v
                    s_si = ph.op("scalar", lambda e_, gb=gb, sAb=(w % 2): e_.activation(
                        out=sA2[sAb][:], in_=ps8[:, gb * WIN:(gb + 1) * WIN], func=AF.Silu),
                        waits=[(st, mmv)] + self._m_ev[w % 2], inc=(ss, 1))
                    v_ac = ph.op("vector", lambda e_, j=j, w=w, gb=gb, sAb=(w % 2): e_.tensor_tensor(
                        out=act_sb[:, j * T + w * WIN: j * T + (w + 1) * WIN],
                        in0=sA2[sAb][:], in1=ps8[:, (gb + 1) * WIN:(gb + 2) * WIN],
                        op=ALU.mult),
                        waits=[(ss, s_si), (st, mmv)], inc=(sv, 1))
                    act_war2[w % 2] = [s_si, v_ac]
                gu_war[r] = [(st, st.v)]
            act_done = sv.v
            for gi, (dq, w) in enumerate([(dq, w) for dq in range(4) for w in range(NW)]):
                bs = (gi % 2) * 4
                for j in range(NJC):
                    for dcl in range(4):
                        dc = dq * 4 + dcl
                        ph.op("tensor", lambda e_, j=j, dcl=dcl, dc=dc, w=w, bs=bs: e_.matmul(
                            ps8[:, (bs + dcl) * WIN:(bs + dcl + 1) * WIN],
                            dn_sb[:, j * 2048 + dc * 128: j * 2048 + (dc + 1) * 128],
                            act_sb[:, j * T + w * WIN: j * T + (w + 1) * WIN],
                            start=(j == 0), stop=(j == NJC - 1)),
                            waits=([(sv, act_done), dn_ld] + self._m_bank_war[gi % 2] +
                                   (self._m_ev[0] + self._m_ev[1] + self._m_ev[2] +
                                    self._m_ev[3] if ei == 0 and gi == 0 else []))
                                  if (j == 0 and dcl == 0) else (),
                            inc=(st, 1) if (j == NJC - 1 and dcl == 3) else None)
                mmv = st.v
                evs = []
                for dcl in range(4):
                    dc = dq * 4 + dcl
                    gsl = dcl
                    if routed:
                        ev = ph.op("vector", lambda e_, dcl=dcl, w=w, gsl=gsl, bs=bs: e_.tensor_tensor(
                            out=stage[gsl][:], in0=ps8[:, (bs + dcl) * WIN:(bs + dcl + 1) * WIN],
                            in1=c_bc[:, w * WIN:(w + 1) * WIN], op=ALU.mult),
                            waits=[(st, mmv), (sv, cbc_done)] + self._m_ev[gsl],
                            inc=(sv, 1))
                    else:
                        ev = ph.op("vector", lambda e_, dcl=dcl, gsl=gsl, bs=bs: e_.tensor_copy(
                            stage[gsl][:], ps8[:, (bs + dcl) * WIN:(bs + dcl + 1) * WIN]),
                            waits=[(st, mmv)] + self._m_ev[gsl], inc=(sv, 1))
                    evs.append(ev)
                    dst = x2T_dram[dc * 128:(dc + 1) * 128, w * WIN:(w + 1) * WIN]
                    if ei == 0:
                        sw = ph.dma("gpsimd", dst, stage[gsl][:], waits=[(sv, ev)],
                                    sem=f"mst{gsl}")
                        x2_base[dc][w] = sw
                    else:
                        sw = ph.dma("gpsimd", dst, stage[gsl][:],
                                    waits=[(sv, ev), x2_base[dc][w]],
                                    accum_op=ALU.add, sem=f"mst{gsl}")
                    self._m_ev[gsl] = [sw]
                self._m_bank_war[gi % 2] = [(sv, evs[-1])]
            dn_war = [(st, st.v)]
            cbc_war = st.v
        ph.fence()
        ph.run()
        es.close()

    def emit_oproj1(self, xlnT_dram, wo_area, x1T_dram, x2T_dram):
        sv, ss, st, qs, qg = self.sv, self.ss, self.st, self.qs, self.qg
        es = ExitStack()
        ph = Ph(self)
        xl = [self.sb(es, f"o1_x{i}", [128, 16 * WIN], BF16) for i in range(2)]
        x1s = [self.sb(es, f"o1_r{i}", [128, WIN], F32) for i in range(2)]
        stage = [self.sb(es, f"o1_s{i}", [128, WIN], F32) for i in range(2)]
        ps_o = [self.ps(es, f"o1_p{i}") for i in range(2)]
        for dic in range(NDC):
            wlds = ph.dma("gpsimd", wo_area[:, dic * 2048:(dic + 1) * 2048],
                          self.inp["wo1"][dic], sem="o1wo")
        xl_war = [[], []]
        x1_war = [[], []]
        stage_war = [[], []]
        evac = [0, 0]
        for w in range(NW):
            rw = w % 2
            xld = ph.dma("sync", rr(xl[rw][:], "p (dic t) -> p dic t", dic=16),
                         rr(xlnT_dram, "(dic p) t -> p dic t", p=128)[:, :, w * WIN:(w + 1) * WIN],
                         waits=xl_war[rw], sem=f"o1x{rw}")
            for dcp in range(NDC):
                g = (w * NDC + dcp) % 2
                rx = dcp % 2
                x1d = ph.dma("sync", x1s[rx][:],
                             x1T_dram[dcp * 128:(dcp + 1) * 128, w * WIN:(w + 1) * WIN],
                             waits=x1_war[rx], sem=f"o1r{rx}")
                for dic in range(NDC):
                    ph.op("tensor", lambda e, dic=dic, dcp=dcp, w=w, g=g, rw=rw: e.matmul(
                        ps_o[g][:, 0:WIN],
                        wo_area[:, dic * 2048 + dcp * 128: dic * 2048 + (dcp + 1) * 128],
                        xl[rw][:, dic * WIN:(dic + 1) * WIN],
                        start=(dic == 0), stop=(dic == NDC - 1)),
                        waits=[wlds, xld, (sv, evac[g])] if dic == 0 else (),
                        inc=(st, 1) if dic == NDC - 1 else None)
                mmv = st.v
                ev = ph.op("vector", lambda e, g=g, rx=rx: e.tensor_tensor(
                    out=stage[g][:], in0=ps_o[g][:, 0:WIN], in1=x1s[rx][:], op=ALU.add),
                    waits=[(st, mmv), x1d] + stage_war[g], inc=(sv, 1))
                evac[g] = ev
                x1_war[rx] = [(sv, ev)]
                sw = ph.dma("gpsimd",
                            x2T_dram[dcp * 128:(dcp + 1) * 128, w * WIN:(w + 1) * WIN],
                            stage[g][:], waits=[(sv, ev)], accum_op=ALU.add,
                            sem=f"o1s{g}")
                stage_war[g] = [sw]
            xl_war[rw] = [(st, st.v)]
        ph.fence()
        ph.run()
        es.close()

    def emit_out(self, x2T_dram, out_tok):
        sv, st, qs = self.sv, self.st, self.qs
        es = ExitStack()
        ph = Ph(self)
        xin = [self.sb(es, f"fo_x{i}", [128, T], F32) for i in range(2)]
        stg = [self.sb(es, f"fo_s{i}", [128, 2048], F32) for i in range(2)]
        pt = [self.ps(es, f"fo_p{i}", (128, 1024), F32) for i in range(2)]
        evac_done = [0, 0]
        store_done = [(qs, 0), (qs, 0)]
        xin_war = [0, 0]
        for dc in range(NDC):
            r = dc % 2
            wld = ph.dma("sync", xin[r][:], x2T_dram[dc * 128:(dc + 1) * 128, :],
                         waits=[(st, xin_war[r])], sem=f"fo{r}")
            for half in range(2):
                for tc8 in range(8):
                    tc = half * 8 + tc8
                    w0 = []
                    if tc == 0:
                        w0.append(wld)
                    if tc8 == 0:
                        w0.append((sv, evac_done[r]))
                    ph.op("tensor", lambda e, r=r, tc=tc, tc8=tc8: e.transpose(
                        pt[r][:, tc8 * 128:(tc8 + 1) * 128],
                        xin[r][:, tc * 128:(tc + 1) * 128], self.ident_f[:]),
                        waits=w0, inc=(st, 1) if tc8 == 7 else None)
                tmm = st.v
                evac_done[r] = ph.op("vector", lambda e, r=r, half=half: e.tensor_copy(
                    stg[r][:, half * 1024:(half + 1) * 1024], pt[r][:]),
                    waits=[(st, tmm), store_done[r]], inc=(sv, 1))
            xin_war[r] = st.v
            store_done[r] = ph.dma(
                "sync",
                rr(out_tok, "(tc p) d -> p tc d", p=128)[:, :, dc * 128:(dc + 1) * 128],
                rr(stg[r][:], "p (tc t) -> p tc t", tc=16),
                waits=[(sv, evac_done[r])], sem=f"fos{r}")
        ph.fence()
        ph.run()
        es.close()



def build_full():
    b = Build()
    x_in = b.din("x", [T, D], F32)
    for nm, shp in (("wq0", [16, 128, 16, 128]), ("wk0", [16, 128, 16, 128]),
                    ("wv0", [16, 128, 16, 128]), ("wo0", [16, 128, 16, 128]),
                    ("w1p", [64, 128, 16, 128]), ("w2p", [64, 128, 2048]),
                    ("b1p", [64, 128]), ("b2p", [16, 128]), ("wrp", [16, 128]),
                    ("wq1", [16, 128, 16, 128]), ("wk1", [16, 128, 16, 128]),
                    ("wv1", [16, 128, 16, 128]), ("wg1", [16, 128, 16, 128]),
                    ("wo1", [16, 128, 2048]), ("gmp", [16, 128]), ("btp", [16, 128]),
                    ("gwp", [128, 16, 8]), ("gup", [8, 22, 128, 16, 128]),
                    ("gsa", [22, 128, 16, 128]), ("gsb", [22, 128, 16, 128]),
                    ("dnp", [8, 128, 11, 2048]), ("dsa", [128, 11, 2048]),
                    ("dsb", [128, 11, 2048])):
        b.din(nm, shp, F32)
    x0T = b.dtmp("x0T", [D, T], F32)
    x1T = b.dtmp("x1T", [D, T], F32)
    x2T = b.dtmp("x2T", [D, T], F32)
    h_d = b.dtmp("h_d", [64, 128, 2048], BF16)
    xlnT = b.dtmp("xlnT", [D, T], BF16)
    out = b.dout("out", [T, D], F32)
    biga = b.sb(b.estack, "BIGA", [128, NDC * T], BF16)

    b.emit_consts()
    b.bstack = ExitStack()
    b.emit_transpose_in("b0", x_in, x0T)
    b.emit_ln("b0", x0T, biga, router=True)
    zstack = ExitStack()
    z_sb = b.sb(zstack, "ZBUF", [128, NH * T], BF16)
    b.emit_attn0(biga, z_sb)
    b.emit_ffn0(biga, h_d, x1T)
    b.emit_oproj0(z_sb, biga, x0T, x1T, accum=True)
    zstack.close()
    b.bstack.close()
    b.bstack = ExitStack()
    b.emit_ln("b1", x1T, biga, gate=True)
    ystack = ExitStack()
    ybuf = b.sb(ystack, "YBUF", [128, NH * T], BF16)
    b.emit_gla(biga, ybuf)
    b.emit_ln("yg", None, None, src_bf16=ybuf, gb=True, out_dram=xlnT)
    ystack.close()
    astack = ExitStack()
    act_sb = b.sb(astack, "ACT", [128, NJC * T], BF16)
    b.emit_moe(biga, act_sb, x2T)
    astack.close()
    b.emit_oproj1(xlnT, biga, x1T, x2T)
    b.bstack.close()
    b.emit_out(x2T, out)
    b.estack.close()
    return b


def kernel(**inputs):
    pk = pack_inputs(inputs)
    x = np.ascontiguousarray(inputs["L0_attn_x"].astype(np.float32))
    b = build_full()
    in_maps = []
    for c in range(8):
        m = dict(pk)
        m["x"] = np.ascontiguousarray(x[c])
        in_maps.append(m)
    res = run_bass_kernel_spmd(b.nc, in_maps, core_ids=list(range(8)))
    return np.stack([np.asarray(res.results[c]["out"]) for c in range(8)], axis=0)


# version 05
# speedup vs baseline: 1.4001x; 1.0456x over previous
import numpy as np
import ml_dtypes
from contextlib import ExitStack
import concourse.bass as bass
import concourse.mybir as mybir
from concourse.bass_utils import run_bass_kernel_spmd

F32 = mybir.dt.float32
BF16 = mybir.dt.bfloat16
AF = mybir.ActivationFunctionType
ALU = mybir.AluOpType
AX = mybir.AxisListType

D = 2048
T = 2048
NDC = 16
NTC = 16
NW = 4
WIN = 512
NH = 16
DH = 128
DF = 8192
NFC = 64
KSEL = 1024
NE = 8
FMOE = 1408
NJC = 11
GLA_C = 32
INV_SQRT_DH = float(1.0 / np.sqrt(128.0))
EXP_BIAS = -8.0

ENGS = ("sync", "scalar", "vector", "tensor", "gpsimd")



def pack_inputs(i):
    f = np.ascontiguousarray
    p = {}
    p["wq0"] = f(i["L0_attn_W_Q"].reshape(16, 128, 16, 128).transpose(0, 3, 2, 1))
    p["wk0"] = f(i["L0_attn_W_K"].reshape(16, 128, 16, 128).transpose(0, 3, 2, 1))
    p["wv0"] = f(i["L0_attn_W_V"].reshape(16, 128, 16, 128).transpose(0, 3, 2, 1))
    p["wo0"] = f(i["L0_attn_W_O"].reshape(16, 16, 128, 128).transpose(0, 3, 1, 2))
    p["w1p"] = f(i["L0_ffn_w1"].reshape(16, 128, 64, 128).transpose(2, 1, 0, 3))
    p["w2p"] = f(i["L0_ffn_w2"].reshape(64, 128, 2048))
    p["b1p"] = f(i["L0_ffn_b1"].reshape(64, 128))
    p["b2p"] = f(i["L0_ffn_b2"].reshape(16, 128))
    p["wrp"] = f(i["L0_ffn_w_r"].reshape(16, 128))
    for nm, k in (("wq1", "L1_attn_Wq"), ("wk1", "L1_attn_Wk"),
                  ("wv1", "L1_attn_Wv"), ("wg1", "L1_attn_Wg")):
        p[nm] = f(i[k].reshape(16, 128, 16, 128).transpose(0, 3, 2, 1))
    p["wo1"] = f(np.ascontiguousarray(i["L1_attn_Wo"].T).reshape(16, 128, 2048))
    p["gmp"] = f(i["L1_attn_gamma"].reshape(16, 128))
    p["btp"] = f(i["L1_attn_beta"].reshape(16, 128))
    p["gwp"] = f(np.ascontiguousarray(i["L1_ffn_gate_weight"].T)
                 .reshape(16, 128, 8).transpose(1, 0, 2))
    gu = i["L1_ffn_gate_up_weights"]
    p["gup"] = f(gu.reshape(8, 22, 128, 16, 128).transpose(0, 1, 4, 3, 2))
    sgu = i["L1_ffn_shared_gate_up"]
    sa = np.concatenate([sgu[0:1408], sgu[2816:4224]], axis=0)
    sb_ = np.concatenate([sgu[1408:2816], sgu[4224:5632]], axis=0)
    p["gsa"] = f(sa.reshape(22, 128, 16, 128).transpose(0, 3, 2, 1))
    p["gsb"] = f(sb_.reshape(22, 128, 16, 128).transpose(0, 3, 2, 1))
    dn = i["L1_ffn_down_weights"]
    p["dnp"] = f(dn.transpose(0, 2, 1).reshape(8, 11, 128, 2048).transpose(0, 2, 1, 3))
    sd = np.ascontiguousarray(i["L1_ffn_shared_down"].T)
    p["dsa"] = f(sd[0:1408].reshape(11, 128, 2048).transpose(1, 0, 2))
    p["dsb"] = f(sd[1408:2816].reshape(11, 128, 2048).transpose(1, 0, 2))
    return p



class SemC:
    def __init__(self, h):
        self.h = h
        self.v = 0

    def inc(self, k=1):
        self.v += k
        return self.v


class Ph:

    def __init__(self, bld):
        self.bld = bld
        self.nc = bld.nc
        self.q = {n: [] for n in ENGS}

    def add(self, en, fn):
        self.q[en].append(fn)

    def op(self, en, fn, waits=(), inc=None):
        tgt = inc[0].inc(inc[1]) if inc is not None else None

        def g(eng, fn=fn, waits=tuple(waits), inc=inc):
            for s, v in waits:
                if v > 0:
                    eng.wait_ge(s.h, v)
            ins = fn(eng)
            if inc is not None:
                ins.then_inc(inc[0].h, inc[1])
        self.add(en, g)
        return tgt

    def dma(self, en, out, in_, waits=(), sem=None, **kw):
        if isinstance(sem, str):
            sem = self.bld.dsem(sem)
        if sem is None:
            sem = {"sync": self.bld.qs, "scalar": self.bld.qc, "gpsimd": self.bld.qg}[en]
        tgt = self.op(en, lambda e: e.dma_start(out=out, in_=in_, **kw),
                      waits=waits, inc=(sem, 16))
        return (sem, tgt)

    def wdma(self, *a, **kw):
        return self.dma(*a, **kw)

    def fence(self):
        b = self.bld
        qtgts = [(b.qs, b.qs.v), (b.qc, b.qc.v), (b.qg, b.qg.v)]
        qtgts += [(s, s.v) for s in b.dsems.values()]
        tgt = b.bar.v + 5
        b.bar.v = tgt
        for en in ENGS:
            def g(eng, tgt=tgt, qtgts=tuple(qtgts)):
                eng.sem_inc(b.bar.h, 1)
                eng.wait_ge(b.bar.h, tgt)
                for s, v in qtgts:
                    if v > 0:
                        eng.wait_ge(s.h, v)
            self.add(en, g)

    def run(self):
        with self.nc.Block() as blk:
            for en in ENGS:
                fns = self.q[en]
                if fns:
                    getattr(blk, en)(lambda eng, fns=fns: [fn(eng) for fn in fns])


def rr(ap, spec, **kw):
    return ap.rearrange(spec, **kw)


class Build:
    def __init__(self):
        self.nc = bass.Bass("TRN2", target_bir_lowering=False, debug=False)
        self.inp = {}
        self.estack = ExitStack()
        nc = self.nc
        s = lambda n: SemC(self.estack.enter_context(nc.semaphore(n)))
        self.qs, self.qc, self.qg = s("qs"), s("qc"), s("qg")
        self.sv, self.ss, self.st, self.sg = s("sv"), s("ss"), s("st"), s("sg")
        self.bar = s("bar")
        self._mksem = s
        self.dsems = {}

    def dsem(self, name):
        if name not in self.dsems:
            self.dsems[name] = self._mksem("d_" + name)
        return self.dsems[name]

    def din(self, name, shape, dtype=F32):
        ap = self.nc.dram_tensor(name, list(shape), dtype, kind="ExternalInput").ap()
        self.inp[name] = ap
        return ap

    def dout(self, name, shape, dtype=F32):
        return self.nc.dram_tensor(name, list(shape), dtype, kind="ExternalOutput").ap()

    def dtmp(self, name, shape, dtype=F32):
        return self.nc.dram_tensor(name, list(shape), dtype, kind="Internal").ap()

    def sb(self, es, name, shape, dtype):
        return es.enter_context(self.nc.sbuf_tensor(name, list(shape), dtype))

    def ps(self, es, name, shape=(128, 512), dtype=F32):
        return es.enter_context(self.nc.psum_tensor(name, list(shape), dtype))

    def emit_consts(self):
        es = self.estack
        self.ident_f = self.sb(es, "ident_f", [128, 128], F32)
        self.ident_b = self.sb(es, "ident_b", [128, 128], BF16)
        self.ones_col_f = self.sb(es, "ones_col_f", [128, 1], F32)
        self.ones_col_b = self.sb(es, "ones_col_b", [128, 1], BF16)
        self.ones_row_f = self.sb(es, "ones_row_f", [1, 128], F32)
        self.bdtril = self.sb(es, "bdtril", [128, 128], BF16)
        self.scanmask = self.sb(es, "scanmask", [128, WIN], BF16)
        self.fc = self.sb(es, "fconsts", [128, 8], F32)
        ph = Ph(self)

        def mk(_):
            P = self._pool_ops = []
            for t, fill in ((self.ident_f, 1.0), (self.ident_b, 1.0)):
                P.append(lambda e, t=t: e.memset(t[:], 0.0))
                P.append(lambda e, t=t, fill=fill: e.affine_select(
                    out=t[:], in_=t[:], compare_op=ALU.not_equal, fill=fill,
                    base=0, pattern=[[-1, 128]], channel_multiplier=1))
            P.append(lambda e: e.memset(self.ones_col_f[:], 1.0))
            P.append(lambda e: e.memset(self.ones_col_b[:], 1.0))
            P.append(lambda e: e.memset(self.ones_row_f[:], 1.0))
            P.append(lambda e: e.memset(self.fc[:, 0:1], 1e-5))
            P.append(lambda e: e.memset(self.fc[:, 1:2], EXP_BIAS))
            P.append(lambda e: e.memset(self.fc[:, 2:3], 1.0))
            P.append(lambda e: e.memset(self.bdtril[:], 0.0))
            for b32 in range(4):
                sub = self.bdtril[:, b32 * 32:(b32 + 1) * 32]
                C = b32 * 32
                P.append(lambda e, sub=sub, C=C: e.affine_select(
                    out=sub, in_=sub, compare_op=ALU.is_ge, fill=1.0,
                    base=C - 1, pattern=[[0, 32]], channel_multiplier=-1))
                P.append(lambda e, sub=sub, C=C: e.affine_select(
                    out=sub, in_=sub, compare_op=ALU.is_ge, fill=0.0,
                    base=C, pattern=[[1, 32]], channel_multiplier=-1))
            P.append(lambda e: e.memset(self.scanmask[:], 1.0))
            P.append(lambda e: e.memset(self.scanmask[:, 0:WIN:GLA_C], 0.0))
        def mk_chained(e):
            self._pool_ops = []
            mk(None)
            for f in self._pool_ops:
                tgt = self.sg.inc(1)
                f(e).then_inc(self.sg.h, 1)
                e.wait_ge(self.sg.h, tgt)
        ph.op("gpsimd", mk_chained)
        ph.fence()
        ph.run()

    def emit_transpose_in(self, tag, x_tok, xT_dram):
        es = ExitStack()
        ph = Ph(self)
        st, sv, qs = self.st, self.sv, self.qs
        xin = [self.sb(es, f"ti_x{i}_{tag}", [128, D], F32) for i in range(2)]
        stg = [self.sb(es, f"ti_s{i}_{tag}", [128, D], F32) for i in range(2)]
        pt = [self.ps(es, f"ti_p{i}_{tag}", (128, 1024), F32) for i in range(2)]
        evac_done = [0, 0]
        store_done = [(qs, 0), (qs, 0)]
        xin_war = [0, 0]
        for tc in range(NTC):
            r = tc % 2
            wld = ph.dma("sync", xin[r][:], x_tok[tc * 128:(tc + 1) * 128, :],
                         waits=[(st, xin_war[r])], sem=f"ti{r}_{tag}")
            for half in range(2):
                for dc8 in range(8):
                    dc = half * 8 + dc8
                    w0 = []
                    if dc == 0:
                        w0.append(wld)
                    if dc8 == 0:
                        w0.append((sv, evac_done[r]))
                    ph.op("tensor", lambda e, r=r, dc=dc, dc8=dc8: e.transpose(
                        pt[r][:, dc8 * 128:(dc8 + 1) * 128],
                        xin[r][:, dc * 128:(dc + 1) * 128], self.ident_f[:]),
                        waits=w0,
                        inc=(st, 1) if dc8 == 7 else None)
                tmm = st.v
                evac_done[r] = ph.op("vector", lambda e, r=r, half=half: e.tensor_copy(
                    stg[r][:, half * 1024:(half + 1) * 1024], pt[r][:]),
                    waits=[(st, tmm), store_done[r]], inc=(sv, 1))
            xin_war[r] = st.v
            store_done[r] = ph.dma(
                "sync",
                rr(xT_dram, "(dc p) t -> p dc t", p=128)[:, :, tc * 128:(tc + 1) * 128],
                rr(stg[r][:], "p (dc t) -> p dc t", dc=16),
                waits=[(sv, evac_done[r])], sem=f"tis{r}_{tag}")
        ph.fence()
        ph.run()
        es.close()

    def emit_ln(self, tag, xT_dram, biga, router=False, gate=False, src_bf16=None,
                gb=False, out_dram=None):
        sv, ss, st, qs = self.sv, self.ss, self.st, self.qs
        if router:
            self.maskrow = self.sb(getattr(self, "bstack", None) or self.estack,
                                   "maskrow", [1, T], F32)
            self.rowtmp = self.dtmp("rowtmp", [16, 128], F32)
        if gate:
            self.gl_dram = self.dtmp("gl_dram", [8, T], F32)
        es = ExitStack()
        ph = Ph(self)
        rows = self.sb(es, f"ln_rows_{tag}", [1, 4 * T], F32)
        r_sumx = rows[0:1, 0:T]
        r_sumsq = rows[0:1, T:2 * T]
        r_mu = rows[0:1, 2 * T:3 * T]
        r_rs = rows[0:1, 3 * T:4 * T]
        xt = ([self.sb(es, f"ln_x{i}_{tag}", [128, T], F32) for i in range(2)]
              if src_bf16 is None else None)
        sq = self.sb(es, f"ln_sq_{tag}", [128, T], F32)
        mu_bc = self.sb(es, f"ln_mu_{tag}", [128, T], F32)
        rs_bc = self.sb(es, f"ln_rs_{tag}", [128, T], F32)
        t1 = self.sb(es, f"ln_t1_{tag}", [128, T], F32)
        t2 = sq
        if out_dram is not None:
            ostg = [self.sb(es, f"ln_os0_{tag}", [128, T], BF16)]
        if gb:
            gm_sb = self.sb(es, f"ln_gm_{tag}", [128, 16], F32)
            bt_sb = self.sb(es, f"ln_bt_{tag}", [128, 16], F32)
            g16 = self.sb(es, f"ln_g16_{tag}", [16, 128], F32)
        ps4 = self.ps(es, f"ln_ps_{tag}", (128, 2048), F32)

        xt_war = [[], []]
        gb_done = 0
        if gb:
            for nm, dst in (("gmp", gm_sb), ("btp", bt_sb)):
                wld = ph.dma("sync", g16[:], self.inp[nm], sem=f"lng16_{tag}",
                             waits=[(st, self.st.v)])
                tmm = ph.op("tensor", lambda e, dst=dst: e.transpose(
                    ps4[0:128, 0:16], g16[:], self.ident_f[0:16, 0:16]),
                    waits=[wld, (sv, gb_done)], inc=(st, 1))
                gb_done = ph.op("vector", lambda e, dst=dst: e.tensor_copy(
                    dst[:], ps4[0:128, 0:16]), waits=[(st, tmm)], inc=(sv, 1))

        def src_tile(dc):
            if src_bf16 is not None:
                return src_bf16[:, dc * T:(dc + 1) * T], []
            b = xt[dc % 2]
            wld = ph.dma("sync", b[:], xT_dram[dc * 128:(dc + 1) * 128, :],
                         waits=list(xt_war[dc % 2]), sem=f"lnx{dc % 2}_{tag}")
            return b[:], [wld]

        ones = self.ones_col_b if src_bf16 is not None else self.ones_col_f

        for dc in range(NDC):
            bap, w_ld = src_tile(dc)
            for w in range(NW):
                ph.op("tensor", lambda e, bap=bap, w=w, dc=dc: e.matmul(
                    ps4[0:1, w * WIN:(w + 1) * WIN], ones[:],
                    bap[:, w * WIN:(w + 1) * WIN], start=(dc == 0), stop=(dc == NDC - 1)),
                    waits=(w_ld + ([(sv, gb_done)] if dc == 0 else [])) if w == 0 else (),
                    inc=(st, 1))
            if src_bf16 is None:
                xt_war[dc % 2] = [(st, st.v)]
        v_sumx = ph.op("vector", lambda e: e.tensor_copy(r_sumx, ps4[0:1, 0:T]),
                       waits=[(st, st.v)], inc=(sv, 1))
        sq_war = 0
        for dc in range(NDC):
            bap, w_ld = src_tile(dc)
            sqd = ph.op("scalar", lambda e, bap=bap: e.activation(
                out=sq[:], in_=bap, func=AF.Square),
                waits=w_ld + [(st, sq_war)], inc=(ss, 1))
            for w in range(NW):
                ph.op("tensor", lambda e, w=w, dc=dc: e.matmul(
                    ps4[0:1, w * WIN:(w + 1) * WIN], self.ones_col_f[:],
                    sq[:, w * WIN:(w + 1) * WIN], start=(dc == 0), stop=(dc == NDC - 1)),
                    waits=[(ss, sqd), (sv, v_sumx)] if w == 0 else (),
                    inc=(st, 1))
            sq_war = st.v
            if src_bf16 is None:
                xt_war[dc % 2] = [(ss, sqd)]
        v_sums = ph.op("vector", lambda e: e.tensor_copy(r_sumsq, ps4[0:1, 0:T]),
                       waits=[(st, st.v)], inc=(sv, 1))
        ph.op("scalar", lambda e: e.activation(out=r_mu, in_=r_sumx, func=AF.Copy,
                                               scale=1.0 / D), waits=[(sv, v_sums)])
        ph.op("scalar", lambda e: e.activation(out=r_rs, in_=r_sumsq, func=AF.Copy,
                                               scale=1.0 / D))
        ph.op("scalar", lambda e: e.activation(out=r_sumx, in_=r_mu, func=AF.Square),
              inc=(ss, 1))
        s_mu2 = ss.v
        v_var = ph.op("vector", lambda e: e.tensor_sub(r_sumsq, r_rs, r_sumx),
                      waits=[(ss, s_mu2)], inc=(sv, 1))
        ph.op("scalar", lambda e: e.activation(out=r_sumx, in_=r_sumsq, func=AF.Ln,
                                               bias=self.fc[0:1, 0:1]),
              waits=[(sv, v_var)])
        s_rows = ph.op("scalar", lambda e: e.activation(out=r_rs, in_=r_sumx, func=AF.Exp,
                                                        scale=-0.5), inc=(ss, 1))
        v_bc = v_sums
        for (row, dstb) in ((r_mu, mu_bc), (r_rs, rs_bc)):
            for w in range(NW):
                tmm = ph.op("tensor", lambda e, row=row, w=w: e.matmul(
                    ps4[:, 0:WIN], self.ones_row_f[:], row[0:1, w * WIN:(w + 1) * WIN],
                    start=True, stop=True),
                    waits=[(ss, s_rows), (sv, v_bc)], inc=(st, 1))
                v_bc = ph.op("vector", lambda e, dstb=dstb, w=w: e.tensor_copy(
                    dstb[:, w * WIN:(w + 1) * WIN], ps4[:, 0:WIN]),
                    waits=[(st, tmm)], inc=(sv, 1))

        if router:
            wr_sb = self.sb(es, "wr_sb", [128, 16], F32)
            m16 = self.sb(es, "m16", [16, 128], F32)
            wld = ph.dma("sync", m16[:], self.inp["wrp"], sem=f"lnr1_{tag}")
            tmm = ph.op("tensor", lambda e: e.transpose(
                ps4[0:128, 0:16], m16[:], self.ident_f[0:16, 0:16]),
                waits=[wld, (sv, v_bc)], inc=(st, 1))
            v_wr = ph.op("vector", lambda e: e.tensor_copy(wr_sb[:], ps4[0:128, 0:16]),
                         waits=[(st, tmm)], inc=(sv, 1))
        if gate:
            glrow_sb = self.sb(es, "glrow_sb", [8, T], F32)
            gw_sb = self.sb(es, "gw_sb", [128, 16 * 8], F32)
            gwld = ph.dma("sync", gw_sb[:], rr(self.inp["gwp"], "p dc e -> p (dc e)"),
                          sem=f"lnr2_{tag}")

        self._ln_ost_war = [[], []]
        rtr_war = self.st.v if (router or gate) else 0
        for dc in range(NDC):
            bap, w_ld = src_tile(dc)
            ph.op("vector", lambda e, bap=bap: e.tensor_sub(t1[:], bap, mu_bc[:]),
                  waits=w_ld + [(sv, v_bc), (st, rtr_war)], inc=(sv, 1))
            if src_bf16 is None:
                xt_war[dc % 2] = [(sv, sv.v)]
            nrm = ph.op("vector", lambda e: e.tensor_mul(t2[:], t1[:], rs_bc[:]),
                        inc=(sv, 1))
            if out_dram is not None:
                ro = 0
                vo = ph.op("vector", lambda e, dc=dc, ro=ro: e.tensor_scalar(
                    out=ostg[ro][:], in0=t2[:],
                    scalar1=gm_sb[:, dc:dc + 1], scalar2=bt_sb[:, dc:dc + 1],
                    op0=ALU.mult, op1=ALU.add) if gb else e.tensor_copy(ostg[ro][:], t2[:]),
                    waits=self._ln_ost_war[ro], inc=(sv, 1))
                osw = ph.dma("sync", out_dram[dc * 128:(dc + 1) * 128, :], ostg[ro][:],
                             waits=[(sv, vo)], sem=f"lnos_{tag}")
                self._ln_ost_war[ro] = [osw]
            elif gb:
                ph.op("vector", lambda e, dc=dc: e.tensor_scalar(
                    out=biga[:, dc * T:(dc + 1) * T], in0=t2[:],
                    scalar1=gm_sb[:, dc:dc + 1], scalar2=bt_sb[:, dc:dc + 1],
                    op0=ALU.mult, op1=ALU.add), inc=(sv, 1))
            else:
                ph.op("vector", lambda e, dc=dc: e.tensor_copy(
                    biga[:, dc * T:(dc + 1) * T], t2[:]), inc=(sv, 1))
            if router:
                for w in range(NW):
                    ph.op("tensor", lambda e, w=w, dc=dc: e.matmul(
                        ps4[0:1, w * WIN:(w + 1) * WIN], wr_sb[:, dc:dc + 1],
                        t2[:, w * WIN:(w + 1) * WIN],
                        start=(dc == 0), stop=(dc == NDC - 1)),
                        waits=[(sv, nrm), (sv, v_wr)] if w == 0 else (),
                        inc=(st, 1) if w == NW - 1 else None)
                rtr_war = st.v
            if gate:
                for w in range(NW):
                    ph.op("tensor", lambda e, w=w, dc=dc: e.matmul(
                        ps4[0:8, w * WIN:(w + 1) * WIN], gw_sb[:, dc * 8:(dc + 1) * 8],
                        t2[:, w * WIN:(w + 1) * WIN],
                        start=(dc == 0), stop=(dc == NDC - 1)),
                        waits=([(sv, nrm)] + ([gwld] if dc == 0 else [])) if w == 0 else (),
                        inc=(st, 1) if w == NW - 1 else None)
                rtr_war = st.v

        if gate:
            vgl = ph.op("vector", lambda e: e.tensor_copy(glrow_sb[:], ps4[0:8, 0:T]),
                        waits=[(st, st.v)], inc=(sv, 1))
            ph.dma("sync", self.gl_dram, glrow_sb[:], waits=[(sv, vgl)])

        if getattr(self, "dbg_ln", None):
            d_rows = self.dbg_ln["rows"]
            d_mu = self.dbg_ln["mu"]
            d_rs = self.dbg_ln["rs"]
            ph.dma("sync", d_rows, rows[:], waits=[(self.ss, self.ss.v), (self.sv, self.sv.v)])
            ph.dma("sync", d_mu, mu_bc[:], waits=[(self.sv, v_bc)])
            ph.dma("sync", d_rs, rs_bc[:], waits=[(self.sv, v_bc)])

        if router:
            sc_bc = mu_bc
            sc_col = self.sb(es, "sc_col", [128, 16], F32)
            rank_col = self.sb(es, "rank_col", [128, 16], F32)
            mask_col = self.sb(es, "mask_col", [128, 16], F32)
            scr_row = r_sumx
            v_scrow = 0
            for w in range(NW):
                v_scrow = ph.op("vector", lambda e, w=w: e.tensor_copy(
                    scr_row[0:1, w * WIN:(w + 1) * WIN], ps4[0:1, w * WIN:(w + 1) * WIN]),
                    waits=[(st, st.v)] if w == 0 else (), inc=(sv, 1))
            v_bc2 = 0
            for w in range(NW):
                tmm = ph.op("tensor", lambda e, w=w: e.matmul(
                    ps4[:, 0:WIN], self.ones_row_f[:], scr_row[0:1, w * WIN:(w + 1) * WIN],
                    start=True, stop=True), waits=[(sv, v_scrow), (sv, v_bc2)],
                    inc=(st, 1))
                v_bc2 = ph.op("vector", lambda e, w=w: e.tensor_copy(
                    sc_bc[:, w * WIN:(w + 1) * WIN], ps4[:, 0:WIN]),
                    waits=[(st, tmm)], inc=(sv, 1))
            wst = ph.dma("sync", rr(self.rowtmp, "a t -> (a t)").rearrange("(a t) -> a t", a=1),
                         scr_row, waits=[(sv, v_scrow)], sem=f"lnr3_{tag}")
            wld = ph.dma("sync", m16[:], self.rowtmp, waits=[wst], sem=f"lnr4_{tag}")
            tmm = ph.op("tensor", lambda e: e.transpose(
                ps4[0:128, 0:16], m16[:], self.ident_f[0:16, 0:16]),
                waits=[wld, (sv, v_bc2)], inc=(st, 1))
            v_col = ph.op("vector", lambda e: e.tensor_copy(sc_col[:], ps4[0:128, 0:16]),
                          waits=[(st, tmm)], inc=(sv, 1))
            for tc in range(NTC):
                ph.op("vector", lambda e, tc=tc: e.tensor_scalar(
                    out=t1[:], in0=sc_bc[:], scalar1=sc_col[:, tc:tc + 1], scalar2=None,
                    op0=ALU.is_gt, op1=ALU.add, accum_out=rank_col[:, tc:tc + 1]),
                    waits=[(sv, v_col)] if tc == 0 else ())
            ph.op("vector", lambda e: e.tensor_copy(t1[:, 0:128], sc_bc[:, 0:128]))
            v_rank = ph.op("vector", lambda e: e.tensor_scalar(
                out=mask_col[:], in0=rank_col[:], scalar1=float(KSEL), scalar2=None,
                op0=ALU.is_lt), inc=(sv, 1))
            tmm = ph.op("tensor", lambda e: e.transpose(
                ps4[0:16, 0:128], mask_col[:], self.ident_f[:]),
                waits=[(sv, v_rank)], inc=(st, 1))
            v_m16 = ph.op("vector", lambda e: e.tensor_copy(m16[:], ps4[0:16, 0:128]),
                          waits=[(st, tmm)], inc=(sv, 1))
            wst = ph.dma("sync", self.rowtmp, m16[:], waits=[(sv, v_m16)], sem=f"lnr5_{tag}")
            ph.dma("sync", self.maskrow[:],
                   rr(self.rowtmp, "a t -> (a t)").rearrange("(a t) -> a t", a=1),
                   waits=[wst], sem=f"lnr6_{tag}")
            if getattr(self, "dbg_ln", None):
                ph.dma("sync", self.dbg_ln["sc_col"], sc_col[:], waits=[(sv, v_col)])
                ph.dma("sync", self.dbg_ln["rank"], rank_col[:], waits=[(sv, v_rank)])
                ph.dma("sync", self.dbg_ln["sc_bc"], sc_bc[:], waits=[(sv, v_bc2)])
                ph.dma("sync", self.dbg_ln["mask_col"], mask_col[:], waits=[(sv, v_rank)])
                ph.dma("sync", self.dbg_ln["m16"], m16[:], waits=[(sv, v_m16)])
        ph.fence()
        ph.run()
        es.close()

    def emit_attn0(self, biga, z_sb):
        sv, ss, st, qg = self.sv, self.ss, self.st, self.qg
        es = ExitStack()
        ph = Ph(self)
        q_sb = [self.sb(es, f"a0_q{r}", [128, T], BF16) for r in range(2)]
        k_sb = [self.sb(es, f"a0_k{r}", [128, T], BF16) for r in range(2)]
        v_sb = [self.sb(es, f"a0_v{r}", [128, T], BF16) for r in range(2)]
        self.cmask = self.sb(es, "a0_cmask", [128, 4 * WIN], BF16)
        pt_sb = self.sb(es, "a0_pt", [128, 8 * WIN], BF16)
        linv_bc = self.sb(es, "a0_li", [128, WIN], F32)
        lrow = self.sb(es, "a0_lr", [1, WIN], F32)
        wring = [[self.sb(es, f"a0_w{j}{r}", [128, 2048], BF16) for j in range(3)]
                 for r in range(2)]
        ps_qk = self.ps(es, "a0_pq", (128, 1024), F32)
        ps_s = self.ps(es, "a0_ps", (128, 2048), F32)
        ps_z = self.ps(es, "a0_pz")
        ps_lc = self.ps(es, "a0_pl")

        def _mkcmask(e):
            tgt = self.sg.inc(1)
            e.memset(self.cmask[:], 0.0).then_inc(self.sg.h, 1)
            e.wait_ge(self.sg.h, tgt)
            for dlt in range(4):
                sub = self.cmask[:, dlt * WIN:(dlt + 1) * WIN]
                tgt = self.sg.inc(1)
                e.affine_select(out=sub, in_=sub, compare_op=ALU.is_ge, fill=-1.0e5,
                                base=-128 * dlt, pattern=[[1, WIN]],
                                channel_multiplier=-1).then_inc(self.sg.h, 1)
                e.wait_ge(self.sg.h, tgt)
        ph.op("gpsimd", _mkcmask)
        cmask_rdy = self.sg.v

        wnames = ("wq0", "wk0", "wv0")
        wld = [(qg, 0), (qg, 0)]
        wfree = [[], []]
        for j, nm in enumerate(wnames):
            wld[0] = ph.dma("gpsimd", wring[0][j][:],
                            rr(self.inp[nm], "h p dc e -> h p (dc e)")[0], sem="a0w0")

        self._a0_qk_war = [[], []]

        def prep_gen(h):
            r = h % 2
            wq, wk, wv = wring[r]
            g = 0
            for (wt, dst, eng) in ((wq, q_sb[r], "scalar"), (wk, k_sb[r], "vector")):
                for w in range(NW):
                    bank = g % 2
                    for dc in range(NDC):
                        ph.op("tensor", lambda e, wt=wt, dc=dc, w=w, bank=bank: e.matmul(
                            ps_qk[:, bank * WIN:(bank + 1) * WIN],
                            wt[:, dc * 128:(dc + 1) * 128],
                            biga[:, dc * T + w * WIN: dc * T + (w + 1) * WIN],
                            start=(dc == 0), stop=(dc == NDC - 1)),
                            waits=([wld[r]] + self._a0_qk_war[bank]) if dc == 0 else (),
                            inc=(st, 1) if dc == NDC - 1 else None)
                    mmv = st.v
                    if eng == "scalar":
                        tgt = ph.op("scalar", lambda e, dst=dst, w=w, bank=bank: e.activation(
                            out=dst[:, w * WIN:(w + 1) * WIN],
                            in_=ps_qk[:, bank * WIN:(bank + 1) * WIN],
                            func=AF.Copy), waits=[(st, mmv)], inc=(ss, 1))
                        self._a0_qk_war[bank] = [(ss, tgt)]
                    else:
                        tgt = ph.op("vector", lambda e, dst=dst, w=w, bank=bank: e.tensor_copy(
                            dst[:, w * WIN:(w + 1) * WIN],
                            ps_qk[:, bank * WIN:(bank + 1) * WIN]),
                            waits=[(st, mmv)], inc=(sv, 1))
                        self._a0_qk_war[bank] = [(sv, tgt)]
                    g += 1
                    yield ("qk", tgt)
            for tcg in range(4):
                bank = g % 2
                for tc4 in range(4):
                    tc = tcg * 4 + tc4
                    for dc in range(NDC):
                        ph.op("tensor", lambda e, tc=tc, tc4=tc4, dc=dc, wv=wv, bank=bank: e.matmul(
                            ps_qk[:, bank * WIN + tc4 * 128: bank * WIN + (tc4 + 1) * 128],
                            biga[:, dc * T + tc * 128: dc * T + (tc + 1) * 128],
                            wv[:, dc * 128:(dc + 1) * 128],
                            start=(dc == 0), stop=(dc == NDC - 1)),
                            waits=self._a0_qk_war[bank] if (dc == 0 and tc4 == 0) else (),
                            inc=(st, 1) if (dc == NDC - 1 and tc4 == 3) else None)
                mmv = st.v
                tgt = ph.op("vector", lambda e, tcg=tcg, r=r, bank=bank: e.tensor_copy(
                    v_sb[r][:, tcg * 512:(tcg + 1) * 512],
                    ps_qk[:, bank * WIN:(bank + 1) * WIN]),
                    waits=[(st, mmv)], inc=(sv, 1))
                self._a0_qk_war[bank] = [(sv, tgt)]
                g += 1
                yield ("v", tgt)

        gen = prep_gen(0)
        last = None
        for last in gen:
            pass
        prep_done = {0: last[1]}

        z_war = 0
        l_war = 0
        exp_done = 0
        ptwar = [0, 0]
        for h in range(NH):
            r = h % 2
            if h + 1 < NH:
                for j, nm in enumerate(wnames):
                    wld[1 - r] = ph.dma("gpsimd", wring[1 - r][j][:],
                                        rr(self.inp[nm], "h p dc e -> h p (dc e)")[h + 1],
                                        waits=wfree[1 - r] if j == 0 else (),
                                        sem=f"a0w{1 - r}")
                nxt = prep_gen(h + 1)
            else:
                nxt = None
            qv = prep_done[h]
            for w in range(NW):
                ngrp = w + 1
                for grp in range(ngrp):
                    diag = grp == ngrp - 1
                    n4 = 4
                    for i4 in range(n4):
                        kc = grp * 4 + i4
                        ph.op("tensor", lambda e, kc=kc, w=w, i4=i4, r=r, diag=diag: e.matmul(
                            ps_s[:, i4 * WIN:(i4 + 1) * WIN],
                            k_sb[r][:, kc * 128:(kc + 1) * 128],
                            q_sb[r][:, w * WIN:(w + 1) * WIN], start=True, stop=not diag),
                            waits=([(ss, exp_done), (sv, qv),
                                    (self.sg, cmask_rdy)] if i4 == 0 else ()),
                            inc=(st, 1) if (i4 == n4 - 1 and not diag) else None)
                        if diag:
                            ph.op("tensor", lambda e, i4=i4: e.matmul(
                                ps_s[:, i4 * WIN:(i4 + 1) * WIN], self.ident_b[:],
                                self.cmask[:, i4 * WIN:(i4 + 1) * WIN],
                                start=False, stop=True),
                                inc=(st, 1) if i4 == n4 - 1 else None)
                    smm = st.v
                    gslot = grp % 2
                    exp_done = ph.op("scalar", lambda e, gslot=gslot: e.activation(
                        out=pt_sb[:, gslot * 4 * WIN:(gslot + 1) * 4 * WIN],
                        in_=ps_s[:, 0:4 * WIN], func=AF.Exp,
                        bias=self.fc[:, 1:2], scale=INV_SQRT_DH),
                        waits=[(st, smm), (st, ptwar[gslot])], inc=(ss, 1))
                    if nxt is not None:
                        n = next(nxt, None)
                        if n is None:
                            nxt = None
                        else:
                            prep_done[h + 1] = n[1]
                    nkc = 4 * w + 4
                    for i4 in range(n4):
                        kc = grp * 4 + i4
                        pcol = (gslot * 4 + i4) * WIN
                        ph.op("tensor", lambda e, kc=kc, pcol=pcol: e.matmul(
                            ps_lc[0:1, 0:WIN], self.ones_col_b[:],
                            pt_sb[:, pcol:pcol + WIN],
                            start=(kc == 0), stop=(kc == nkc - 1)),
                            waits=([(ss, exp_done)] + ([(sv, l_war)] if kc == 0 else []))
                                  if i4 == 0 else ())
                        ph.op("tensor", lambda e, kc=kc, r=r, pcol=pcol: e.matmul(
                            ps_z[:, 0:WIN], v_sb[r][:, kc * 128:(kc + 1) * 128],
                            pt_sb[:, pcol:pcol + WIN],
                            start=(kc == 0), stop=(kc == nkc - 1)),
                            waits=[(sv, z_war)] if kc == 0 else (),
                            inc=(st, 1) if kc == nkc - 1 else None)
                    ptwar[gslot] = st.v
                pvmm = st.v
                vr = ph.op("vector", lambda e: e.reciprocal(lrow[:], ps_lc[0:1, 0:WIN]),
                           waits=[(st, pvmm)], inc=(sv, 1))
                l_war = vr
                bmm = ph.op("tensor", lambda e: e.matmul(
                    ps_lc[:, 0:WIN], self.ones_row_f[:], lrow[:], start=True, stop=True),
                    waits=[(sv, vr)], inc=(st, 1))
                bev = ph.op("vector", lambda e: e.tensor_copy(linv_bc[:], ps_lc[:, 0:WIN]),
                            waits=[(st, bmm)], inc=(sv, 1))
                z_war = ph.op("vector", lambda e, h=h, w=w: e.tensor_tensor(
                    out=z_sb[:, h * T + w * WIN: h * T + (w + 1) * WIN],
                    in0=ps_z[:, 0:WIN], in1=linv_bc[:], op=ALU.mult),
                    waits=[(sv, bev)], inc=(sv, 1))
            if nxt is not None:
                for n in nxt:
                    prep_done[h + 1] = n[1]
                nxt = None
            wfree[r] = [(st, st.v)]
        ph.fence()
        ph.run()
        es.close()

    def emit_oproj0(self, z_sb, wo_area, x0T_dram, x1T_dram, accum):
        sv, ss, st, qs, qg = self.sv, self.ss, self.st, self.qs, self.qg
        es = ExitStack()
        ph = Ph(self)
        x0s = [self.sb(es, f"o0_x{i}", [128, T], F32) for i in range(2)]
        stage = [self.sb(es, f"o0_s{i}", [128, WIN], F32) for i in range(2)]
        ps_o = [self.ps(es, f"o0_p{i}") for i in range(2)]
        for h in range(NH):
            wo_ld = ph.dma("gpsimd", wo_area[:, h * 2048:(h + 1) * 2048],
                           rr(self.inp["wo0"], "h p dc e -> h p (dc e)")[h], sem="o0wo")
        x0_war = [[], []]
        stage_war = [[], []]
        evac = [0, 0]
        for dc in range(NDC):
            r = dc % 2
            xw = ph.dma("sync", x0s[r][:], x0T_dram[dc * 128:(dc + 1) * 128, :],
                        waits=x0_war[r], sem=f"o0x{r}")
            for w in range(NW):
                g = (dc * NW + w) % 2
                for h in range(NH):
                    ph.op("tensor", lambda e, h=h, dc=dc, w=w, g=g: e.matmul(
                        ps_o[g][:, 0:WIN],
                        wo_area[:, h * 2048 + dc * 128: h * 2048 + (dc + 1) * 128],
                        z_sb[:, h * T + w * WIN: h * T + (w + 1) * WIN],
                        start=(h == 0), stop=(h == NH - 1)),
                        waits=[wo_ld, (sv, evac[g])] if h == 0 else (),
                        inc=(st, 1) if h == NH - 1 else None)
                mmv = st.v
                ev = ph.op("vector", lambda e, r=r, w=w, g=g: e.tensor_tensor(
                    out=stage[g][:], in0=ps_o[g][:, 0:WIN],
                    in1=x0s[r][:, w * WIN:(w + 1) * WIN], op=ALU.add),
                    waits=[(st, mmv), xw] + stage_war[g],
                    inc=(sv, 1))
                evac[g] = ev
                dst = x1T_dram[dc * 128:(dc + 1) * 128, w * WIN:(w + 1) * WIN]
                sw = ph.dma("gpsimd" if accum else "sync", dst, stage[g][:],
                            waits=[(sv, ev)], sem=f"o0s{g}",
                            **({"accum_op": ALU.add} if accum else {}))
                stage_war[g] = [sw]
            x0_war[r] = [(sv, evac[0]), (sv, evac[1])]
        ph.fence()
        ph.run()
        es.close()

    def emit_ffn0(self, biga, h_dram, x1T_dram):
        sv, ss, st, qs, qg = self.sv, self.ss, self.st, self.qs, self.qg
        es = ExitStack()
        ph = Ph(self)
        w1t = [self.sb(es, f"f0_w1{i}", [128, 2048], BF16) for i in range(2)]
        hstg = [self.sb(es, f"f0_h{i}", [128, 2048], BF16) for i in range(2)]
        b1c = self.sb(es, "f0_b1c", [128, 64], F32)
        b2c = self.sb(es, "f0_b2c", [128, 16], F32)
        m16 = self.sb(es, "f0_m16", [64, 128], F32)
        mask_bc = self.sb(es, "f0_mbc", [128, T], F32)
        es_a = ExitStack()
        ps_h = self.ps(es_a, "f0_ph", (128, 1024), F32)
        ps_c = self.ps(es_a, "f0_pc")

        wld = ph.dma("sync", m16[:], self.inp["b1p"], sem="f0b")
        tmm = ph.op("tensor", lambda e: e.transpose(
            ps_c[0:128, 0:64], m16[:], self.ident_f[0:64, 0:64]),
            waits=[wld], inc=(st, 1))
        v1 = ph.op("vector", lambda e: e.tensor_copy(b1c[:], ps_c[0:128, 0:64]),
                   waits=[(st, tmm)], inc=(sv, 1))
        wld = ph.dma("sync", m16[0:16, :], self.inp["b2p"], sem="f0b",
                     waits=[(st, st.v)])
        tmm = ph.op("tensor", lambda e: e.transpose(
            ps_c[0:128, 0:16], m16[0:16, :], self.ident_f[0:16, 0:16]),
            waits=[wld, (sv, v1)], inc=(st, 1))
        v2 = ph.op("vector", lambda e: e.tensor_copy(b2c[:], ps_c[0:128, 0:16]),
                   waits=[(st, tmm)], inc=(sv, 1))
        v_mb = v2
        for w in range(NW):
            tmm = ph.op("tensor", lambda e, w=w: e.matmul(
                ps_c[:, 0:WIN], self.ones_row_f[:],
                self.maskrow[0:1, w * WIN:(w + 1) * WIN], start=True, stop=True),
                waits=[(sv, v_mb)], inc=(st, 1))
            v_mb = ph.op("vector", lambda e, w=w: e.tensor_copy(
                mask_bc[:, w * WIN:(w + 1) * WIN], ps_c[:, 0:WIN]),
                waits=[(st, tmm)], inc=(sv, 1))

        if getattr(self, "dbg_ffn", None):
            ph.dma("sync", self.dbg_ffn["mask_bc"], mask_bc[:], waits=[(sv, v_mb)])
            ph.dma("sync", self.dbg_ffn["maskrow"], self.maskrow[:])
        h_st = [(qs, 0)] * NFC
        w1ld = [(qg, 0), (qg, 0)]
        w1war = [[], []]
        w1ld[0] = ph.dma("gpsimd", w1t[0][:],
                         rr(self.inp["w1p"], "f p dc e -> f p (dc e)")[0], sem="f0w10")
        hst_war = [[], []]
        gel = [0, 0]
        for fc in range(NFC):
            r = fc % 2
            if fc + 1 < NFC:
                w1ld[1 - r] = ph.dma("gpsimd", w1t[1 - r][:],
                                     rr(self.inp["w1p"], "f p dc e -> f p (dc e)")[fc + 1],
                                     waits=w1war[1 - r], sem=f"f0w1{1 - r}")
            for w in range(NW):
                g = w % 2
                for dc in range(NDC):
                    ph.op("tensor", lambda e, fc=fc, w=w, dc=dc, r=r, g=g: e.matmul(
                        ps_h[:, g * WIN:(g + 1) * WIN],
                        w1t[r][:, dc * 128:(dc + 1) * 128],
                        biga[:, dc * T + w * WIN: dc * T + (w + 1) * WIN],
                        start=(dc == 0), stop=(dc == NDC - 1)),
                        waits=[w1ld[r], (ss, gel[g])] if dc == 0 else (),
                        inc=(st, 1) if dc == NDC - 1 else None)
                mmv = st.v
                gel[g] = ph.op("scalar", lambda e, fc=fc, w=w, r=r, g=g: e.activation(
                    out=hstg[r][:, w * WIN:(w + 1) * WIN], in_=ps_h[:, g * WIN:(g + 1) * WIN],
                    func=AF.Gelu_apprx_tanh, bias=b1c[:, fc:fc + 1]),
                    waits=[(st, mmv), (sv, v2)] + (hst_war[r] if w == 0 else []),
                    inc=(ss, 1))
            w1war[r] = [(st, st.v)]
            hw = ph.dma("sync", h_dram[fc], hstg[r][:], waits=[(ss, gel[0]), (ss, gel[1])],
                        sem=f"f0h{r}")
            h_st[fc] = hw
            hst_war[r] = [hw]

        ph.fence()
        ph.run()
        es_a.close()
        ph = Ph(self)
        h_all = [(self.dsem("f0h0"), self.dsem("f0h0").v),
                 (self.dsem("f0h1"), self.dsem("f0h1").v)]
        w2t = [self.sb(es, f"f0_w2{i}", [128, WIN], BF16) for i in range(3)]
        ht = [self.sb(es, f"f0_ht{i}", [128, 1024], BF16) for i in range(3)]
        stg = [self.sb(es, f"f0_s{i}", [128, WIN], F32) for i in range(2)]
        es_b = ExitStack()
        ps8 = self.ps(es_b, "f0_p8", (128, 4096), F32)
        self._f0_ev = 0
        w2war = [[] for _ in range(3)]
        htwar = [[] for _ in range(3)]
        stg_war = [[], []]
        for dq in range(4):
            for wp in range(2):
                evacs = st.v
                for fc in range(NFC):
                    r3 = fc % 3
                    w2ld = ph.dma("gpsimd", w2t[r3][:],
                                  self.inp["w2p"][fc][:, dq * WIN:(dq + 1) * WIN],
                                  waits=w2war[r3], sem=f"f0w2{r3}")
                    htld = ph.dma("sync", ht[r3][:],
                                  h_dram[fc][:, wp * 1024:(wp + 1) * 1024],
                                  waits=htwar[r3] + (h_all if fc < 3 else []),
                                  sem=f"f0ht{r3}")
                    for dcl in range(4):
                        for wl in range(2):
                            ph.op("tensor", lambda e, fc=fc, dcl=dcl, wl=wl, r3=r3: e.matmul(
                                ps8[:, (dcl * 2 + wl) * WIN:(dcl * 2 + wl + 1) * WIN],
                                w2t[r3][:, dcl * 128:(dcl + 1) * 128],
                                ht[r3][:, wl * WIN:(wl + 1) * WIN],
                                start=(fc == 0), stop=(fc == NFC - 1)),
                                waits=([w2ld, htld] +
                                       ([(sv, self._f0_ev)] if fc == 0 else []))
                                      if (dcl == 0 and wl == 0) else (),
                                inc=(st, 1) if (dcl == 3 and wl == 1) else None)
                    w2war[r3] = [(st, st.v)]
                    htwar[r3] = [(st, st.v)]
                mmv = st.v
                for dcl in range(4):
                    for wl in range(2):
                        dc = dq * 4 + dcl
                        w = wp * 2 + wl
                        g = (dcl * 2 + wl) % 2
                        ev = ph.op("vector", lambda e, dcl=dcl, wl=wl, dc=dc, w=w, g=g:
                                   e.scalar_tensor_tensor(
                                       out=stg[g][:],
                                       in0=ps8[:, (dcl * 2 + wl) * WIN:(dcl * 2 + wl + 1) * WIN],
                                       scalar=b2c[:, dc:dc + 1],
                                       in1=mask_bc[:, w * WIN:(w + 1) * WIN],
                                       op0=ALU.add, op1=ALU.mult),
                                   waits=[(st, mmv), (sv, v_mb)] + stg_war[g],
                                   inc=(sv, 1))
                        sw = ph.dma("sync",
                                    x1T_dram[dc * 128:(dc + 1) * 128, w * WIN:(w + 1) * WIN],
                                    stg[g][:], waits=[(sv, ev)], sem=f"f0s{g}")
                        stg_war[g] = [sw]
                self._f0_ev = sv.v
        ph.fence()
        ph.run()
        es_b.close()
        es.close()

    def emit_gla(self, biga, ybuf):
        sv, ss, st, qs, qg = self.sv, self.ss, self.st, self.qs, self.qg
        es = ExitStack()
        ph = Ph(self)
        wring = [[self.sb(es, f"g_w{j}{r}", [128, 2048], BF16) for j in range(4)]
                 for r in range(2)]
        sA = self.sb(es, "g_sA", [128, WIN], F32)
        sB = self.sb(es, "g_sB", [128, WIN], F32)
        sD = self.sb(es, "g_sD", [128, WIN], F32)
        sC = self.sb(es, "g_sC", [128, WIN], F32)
        sF = sC
        ktf = sB
        kpw = self.sb(es, "g_kpw", [128, WIN], BF16)
        eend = [self.sb(es, f"g_ee{u}", [128, 64], F32) for u in range(2)]
        qt = [self.sb(es, f"g_qt{u}", [128, T], BF16) for u in range(2)]
        ktb = [self.sb(es, f"g_ktb{u}", [128, T], BF16) for u in range(2)]
        kptok = [self.sb(es, f"g_kpt{u}", [128, T], BF16) for u in range(2)]
        v_sb = [self.sb(es, f"g_v{u}", [128, T], BF16) for u in range(2)]
        S_f = [[self.sb(es, f"g_S{u}{i}", [128, 128], F32) for i in range(2)]
               for u in range(2)]
        S_b = [self.sb(es, f"g_Sb{u}", [128, 128], BF16) for u in range(2)]
        Mm = [self.sb(es, f"g_Mm{u}", [128, 128], BF16) for u in range(2)]
        ps_a = self.ps(es, "g_pa")
        ps_b = self.ps(es, "g_pb")
        ps_t = self.ps(es, "g_pt", (128, 2048), BF16)
        ps_y = self.ps(es, "g_py", (128, 1024), F32)
        ps_o = self.ps(es, "g_po", (128, 1024), F32)

        wnames = ("wg1", "wq1", "wk1", "wv1")
        wld = [(qg, 0), (qg, 0)]
        wfree = [[], []]
        for j, nm in enumerate(wnames):
            wld[0] = ph.dma("gpsimd", wring[0][j][:],
                            rr(self.inp[nm], "h p dc e -> h p (dc e)")[0], sem="gw0")
        for j, nm in enumerate(wnames):
            wld[1] = ph.dma("gpsimd", wring[1][j][:],
                            rr(self.inp[nm], "h p dc e -> h p (dc e)")[1], sem="gw1")

        def eend_ap(u, w):
            hdl = eend[u][:].tensor
            return bass.AP(hdl, w * 16, [[64, 128], [1, 16], [0, 32]])

        self._g_pp = [[], []]
        self._g_scr = [0, 0, 0]
        self._g_tp = [0, 0]

        def emit_prep(h, u):
            r = h % 2
            wg, wq, wk, wv = wring[r]
            scan_done = 0
            for w in range(NW):
                bank, bank2 = (ps_a, ps_b) if w % 2 == 0 else (ps_b, ps_a)
                bsl, bsl2 = (0, 1) if w % 2 == 0 else (1, 0)
                for dc in range(NDC):
                    ph.op("tensor", lambda e, dc=dc, w=w, wg=wg, bank=bank: e.matmul(
                        bank[:, 0:WIN], wg[:, dc * 128:(dc + 1) * 128],
                        biga[:, dc * T + w * WIN: dc * T + (w + 1) * WIN],
                        start=(dc == 0), stop=(dc == NDC - 1)),
                        waits=([wld[r]] + self._g_pp[bsl]) if dc == 0 else (),
                        inc=(st, 1) if dc == NDC - 1 else None)
                mmv = st.v
                vzm = ph.op("vector", lambda e, bank=bank: e.tensor_scalar_min(
                    sA[:], bank[:, 0:WIN], 0.0),
                    waits=[(st, mmv), (ss, self._g_scr[0])], inc=(sv, 1))
                saz = ph.op("scalar", lambda e, bank=bank: e.activation(
                    out=sB[:], in_=bank[:, 0:WIN], func=AF.Abs),
                    waits=[(st, mmv), (sv, vzm)], inc=(ss, 1))
                self._g_pp[bsl] = [(ss, saz)]
                ph.op("scalar", lambda e: e.activation(out=sC[:], in_=sB[:], func=AF.Exp,
                                                       scale=-1.0))
                slb = ph.op("scalar", lambda e: e.activation(
                    out=sB[:], in_=sC[:], func=AF.Ln, bias=self.fc[:, 2:3]), inc=(ss, 1))
                self._g_scr[0] = slb
                vg_ = ph.op("vector", lambda e: e.tensor_sub(sF[:], sA[:], sB[:]),
                            waits=[(ss, slb), (sv, scan_done)], inc=(sv, 1))
                scan_done = ph.op("vector", lambda e: e.tensor_tensor_scan(
                    out=sD[:], data0=self.scanmask[:], data1=sF[:], initial=0.0,
                    op0=ALU.mult, op1=ALU.add),
                    waits=[(ss, self._g_scr[1])], inc=(sv, 1))
                sEe = ph.op("scalar", lambda e, u=u, w=w: e.activation(
                    out=eend[u][:, w * 16:(w + 1) * 16], in_=sD[:, 31:WIN:GLA_C],
                    func=AF.Exp), waits=[(sv, scan_done)], inc=(ss, 1))
                for dc in range(NDC):
                    ph.op("tensor", lambda e, dc=dc, w=w, wq=wq, bank2=bank2: e.matmul(
                        bank2[:, 0:WIN], wq[:, dc * 128:(dc + 1) * 128],
                        biga[:, dc * T + w * WIN: dc * T + (w + 1) * WIN],
                        start=(dc == 0), stop=(dc == NDC - 1)),
                        waits=self._g_pp[bsl2] if dc == 0 else (),
                        inc=(st, 1) if dc == NDC - 1 else None)
                mmq = st.v
                sEw = ph.op("scalar", lambda e: e.activation(
                    out=sC[:], in_=sD[:], func=AF.Exp),
                    waits=[(sv, scan_done), (sv, self._g_scr[2])], inc=(ss, 1))
                vq = ph.op("vector", lambda e, u=u, w=w, bank2=bank2: e.scalar_tensor_tensor(
                    out=qt[u][:, w * WIN:(w + 1) * WIN], in0=bank2[:, 0:WIN],
                    scalar=INV_SQRT_DH, in1=sC[:], op0=ALU.mult, op1=ALU.mult),
                    waits=[(st, mmq), (ss, sEw)], inc=(sv, 1))
                self._g_pp[bsl2] = [(sv, vq)]
                for dc in range(NDC):
                    ph.op("tensor", lambda e, dc=dc, w=w, wk=wk, bank=bank: e.matmul(
                        bank[:, 0:WIN], wk[:, dc * 128:(dc + 1) * 128],
                        biga[:, dc * T + w * WIN: dc * T + (w + 1) * WIN],
                        start=(dc == 0), stop=(dc == NDC - 1)),
                        waits=self._g_pp[bsl] if dc == 0 else (),
                        inc=(st, 1) if dc == NDC - 1 else None)
                mmk = st.v
                sEm = ph.op("scalar", lambda e: e.activation(
                    out=sC[:], in_=sD[:], func=AF.Exp, scale=-1.0),
                    waits=[(sv, vq)], inc=(ss, 1))
                self._g_scr[1] = sEm
                vktf = ph.op("vector", lambda e, bank=bank: e.tensor_mul(
                    ktf[:], bank[:, 0:WIN], sC[:]),
                    waits=[(st, mmk), (ss, sEm)], inc=(sv, 1))
                self._g_pp[bsl] = [(sv, vktf)]
                self._g_scr[2] = vktf
                ph.op("vector", lambda e, u=u, w=w: e.tensor_copy(
                    ktb[u][:, w * WIN:(w + 1) * WIN], ktf[:]))
                vkp = ph.op("vector", lambda e, u=u, w=w: e.tensor_tensor(
                    out=kpw[:], in0=ktf[:], in1=eend_ap(u, w), op=ALU.mult),
                    waits=[(ss, sEe), (sv, self._g_tp[1])], inc=(sv, 1))
                for j4 in range(4):
                    tc = w * 4 + j4
                    g2 = j4 % 2
                    tmm = ph.op("tensor", lambda e, j4=j4, g2=g2: e.transpose(
                        ps_t[:, g2 * 1024:g2 * 1024 + 128],
                        kpw[:, j4 * 128:(j4 + 1) * 128], self.ident_b[:]),
                        waits=[(sv, vkp)] if j4 == 0 else [(sv, self._g_tp[0])],
                        inc=(st, 1))
                    kd = ph.op("vector", lambda e, u=u, tc=tc, g2=g2: e.tensor_copy(
                        kptok[u][:, tc * 128:(tc + 1) * 128],
                        ps_t[:, g2 * 1024:g2 * 1024 + 128]),
                        waits=[(st, tmm)], inc=(sv, 1))
                    if j4 >= 1:
                        self._g_tp[0] = kd
                self._g_tp[1] = kd
                for tc4 in range(4):
                    tc = w * 4 + tc4
                    for dc in range(NDC):
                        ph.op("tensor", lambda e, tc=tc, tc4=tc4, dc=dc, wv=wv, bank2=bank2: e.matmul(
                            bank2[:, tc4 * 128:(tc4 + 1) * 128],
                            biga[:, dc * T + tc * 128: dc * T + (tc + 1) * 128],
                            wv[:, dc * 128:(dc + 1) * 128],
                            start=(dc == 0), stop=(dc == NDC - 1)),
                            waits=self._g_pp[bsl2] if (dc == 0 and tc4 == 0) else (),
                            inc=(st, 1) if (dc == NDC - 1 and tc4 == 3) else None)
                mmv2 = st.v
                vv = ph.op("vector", lambda e, u=u, w=w, bank2=bank2: e.tensor_copy(
                    v_sb[u][:, w * WIN:(w + 1) * WIN], bank2[:, 0:WIN]),
                    waits=[(st, mmv2)], inc=(sv, 1))
                self._g_pp[bsl2] = [(sv, vv)]
            vS = ph.op("vector", lambda e, u=u: e.memset(S_f[u][1][:], 0.0),
                       waits=[(sv, self._g_Swar[u])], inc=(sv, 1))
            sSb = ph.op("scalar", lambda e, u=u: e.activation(
                out=S_b[u][:], in_=S_f[u][1][:], func=AF.Copy),
                waits=[(sv, vS)], inc=(ss, 1))
            self._g_sbd[u] = sSb
            return sv.v

        self._g_Swar = [0, 0]
        self._g_sbd = [0, 0]
        self._g_mmw = [0, 0]
        self._g_ow = [0, 0]

        def emit_pair(h0, pd):
            y_evac = list(self._g_Swar)
            for tb in range(NTC):
                for u in (0, 1):
                    smm = ph.op("tensor", lambda e, u=u, tb=tb: e.matmul(
                        ps_y[:, u * WIN + 128:u * WIN + 256],
                        ktb[u][:, tb * 128:(tb + 1) * 128],
                        qt[u][:, tb * 128:(tb + 1) * 128], start=True, stop=True),
                        waits=[(sv, self._g_mmw[u]), (sv, pd[u]), (sv, y_evac[u]),
                               (sv, self._g_ow[u])], inc=(st, 1))
                    vMm = ph.op("vector", lambda e, u=u: e.scalar_tensor_tensor(
                        out=Mm[u][:], in0=ps_y[:, u * WIN + 128:u * WIN + 256], scalar=1.0,
                        in1=self.bdtril[:], op0=ALU.mult, op1=ALU.mult),
                        waits=[(st, smm)], inc=(sv, 1))
                    self._g_mmw[u] = vMm
                    ph.op("tensor", lambda e, u=u, tb=tb: e.matmul(
                        ps_y[:, u * WIN:u * WIN + 128],
                        v_sb[u][:, tb * 128:(tb + 1) * 128], Mm[u][:],
                        start=True, stop=False),
                        waits=[(sv, vMm), (sv, y_evac[u])])
                for c4 in range(4):
                    for u in (0, 1):
                        tbc = tb * 128 + c4 * 32
                        chunk = tb * 4 + c4
                        base = c4 * 32
                        tp = (96, 0) if base == 96 else None
                        par = chunk % 2
                        ph.op("tensor", lambda e, u=u, tbc=tbc, c4=c4: e.matmul(
                            ps_y[:, u * WIN + c4 * 32:u * WIN + (c4 + 1) * 32],
                            S_b[u][:], qt[u][:, tbc:tbc + 32],
                            start=False, stop=(c4 == 3)),
                            waits=[(ss, self._g_sbd[u])])
                        omm = ph.op("tensor", lambda e, u=u, tb=tb, base=base, tp=tp: e.matmul(
                            ps_o[:, u * 512:u * 512 + 128],
                            kptok[u][base:base + 32, tb * 128:(tb + 1) * 128],
                            v_sb[u][base:base + 32, tb * 128:(tb + 1) * 128],
                            start=True, stop=True, tile_position=tp),
                            waits=[(sv, self._g_ow[u])], inc=(st, 1))
                        vSn = ph.op("vector", lambda e, u=u, chunk=chunk, par=par: e.scalar_tensor_tensor(
                            out=S_f[u][par][:], in0=S_f[u][1 - par][:],
                            scalar=eend[u][:, chunk:chunk + 1],
                            in1=ps_o[:, u * 512:u * 512 + 128],
                            op0=ALU.mult, op1=ALU.add),
                            waits=[(st, omm)], inc=(sv, 1))
                        self._g_ow[u] = vSn
                        self._g_sbd[u] = ph.op("scalar", lambda e, u=u, par=par: e.activation(
                            out=S_b[u][:], in_=S_f[u][par][:], func=AF.Copy),
                            waits=[(sv, vSn)], inc=(ss, 1))
                for u in (0, 1):
                    y_evac[u] = ph.op("vector", lambda e, u=u, tb=tb, h0=h0: e.tensor_copy(
                        ybuf[:, (h0 + u) * T + tb * 128:(h0 + u) * T + (tb + 1) * 128],
                        ps_y[:, u * WIN:u * WIN + 128]),
                        waits=[(st, st.v)], inc=(sv, 1))
            self._g_Swar = y_evac

        pd = [0, 0]
        for hp in range(NH // 2):
            h0 = 2 * hp
            pd[0] = emit_prep(h0, 0)
            if h0 + 2 < NH:
                for j, nm in enumerate(wnames):
                    wld[0] = ph.dma("gpsimd", wring[0][j][:],
                                    rr(self.inp[nm], "h p dc e -> h p (dc e)")[h0 + 2],
                                    waits=[(st, self.st.v)] if j == 0 else (), sem="gw0")
            pd[1] = emit_prep(h0 + 1, 1)
            if h0 + 3 < NH:
                for j, nm in enumerate(wnames):
                    wld[1] = ph.dma("gpsimd", wring[1][j][:],
                                    rr(self.inp[nm], "h p dc e -> h p (dc e)")[h0 + 3],
                                    waits=[(st, self.st.v)] if j == 0 else (), sem="gw1")
            emit_pair(h0, pd)
        ph.fence()
        ph.run()
        es.close()

    def emit_moe(self, biga, act_sb, x2T_dram):
        sv, ss, st, qs, qg = self.sv, self.ss, self.st, self.qs, self.qg
        es = ExitStack()
        ph = Ph(self)
        ps8 = self.ps(es, "m_p8", (128, 4096), F32)
        glr = [self.sb(es, f"m_glr{i}", [8, 128], F32) for i in range(2)]
        self._m_s0war = 0
        self._m_s8war = 0
        compr = self.sb(es, "m_cpr", [8, T], F32)
        crow = self.sb(es, "m_crw", [1, T], F32)
        dn_sb = self.sb(es, "m_dn", [128, NJC * 2048], BF16)
        gur = [[self.sb(es, f"m_gu{i}{r}", [128, 2048], BF16) for i in range(2)]
               for r in range(2)]
        c_bc = self.sb(es, "m_cbc", [128, T], F32)

        stage = [self.sb(es, f"m_st{i}", [128, WIN], F32) for i in range(4)]
        sA2 = stage[0:2]

        glt_all = self.sb(es, "m_glta", [128, 128], F32)
        prb_all = self.sb(es, "m_prba", [128, 128], F32)
        prb2_all = self.sb(es, "m_prb2a", [128, 128], F32)
        m8_all = self.sb(es, "m_m8a", [128, 128], F32)
        msk_all = glt_all
        cmb_all = self.sb(es, "m_cmba", [128, 128], F32)
        rca = self.sb(es, "m_rca", [128, 64], F32)

        s0 = []
        glwar = [0, 0]
        for tc in range(NTC):
            rg = tc % 2
            glld = ph.dma("sync", glr[rg][:], self.gl_dram[0:8, tc * 128:(tc + 1) * 128],
                          waits=[(self.st, glwar[rg])], sem=f"mgl{rg}")
            tmm = ph.op("tensor", lambda e, rg=rg, tc=tc: e.transpose(
                ps8[0:128, (tc % 4) * 512:(tc % 4) * 512 + 8], glr[rg][:],
                self.ident_f[0:8, 0:8]),
                waits=[glld, (sv, self._m_s0war)], inc=(st, 1))
            glwar[rg] = tmm
            s0.append(ph.op("vector", lambda e, tc=tc: e.tensor_copy(
                glt_all[:, tc * 8:(tc + 1) * 8],
                ps8[0:128, (tc % 4) * 512:(tc % 4) * 512 + 8]),
                waits=[(st, tmm)], inc=(sv, 1)))
            if tc % 4 == 3:
                self._m_s0war = s0[-1]
        s1 = []
        for tc in range(NTC):
            s1.append(ph.op("vector", lambda e, tc=tc: e.reduce_max(
                rca[:, tc * 4:tc * 4 + 1], glt_all[:, tc * 8:(tc + 1) * 8],
                axis=AX.X, negate=True), inc=(sv, 1)))
        s2 = []
        for tc in range(NTC):
            s2.append(ph.op("scalar", lambda e, tc=tc: e.activation(
                out=prb_all[:, tc * 8:(tc + 1) * 8], in_=glt_all[:, tc * 8:(tc + 1) * 8],
                func=AF.Exp, bias=rca[:, tc * 4:tc * 4 + 1],
                accum_out=rca[:, tc * 4 + 1:tc * 4 + 2]),
                waits=[(sv, s1[tc])] if tc == 0 else [(sv, s1[tc])], inc=(ss, 1)))
        ph.op("scalar", lambda e: e.activation(out=rca[:, 3:4], in_=rca[:, 0:1],
                                               func=AF.Copy), inc=(ss, 1))
        s3 = []
        for tc in range(NTC):
            s3.append(ph.op("vector", lambda e, tc=tc: e.reciprocal(
                rca[:, tc * 4 + 2:tc * 4 + 3], rca[:, tc * 4 + 1:tc * 4 + 2]),
                waits=[(ss, s2[tc] + 1)], inc=(sv, 1)))
        s4 = []
        for tc in range(NTC):
            s4.append(ph.op("vector", lambda e, tc=tc: e.tensor_scalar_mul(
                prb2_all[:, tc * 8:(tc + 1) * 8], prb_all[:, tc * 8:(tc + 1) * 8],
                rca[:, tc * 4 + 2:tc * 4 + 3]), inc=(sv, 1)))
        s5 = []
        for tc in range(NTC):
            s5.append(ph.op("vector", lambda e, tc=tc: e.max(
                out=m8_all[:, tc * 8:(tc + 1) * 8], in_=prb2_all[:, tc * 8:(tc + 1) * 8]),
                inc=(sv, 1)))
        for tc in range(NTC):
            ph.op("vector", lambda e, tc=tc: e.tensor_scalar(
                out=msk_all[:, tc * 8:(tc + 1) * 8], in0=prb2_all[:, tc * 8:(tc + 1) * 8],
                scalar1=m8_all[:, tc * 8 + 1:tc * 8 + 2], scalar2=None, op0=ALU.is_ge))
        s7 = []
        for tc in range(NTC):
            s7.append(ph.op("vector", lambda e, tc=tc: e.tensor_mul(
                cmb_all[:, tc * 8:(tc + 1) * 8], prb2_all[:, tc * 8:(tc + 1) * 8],
                msk_all[:, tc * 8:(tc + 1) * 8]), inc=(sv, 1)))
        v_cmb = 0
        for tc in range(NTC):
            tmm = ph.op("tensor", lambda e, tc=tc: e.transpose(
                ps8[0:8, (4 + tc % 4) * 512:(4 + tc % 4) * 512 + 128],
                cmb_all[:, tc * 8:(tc + 1) * 8], self.ident_f[:]),
                waits=[(sv, s7[min(tc + 2, NTC - 1)]), (sv, self._m_s8war)], inc=(st, 1))
            v_cmb = ph.op("vector", lambda e, tc=tc: e.tensor_copy(
                compr[0:8, tc * 128:(tc + 1) * 128],
                ps8[0:8, (4 + tc % 4) * 512:(4 + tc % 4) * 512 + 128]),
                waits=[(st, tmm)], inc=(sv, 1))
            if tc % 4 == 3:
                self._m_s8war = v_cmb
        if getattr(self, "dbg_moe", None):
            ph.dma("sync", self.dbg_moe["compr"], compr[:], waits=[(sv, v_cmb)])

        guw = {"g": ("gup", None), "a": ("gsa", "dsa"), "b": ("gsb", "dsb")}
        experts = [("g", e) for e in range(NE)] + [("a", 0), ("b", 0)]
        dn_ld = 0
        dn_war = []
        gu_ld = [[0, 0], [0, 0]]
        gu_war = [[], []]
        cbc_war = 0
        act_war = [0, 0]
        self._m_ev = [[], [], [], []]
        self._m_bank_war = [[], []]
        act_war2 = [[0, 0], [0, 0]]
        x2_base = [[0] * NW for _ in range(NDC)]

        def gu_ap(kind, e, j):
            if kind == "g":
                return rr(self.inp["gup"], "e f p dc q -> e f p (dc q)")[e][j]
            return rr(self.inp["gsa" if kind == "a" else "gsb"],
                      "f p dc q -> f p (dc q)")[j]

        def dn_ap(kind, e):
            if kind == "g":
                return rr(self.inp["dnp"], "e p f d -> e p (f d)")[e]
            return rr(self.inp["dsa" if kind == "a" else "dsb"], "p f d -> p (f d)")

        for ei, (kind, e) in enumerate(experts):
            routed = kind == "g"
            prev_ps_users = (self._m_ev[0] + self._m_ev[1] + self._m_ev[2] +
                             self._m_ev[3] + [(sv, v_cmb)])
            dn_ld = ph.dma("gpsimd", dn_sb[:], dn_ap(kind, e), waits=dn_war, sem="mdn")
            if routed:
                v_cb = 0
                cld = ph.dma("sync", crow[:], compr[e:e + 1, :],
                             waits=[(sv, v_cmb), (st, cbc_war)], sem="mcrw")
                for w in range(NW):
                    tmm = ph.op("tensor", lambda e_, w=w: e_.matmul(
                        ps8[:, 3 * 512:4 * 512], self.ones_row_f[:],
                        crow[0:1, w * WIN:(w + 1) * WIN], start=True, stop=True),
                        waits=[cld, (st, cbc_war), (sv, v_cb)] +
                              self._m_ev[0] + self._m_ev[1] + self._m_ev[2] +
                              self._m_ev[3], inc=(st, 1))
                    v_cb = ph.op("vector", lambda e_, w=w: e_.tensor_copy(
                        c_bc[:, w * WIN:(w + 1) * WIN], ps8[:, 3 * 512:4 * 512]),
                        waits=[(st, tmm)] + self._m_ev[0] + self._m_ev[1] +
                              self._m_ev[2] + self._m_ev[3], inc=(sv, 1))
                cbc_done = v_cb
            for j in range(NJC):
                r = j % 2
                gu_ld[r][0] = ph.dma("gpsimd", gur[r][0][:], gu_ap(kind, e, j),
                                     waits=gu_war[r], sem=f"mgu{r}")
                gu_ld[r][1] = ph.dma("gpsimd", gur[r][1][:], gu_ap(kind, e, j + NJC),
                                     waits=(), sem=f"mgu{r}")
                for w in range(NW):
                    gb = (w % 2) * 4
                    for half, buf in enumerate(gur[r]):
                        for dc in range(NDC):
                            ph.op("tensor", lambda e_, dc=dc, w=w, buf=buf, half=half, gb=gb: e_.matmul(
                                ps8[:, (gb + half) * WIN:(gb + half + 1) * WIN],
                                buf[:, dc * 128:(dc + 1) * 128],
                                biga[:, dc * T + w * WIN: dc * T + (w + 1) * WIN],
                                start=(dc == 0), stop=(dc == NDC - 1)),
                                waits=([gu_ld[r][half]] +
                                       ([(ss, act_war2[w % 2][0]), (sv, act_war2[w % 2][1])]
                                        if half == 0 else []) +
                                       (prev_ps_users if (j == 0 and w == 0) else []))
                                      if dc == 0 else (),
                                inc=(st, 1) if dc == NDC - 1 else None)
                    mmv = st.v
                    s_si = ph.op("scalar", lambda e_, gb=gb, sAb=(w % 2): e_.activation(
                        out=sA2[sAb][:], in_=ps8[:, gb * WIN:(gb + 1) * WIN], func=AF.Silu),
                        waits=[(st, mmv)] + self._m_ev[w % 2], inc=(ss, 1))
                    v_ac = ph.op("vector", lambda e_, j=j, w=w, gb=gb, sAb=(w % 2): e_.tensor_tensor(
                        out=act_sb[:, j * T + w * WIN: j * T + (w + 1) * WIN],
                        in0=sA2[sAb][:], in1=ps8[:, (gb + 1) * WIN:(gb + 2) * WIN],
                        op=ALU.mult),
                        waits=[(ss, s_si), (st, mmv)], inc=(sv, 1))
                    act_war2[w % 2] = [s_si, v_ac]
                gu_war[r] = [(st, st.v)]
            act_done = sv.v
            for gi, (dq, w) in enumerate([(dq, w) for dq in range(4) for w in range(NW)]):
                bs = (gi % 2) * 4
                for j in range(NJC):
                    for dcl in range(4):
                        dc = dq * 4 + dcl
                        ph.op("tensor", lambda e_, j=j, dcl=dcl, dc=dc, w=w, bs=bs: e_.matmul(
                            ps8[:, (bs + dcl) * WIN:(bs + dcl + 1) * WIN],
                            dn_sb[:, j * 2048 + dc * 128: j * 2048 + (dc + 1) * 128],
                            act_sb[:, j * T + w * WIN: j * T + (w + 1) * WIN],
                            start=(j == 0), stop=(j == NJC - 1)),
                            waits=([(sv, act_done), dn_ld] + self._m_bank_war[gi % 2] +
                                   (self._m_ev[0] + self._m_ev[1] + self._m_ev[2] +
                                    self._m_ev[3] if ei == 0 and gi == 0 else []))
                                  if (j == 0 and dcl == 0) else (),
                            inc=(st, 1) if (j == NJC - 1 and dcl == 3) else None)
                mmv = st.v
                evs = []
                for dcl in range(4):
                    dc = dq * 4 + dcl
                    gsl = dcl
                    if routed:
                        ev = ph.op("vector", lambda e_, dcl=dcl, w=w, gsl=gsl, bs=bs: e_.tensor_tensor(
                            out=stage[gsl][:], in0=ps8[:, (bs + dcl) * WIN:(bs + dcl + 1) * WIN],
                            in1=c_bc[:, w * WIN:(w + 1) * WIN], op=ALU.mult),
                            waits=[(st, mmv), (sv, cbc_done)] + self._m_ev[gsl],
                            inc=(sv, 1))
                    else:
                        ev = ph.op("vector", lambda e_, dcl=dcl, gsl=gsl, bs=bs: e_.tensor_copy(
                            stage[gsl][:], ps8[:, (bs + dcl) * WIN:(bs + dcl + 1) * WIN]),
                            waits=[(st, mmv)] + self._m_ev[gsl], inc=(sv, 1))
                    evs.append(ev)
                    dst = x2T_dram[dc * 128:(dc + 1) * 128, w * WIN:(w + 1) * WIN]
                    if ei == 0:
                        sw = ph.dma("gpsimd", dst, stage[gsl][:], waits=[(sv, ev)],
                                    sem=f"mst{gsl}")
                        x2_base[dc][w] = sw
                    else:
                        sw = ph.dma("gpsimd", dst, stage[gsl][:],
                                    waits=[(sv, ev), x2_base[dc][w]],
                                    accum_op=ALU.add, sem=f"mst{gsl}")
                    self._m_ev[gsl] = [sw]
                self._m_bank_war[gi % 2] = [(sv, evs[-1])]
            dn_war = [(st, st.v)]
            cbc_war = st.v
        ph.fence()
        ph.run()
        es.close()

    def emit_oproj1(self, xlnT_dram, wo_area, x1T_dram, x2T_dram):
        sv, ss, st, qs, qg = self.sv, self.ss, self.st, self.qs, self.qg
        es = ExitStack()
        ph = Ph(self)
        xl = [self.sb(es, f"o1_x{i}", [128, 16 * WIN], BF16) for i in range(2)]
        x1s = [self.sb(es, f"o1_r{i}", [128, WIN], F32) for i in range(2)]
        stage = [self.sb(es, f"o1_s{i}", [128, WIN], F32) for i in range(2)]
        ps_o = [self.ps(es, f"o1_p{i}") for i in range(2)]
        for dic in range(NDC):
            wlds = ph.dma("gpsimd", wo_area[:, dic * 2048:(dic + 1) * 2048],
                          self.inp["wo1"][dic], sem="o1wo")
        xl_war = [[], []]
        x1_war = [[], []]
        stage_war = [[], []]
        evac = [0, 0]
        for w in range(NW):
            rw = w % 2
            xld = ph.dma("sync", rr(xl[rw][:], "p (dic t) -> p dic t", dic=16),
                         rr(xlnT_dram, "(dic p) t -> p dic t", p=128)[:, :, w * WIN:(w + 1) * WIN],
                         waits=xl_war[rw], sem=f"o1x{rw}")
            for dcp in range(NDC):
                g = (w * NDC + dcp) % 2
                rx = dcp % 2
                x1d = ph.dma("sync", x1s[rx][:],
                             x1T_dram[dcp * 128:(dcp + 1) * 128, w * WIN:(w + 1) * WIN],
                             waits=x1_war[rx], sem=f"o1r{rx}")
                for dic in range(NDC):
                    ph.op("tensor", lambda e, dic=dic, dcp=dcp, w=w, g=g, rw=rw: e.matmul(
                        ps_o[g][:, 0:WIN],
                        wo_area[:, dic * 2048 + dcp * 128: dic * 2048 + (dcp + 1) * 128],
                        xl[rw][:, dic * WIN:(dic + 1) * WIN],
                        start=(dic == 0), stop=(dic == NDC - 1)),
                        waits=[wlds, xld, (sv, evac[g])] if dic == 0 else (),
                        inc=(st, 1) if dic == NDC - 1 else None)
                mmv = st.v
                ev = ph.op("vector", lambda e, g=g, rx=rx: e.tensor_tensor(
                    out=stage[g][:], in0=ps_o[g][:, 0:WIN], in1=x1s[rx][:], op=ALU.add),
                    waits=[(st, mmv), x1d] + stage_war[g], inc=(sv, 1))
                evac[g] = ev
                x1_war[rx] = [(sv, ev)]
                sw = ph.dma("gpsimd",
                            x2T_dram[dcp * 128:(dcp + 1) * 128, w * WIN:(w + 1) * WIN],
                            stage[g][:], waits=[(sv, ev)], accum_op=ALU.add,
                            sem=f"o1s{g}")
                stage_war[g] = [sw]
            xl_war[rw] = [(st, st.v)]
        ph.fence()
        ph.run()
        es.close()

    def emit_out(self, x2T_dram, out_tok):
        sv, st, qs = self.sv, self.st, self.qs
        es = ExitStack()
        ph = Ph(self)
        xin = [self.sb(es, f"fo_x{i}", [128, T], F32) for i in range(2)]
        stg = [self.sb(es, f"fo_s{i}", [128, 2048], F32) for i in range(2)]
        pt = [self.ps(es, f"fo_p{i}", (128, 1024), F32) for i in range(2)]
        evac_done = [0, 0]
        store_done = [(qs, 0), (qs, 0)]
        xin_war = [0, 0]
        for dc in range(NDC):
            r = dc % 2
            wld = ph.dma("sync", xin[r][:], x2T_dram[dc * 128:(dc + 1) * 128, :],
                         waits=[(st, xin_war[r])], sem=f"fo{r}")
            for half in range(2):
                for tc8 in range(8):
                    tc = half * 8 + tc8
                    w0 = []
                    if tc == 0:
                        w0.append(wld)
                    if tc8 == 0:
                        w0.append((sv, evac_done[r]))
                    ph.op("tensor", lambda e, r=r, tc=tc, tc8=tc8: e.transpose(
                        pt[r][:, tc8 * 128:(tc8 + 1) * 128],
                        xin[r][:, tc * 128:(tc + 1) * 128], self.ident_f[:]),
                        waits=w0, inc=(st, 1) if tc8 == 7 else None)
                tmm = st.v
                evac_done[r] = ph.op("vector", lambda e, r=r, half=half: e.tensor_copy(
                    stg[r][:, half * 1024:(half + 1) * 1024], pt[r][:]),
                    waits=[(st, tmm), store_done[r]], inc=(sv, 1))
            xin_war[r] = st.v
            store_done[r] = ph.dma(
                "sync",
                rr(out_tok, "(tc p) d -> p tc d", p=128)[:, :, dc * 128:(dc + 1) * 128],
                rr(stg[r][:], "p (tc t) -> p tc t", tc=16),
                waits=[(sv, evac_done[r])], sem=f"fos{r}")
        ph.fence()
        ph.run()
        es.close()



def build_full():
    b = Build()
    x_in = b.din("x", [T, D], F32)
    for nm, shp in (("wq0", [16, 128, 16, 128]), ("wk0", [16, 128, 16, 128]),
                    ("wv0", [16, 128, 16, 128]), ("wo0", [16, 128, 16, 128]),
                    ("w1p", [64, 128, 16, 128]), ("w2p", [64, 128, 2048]),
                    ("b1p", [64, 128]), ("b2p", [16, 128]), ("wrp", [16, 128]),
                    ("wq1", [16, 128, 16, 128]), ("wk1", [16, 128, 16, 128]),
                    ("wv1", [16, 128, 16, 128]), ("wg1", [16, 128, 16, 128]),
                    ("wo1", [16, 128, 2048]), ("gmp", [16, 128]), ("btp", [16, 128]),
                    ("gwp", [128, 16, 8]), ("gup", [8, 22, 128, 16, 128]),
                    ("gsa", [22, 128, 16, 128]), ("gsb", [22, 128, 16, 128]),
                    ("dnp", [8, 128, 11, 2048]), ("dsa", [128, 11, 2048]),
                    ("dsb", [128, 11, 2048])):
        b.din(nm, shp, F32)
    x0T = b.dtmp("x0T", [D, T], F32)
    x1T = b.dtmp("x1T", [D, T], F32)
    x2T = b.dtmp("x2T", [D, T], F32)
    h_d = b.dtmp("h_d", [64, 128, 2048], BF16)
    xlnT = b.dtmp("xlnT", [D, T], BF16)
    out = b.dout("out", [T, D], F32)
    biga = b.sb(b.estack, "BIGA", [128, NDC * T], BF16)

    b.emit_consts()
    b.bstack = ExitStack()
    b.emit_transpose_in("b0", x_in, x0T)
    b.emit_ln("b0", x0T, biga, router=True)
    zstack = ExitStack()
    z_sb = b.sb(zstack, "ZBUF", [128, NH * T], BF16)
    b.emit_attn0(biga, z_sb)
    b.emit_ffn0(biga, h_d, x1T)
    b.emit_oproj0(z_sb, biga, x0T, x1T, accum=True)
    zstack.close()
    b.bstack.close()
    b.bstack = ExitStack()
    b.emit_ln("b1", x1T, biga, gate=True)
    ystack = ExitStack()
    ybuf = b.sb(ystack, "YBUF", [128, NH * T], BF16)
    b.emit_gla(biga, ybuf)
    b.emit_ln("yg", None, None, src_bf16=ybuf, gb=True, out_dram=xlnT)
    ystack.close()
    astack = ExitStack()
    act_sb = b.sb(astack, "ACT", [128, NJC * T], BF16)
    b.emit_moe(biga, act_sb, x2T)
    astack.close()
    b.emit_oproj1(xlnT, biga, x1T, x2T)
    b.bstack.close()
    b.emit_out(x2T, out)
    b.estack.close()
    return b


def kernel(**inputs):
    pk = pack_inputs(inputs)
    x = np.ascontiguousarray(inputs["L0_attn_x"].astype(np.float32))
    b = build_full()
    in_maps = []
    for c in range(8):
        m = dict(pk)
        m["x"] = np.ascontiguousarray(x[c])
        in_maps.append(m)
    res = run_bass_kernel_spmd(b.nc, in_maps, core_ids=list(range(8)))
    return np.stack([np.asarray(res.results[c]["out"]) for c in range(8)], axis=0)


# version 06
# speedup vs baseline: 1.4647x; 1.0461x over previous
import numpy as np
import ml_dtypes
from contextlib import ExitStack
import concourse.bass as bass
import concourse.mybir as mybir
from concourse.bass_utils import run_bass_kernel_spmd

F32 = mybir.dt.float32
BF16 = mybir.dt.bfloat16
AF = mybir.ActivationFunctionType
ALU = mybir.AluOpType
AX = mybir.AxisListType

D = 2048
T = 2048
NDC = 16
NTC = 16
NW = 4
WIN = 512
NH = 16
DH = 128
DF = 8192
NFC = 64
KSEL = 1024
NE = 8
FMOE = 1408
NJC = 11
GLA_C = 32
INV_SQRT_DH = float(1.0 / np.sqrt(128.0))
EXP_BIAS = -8.0

ENGS = ("sync", "scalar", "vector", "tensor", "gpsimd")



def pack_inputs(i):
    f = np.ascontiguousarray
    p = {}
    p["wq0"] = f(i["L0_attn_W_Q"].reshape(16, 128, 16, 128).transpose(0, 3, 2, 1))
    p["wk0"] = f(i["L0_attn_W_K"].reshape(16, 128, 16, 128).transpose(0, 3, 2, 1))
    p["wv0"] = f(i["L0_attn_W_V"].reshape(16, 128, 16, 128).transpose(0, 3, 2, 1))
    p["wo0"] = f(i["L0_attn_W_O"].reshape(16, 16, 128, 128).transpose(0, 3, 1, 2))
    p["w1p"] = f(i["L0_ffn_w1"].reshape(16, 128, 64, 128).transpose(2, 1, 0, 3))
    p["w2p"] = f(i["L0_ffn_w2"].reshape(64, 128, 2048))
    p["b1p"] = f(i["L0_ffn_b1"].reshape(64, 128))
    p["b2p"] = f(i["L0_ffn_b2"].reshape(16, 128))
    p["wrp"] = f(i["L0_ffn_w_r"].reshape(16, 128))
    p["brp"] = f(np.asarray(i["L0_ffn_b_r"], np.float32).reshape(1, 1))
    for nm, k in (("wq1", "L1_attn_Wq"), ("wk1", "L1_attn_Wk"),
                  ("wv1", "L1_attn_Wv"), ("wg1", "L1_attn_Wg")):
        p[nm] = f(i[k].reshape(16, 128, 16, 128).transpose(0, 3, 2, 1))
    p["wo1"] = f(np.ascontiguousarray(i["L1_attn_Wo"].T).reshape(16, 128, 2048))
    p["gmp"] = f(i["L1_attn_gamma"].reshape(16, 128))
    p["btp"] = f(i["L1_attn_beta"].reshape(16, 128))
    p["gwp"] = f(np.ascontiguousarray(i["L1_ffn_gate_weight"].T)
                 .reshape(16, 128, 8).transpose(1, 0, 2))
    gu = i["L1_ffn_gate_up_weights"]
    p["gup"] = f(gu.reshape(8, 22, 128, 16, 128).transpose(0, 1, 4, 3, 2))
    sgu = i["L1_ffn_shared_gate_up"]
    sa = np.concatenate([sgu[0:1408], sgu[2816:4224]], axis=0)
    sb_ = np.concatenate([sgu[1408:2816], sgu[4224:5632]], axis=0)
    p["gsa"] = f(sa.reshape(22, 128, 16, 128).transpose(0, 3, 2, 1))
    p["gsb"] = f(sb_.reshape(22, 128, 16, 128).transpose(0, 3, 2, 1))
    dn = i["L1_ffn_down_weights"]
    p["dnp"] = f(dn.transpose(0, 2, 1).reshape(8, 11, 128, 2048).transpose(0, 2, 1, 3))
    sd = np.ascontiguousarray(i["L1_ffn_shared_down"].T)
    p["dsa"] = f(sd[0:1408].reshape(11, 128, 2048).transpose(1, 0, 2))
    p["dsb"] = f(sd[1408:2816].reshape(11, 128, 2048).transpose(1, 0, 2))
    return p



class SemC:
    def __init__(self, h):
        self.h = h
        self.v = 0

    def inc(self, k=1):
        self.v += k
        return self.v


class Ph:

    def __init__(self, bld):
        self.bld = bld
        self.nc = bld.nc
        self.q = {n: [] for n in ENGS}

    def add(self, en, fn):
        self.q[en].append(fn)

    def op(self, en, fn, waits=(), inc=None):
        tgt = inc[0].inc(inc[1]) if inc is not None else None

        def g(eng, fn=fn, waits=tuple(waits), inc=inc):
            for s, v in waits:
                if v > 0:
                    eng.wait_ge(s.h, v)
            ins = fn(eng)
            if inc is not None:
                ins.then_inc(inc[0].h, inc[1])
        self.add(en, g)
        return tgt

    def dma(self, en, out, in_, waits=(), sem=None, **kw):
        if isinstance(sem, str):
            sem = self.bld.dsem(sem)
        if sem is None:
            sem = {"sync": self.bld.qs, "scalar": self.bld.qc, "gpsimd": self.bld.qg}[en]
        tgt = self.op(en, lambda e: e.dma_start(out=out, in_=in_, **kw),
                      waits=waits, inc=(sem, 16))
        return (sem, tgt)

    def wdma(self, *a, **kw):
        return self.dma(*a, **kw)

    def fence(self):
        b = self.bld
        qtgts = [(b.qs, b.qs.v), (b.qc, b.qc.v), (b.qg, b.qg.v)]
        qtgts += [(s, s.v) for s in b.dsems.values()]
        tgt = b.bar.v + 5
        b.bar.v = tgt
        for en in ENGS:
            def g(eng, tgt=tgt, qtgts=tuple(qtgts)):
                eng.sem_inc(b.bar.h, 1)
                eng.wait_ge(b.bar.h, tgt)
                for s, v in qtgts:
                    if v > 0:
                        eng.wait_ge(s.h, v)
            self.add(en, g)

    def run(self):
        with self.nc.Block() as blk:
            for en in ENGS:
                fns = self.q[en]
                if fns:
                    getattr(blk, en)(lambda eng, fns=fns: [fn(eng) for fn in fns])


def rr(ap, spec, **kw):
    return ap.rearrange(spec, **kw)


class Build:
    def __init__(self):
        self.nc = bass.Bass("TRN2", target_bir_lowering=False, debug=False)
        self.inp = {}
        self.estack = ExitStack()
        nc = self.nc
        s = lambda n: SemC(self.estack.enter_context(nc.semaphore(n)))
        self.qs, self.qc, self.qg = s("qs"), s("qc"), s("qg")
        self.sv, self.ss, self.st, self.sg = s("sv"), s("ss"), s("st"), s("sg")
        self.bar = s("bar")
        self._mksem = s
        self.dsems = {}

    def dsem(self, name):
        if name not in self.dsems:
            self.dsems[name] = self._mksem("d_" + name)
        return self.dsems[name]

    def din(self, name, shape, dtype=F32):
        ap = self.nc.dram_tensor(name, list(shape), dtype, kind="ExternalInput").ap()
        self.inp[name] = ap
        return ap

    def dout(self, name, shape, dtype=F32):
        return self.nc.dram_tensor(name, list(shape), dtype, kind="ExternalOutput").ap()

    def dtmp(self, name, shape, dtype=F32):
        return self.nc.dram_tensor(name, list(shape), dtype, kind="Internal").ap()

    def sb(self, es, name, shape, dtype):
        return es.enter_context(self.nc.sbuf_tensor(name, list(shape), dtype))

    def ps(self, es, name, shape=(128, 512), dtype=F32):
        return es.enter_context(self.nc.psum_tensor(name, list(shape), dtype))

    def emit_consts(self):
        es = self.estack
        self.ident_f = self.sb(es, "ident_f", [128, 128], F32)
        self.ident_b = self.sb(es, "ident_b", [128, 128], BF16)
        self.ones_col_f = self.sb(es, "ones_col_f", [128, 1], F32)
        self.ones_col_b = self.sb(es, "ones_col_b", [128, 1], BF16)
        self.ones_row_f = self.sb(es, "ones_row_f", [1, 128], F32)
        self.bdtril = self.sb(es, "bdtril", [128, 128], BF16)
        self.scanmask = self.sb(es, "scanmask", [128, WIN], BF16)
        self.fc = self.sb(es, "fconsts", [128, 8], F32)
        ph = Ph(self)

        def mk(_):
            P = self._pool_ops = []
            for t, fill in ((self.ident_f, 1.0), (self.ident_b, 1.0)):
                P.append(lambda e, t=t: e.memset(t[:], 0.0))
                P.append(lambda e, t=t, fill=fill: e.affine_select(
                    out=t[:], in_=t[:], compare_op=ALU.not_equal, fill=fill,
                    base=0, pattern=[[-1, 128]], channel_multiplier=1))
            P.append(lambda e: e.memset(self.ones_col_f[:], 1.0))
            P.append(lambda e: e.memset(self.ones_col_b[:], 1.0))
            P.append(lambda e: e.memset(self.ones_row_f[:], 1.0))
            P.append(lambda e: e.memset(self.fc[:, 0:1], 1e-5))
            P.append(lambda e: e.memset(self.fc[:, 1:2], EXP_BIAS))
            P.append(lambda e: e.memset(self.fc[:, 2:3], 1.0))
            P.append(lambda e: e.memset(self.bdtril[:], 0.0))
            for b32 in range(4):
                sub = self.bdtril[:, b32 * 32:(b32 + 1) * 32]
                C = b32 * 32
                P.append(lambda e, sub=sub, C=C: e.affine_select(
                    out=sub, in_=sub, compare_op=ALU.is_ge, fill=1.0,
                    base=C - 1, pattern=[[0, 32]], channel_multiplier=-1))
                P.append(lambda e, sub=sub, C=C: e.affine_select(
                    out=sub, in_=sub, compare_op=ALU.is_ge, fill=0.0,
                    base=C, pattern=[[1, 32]], channel_multiplier=-1))
            P.append(lambda e: e.memset(self.scanmask[:], 1.0))
            P.append(lambda e: e.memset(self.scanmask[:, 0:WIN:GLA_C], 0.0))
        def mk_chained(e):
            self._pool_ops = []
            mk(None)
            for f in self._pool_ops:
                tgt = self.sg.inc(1)
                f(e).then_inc(self.sg.h, 1)
                e.wait_ge(self.sg.h, tgt)
        ph.op("gpsimd", mk_chained)
        ph.fence()
        ph.run()

    def emit_transpose_in(self, tag, x_tok, xT_dram):
        es = ExitStack()
        ph = Ph(self)
        st, sv, qs = self.st, self.sv, self.qs
        xin = [self.sb(es, f"ti_x{i}_{tag}", [128, D], F32) for i in range(2)]
        stg = [self.sb(es, f"ti_s{i}_{tag}", [128, D], F32) for i in range(2)]
        pt = [self.ps(es, f"ti_p{i}_{tag}", (128, 1024), F32) for i in range(2)]
        evac_done = [0, 0]
        store_done = [(qs, 0), (qs, 0)]
        xin_war = [0, 0]
        for tc in range(NTC):
            r = tc % 2
            wld = ph.dma("sync", xin[r][:], x_tok[tc * 128:(tc + 1) * 128, :],
                         waits=[(st, xin_war[r])], sem=f"ti{r}_{tag}")
            for half in range(2):
                for dc8 in range(8):
                    dc = half * 8 + dc8
                    w0 = []
                    if dc == 0:
                        w0.append(wld)
                    if dc8 == 0:
                        w0.append((sv, evac_done[r]))
                    ph.op("tensor", lambda e, r=r, dc=dc, dc8=dc8: e.transpose(
                        pt[r][:, dc8 * 128:(dc8 + 1) * 128],
                        xin[r][:, dc * 128:(dc + 1) * 128], self.ident_f[:]),
                        waits=w0,
                        inc=(st, 1) if dc8 == 7 else None)
                tmm = st.v
                evac_done[r] = ph.op("vector", lambda e, r=r, half=half: e.tensor_copy(
                    stg[r][:, half * 1024:(half + 1) * 1024], pt[r][:]),
                    waits=[(st, tmm), store_done[r]], inc=(sv, 1))
            xin_war[r] = st.v
            store_done[r] = ph.dma(
                "sync",
                rr(xT_dram, "(dc p) t -> p dc t", p=128)[:, :, tc * 128:(tc + 1) * 128],
                rr(stg[r][:], "p (dc t) -> p dc t", dc=16),
                waits=[(sv, evac_done[r])], sem=f"tis{r}_{tag}")
        ph.fence()
        ph.run()
        es.close()

    def emit_ln(self, tag, xT_dram, biga, router=False, gate=False, src_bf16=None,
                gb=False, out_dram=None):
        sv, ss, st, qs = self.sv, self.ss, self.st, self.qs
        if router:
            self.maskrow = self.sb(getattr(self, "bstack", None) or self.estack,
                                   "maskrow", [1, T], F32)
            self.rowtmp = self.dtmp("rowtmp", [16, 128], F32)
        if gate:
            self.gl_dram = self.dtmp("gl_dram", [8, T], F32)
        es = ExitStack()
        ph = Ph(self)
        rows = self.sb(es, f"ln_rows_{tag}", [1, 4 * T], F32)
        r_sumx = rows[0:1, 0:T]
        r_sumsq = rows[0:1, T:2 * T]
        r_mu = rows[0:1, 2 * T:3 * T]
        r_rs = rows[0:1, 3 * T:4 * T]
        xt = ([self.sb(es, f"ln_x{i}_{tag}", [128, T], F32) for i in range(2)]
              if src_bf16 is None else None)
        sq = self.sb(es, f"ln_sq_{tag}", [128, T], F32)
        mu_bc = self.sb(es, f"ln_mu_{tag}", [128, T], F32)
        rs_bc = self.sb(es, f"ln_rs_{tag}", [128, T], F32)
        t1 = self.sb(es, f"ln_t1_{tag}", [128, T], F32)
        t2 = sq
        if out_dram is not None:
            ostg = [self.sb(es, f"ln_os0_{tag}", [128, T], BF16)]
        if gb:
            gm_sb = self.sb(es, f"ln_gm_{tag}", [128, 16], F32)
            bt_sb = self.sb(es, f"ln_bt_{tag}", [128, 16], F32)
            g16 = self.sb(es, f"ln_g16_{tag}", [16, 128], F32)
        ps4 = self.ps(es, f"ln_ps_{tag}", (128, 2048), F32)

        xt_war = [[], []]
        gb_done = 0
        if gb:
            for nm, dst in (("gmp", gm_sb), ("btp", bt_sb)):
                wld = ph.dma("sync", g16[:], self.inp[nm], sem=f"lng16_{tag}",
                             waits=[(st, self.st.v)])
                tmm = ph.op("tensor", lambda e, dst=dst: e.transpose(
                    ps4[0:128, 0:16], g16[:], self.ident_f[0:16, 0:16]),
                    waits=[wld, (sv, gb_done)], inc=(st, 1))
                gb_done = ph.op("vector", lambda e, dst=dst: e.tensor_copy(
                    dst[:], ps4[0:128, 0:16]), waits=[(st, tmm)], inc=(sv, 1))

        def src_tile(dc):
            if src_bf16 is not None:
                return src_bf16[:, dc * T:(dc + 1) * T], []
            b = xt[dc % 2]
            wld = ph.dma("sync", b[:], xT_dram[dc * 128:(dc + 1) * 128, :],
                         waits=list(xt_war[dc % 2]), sem=f"lnx{dc % 2}_{tag}")
            return b[:], [wld]

        ones = self.ones_col_b if src_bf16 is not None else self.ones_col_f

        for dc in range(NDC):
            bap, w_ld = src_tile(dc)
            for w in range(NW):
                ph.op("tensor", lambda e, bap=bap, w=w, dc=dc: e.matmul(
                    ps4[0:1, w * WIN:(w + 1) * WIN], ones[:],
                    bap[:, w * WIN:(w + 1) * WIN], start=(dc == 0), stop=(dc == NDC - 1)),
                    waits=(w_ld + ([(sv, gb_done)] if dc == 0 else [])) if w == 0 else (),
                    inc=(st, 1))
            if src_bf16 is None:
                xt_war[dc % 2] = [(st, st.v)]
        v_sumx = ph.op("vector", lambda e: e.tensor_copy(r_sumx, ps4[0:1, 0:T]),
                       waits=[(st, st.v)], inc=(sv, 1))
        sq_war = 0
        for dc in range(NDC):
            bap, w_ld = src_tile(dc)
            sqd = ph.op("scalar", lambda e, bap=bap: e.activation(
                out=sq[:], in_=bap, func=AF.Square),
                waits=w_ld + [(st, sq_war)], inc=(ss, 1))
            for w in range(NW):
                ph.op("tensor", lambda e, w=w, dc=dc: e.matmul(
                    ps4[0:1, w * WIN:(w + 1) * WIN], self.ones_col_f[:],
                    sq[:, w * WIN:(w + 1) * WIN], start=(dc == 0), stop=(dc == NDC - 1)),
                    waits=[(ss, sqd), (sv, v_sumx)] if w == 0 else (),
                    inc=(st, 1))
            sq_war = st.v
            if src_bf16 is None:
                xt_war[dc % 2] = [(ss, sqd)]
        v_sums = ph.op("vector", lambda e: e.tensor_copy(r_sumsq, ps4[0:1, 0:T]),
                       waits=[(st, st.v)], inc=(sv, 1))
        ph.op("scalar", lambda e: e.activation(out=r_mu, in_=r_sumx, func=AF.Copy,
                                               scale=1.0 / D), waits=[(sv, v_sums)])
        ph.op("scalar", lambda e: e.activation(out=r_rs, in_=r_sumsq, func=AF.Copy,
                                               scale=1.0 / D))
        ph.op("scalar", lambda e: e.activation(out=r_sumx, in_=r_mu, func=AF.Square),
              inc=(ss, 1))
        s_mu2 = ss.v
        v_var = ph.op("vector", lambda e: e.tensor_sub(r_sumsq, r_rs, r_sumx),
                      waits=[(ss, s_mu2)], inc=(sv, 1))
        ph.op("scalar", lambda e: e.activation(out=r_sumx, in_=r_sumsq, func=AF.Ln,
                                               bias=self.fc[0:1, 0:1]),
              waits=[(sv, v_var)])
        s_rows = ph.op("scalar", lambda e: e.activation(out=r_rs, in_=r_sumx, func=AF.Exp,
                                                        scale=-0.5), inc=(ss, 1))
        v_bc = v_sums
        for (row, dstb) in ((r_mu, mu_bc), (r_rs, rs_bc)):
            for w in range(NW):
                tmm = ph.op("tensor", lambda e, row=row, w=w: e.matmul(
                    ps4[:, 0:WIN], self.ones_row_f[:], row[0:1, w * WIN:(w + 1) * WIN],
                    start=True, stop=True),
                    waits=[(ss, s_rows), (sv, v_bc)], inc=(st, 1))
                v_bc = ph.op("vector", lambda e, dstb=dstb, w=w: e.tensor_copy(
                    dstb[:, w * WIN:(w + 1) * WIN], ps4[:, 0:WIN]),
                    waits=[(st, tmm)], inc=(sv, 1))

        if router:
            wr_sb = self.sb(es, "wr_sb", [128, 16], F32)
            br_sb = self.sb(es, "br_sb", [1, 1], F32)
            m16 = self.sb(es, "m16", [16, 128], F32)
            brld = ph.dma("sync", br_sb[:], self.inp["brp"], sem=f"lnr0_{tag}")
            wld = ph.dma("sync", m16[:], self.inp["wrp"], sem=f"lnr1_{tag}")
            tmm = ph.op("tensor", lambda e: e.transpose(
                ps4[0:128, 0:16], m16[:], self.ident_f[0:16, 0:16]),
                waits=[wld, (sv, v_bc)], inc=(st, 1))
            v_wr = ph.op("vector", lambda e: e.tensor_copy(wr_sb[:], ps4[0:128, 0:16]),
                         waits=[(st, tmm)], inc=(sv, 1))
        if gate:
            glrow_sb = self.sb(es, "glrow_sb", [8, T], F32)
            gw_sb = self.sb(es, "gw_sb", [128, 16 * 8], F32)
            gwld = ph.dma("sync", gw_sb[:], rr(self.inp["gwp"], "p dc e -> p (dc e)"),
                          sem=f"lnr2_{tag}")

        self._ln_ost_war = [[], []]
        rtr_war = self.st.v if (router or gate) else 0
        for dc in range(NDC):
            bap, w_ld = src_tile(dc)
            ph.op("vector", lambda e, bap=bap: e.tensor_sub(t1[:], bap, mu_bc[:]),
                  waits=w_ld + [(sv, v_bc), (st, rtr_war)], inc=(sv, 1))
            if src_bf16 is None:
                xt_war[dc % 2] = [(sv, sv.v)]
            nrm = ph.op("vector", lambda e: e.tensor_mul(t2[:], t1[:], rs_bc[:]),
                        inc=(sv, 1))
            if out_dram is not None:
                ro = 0
                vo = ph.op("vector", lambda e, dc=dc, ro=ro: e.tensor_scalar(
                    out=ostg[ro][:], in0=t2[:],
                    scalar1=gm_sb[:, dc:dc + 1], scalar2=bt_sb[:, dc:dc + 1],
                    op0=ALU.mult, op1=ALU.add) if gb else e.tensor_copy(ostg[ro][:], t2[:]),
                    waits=self._ln_ost_war[ro], inc=(sv, 1))
                osw = ph.dma("sync", out_dram[dc * 128:(dc + 1) * 128, :], ostg[ro][:],
                             waits=[(sv, vo)], sem=f"lnos_{tag}")
                self._ln_ost_war[ro] = [osw]
            elif gb:
                ph.op("vector", lambda e, dc=dc: e.tensor_scalar(
                    out=biga[:, dc * T:(dc + 1) * T], in0=t2[:],
                    scalar1=gm_sb[:, dc:dc + 1], scalar2=bt_sb[:, dc:dc + 1],
                    op0=ALU.mult, op1=ALU.add), inc=(sv, 1))
            else:
                ph.op("vector", lambda e, dc=dc: e.tensor_copy(
                    biga[:, dc * T:(dc + 1) * T], t2[:]), inc=(sv, 1))
            if router:
                for w in range(NW):
                    ph.op("tensor", lambda e, w=w, dc=dc: e.matmul(
                        ps4[0:1, w * WIN:(w + 1) * WIN], wr_sb[:, dc:dc + 1],
                        t2[:, w * WIN:(w + 1) * WIN],
                        start=(dc == 0), stop=(dc == NDC - 1)),
                        waits=[(sv, nrm), (sv, v_wr)] if w == 0 else (),
                        inc=(st, 1) if w == NW - 1 else None)
                rtr_war = st.v
            if gate:
                for w in range(NW):
                    ph.op("tensor", lambda e, w=w, dc=dc: e.matmul(
                        ps4[0:8, w * WIN:(w + 1) * WIN], gw_sb[:, dc * 8:(dc + 1) * 8],
                        t2[:, w * WIN:(w + 1) * WIN],
                        start=(dc == 0), stop=(dc == NDC - 1)),
                        waits=([(sv, nrm)] + ([gwld] if dc == 0 else [])) if w == 0 else (),
                        inc=(st, 1) if w == NW - 1 else None)
                rtr_war = st.v

        if gate:
            vgl = ph.op("vector", lambda e: e.tensor_copy(glrow_sb[:], ps4[0:8, 0:T]),
                        waits=[(st, st.v)], inc=(sv, 1))
            ph.dma("sync", self.gl_dram, glrow_sb[:], waits=[(sv, vgl)])

        if getattr(self, "dbg_ln", None):
            d_rows = self.dbg_ln["rows"]
            d_mu = self.dbg_ln["mu"]
            d_rs = self.dbg_ln["rs"]
            ph.dma("sync", d_rows, rows[:], waits=[(self.ss, self.ss.v), (self.sv, self.sv.v)])
            ph.dma("sync", d_mu, mu_bc[:], waits=[(self.sv, v_bc)])
            ph.dma("sync", d_rs, rs_bc[:], waits=[(self.sv, v_bc)])

        if router:
            sc_bc = mu_bc
            sc_col = self.sb(es, "sc_col", [128, 16], F32)
            rank_col = self.sb(es, "rank_col", [128, 16], F32)
            mask_col = self.sb(es, "mask_col", [128, 16], F32)
            scr_row = r_sumx
            v_scrow = 0
            for w in range(NW):
                v_scrow = ph.op("vector", lambda e, w=w: e.tensor_scalar_add(
                    scr_row[0:1, w * WIN:(w + 1) * WIN], ps4[0:1, w * WIN:(w + 1) * WIN],
                    br_sb[0:1, 0:1]),
                    waits=([(st, st.v), brld] if w == 0 else ()), inc=(sv, 1))
            v_bc2 = 0
            for w in range(NW):
                tmm = ph.op("tensor", lambda e, w=w: e.matmul(
                    ps4[:, 0:WIN], self.ones_row_f[:], scr_row[0:1, w * WIN:(w + 1) * WIN],
                    start=True, stop=True), waits=[(sv, v_scrow), (sv, v_bc2)],
                    inc=(st, 1))
                v_bc2 = ph.op("vector", lambda e, w=w: e.tensor_copy(
                    sc_bc[:, w * WIN:(w + 1) * WIN], ps4[:, 0:WIN]),
                    waits=[(st, tmm)], inc=(sv, 1))
            wst = ph.dma("sync", rr(self.rowtmp, "a t -> (a t)").rearrange("(a t) -> a t", a=1),
                         scr_row, waits=[(sv, v_scrow)], sem=f"lnr3_{tag}")
            wld = ph.dma("sync", m16[:], self.rowtmp, waits=[wst], sem=f"lnr4_{tag}")
            tmm = ph.op("tensor", lambda e: e.transpose(
                ps4[0:128, 0:16], m16[:], self.ident_f[0:16, 0:16]),
                waits=[wld, (sv, v_bc2)], inc=(st, 1))
            v_col = ph.op("vector", lambda e: e.tensor_copy(sc_col[:], ps4[0:128, 0:16]),
                          waits=[(st, tmm)], inc=(sv, 1))
            for tc in range(NTC):
                ph.op("vector", lambda e, tc=tc: e.tensor_scalar(
                    out=t1[:], in0=sc_bc[:], scalar1=sc_col[:, tc:tc + 1], scalar2=None,
                    op0=ALU.is_gt, op1=ALU.add, accum_out=rank_col[:, tc:tc + 1]),
                    waits=[(sv, v_col)] if tc == 0 else ())
            ph.op("vector", lambda e: e.tensor_copy(t1[:, 0:128], sc_bc[:, 0:128]))
            v_rank = ph.op("vector", lambda e: e.tensor_scalar(
                out=mask_col[:], in0=rank_col[:], scalar1=float(KSEL), scalar2=None,
                op0=ALU.is_lt), inc=(sv, 1))
            tmm = ph.op("tensor", lambda e: e.transpose(
                ps4[0:16, 0:128], mask_col[:], self.ident_f[:]),
                waits=[(sv, v_rank)], inc=(st, 1))
            v_m16 = ph.op("vector", lambda e: e.tensor_copy(m16[:], ps4[0:16, 0:128]),
                          waits=[(st, tmm)], inc=(sv, 1))
            wst = ph.dma("sync", self.rowtmp, m16[:], waits=[(sv, v_m16)], sem=f"lnr5_{tag}")
            ph.dma("sync", self.maskrow[:],
                   rr(self.rowtmp, "a t -> (a t)").rearrange("(a t) -> a t", a=1),
                   waits=[wst], sem=f"lnr6_{tag}")
            if getattr(self, "dbg_ln", None):
                ph.dma("sync", self.dbg_ln["sc_col"], sc_col[:], waits=[(sv, v_col)])
                ph.dma("sync", self.dbg_ln["rank"], rank_col[:], waits=[(sv, v_rank)])
                ph.dma("sync", self.dbg_ln["sc_bc"], sc_bc[:], waits=[(sv, v_bc2)])
                ph.dma("sync", self.dbg_ln["mask_col"], mask_col[:], waits=[(sv, v_rank)])
                ph.dma("sync", self.dbg_ln["m16"], m16[:], waits=[(sv, v_m16)])
        ph.fence()
        ph.run()
        es.close()

    def emit_attn0(self, biga, z_sb):
        sv, ss, st, qg = self.sv, self.ss, self.st, self.qg
        es = ExitStack()
        ph = Ph(self)
        q_sb = [self.sb(es, f"a0_q{r}", [128, T], BF16) for r in range(2)]
        k_sb = [self.sb(es, f"a0_k{r}", [128, T], BF16) for r in range(2)]
        v_sb = [self.sb(es, f"a0_v{r}", [128, T], BF16) for r in range(2)]
        self.cmask = self.sb(es, "a0_cmask", [128, 4 * WIN], BF16)
        pt_sb = self.sb(es, "a0_pt", [128, 8 * WIN], BF16)
        linv_bc = self.sb(es, "a0_li", [128, WIN], F32)
        lrow = self.sb(es, "a0_lr", [1, WIN], F32)
        wring = [[self.sb(es, f"a0_w{j}{r}", [128, 2048], BF16) for j in range(3)]
                 for r in range(2)]
        ps_qk = self.ps(es, "a0_pq", (128, 1024), F32)
        ps_s = self.ps(es, "a0_ps", (128, 2048), F32)
        ps_z = self.ps(es, "a0_pz")
        ps_lc = self.ps(es, "a0_pl")

        def _mkcmask(e):
            tgt = self.sg.inc(1)
            e.memset(self.cmask[:], 0.0).then_inc(self.sg.h, 1)
            e.wait_ge(self.sg.h, tgt)
            for dlt in range(4):
                sub = self.cmask[:, dlt * WIN:(dlt + 1) * WIN]
                tgt = self.sg.inc(1)
                e.affine_select(out=sub, in_=sub, compare_op=ALU.is_ge, fill=-1.0e5,
                                base=-128 * dlt, pattern=[[1, WIN]],
                                channel_multiplier=-1).then_inc(self.sg.h, 1)
                e.wait_ge(self.sg.h, tgt)
        ph.op("gpsimd", _mkcmask)
        cmask_rdy = self.sg.v

        wnames = ("wq0", "wk0", "wv0")
        wld = [(qg, 0), (qg, 0)]
        wfree = [[], []]
        for j, nm in enumerate(wnames):
            wld[0] = ph.dma("gpsimd", wring[0][j][:],
                            rr(self.inp[nm], "h p dc e -> h p (dc e)")[0], sem="a0w0")

        self._a0_qk_war = [[], []]

        def prep_gen(h):
            r = h % 2
            wq, wk, wv = wring[r]
            g = 0
            for (wt, dst, eng) in ((wq, q_sb[r], "scalar"), (wk, k_sb[r], "vector")):
                for w in range(NW):
                    bank = g % 2
                    for dc in range(NDC):
                        ph.op("tensor", lambda e, wt=wt, dc=dc, w=w, bank=bank: e.matmul(
                            ps_qk[:, bank * WIN:(bank + 1) * WIN],
                            wt[:, dc * 128:(dc + 1) * 128],
                            biga[:, dc * T + w * WIN: dc * T + (w + 1) * WIN],
                            start=(dc == 0), stop=(dc == NDC - 1)),
                            waits=([wld[r]] + self._a0_qk_war[bank]) if dc == 0 else (),
                            inc=(st, 1) if dc == NDC - 1 else None)
                    mmv = st.v
                    if eng == "scalar":
                        tgt = ph.op("scalar", lambda e, dst=dst, w=w, bank=bank: e.activation(
                            out=dst[:, w * WIN:(w + 1) * WIN],
                            in_=ps_qk[:, bank * WIN:(bank + 1) * WIN],
                            func=AF.Copy), waits=[(st, mmv)], inc=(ss, 1))
                        self._a0_qk_war[bank] = [(ss, tgt)]
                    else:
                        tgt = ph.op("vector", lambda e, dst=dst, w=w, bank=bank: e.tensor_copy(
                            dst[:, w * WIN:(w + 1) * WIN],
                            ps_qk[:, bank * WIN:(bank + 1) * WIN]),
                            waits=[(st, mmv)], inc=(sv, 1))
                        self._a0_qk_war[bank] = [(sv, tgt)]
                    g += 1
                    yield ("qk", tgt)
            for tcg in range(4):
                bank = g % 2
                for tc4 in range(4):
                    tc = tcg * 4 + tc4
                    for dc in range(NDC):
                        ph.op("tensor", lambda e, tc=tc, tc4=tc4, dc=dc, wv=wv, bank=bank: e.matmul(
                            ps_qk[:, bank * WIN + tc4 * 128: bank * WIN + (tc4 + 1) * 128],
                            biga[:, dc * T + tc * 128: dc * T + (tc + 1) * 128],
                            wv[:, dc * 128:(dc + 1) * 128],
                            start=(dc == 0), stop=(dc == NDC - 1)),
                            waits=self._a0_qk_war[bank] if (dc == 0 and tc4 == 0) else (),
                            inc=(st, 1) if (dc == NDC - 1 and tc4 == 3) else None)
                mmv = st.v
                tgt = ph.op("vector", lambda e, tcg=tcg, r=r, bank=bank: e.tensor_copy(
                    v_sb[r][:, tcg * 512:(tcg + 1) * 512],
                    ps_qk[:, bank * WIN:(bank + 1) * WIN]),
                    waits=[(st, mmv)], inc=(sv, 1))
                self._a0_qk_war[bank] = [(sv, tgt)]
                g += 1
                yield ("v", tgt)

        gen = prep_gen(0)
        last = None
        for last in gen:
            pass
        prep_done = {0: last[1]}

        z_war = 0
        l_war = 0
        exp_done = 0
        ptwar = [0, 0]
        for h in range(NH):
            r = h % 2
            if h + 1 < NH:
                for j, nm in enumerate(wnames):
                    wld[1 - r] = ph.dma("gpsimd", wring[1 - r][j][:],
                                        rr(self.inp[nm], "h p dc e -> h p (dc e)")[h + 1],
                                        waits=wfree[1 - r] if j == 0 else (),
                                        sem=f"a0w{1 - r}")
                nxt = prep_gen(h + 1)
            else:
                nxt = None
            qv = prep_done[h]
            for w in range(NW):
                ngrp = w + 1
                for grp in range(ngrp):
                    diag = grp == ngrp - 1
                    n4 = 4
                    for i4 in range(n4):
                        kc = grp * 4 + i4
                        ph.op("tensor", lambda e, kc=kc, w=w, i4=i4, r=r, diag=diag: e.matmul(
                            ps_s[:, i4 * WIN:(i4 + 1) * WIN],
                            k_sb[r][:, kc * 128:(kc + 1) * 128],
                            q_sb[r][:, w * WIN:(w + 1) * WIN], start=True, stop=not diag),
                            waits=([(ss, exp_done), (sv, qv),
                                    (self.sg, cmask_rdy)] if i4 == 0 else ()),
                            inc=(st, 1) if (i4 == n4 - 1 and not diag) else None)
                        if diag:
                            ph.op("tensor", lambda e, i4=i4: e.matmul(
                                ps_s[:, i4 * WIN:(i4 + 1) * WIN], self.ident_b[:],
                                self.cmask[:, i4 * WIN:(i4 + 1) * WIN],
                                start=False, stop=True),
                                inc=(st, 1) if i4 == n4 - 1 else None)
                    smm = st.v
                    gslot = grp % 2
                    exp_done = ph.op("scalar", lambda e, gslot=gslot: e.activation(
                        out=pt_sb[:, gslot * 4 * WIN:(gslot + 1) * 4 * WIN],
                        in_=ps_s[:, 0:4 * WIN], func=AF.Exp,
                        bias=self.fc[:, 1:2], scale=INV_SQRT_DH),
                        waits=[(st, smm), (st, ptwar[gslot])], inc=(ss, 1))
                    if nxt is not None:
                        n = next(nxt, None)
                        if n is None:
                            nxt = None
                        else:
                            prep_done[h + 1] = n[1]
                    nkc = 4 * w + 4
                    for i4 in range(n4):
                        kc = grp * 4 + i4
                        pcol = (gslot * 4 + i4) * WIN
                        ph.op("tensor", lambda e, kc=kc, pcol=pcol: e.matmul(
                            ps_lc[0:1, 0:WIN], self.ones_col_b[:],
                            pt_sb[:, pcol:pcol + WIN],
                            start=(kc == 0), stop=(kc == nkc - 1)),
                            waits=([(ss, exp_done)] + ([(sv, l_war)] if kc == 0 else []))
                                  if i4 == 0 else ())
                        ph.op("tensor", lambda e, kc=kc, r=r, pcol=pcol: e.matmul(
                            ps_z[:, 0:WIN], v_sb[r][:, kc * 128:(kc + 1) * 128],
                            pt_sb[:, pcol:pcol + WIN],
                            start=(kc == 0), stop=(kc == nkc - 1)),
                            waits=[(sv, z_war)] if kc == 0 else (),
                            inc=(st, 1) if kc == nkc - 1 else None)
                    ptwar[gslot] = st.v
                pvmm = st.v
                vr = ph.op("vector", lambda e: e.reciprocal(lrow[:], ps_lc[0:1, 0:WIN]),
                           waits=[(st, pvmm)], inc=(sv, 1))
                l_war = vr
                bmm = ph.op("tensor", lambda e: e.matmul(
                    ps_lc[:, 0:WIN], self.ones_row_f[:], lrow[:], start=True, stop=True),
                    waits=[(sv, vr)], inc=(st, 1))
                bev = ph.op("vector", lambda e: e.tensor_copy(linv_bc[:], ps_lc[:, 0:WIN]),
                            waits=[(st, bmm)], inc=(sv, 1))
                z_war = ph.op("vector", lambda e, h=h, w=w: e.tensor_tensor(
                    out=z_sb[:, h * T + w * WIN: h * T + (w + 1) * WIN],
                    in0=ps_z[:, 0:WIN], in1=linv_bc[:], op=ALU.mult),
                    waits=[(sv, bev)], inc=(sv, 1))
            if nxt is not None:
                for n in nxt:
                    prep_done[h + 1] = n[1]
                nxt = None
            wfree[r] = [(st, st.v)]
        ph.fence()
        ph.run()
        es.close()

    def emit_oproj0(self, z_sb, wo_area, x0T_dram, x1T_dram, accum):
        sv, ss, st, qs, qg = self.sv, self.ss, self.st, self.qs, self.qg
        es = ExitStack()
        ph = Ph(self)
        x0s = [self.sb(es, f"o0_x{i}", [128, T], F32) for i in range(2)]
        stage = [self.sb(es, f"o0_s{i}", [128, WIN], F32) for i in range(2)]
        ps_o = [self.ps(es, f"o0_p{i}") for i in range(2)]
        for h in range(NH):
            wo_ld = ph.dma("gpsimd", wo_area[:, h * 2048:(h + 1) * 2048],
                           rr(self.inp["wo0"], "h p dc e -> h p (dc e)")[h], sem="o0wo")
        x0_war = [[], []]
        stage_war = [[], []]
        evac = [0, 0]
        for dc in range(NDC):
            r = dc % 2
            xw = ph.dma("sync", x0s[r][:], x0T_dram[dc * 128:(dc + 1) * 128, :],
                        waits=x0_war[r], sem=f"o0x{r}")
            for w in range(NW):
                g = (dc * NW + w) % 2
                for h in range(NH):
                    ph.op("tensor", lambda e, h=h, dc=dc, w=w, g=g: e.matmul(
                        ps_o[g][:, 0:WIN],
                        wo_area[:, h * 2048 + dc * 128: h * 2048 + (dc + 1) * 128],
                        z_sb[:, h * T + w * WIN: h * T + (w + 1) * WIN],
                        start=(h == 0), stop=(h == NH - 1)),
                        waits=[wo_ld, (sv, evac[g])] if h == 0 else (),
                        inc=(st, 1) if h == NH - 1 else None)
                mmv = st.v
                ev = ph.op("vector", lambda e, r=r, w=w, g=g: e.tensor_tensor(
                    out=stage[g][:], in0=ps_o[g][:, 0:WIN],
                    in1=x0s[r][:, w * WIN:(w + 1) * WIN], op=ALU.add),
                    waits=[(st, mmv), xw] + stage_war[g],
                    inc=(sv, 1))
                evac[g] = ev
                dst = x1T_dram[dc * 128:(dc + 1) * 128, w * WIN:(w + 1) * WIN]
                sw = ph.dma("gpsimd" if accum else "sync", dst, stage[g][:],
                            waits=[(sv, ev)], sem=f"o0s{g}",
                            **({"accum_op": ALU.add} if accum else {}))
                stage_war[g] = [sw]
            x0_war[r] = [(sv, evac[0]), (sv, evac[1])]
        ph.fence()
        ph.run()
        es.close()

    def emit_ffn0(self, biga, h_dram, x1T_dram):
        sv, ss, st, qs, qg = self.sv, self.ss, self.st, self.qs, self.qg
        es = ExitStack()
        ph = Ph(self)
        w1t = [self.sb(es, f"f0_w1{i}", [128, 2048], BF16) for i in range(2)]
        hstg = [self.sb(es, f"f0_h{i}", [128, 2048], BF16) for i in range(2)]
        b1c = self.sb(es, "f0_b1c", [128, 64], F32)
        b2c = self.sb(es, "f0_b2c", [128, 16], F32)
        m16 = self.sb(es, "f0_m16", [64, 128], F32)
        mask_bc = self.sb(es, "f0_mbc", [128, T], F32)
        es_a = ExitStack()
        ps_h = self.ps(es_a, "f0_ph", (128, 1024), F32)
        ps_c = self.ps(es_a, "f0_pc")

        wld = ph.dma("sync", m16[:], self.inp["b1p"], sem="f0b")
        tmm = ph.op("tensor", lambda e: e.transpose(
            ps_c[0:128, 0:64], m16[:], self.ident_f[0:64, 0:64]),
            waits=[wld], inc=(st, 1))
        v1 = ph.op("vector", lambda e: e.tensor_copy(b1c[:], ps_c[0:128, 0:64]),
                   waits=[(st, tmm)], inc=(sv, 1))
        wld = ph.dma("sync", m16[0:16, :], self.inp["b2p"], sem="f0b",
                     waits=[(st, st.v)])
        tmm = ph.op("tensor", lambda e: e.transpose(
            ps_c[0:128, 0:16], m16[0:16, :], self.ident_f[0:16, 0:16]),
            waits=[wld, (sv, v1)], inc=(st, 1))
        v2 = ph.op("vector", lambda e: e.tensor_copy(b2c[:], ps_c[0:128, 0:16]),
                   waits=[(st, tmm)], inc=(sv, 1))
        v_mb = v2
        for w in range(NW):
            tmm = ph.op("tensor", lambda e, w=w: e.matmul(
                ps_c[:, 0:WIN], self.ones_row_f[:],
                self.maskrow[0:1, w * WIN:(w + 1) * WIN], start=True, stop=True),
                waits=[(sv, v_mb)], inc=(st, 1))
            v_mb = ph.op("vector", lambda e, w=w: e.tensor_copy(
                mask_bc[:, w * WIN:(w + 1) * WIN], ps_c[:, 0:WIN]),
                waits=[(st, tmm)], inc=(sv, 1))

        if getattr(self, "dbg_ffn", None):
            ph.dma("sync", self.dbg_ffn["mask_bc"], mask_bc[:], waits=[(sv, v_mb)])
            ph.dma("sync", self.dbg_ffn["maskrow"], self.maskrow[:])
        h_st = [(qs, 0)] * NFC
        w1ld = [(qg, 0), (qg, 0)]
        w1war = [[], []]
        w1ld[0] = ph.dma("gpsimd", w1t[0][:],
                         rr(self.inp["w1p"], "f p dc e -> f p (dc e)")[0], sem="f0w10")
        hst_war = [[], []]
        gel = [0, 0]
        for fc in range(NFC):
            r = fc % 2
            if fc + 1 < NFC:
                w1ld[1 - r] = ph.dma("gpsimd", w1t[1 - r][:],
                                     rr(self.inp["w1p"], "f p dc e -> f p (dc e)")[fc + 1],
                                     waits=w1war[1 - r], sem=f"f0w1{1 - r}")
            for w in range(NW):
                g = w % 2
                for dc in range(NDC):
                    ph.op("tensor", lambda e, fc=fc, w=w, dc=dc, r=r, g=g: e.matmul(
                        ps_h[:, g * WIN:(g + 1) * WIN],
                        w1t[r][:, dc * 128:(dc + 1) * 128],
                        biga[:, dc * T + w * WIN: dc * T + (w + 1) * WIN],
                        start=(dc == 0), stop=(dc == NDC - 1)),
                        waits=[w1ld[r], (ss, gel[g])] if dc == 0 else (),
                        inc=(st, 1) if dc == NDC - 1 else None)
                mmv = st.v
                gel[g] = ph.op("scalar", lambda e, fc=fc, w=w, r=r, g=g: e.activation(
                    out=hstg[r][:, w * WIN:(w + 1) * WIN], in_=ps_h[:, g * WIN:(g + 1) * WIN],
                    func=AF.Gelu_apprx_tanh, bias=b1c[:, fc:fc + 1]),
                    waits=[(st, mmv), (sv, v2)] + (hst_war[r] if w == 0 else []),
                    inc=(ss, 1))
            w1war[r] = [(st, st.v)]
            hw = ph.dma("sync", h_dram[fc], hstg[r][:], waits=[(ss, gel[0]), (ss, gel[1])],
                        sem=f"f0h{r}")
            h_st[fc] = hw
            hst_war[r] = [hw]

        ph.fence()
        ph.run()
        es_a.close()
        ph = Ph(self)
        h_all = [(self.dsem("f0h0"), self.dsem("f0h0").v),
                 (self.dsem("f0h1"), self.dsem("f0h1").v)]
        w2t = [self.sb(es, f"f0_w2{i}", [128, WIN], BF16) for i in range(6)]
        ht = [self.sb(es, f"f0_ht{i}", [128, 1024], BF16) for i in range(6)]
        stg = [self.sb(es, f"f0_s{i}", [128, WIN], F32) for i in range(2)]
        es_b = ExitStack()
        ps8 = self.ps(es_b, "f0_p8", (128, 4096), F32)
        self._f0_ev = 0
        w2war = [[] for _ in range(6)]
        htwar = [[] for _ in range(6)]
        stg_war = [[], []]
        for dq in range(4):
            for wp in range(2):
                evacs = st.v
                for fc in range(NFC):
                    r3 = fc % 6
                    w2ld = ph.dma("gpsimd", w2t[r3][:],
                                  self.inp["w2p"][fc][:, dq * WIN:(dq + 1) * WIN],
                                  waits=w2war[r3], sem=f"f0w2{r3}")
                    htld = ph.dma("sync", ht[r3][:],
                                  h_dram[fc][:, wp * 1024:(wp + 1) * 1024],
                                  waits=htwar[r3] + (h_all if fc < 6 else []),
                                  sem=f"f0ht{r3}")
                    for dcl in range(4):
                        for wl in range(2):
                            ph.op("tensor", lambda e, fc=fc, dcl=dcl, wl=wl, r3=r3: e.matmul(
                                ps8[:, (dcl * 2 + wl) * WIN:(dcl * 2 + wl + 1) * WIN],
                                w2t[r3][:, dcl * 128:(dcl + 1) * 128],
                                ht[r3][:, wl * WIN:(wl + 1) * WIN],
                                start=(fc == 0), stop=(fc == NFC - 1)),
                                waits=([w2ld, htld] +
                                       ([(sv, self._f0_ev)] if fc == 0 else []))
                                      if (dcl == 0 and wl == 0) else (),
                                inc=(st, 1) if (dcl == 3 and wl == 1) else None)
                    w2war[r3] = [(st, st.v)]
                    htwar[r3] = [(st, st.v)]
                mmv = st.v
                for dcl in range(4):
                    for wl in range(2):
                        dc = dq * 4 + dcl
                        w = wp * 2 + wl
                        g = (dcl * 2 + wl) % 2
                        ev = ph.op("vector", lambda e, dcl=dcl, wl=wl, dc=dc, w=w, g=g:
                                   e.scalar_tensor_tensor(
                                       out=stg[g][:],
                                       in0=ps8[:, (dcl * 2 + wl) * WIN:(dcl * 2 + wl + 1) * WIN],
                                       scalar=b2c[:, dc:dc + 1],
                                       in1=mask_bc[:, w * WIN:(w + 1) * WIN],
                                       op0=ALU.add, op1=ALU.mult),
                                   waits=[(st, mmv), (sv, v_mb)] + stg_war[g],
                                   inc=(sv, 1))
                        sw = ph.dma("sync",
                                    x1T_dram[dc * 128:(dc + 1) * 128, w * WIN:(w + 1) * WIN],
                                    stg[g][:], waits=[(sv, ev)], sem=f"f0s{g}")
                        stg_war[g] = [sw]
                self._f0_ev = sv.v
        ph.fence()
        ph.run()
        es_b.close()
        es.close()

    def emit_gla(self, biga, ybuf):
        sv, ss, st, qs, qg = self.sv, self.ss, self.st, self.qs, self.qg
        es = ExitStack()
        ph = Ph(self)
        wring = [[self.sb(es, f"g_w{j}{r}", [128, 2048], BF16) for j in range(4)]
                 for r in range(2)]
        sA = self.sb(es, "g_sA", [128, WIN], F32)
        sB = self.sb(es, "g_sB", [128, WIN], F32)
        sD = self.sb(es, "g_sD", [128, WIN], F32)
        sC = self.sb(es, "g_sC", [128, WIN], F32)
        sF = sC
        ktf = sB
        kpw = self.sb(es, "g_kpw", [128, WIN], BF16)
        eend = [self.sb(es, f"g_ee{u}", [128, 64], F32) for u in range(2)]
        qt = [self.sb(es, f"g_qt{u}", [128, T], BF16) for u in range(2)]
        ktb = [self.sb(es, f"g_ktb{u}", [128, T], BF16) for u in range(2)]
        kptok = [self.sb(es, f"g_kpt{u}", [128, T], BF16) for u in range(2)]
        v_sb = [self.sb(es, f"g_v{u}", [128, T], BF16) for u in range(2)]
        S_f = [[self.sb(es, f"g_S{u}{i}", [128, 128], F32) for i in range(2)]
               for u in range(2)]
        S_b = [self.sb(es, f"g_Sb{u}", [128, 128], BF16) for u in range(2)]
        Mm = [self.sb(es, f"g_Mm{u}", [128, 128], BF16) for u in range(2)]
        ps_a = self.ps(es, "g_pa")
        ps_b = self.ps(es, "g_pb")
        ps_t = self.ps(es, "g_pt", (128, 2048), BF16)
        ps_y = self.ps(es, "g_py", (128, 1024), F32)
        ps_o = self.ps(es, "g_po", (128, 1024), F32)

        wnames = ("wg1", "wq1", "wk1", "wv1")
        wld = [(qg, 0), (qg, 0)]
        wfree = [[], []]
        for j, nm in enumerate(wnames):
            wld[0] = ph.dma("gpsimd", wring[0][j][:],
                            rr(self.inp[nm], "h p dc e -> h p (dc e)")[0], sem="gw0")
        for j, nm in enumerate(wnames):
            wld[1] = ph.dma("gpsimd", wring[1][j][:],
                            rr(self.inp[nm], "h p dc e -> h p (dc e)")[1], sem="gw1")

        def eend_ap(u, w):
            hdl = eend[u][:].tensor
            return bass.AP(hdl, w * 16, [[64, 128], [1, 16], [0, 32]])

        self._g_pp = [[], []]
        self._g_scr = [0, 0, 0]
        self._g_tp = [0, 0]

        def emit_prep(h, u):
            r = h % 2
            wg, wq, wk, wv = wring[r]
            scan_done = 0
            for w in range(NW):
                bank, bank2 = (ps_a, ps_b) if w % 2 == 0 else (ps_b, ps_a)
                bsl, bsl2 = (0, 1) if w % 2 == 0 else (1, 0)
                for dc in range(NDC):
                    ph.op("tensor", lambda e, dc=dc, w=w, wg=wg, bank=bank: e.matmul(
                        bank[:, 0:WIN], wg[:, dc * 128:(dc + 1) * 128],
                        biga[:, dc * T + w * WIN: dc * T + (w + 1) * WIN],
                        start=(dc == 0), stop=(dc == NDC - 1)),
                        waits=([wld[r]] + self._g_pp[bsl]) if dc == 0 else (),
                        inc=(st, 1) if dc == NDC - 1 else None)
                mmv = st.v
                vzm = ph.op("vector", lambda e, bank=bank: e.tensor_scalar_min(
                    sA[:], bank[:, 0:WIN], 0.0),
                    waits=[(st, mmv), (ss, self._g_scr[0])], inc=(sv, 1))
                saz = ph.op("scalar", lambda e, bank=bank: e.activation(
                    out=sB[:], in_=bank[:, 0:WIN], func=AF.Abs),
                    waits=[(st, mmv), (sv, vzm)], inc=(ss, 1))
                self._g_pp[bsl] = [(ss, saz)]
                ph.op("scalar", lambda e: e.activation(out=sC[:], in_=sB[:], func=AF.Exp,
                                                       scale=-1.0))
                slb = ph.op("scalar", lambda e: e.activation(
                    out=sB[:], in_=sC[:], func=AF.Ln, bias=self.fc[:, 2:3]), inc=(ss, 1))
                self._g_scr[0] = slb
                vg_ = ph.op("vector", lambda e: e.tensor_sub(sF[:], sA[:], sB[:]),
                            waits=[(ss, slb), (sv, scan_done)], inc=(sv, 1))
                scan_done = ph.op("vector", lambda e: e.tensor_tensor_scan(
                    out=sD[:], data0=self.scanmask[:], data1=sF[:], initial=0.0,
                    op0=ALU.mult, op1=ALU.add),
                    waits=[(ss, self._g_scr[1])], inc=(sv, 1))
                sEe = ph.op("scalar", lambda e, u=u, w=w: e.activation(
                    out=eend[u][:, w * 16:(w + 1) * 16], in_=sD[:, 31:WIN:GLA_C],
                    func=AF.Exp), waits=[(sv, scan_done)], inc=(ss, 1))
                for dc in range(NDC):
                    ph.op("tensor", lambda e, dc=dc, w=w, wq=wq, bank2=bank2: e.matmul(
                        bank2[:, 0:WIN], wq[:, dc * 128:(dc + 1) * 128],
                        biga[:, dc * T + w * WIN: dc * T + (w + 1) * WIN],
                        start=(dc == 0), stop=(dc == NDC - 1)),
                        waits=self._g_pp[bsl2] if dc == 0 else (),
                        inc=(st, 1) if dc == NDC - 1 else None)
                mmq = st.v
                sEw = ph.op("scalar", lambda e: e.activation(
                    out=sC[:], in_=sD[:], func=AF.Exp),
                    waits=[(sv, scan_done), (sv, self._g_scr[2])], inc=(ss, 1))
                vq = ph.op("vector", lambda e, u=u, w=w, bank2=bank2: e.scalar_tensor_tensor(
                    out=qt[u][:, w * WIN:(w + 1) * WIN], in0=bank2[:, 0:WIN],
                    scalar=INV_SQRT_DH, in1=sC[:], op0=ALU.mult, op1=ALU.mult),
                    waits=[(st, mmq), (ss, sEw)], inc=(sv, 1))
                self._g_pp[bsl2] = [(sv, vq)]
                for dc in range(NDC):
                    ph.op("tensor", lambda e, dc=dc, w=w, wk=wk, bank=bank: e.matmul(
                        bank[:, 0:WIN], wk[:, dc * 128:(dc + 1) * 128],
                        biga[:, dc * T + w * WIN: dc * T + (w + 1) * WIN],
                        start=(dc == 0), stop=(dc == NDC - 1)),
                        waits=self._g_pp[bsl] if dc == 0 else (),
                        inc=(st, 1) if dc == NDC - 1 else None)
                mmk = st.v
                sEm = ph.op("scalar", lambda e: e.activation(
                    out=sC[:], in_=sD[:], func=AF.Exp, scale=-1.0),
                    waits=[(sv, vq)], inc=(ss, 1))
                self._g_scr[1] = sEm
                vktf = ph.op("vector", lambda e, bank=bank: e.tensor_mul(
                    ktf[:], bank[:, 0:WIN], sC[:]),
                    waits=[(st, mmk), (ss, sEm)], inc=(sv, 1))
                self._g_pp[bsl] = [(sv, vktf)]
                self._g_scr[2] = vktf
                ph.op("vector", lambda e, u=u, w=w: e.tensor_copy(
                    ktb[u][:, w * WIN:(w + 1) * WIN], ktf[:]))
                vkp = ph.op("vector", lambda e, u=u, w=w: e.tensor_tensor(
                    out=kpw[:], in0=ktf[:], in1=eend_ap(u, w), op=ALU.mult),
                    waits=[(ss, sEe), (sv, self._g_tp[1])], inc=(sv, 1))
                for j4 in range(4):
                    tc = w * 4 + j4
                    g2 = j4 % 2
                    tmm = ph.op("tensor", lambda e, j4=j4, g2=g2: e.transpose(
                        ps_t[:, g2 * 1024:g2 * 1024 + 128],
                        kpw[:, j4 * 128:(j4 + 1) * 128], self.ident_b[:]),
                        waits=[(sv, vkp)] if j4 == 0 else [(sv, self._g_tp[0])],
                        inc=(st, 1))
                    kd = ph.op("vector", lambda e, u=u, tc=tc, g2=g2: e.tensor_copy(
                        kptok[u][:, tc * 128:(tc + 1) * 128],
                        ps_t[:, g2 * 1024:g2 * 1024 + 128]),
                        waits=[(st, tmm)], inc=(sv, 1))
                    if j4 >= 1:
                        self._g_tp[0] = kd
                self._g_tp[1] = kd
                for tc4 in range(4):
                    tc = w * 4 + tc4
                    for dc in range(NDC):
                        ph.op("tensor", lambda e, tc=tc, tc4=tc4, dc=dc, wv=wv, bank2=bank2: e.matmul(
                            bank2[:, tc4 * 128:(tc4 + 1) * 128],
                            biga[:, dc * T + tc * 128: dc * T + (tc + 1) * 128],
                            wv[:, dc * 128:(dc + 1) * 128],
                            start=(dc == 0), stop=(dc == NDC - 1)),
                            waits=self._g_pp[bsl2] if (dc == 0 and tc4 == 0) else (),
                            inc=(st, 1) if (dc == NDC - 1 and tc4 == 3) else None)
                mmv2 = st.v
                vv = ph.op("vector", lambda e, u=u, w=w, bank2=bank2: e.tensor_copy(
                    v_sb[u][:, w * WIN:(w + 1) * WIN], bank2[:, 0:WIN]),
                    waits=[(st, mmv2)], inc=(sv, 1))
                self._g_pp[bsl2] = [(sv, vv)]
            vS = ph.op("vector", lambda e, u=u: e.memset(S_f[u][1][:], 0.0),
                       waits=[(sv, self._g_Swar[u])], inc=(sv, 1))
            sSb = ph.op("scalar", lambda e, u=u: e.activation(
                out=S_b[u][:], in_=S_f[u][1][:], func=AF.Copy),
                waits=[(sv, vS)], inc=(ss, 1))
            self._g_sbd[u] = sSb
            return sv.v

        self._g_Swar = [0, 0]
        self._g_sbd = [0, 0]
        self._g_mmw = [0, 0]
        self._g_ow = [0, 0]

        def emit_pair(h0, pd):
            y_evac = list(self._g_Swar)
            for tb in range(NTC):
                for u in (0, 1):
                    smm = ph.op("tensor", lambda e, u=u, tb=tb: e.matmul(
                        ps_y[:, u * WIN + 128:u * WIN + 256],
                        ktb[u][:, tb * 128:(tb + 1) * 128],
                        qt[u][:, tb * 128:(tb + 1) * 128], start=True, stop=True),
                        waits=[(sv, self._g_mmw[u]), (sv, pd[u]), (sv, y_evac[u]),
                               (sv, self._g_ow[u])], inc=(st, 1))
                    vMm = ph.op("vector", lambda e, u=u: e.scalar_tensor_tensor(
                        out=Mm[u][:], in0=ps_y[:, u * WIN + 128:u * WIN + 256], scalar=1.0,
                        in1=self.bdtril[:], op0=ALU.mult, op1=ALU.mult),
                        waits=[(st, smm)], inc=(sv, 1))
                    self._g_mmw[u] = vMm
                    ph.op("tensor", lambda e, u=u, tb=tb: e.matmul(
                        ps_y[:, u * WIN:u * WIN + 128],
                        v_sb[u][:, tb * 128:(tb + 1) * 128], Mm[u][:],
                        start=True, stop=False),
                        waits=[(sv, vMm), (sv, y_evac[u])])
                for c4 in range(4):
                    for u in (0, 1):
                        tbc = tb * 128 + c4 * 32
                        chunk = tb * 4 + c4
                        base = c4 * 32
                        tp = (96, 0) if base == 96 else None
                        par = chunk % 2
                        ph.op("tensor", lambda e, u=u, tbc=tbc, c4=c4: e.matmul(
                            ps_y[:, u * WIN + c4 * 32:u * WIN + (c4 + 1) * 32],
                            S_b[u][:], qt[u][:, tbc:tbc + 32],
                            start=False, stop=(c4 == 3)),
                            waits=[(ss, self._g_sbd[u])])
                        omm = ph.op("tensor", lambda e, u=u, tb=tb, base=base, tp=tp: e.matmul(
                            ps_o[:, u * 512:u * 512 + 128],
                            kptok[u][base:base + 32, tb * 128:(tb + 1) * 128],
                            v_sb[u][base:base + 32, tb * 128:(tb + 1) * 128],
                            start=True, stop=True, tile_position=tp),
                            waits=[(sv, self._g_ow[u])], inc=(st, 1))
                        vSn = ph.op("vector", lambda e, u=u, chunk=chunk, par=par: e.scalar_tensor_tensor(
                            out=S_f[u][par][:], in0=S_f[u][1 - par][:],
                            scalar=eend[u][:, chunk:chunk + 1],
                            in1=ps_o[:, u * 512:u * 512 + 128],
                            op0=ALU.mult, op1=ALU.add),
                            waits=[(st, omm)], inc=(sv, 1))
                        self._g_ow[u] = vSn
                        self._g_sbd[u] = ph.op("scalar", lambda e, u=u, par=par: e.activation(
                            out=S_b[u][:], in_=S_f[u][par][:], func=AF.Copy),
                            waits=[(sv, vSn)], inc=(ss, 1))
                for u in (0, 1):
                    y_evac[u] = ph.op("vector", lambda e, u=u, tb=tb, h0=h0: e.tensor_copy(
                        ybuf[:, (h0 + u) * T + tb * 128:(h0 + u) * T + (tb + 1) * 128],
                        ps_y[:, u * WIN:u * WIN + 128]),
                        waits=[(st, st.v)], inc=(sv, 1))
            self._g_Swar = y_evac

        pd = [0, 0]
        for hp in range(NH // 2):
            h0 = 2 * hp
            pd[0] = emit_prep(h0, 0)
            if h0 + 2 < NH:
                for j, nm in enumerate(wnames):
                    wld[0] = ph.dma("gpsimd", wring[0][j][:],
                                    rr(self.inp[nm], "h p dc e -> h p (dc e)")[h0 + 2],
                                    waits=[(st, self.st.v)] if j == 0 else (), sem="gw0")
            pd[1] = emit_prep(h0 + 1, 1)
            if h0 + 3 < NH:
                for j, nm in enumerate(wnames):
                    wld[1] = ph.dma("gpsimd", wring[1][j][:],
                                    rr(self.inp[nm], "h p dc e -> h p (dc e)")[h0 + 3],
                                    waits=[(st, self.st.v)] if j == 0 else (), sem="gw1")
            emit_pair(h0, pd)
        ph.fence()
        ph.run()
        es.close()

    def emit_moe(self, biga, act_sb, x2T_dram):
        sv, ss, st, qs, qg = self.sv, self.ss, self.st, self.qs, self.qg
        es = ExitStack()
        ph = Ph(self)
        ps8 = self.ps(es, "m_p8", (128, 4096), F32)
        glr = [self.sb(es, f"m_glr{i}", [8, 128], F32) for i in range(2)]
        self._m_s0war = 0
        self._m_s8war = 0
        compr = self.sb(es, "m_cpr", [8, T], F32)
        crow = self.sb(es, "m_crw", [1, T], F32)
        dn_sb = self.sb(es, "m_dn", [128, NJC * 2048], BF16)
        gur = [[self.sb(es, f"m_gu{i}{r}", [128, 2048], BF16) for i in range(2)]
               for r in range(2)]
        c_bc = self.sb(es, "m_cbc", [128, T], F32)

        stage = [self.sb(es, f"m_st{i}", [128, WIN], F32) for i in range(4)]
        sA2 = stage[0:2]

        glt_all = self.sb(es, "m_glta", [128, 128], F32)
        prb_all = self.sb(es, "m_prba", [128, 128], F32)
        prb2_all = self.sb(es, "m_prb2a", [128, 128], F32)
        m8_all = self.sb(es, "m_m8a", [128, 128], F32)
        msk_all = glt_all
        cmb_all = self.sb(es, "m_cmba", [128, 128], F32)
        rca = self.sb(es, "m_rca", [128, 64], F32)

        s0 = []
        glwar = [0, 0]
        for tc in range(NTC):
            rg = tc % 2
            glld = ph.dma("sync", glr[rg][:], self.gl_dram[0:8, tc * 128:(tc + 1) * 128],
                          waits=[(self.st, glwar[rg])], sem=f"mgl{rg}")
            tmm = ph.op("tensor", lambda e, rg=rg, tc=tc: e.transpose(
                ps8[0:128, (tc % 4) * 512:(tc % 4) * 512 + 8], glr[rg][:],
                self.ident_f[0:8, 0:8]),
                waits=[glld, (sv, self._m_s0war)], inc=(st, 1))
            glwar[rg] = tmm
            s0.append(ph.op("vector", lambda e, tc=tc: e.tensor_copy(
                glt_all[:, tc * 8:(tc + 1) * 8],
                ps8[0:128, (tc % 4) * 512:(tc % 4) * 512 + 8]),
                waits=[(st, tmm)], inc=(sv, 1)))
            if tc % 4 == 3:
                self._m_s0war = s0[-1]
        s1 = []
        for tc in range(NTC):
            s1.append(ph.op("vector", lambda e, tc=tc: e.reduce_max(
                rca[:, tc * 4:tc * 4 + 1], glt_all[:, tc * 8:(tc + 1) * 8],
                axis=AX.X, negate=True), inc=(sv, 1)))
        s2 = []
        for tc in range(NTC):
            s2.append(ph.op("scalar", lambda e, tc=tc: e.activation(
                out=prb_all[:, tc * 8:(tc + 1) * 8], in_=glt_all[:, tc * 8:(tc + 1) * 8],
                func=AF.Exp, bias=rca[:, tc * 4:tc * 4 + 1],
                accum_out=rca[:, tc * 4 + 1:tc * 4 + 2]),
                waits=[(sv, s1[tc])] if tc == 0 else [(sv, s1[tc])], inc=(ss, 1)))
        ph.op("scalar", lambda e: e.activation(out=rca[:, 3:4], in_=rca[:, 0:1],
                                               func=AF.Copy), inc=(ss, 1))
        s3 = []
        for tc in range(NTC):
            s3.append(ph.op("vector", lambda e, tc=tc: e.reciprocal(
                rca[:, tc * 4 + 2:tc * 4 + 3], rca[:, tc * 4 + 1:tc * 4 + 2]),
                waits=[(ss, s2[tc] + 1)], inc=(sv, 1)))
        s4 = []
        for tc in range(NTC):
            s4.append(ph.op("vector", lambda e, tc=tc: e.tensor_scalar_mul(
                prb2_all[:, tc * 8:(tc + 1) * 8], prb_all[:, tc * 8:(tc + 1) * 8],
                rca[:, tc * 4 + 2:tc * 4 + 3]), inc=(sv, 1)))
        s5 = []
        for tc in range(NTC):
            s5.append(ph.op("vector", lambda e, tc=tc: e.max(
                out=m8_all[:, tc * 8:(tc + 1) * 8], in_=prb2_all[:, tc * 8:(tc + 1) * 8]),
                inc=(sv, 1)))
        for tc in range(NTC):
            ph.op("vector", lambda e, tc=tc: e.tensor_scalar(
                out=msk_all[:, tc * 8:(tc + 1) * 8], in0=prb2_all[:, tc * 8:(tc + 1) * 8],
                scalar1=m8_all[:, tc * 8 + 1:tc * 8 + 2], scalar2=None, op0=ALU.is_ge))
        s7 = []
        for tc in range(NTC):
            s7.append(ph.op("vector", lambda e, tc=tc: e.tensor_mul(
                cmb_all[:, tc * 8:(tc + 1) * 8], prb2_all[:, tc * 8:(tc + 1) * 8],
                msk_all[:, tc * 8:(tc + 1) * 8]), inc=(sv, 1)))
        v_cmb = 0
        for tc in range(NTC):
            tmm = ph.op("tensor", lambda e, tc=tc: e.transpose(
                ps8[0:8, (4 + tc % 4) * 512:(4 + tc % 4) * 512 + 128],
                cmb_all[:, tc * 8:(tc + 1) * 8], self.ident_f[:]),
                waits=[(sv, s7[min(tc + 2, NTC - 1)]), (sv, self._m_s8war)], inc=(st, 1))
            v_cmb = ph.op("vector", lambda e, tc=tc: e.tensor_copy(
                compr[0:8, tc * 128:(tc + 1) * 128],
                ps8[0:8, (4 + tc % 4) * 512:(4 + tc % 4) * 512 + 128]),
                waits=[(st, tmm)], inc=(sv, 1))
            if tc % 4 == 3:
                self._m_s8war = v_cmb
        if getattr(self, "dbg_moe", None):
            ph.dma("sync", self.dbg_moe["compr"], compr[:], waits=[(sv, v_cmb)])

        guw = {"g": ("gup", None), "a": ("gsa", "dsa"), "b": ("gsb", "dsb")}
        experts = [("g", e) for e in range(NE)] + [("a", 0), ("b", 0)]
        dn_ld = 0
        dn_war = []
        gu_ld = [[0, 0], [0, 0]]
        gu_war = [[], []]
        cbc_war = 0
        act_war = [0, 0]
        self._m_ev = [[], [], [], []]
        self._m_bank_war = [[], []]
        act_war2 = [[0, 0], [0, 0]]
        x2_base = [[0] * NW for _ in range(NDC)]

        def gu_ap(kind, e, j):
            if kind == "g":
                return rr(self.inp["gup"], "e f p dc q -> e f p (dc q)")[e][j]
            return rr(self.inp["gsa" if kind == "a" else "gsb"],
                      "f p dc q -> f p (dc q)")[j]

        def dn_ap(kind, e):
            if kind == "g":
                return rr(self.inp["dnp"], "e p f d -> e p (f d)")[e]
            return rr(self.inp["dsa" if kind == "a" else "dsb"], "p f d -> p (f d)")

        for ei, (kind, e) in enumerate(experts):
            routed = kind == "g"
            prev_ps_users = (self._m_ev[0] + self._m_ev[1] + self._m_ev[2] +
                             self._m_ev[3] + [(sv, v_cmb)])
            dn_ld = ph.dma("gpsimd", dn_sb[:], dn_ap(kind, e), waits=dn_war, sem="mdn")
            if routed:
                v_cb = 0
                cld = ph.dma("sync", crow[:], compr[e:e + 1, :],
                             waits=[(sv, v_cmb), (st, cbc_war)], sem="mcrw")
                for w in range(NW):
                    tmm = ph.op("tensor", lambda e_, w=w: e_.matmul(
                        ps8[:, 3 * 512:4 * 512], self.ones_row_f[:],
                        crow[0:1, w * WIN:(w + 1) * WIN], start=True, stop=True),
                        waits=[cld, (st, cbc_war), (sv, v_cb)] +
                              self._m_ev[0] + self._m_ev[1] + self._m_ev[2] +
                              self._m_ev[3], inc=(st, 1))
                    v_cb = ph.op("vector", lambda e_, w=w: e_.tensor_copy(
                        c_bc[:, w * WIN:(w + 1) * WIN], ps8[:, 3 * 512:4 * 512]),
                        waits=[(st, tmm)] + self._m_ev[0] + self._m_ev[1] +
                              self._m_ev[2] + self._m_ev[3], inc=(sv, 1))
                cbc_done = v_cb
            for j in range(NJC):
                r = j % 2
                gu_ld[r][0] = ph.dma("gpsimd", gur[r][0][:], gu_ap(kind, e, j),
                                     waits=gu_war[r], sem=f"mgu{r}")
                gu_ld[r][1] = ph.dma("gpsimd", gur[r][1][:], gu_ap(kind, e, j + NJC),
                                     waits=(), sem=f"mgu{r}")
                for w in range(NW):
                    gb = (w % 2) * 4
                    for half, buf in enumerate(gur[r]):
                        for dc in range(NDC):
                            ph.op("tensor", lambda e_, dc=dc, w=w, buf=buf, half=half, gb=gb: e_.matmul(
                                ps8[:, (gb + half) * WIN:(gb + half + 1) * WIN],
                                buf[:, dc * 128:(dc + 1) * 128],
                                biga[:, dc * T + w * WIN: dc * T + (w + 1) * WIN],
                                start=(dc == 0), stop=(dc == NDC - 1)),
                                waits=([gu_ld[r][half]] +
                                       ([(ss, act_war2[w % 2][0]), (sv, act_war2[w % 2][1])]
                                        if half == 0 else []) +
                                       (prev_ps_users if (j == 0 and w == 0) else []))
                                      if dc == 0 else (),
                                inc=(st, 1) if dc == NDC - 1 else None)
                    mmv = st.v
                    s_si = ph.op("scalar", lambda e_, gb=gb, sAb=(w % 2): e_.activation(
                        out=sA2[sAb][:], in_=ps8[:, gb * WIN:(gb + 1) * WIN], func=AF.Silu),
                        waits=[(st, mmv)] + self._m_ev[w % 2], inc=(ss, 1))
                    v_ac = ph.op("vector", lambda e_, j=j, w=w, gb=gb, sAb=(w % 2): e_.tensor_tensor(
                        out=act_sb[:, j * T + w * WIN: j * T + (w + 1) * WIN],
                        in0=sA2[sAb][:], in1=ps8[:, (gb + 1) * WIN:(gb + 2) * WIN],
                        op=ALU.mult),
                        waits=[(ss, s_si), (st, mmv)], inc=(sv, 1))
                    act_war2[w % 2] = [s_si, v_ac]
                gu_war[r] = [(st, st.v)]
            act_done = sv.v
            for gi, (dq, w) in enumerate([(dq, w) for dq in range(4) for w in range(NW)]):
                bs = (gi % 2) * 4
                for j in range(NJC):
                    for dcl in range(4):
                        dc = dq * 4 + dcl
                        ph.op("tensor", lambda e_, j=j, dcl=dcl, dc=dc, w=w, bs=bs: e_.matmul(
                            ps8[:, (bs + dcl) * WIN:(bs + dcl + 1) * WIN],
                            dn_sb[:, j * 2048 + dc * 128: j * 2048 + (dc + 1) * 128],
                            act_sb[:, j * T + w * WIN: j * T + (w + 1) * WIN],
                            start=(j == 0), stop=(j == NJC - 1)),
                            waits=([(sv, act_done), dn_ld] + self._m_bank_war[gi % 2] +
                                   (self._m_ev[0] + self._m_ev[1] + self._m_ev[2] +
                                    self._m_ev[3] if ei == 0 and gi == 0 else []))
                                  if (j == 0 and dcl == 0) else (),
                            inc=(st, 1) if (j == NJC - 1 and dcl == 3) else None)
                mmv = st.v
                evs = []
                for dcl in range(4):
                    dc = dq * 4 + dcl
                    gsl = dcl
                    if routed:
                        ev = ph.op("vector", lambda e_, dcl=dcl, w=w, gsl=gsl, bs=bs: e_.tensor_tensor(
                            out=stage[gsl][:], in0=ps8[:, (bs + dcl) * WIN:(bs + dcl + 1) * WIN],
                            in1=c_bc[:, w * WIN:(w + 1) * WIN], op=ALU.mult),
                            waits=[(st, mmv), (sv, cbc_done)] + self._m_ev[gsl],
                            inc=(sv, 1))
                    else:
                        ev = ph.op("vector", lambda e_, dcl=dcl, gsl=gsl, bs=bs: e_.tensor_copy(
                            stage[gsl][:], ps8[:, (bs + dcl) * WIN:(bs + dcl + 1) * WIN]),
                            waits=[(st, mmv)] + self._m_ev[gsl], inc=(sv, 1))
                    evs.append(ev)
                    dst = x2T_dram[dc * 128:(dc + 1) * 128, w * WIN:(w + 1) * WIN]
                    if ei == 0:
                        sw = ph.dma("gpsimd", dst, stage[gsl][:], waits=[(sv, ev)],
                                    sem=f"mst{gsl}")
                        x2_base[dc][w] = sw
                    else:
                        sw = ph.dma("gpsimd", dst, stage[gsl][:],
                                    waits=[(sv, ev), x2_base[dc][w]],
                                    accum_op=ALU.add, sem=f"mst{gsl}")
                    self._m_ev[gsl] = [sw]
                self._m_bank_war[gi % 2] = [(sv, evs[-1])]
            dn_war = [(st, st.v)]
            cbc_war = st.v
        ph.fence()
        ph.run()
        es.close()

    def emit_oproj1(self, xlnT_dram, wo_area, x1T_dram, x2T_dram):
        sv, ss, st, qs, qg = self.sv, self.ss, self.st, self.qs, self.qg
        es = ExitStack()
        ph = Ph(self)
        xl = [self.sb(es, f"o1_x{i}", [128, 16 * WIN], BF16) for i in range(2)]
        x1s = [self.sb(es, f"o1_r{i}", [128, WIN], F32) for i in range(2)]
        stage = [self.sb(es, f"o1_s{i}", [128, WIN], F32) for i in range(2)]
        ps_o = [self.ps(es, f"o1_p{i}") for i in range(2)]
        for dic in range(NDC):
            wlds = ph.dma("gpsimd", wo_area[:, dic * 2048:(dic + 1) * 2048],
                          self.inp["wo1"][dic], sem="o1wo")
        xl_war = [[], []]
        x1_war = [[], []]
        stage_war = [[], []]
        evac = [0, 0]
        for w in range(NW):
            rw = w % 2
            xld = ph.dma("sync", rr(xl[rw][:], "p (dic t) -> p dic t", dic=16),
                         rr(xlnT_dram, "(dic p) t -> p dic t", p=128)[:, :, w * WIN:(w + 1) * WIN],
                         waits=xl_war[rw], sem=f"o1x{rw}")
            for dcp in range(NDC):
                g = (w * NDC + dcp) % 2
                rx = dcp % 2
                x1d = ph.dma("sync", x1s[rx][:],
                             x1T_dram[dcp * 128:(dcp + 1) * 128, w * WIN:(w + 1) * WIN],
                             waits=x1_war[rx], sem=f"o1r{rx}")
                for dic in range(NDC):
                    ph.op("tensor", lambda e, dic=dic, dcp=dcp, w=w, g=g, rw=rw: e.matmul(
                        ps_o[g][:, 0:WIN],
                        wo_area[:, dic * 2048 + dcp * 128: dic * 2048 + (dcp + 1) * 128],
                        xl[rw][:, dic * WIN:(dic + 1) * WIN],
                        start=(dic == 0), stop=(dic == NDC - 1)),
                        waits=[wlds, xld, (sv, evac[g])] if dic == 0 else (),
                        inc=(st, 1) if dic == NDC - 1 else None)
                mmv = st.v
                ev = ph.op("vector", lambda e, g=g, rx=rx: e.tensor_tensor(
                    out=stage[g][:], in0=ps_o[g][:, 0:WIN], in1=x1s[rx][:], op=ALU.add),
                    waits=[(st, mmv), x1d] + stage_war[g], inc=(sv, 1))
                evac[g] = ev
                x1_war[rx] = [(sv, ev)]
                sw = ph.dma("gpsimd",
                            x2T_dram[dcp * 128:(dcp + 1) * 128, w * WIN:(w + 1) * WIN],
                            stage[g][:], waits=[(sv, ev)], accum_op=ALU.add,
                            sem=f"o1s{g}")
                stage_war[g] = [sw]
            xl_war[rw] = [(st, st.v)]
        ph.fence()
        ph.run()
        es.close()

    def emit_out(self, x2T_dram, out_tok):
        sv, st, qs = self.sv, self.st, self.qs
        es = ExitStack()
        ph = Ph(self)
        xin = [self.sb(es, f"fo_x{i}", [128, T], F32) for i in range(2)]
        stg = [self.sb(es, f"fo_s{i}", [128, 2048], F32) for i in range(2)]
        pt = [self.ps(es, f"fo_p{i}", (128, 1024), F32) for i in range(2)]
        evac_done = [0, 0]
        store_done = [(qs, 0), (qs, 0)]
        xin_war = [0, 0]
        for dc in range(NDC):
            r = dc % 2
            wld = ph.dma("sync", xin[r][:], x2T_dram[dc * 128:(dc + 1) * 128, :],
                         waits=[(st, xin_war[r])], sem=f"fo{r}")
            for half in range(2):
                for tc8 in range(8):
                    tc = half * 8 + tc8
                    w0 = []
                    if tc == 0:
                        w0.append(wld)
                    if tc8 == 0:
                        w0.append((sv, evac_done[r]))
                    ph.op("tensor", lambda e, r=r, tc=tc, tc8=tc8: e.transpose(
                        pt[r][:, tc8 * 128:(tc8 + 1) * 128],
                        xin[r][:, tc * 128:(tc + 1) * 128], self.ident_f[:]),
                        waits=w0, inc=(st, 1) if tc8 == 7 else None)
                tmm = st.v
                evac_done[r] = ph.op("vector", lambda e, r=r, half=half: e.tensor_copy(
                    stg[r][:, half * 1024:(half + 1) * 1024], pt[r][:]),
                    waits=[(st, tmm), store_done[r]], inc=(sv, 1))
            xin_war[r] = st.v
            store_done[r] = ph.dma(
                "sync",
                rr(out_tok, "(tc p) d -> p tc d", p=128)[:, :, dc * 128:(dc + 1) * 128],
                rr(stg[r][:], "p (tc t) -> p tc t", tc=16),
                waits=[(sv, evac_done[r])], sem=f"fos{r}")
        ph.fence()
        ph.run()
        es.close()



def build_full():
    b = Build()
    x_in = b.din("x", [T, D], F32)
    for nm, shp in (("wq0", [16, 128, 16, 128]), ("wk0", [16, 128, 16, 128]),
                    ("wv0", [16, 128, 16, 128]), ("wo0", [16, 128, 16, 128]),
                    ("w1p", [64, 128, 16, 128]), ("w2p", [64, 128, 2048]),
                    ("b1p", [64, 128]), ("b2p", [16, 128]), ("wrp", [16, 128]),
                    ("brp", [1, 1]),
                    ("wq1", [16, 128, 16, 128]), ("wk1", [16, 128, 16, 128]),
                    ("wv1", [16, 128, 16, 128]), ("wg1", [16, 128, 16, 128]),
                    ("wo1", [16, 128, 2048]), ("gmp", [16, 128]), ("btp", [16, 128]),
                    ("gwp", [128, 16, 8]), ("gup", [8, 22, 128, 16, 128]),
                    ("gsa", [22, 128, 16, 128]), ("gsb", [22, 128, 16, 128]),
                    ("dnp", [8, 128, 11, 2048]), ("dsa", [128, 11, 2048]),
                    ("dsb", [128, 11, 2048])):
        b.din(nm, shp, F32)
    x0T = b.dtmp("x0T", [D, T], F32)
    x1T = b.dtmp("x1T", [D, T], F32)
    x2T = b.dtmp("x2T", [D, T], F32)
    h_d = b.dtmp("h_d", [64, 128, 2048], BF16)
    xlnT = b.dtmp("xlnT", [D, T], BF16)
    out = b.dout("out", [T, D], F32)
    biga = b.sb(b.estack, "BIGA", [128, NDC * T], BF16)

    b.emit_consts()
    b.bstack = ExitStack()
    b.emit_transpose_in("b0", x_in, x0T)
    b.emit_ln("b0", x0T, biga, router=True)
    zstack = ExitStack()
    z_sb = b.sb(zstack, "ZBUF", [128, NH * T], BF16)
    b.emit_attn0(biga, z_sb)
    b.emit_ffn0(biga, h_d, x1T)
    b.emit_oproj0(z_sb, biga, x0T, x1T, accum=True)
    zstack.close()
    b.bstack.close()
    b.bstack = ExitStack()
    b.emit_ln("b1", x1T, biga, gate=True)
    ystack = ExitStack()
    ybuf = b.sb(ystack, "YBUF", [128, NH * T], BF16)
    b.emit_gla(biga, ybuf)
    b.emit_ln("yg", None, None, src_bf16=ybuf, gb=True, out_dram=xlnT)
    ystack.close()
    astack = ExitStack()
    act_sb = b.sb(astack, "ACT", [128, NJC * T], BF16)
    b.emit_moe(biga, act_sb, x2T)
    astack.close()
    b.emit_oproj1(xlnT, biga, x1T, x2T)
    b.bstack.close()
    b.emit_out(x2T, out)
    b.estack.close()
    return b


def kernel(**inputs):
    pk = pack_inputs(inputs)
    x = np.ascontiguousarray(inputs["L0_attn_x"].astype(np.float32))
    b = build_full()
    in_maps = []
    for c in range(8):
        m = dict(pk)
        m["x"] = np.ascontiguousarray(x[c])
        in_maps.append(m)
    res = run_bass_kernel_spmd(b.nc, in_maps, core_ids=list(range(8)))
    return np.stack([np.asarray(res.results[c]["out"]) for c in range(8)], axis=0)
